# version 05
# baseline (speedup 1.0000x reference)
import numpy as np
import ml_dtypes
import concourse.bass as bass
import concourse.mybir as mybir
from concourse.bass_utils import run_bass_kernel_spmd

F32 = mybir.dt.float32
BF16 = mybir.dt.bfloat16
AF = mybir.ActivationFunctionType
ALU = mybir.AluOpType
BF = ml_dtypes.bfloat16

D = 1024; TC = 2048; SLAB = 512; NSLAB = 4; TT = 128; NTT = 16
CH = 128; NCH = 16
H = 32; PH = 64; NST = 128; DIN = 2048; CONVD = 2304; NCT = 18
DZ = 4384; DZP = 4480; MZ = 35
NS = 64; DH = 512; DQ = 256
EPS = 1e-5

DBG = []
STAGE = 99
DEBUG_P1 = False
GROUPS = [[0, 1], [2, 3], [4, 5], [6, 7]]


def _core_constants(u):
    t0 = u * TC
    pos = np.arange(t0, t0 + TC, dtype=np.float64)
    inv_freq = 1.0 / 10000.0 ** (np.arange(0, 64, 2, dtype=np.float64) / 64)
    fr = pos[:, None] * inv_freq[None, :]
    cos = np.cos(fr); sin = np.sin(fr)
    rope_cos = np.concatenate([cos, cos], axis=1).astype(np.float32)
    rope_sin = np.concatenate([-sin, sin], axis=1).astype(np.float32)
    return rope_cos, rope_sin

def _shared_constants():
    c = {}
    sel0 = np.zeros((32, 2048), np.float32); sel1 = np.zeros((32, 2048), np.float32)
    for h in range(16):
        sel0[h, h * 128:(h + 1) * 128] = 1.0
        sel1[16 + h, h * 128:(h + 1) * 128] = 1.0
    c['sel_h0'] = sel0.astype(BF); c['sel_h1'] = sel1.astype(BF)
    s32 = np.zeros((32, 2048), np.float32)
    for h in range(32):
        s32[h, h * 64:(h + 1) * 64] = 1.0
    c['sel32'] = s32.astype(BF)
    tl = np.zeros((127, 128), np.float32); tr = np.zeros((127, 2048), np.float32)
    for j in range(127):
        tl[j, j + 1:] = 1.0
        tr[j, j::128] = -1e30
    c['trilL'] = tl.astype(BF); c['trilR'] = tr.astype(BF)
    c['sel64'] = np.vstack([s32, s32]).astype(BF)
    c['ident'] = np.eye(128, dtype=np.float32).astype(BF)
    c['identf'] = np.eye(128, dtype=np.float32)
    c['ones32b'] = np.ones((32, 128), np.float32).astype(BF)
    c['ones_colb'] = np.ones((128, 1), np.float32).astype(BF)
    c['ones_rowf'] = np.ones((1, 128), np.float32)
    return c

def _per_core_inputs(core, inp):
    b, u = core // 2, core % 2
    t0 = u * TC
    m = {}
    x = np.asarray(inp['L0_attn_x'])[b, t0:t0 + TC]
    m['x_tok'] = np.ascontiguousarray(x)
    m['xT'] = np.ascontiguousarray(x.T)
    m['wmod'] = np.ascontiguousarray(np.asarray(inp['L0_wmod']))
    m['c_col'] = np.ascontiguousarray(np.asarray(inp['L0_c'])[b].reshape(8, 128).T)
    m['wqT'] = np.ascontiguousarray(np.asarray(inp['L0_attn_W_in'])[:1024, :].T)
    wz = np.asarray(inp['L0_attn_W_zxbcdt'])
    wzT = np.zeros((1024, DZP), np.float32); wzT[:, :DZ] = wz.T
    m['wzT'] = wzT
    m['woutT'] = np.ascontiguousarray(np.asarray(inp['L0_attn_W_out']).T)
    cw = np.asarray(inp['L0_attn_conv_w'])
    m['conv_w'] = np.ascontiguousarray(cw.reshape(NCT, 128, 4).transpose(1, 0, 2).reshape(128, NCT * 4))
    m['conv_b'] = np.ascontiguousarray(np.asarray(inp['L0_attn_conv_b']).reshape(NCT, 128).T)
    m['dtb_col'] = np.asarray(inp['L0_attn_dt_bias']).reshape(32, 1).astype(np.float32)
    m['alog_col'] = np.asarray(inp['L0_attn_A_log']).reshape(32, 1).astype(np.float32)
    m['d_hp'] = np.ascontiguousarray(np.repeat(np.asarray(inp['L0_attn_D_param']), 64).reshape(16, 128).T)
    m['npw_col'] = np.ascontiguousarray(np.asarray(inp['L0_attn_norm_pre_w']).reshape(8, 128).T)
    m['ngw_col'] = np.ascontiguousarray(np.asarray(inp['L0_attn_norm_gated_w']).reshape(16, 128).T)
    m['w1'] = np.ascontiguousarray(np.asarray(inp['L0_ffn_w1']))
    m['w2'] = np.ascontiguousarray(np.asarray(inp['L0_ffn_w2']))
    m['wihT'] = np.ascontiguousarray(np.asarray(inp['L0_ffn_Wih']).T)
    m['whhT'] = np.ascontiguousarray(np.asarray(inp['L0_ffn_Whh']).T)
    m['slot_init'] = np.ascontiguousarray(np.asarray(inp['L0_ffn_slot_init']))
    m['slot_initT'] = np.ascontiguousarray(np.asarray(inp['L0_ffn_slot_init']).T)
    m['wqf'] = np.ascontiguousarray(np.asarray(inp['L0_ffn_wq']))
    m['wkf'] = np.ascontiguousarray(np.asarray(inp['L0_ffn_wk']))
    m['wvf'] = np.ascontiguousarray(np.asarray(inp['L0_ffn_wv']))
    m['wvp'] = np.ascontiguousarray(np.asarray(inp['L0_ffn_wvp']))
    m['wof'] = np.ascontiguousarray(np.asarray(inp['L0_ffn_wo']))
    m['b1_col'] = np.ascontiguousarray(np.asarray(inp['L0_ffn_b1']).reshape(4, 128).T)
    m['b2_row'] = np.asarray(inp['L0_ffn_b2']).reshape(1, 64).astype(np.float32)
    m['tau'] = np.asarray(inp['L0_ffn_tau']).reshape(1, 1).astype(np.float32)
    m['bih_row'] = np.asarray(inp['L0_ffn_bih']).reshape(1, 3072).astype(np.float32)
    m['bhh_row'] = np.asarray(inp['L0_ffn_bhh']).reshape(1, 3072).astype(np.float32)
    m['bq_col'] = np.ascontiguousarray(np.asarray(inp['L0_ffn_bq']).reshape(2, 128).T)
    m['bk_col'] = np.ascontiguousarray(np.asarray(inp['L0_ffn_bk']).reshape(2, 128).T)
    m['bv_row'] = np.asarray(inp['L0_ffn_bv']).reshape(1, 1024).astype(np.float32)
    m['lng_col'] = np.ascontiguousarray(np.asarray(inp['L0_ffn_ln_g']).reshape(8, 128).T)
    m['lnb_col'] = np.ascontiguousarray(np.asarray(inp['L0_ffn_ln_b']).reshape(8, 128).T)
    m['bvp_col'] = np.ascontiguousarray(np.asarray(inp['L0_ffn_bvp']).reshape(8, 128).T)
    m['bo_row'] = np.asarray(inp['L0_ffn_bo']).reshape(1, 1024).astype(np.float32)
    rc, rs = _core_constants(u)
    m['rope_cos'] = np.ascontiguousarray(rc.reshape(16, 128, 64).transpose(1, 0, 2).reshape(128, 1024))
    m['rope_sin'] = np.ascontiguousarray(rs.reshape(16, 128, 64).transpose(1, 0, 2).reshape(128, 1024))
    m.update(_shared_constants())
    m['g_state'] = np.full((128, 1), 1.0 if u == 0 else 0.0, np.float32)
    m['g_use'] = np.full((128, 1), 0.0 if u == 0 else 1.0, np.float32)
    return m


class Ctx:
    def __init__(self, nc, stack):
        self.nc = nc
        self.ops = {k: [] for k in ('sync', 'tensor', 'vector', 'scalar', 'gpsimd')}
        self.cnt = {}
        self.sems = {}
        self.sb = {}
        self.ps = {}
        self.dr = {}
        self.dma_sems = set()
        self.vlast_w = {}
        self.vlast_r = {}
        self.vwaited = 0
        self._stack = stack

    def sem(self, name):
        if name not in self.sems:
            self.sems[name] = self._stack.enter_context(self.nc.semaphore(name))
            self.cnt[name] = 0
        return self.sems[name]

    def op(self, eng, builder, sem=None, amt=1):
        v = None
        if sem is not None:
            self.sem(sem)
            self.cnt[sem] += amt
            v = self.cnt[sem]
        def f(e, b=builder, s=sem, a=amt):
            ins = b(e)
            if s is not None:
                ins.then_inc(self.sems[s], a)
        self.ops[eng].append(f)
        return v

    def wait(self, eng, sem, val):
        if val is None or val <= 0:
            return
        self.sem(sem)
        assert val <= self.cnt[sem], (eng, sem, val, self.cnt[sem])
        def f(e, s=sem, v=val):
            e.wait_ge(self.sems[s], v)
        self.ops[eng].append(f)

    def phase(self, tag):
        import contextlib
        ctx = self
        class _Ph:
            def __init__(self):
                self.stack = contextlib.ExitStack()
                self.names = []
            def sb(self, name, shape, dtype=F32):
                h = self.stack.enter_context(ctx.nc.sbuf_tensor(name, list(shape), dtype))
                ctx.sb[name] = h
                self.names.append(name)
                return h
            def close(self):
                ctx.barrier(tag)
                for n in self.names:
                    ctx.sb.pop(n, None)
                self.stack.close()
        return _Ph()

    def barrier(self, tag):
        engs = ('sync', 'tensor', 'vector', 'scalar', 'gpsimd')
        for sname in sorted(self.dma_sems):
            self.wait('sync', sname, self.cnt.get(sname, 0))
        for e in engs:
            self.op(e, lambda x: x.nop(), sem='bar_' + e)
        tgt = {e: self.cnt['bar_' + e] for e in engs}
        for e in engs:
            for o in engs:
                if o != e:
                    self.wait(e, 'bar_' + o, tgt[o])

    def vop(self, builder, reads, writes, sem=None, amt=1):
        def names(aps):
            out = []
            for a in aps:
                if a is None:
                    continue
                t = getattr(a, 'tensor', None)
                if t is None and hasattr(a, 'ap'):
                    t = getattr(a.ap(), 'tensor', None)
                if t is not None:
                    out.append(t.name)
            return out
        rn = names(reads); wn = names(writes)
        need = 0
        for t in rn:
            need = max(need, self.vlast_w.get(t, 0))
        for t in wn:
            need = max(need, self.vlast_w.get(t, 0), self.vlast_r.get(t, 0))
        if need > self.vwaited:
            self.wait('vector', 'vv', need)
            self.vwaited = need
        v = self.op('vector', builder, sem='vv', amt=amt)
        for t in wn:
            self.vlast_w[t] = v
        for t in rn:
            self.vlast_r[t] = max(self.vlast_r.get(t, 0), v)
        return v

    def dma(self, q, out, in_, sem):
        self.dma_sems.add(sem)
        return self.op(q, lambda e, o=out, i=in_: e.dma_start(out=o, in_=i), sem=sem, amt=16)

    def dma_slow(self, q, out, in_, sem):
        self.dma_sems.add(sem)
        return self.op(q, lambda e, o=out, i=in_:
                       e.dma_start(out=o, in_=i, allow_slow_non_contiguous=True), sem=sem, amt=16)

    def dmat(self, q, out, in_, sem):
        self.dma_sems.add(sem)
        return self.op(q, lambda e, o=out, i=in_: e.dma_start_transpose(o, i), sem=sem, amt=16)

def mm(ctx, out, lhsT, rhs, start, stop, sem=None):
    return ctx.op('tensor', lambda e, o=out, l=lhsT, r=rhs, st=start, sp=stop:
                  e.matmul(o, l, r, start=st, stop=sp), sem=sem)

def act(ctx, out, in_, func, bias=None, scale=None, accum_out=None, sem=None):
    kw = {}
    if bias is not None: kw['bias'] = bias
    if scale is not None: kw['scale'] = scale
    if accum_out is not None: kw['accum_out'] = accum_out
    return ctx.op('scalar', lambda e, o=out, i=in_, f=func, k=dict(kw):
                  e.activation(out=o, in_=i, func=f, **k), sem=sem)

def tt(ctx, out, in0, in1, alu, sem=None):
    return ctx.vop(lambda e, o=out, a=in0, b=in1, p=alu:
                   e.tensor_tensor(out=o, in0=a, in1=b, op=p), [in0, in1], [out])

def ts(ctx, out, in0, s1, op0, s2=None, op1=None, sem=None):
    def b(e, o=out, i=in0, x1=s1, x2=s2, p0=op0, p1=op1):
        if p1 is None:
            return e.tensor_scalar(out=o, in0=i, scalar1=x1, scalar2=None, op0=p0)
        return e.tensor_scalar(out=o, in0=i, scalar1=x1, scalar2=x2, op0=p0, op1=p1)
    rd = [in0] + [x for x in (s1, s2) if not isinstance(x, (int, float, type(None)))]
    return ctx.vop(b, rd, [out])

def stt(ctx, out, in0, scalar, in1, op0, op1, sem=None):
    rd = [in0, in1] + ([scalar] if not isinstance(scalar, (int, float)) else [])
    return ctx.vop(lambda e, o=out, a=in0, s=scalar, b=in1, p0=op0, p1=op1:
                   e.scalar_tensor_tensor(out=o, in0=a, scalar=s, in1=b, op0=p0, op1=p1), rd, [out])

def vcopy(ctx, out, in_, sem=None):
    return ctx.vop(lambda e, o=out, i=in_: e.tensor_copy(o, i), [in_], [out])

def scopy(ctx, out, in_, sem=None):
    return ctx.op('scalar', lambda e, o=out, i=in_: e.copy(o, i), sem=sem)

def vscan(ctx, out, d0, d1, initial, op0, op1, sem=None):
    rd = [d0, d1] + ([initial] if not isinstance(initial, (int, float)) else [])
    return ctx.vop(lambda e, o=out, a=d0, b=d1, i=initial, p0=op0, p1=op1:
                   e.tensor_tensor_scan(o, a, b, i, p0, p1), rd, [out])


def sinv(ctx, out, in_, power=-1.0, sem=None):
    act(ctx, out, in_, AF.Ln)
    return act(ctx, out, out, AF.Exp, scale=power, sem=sem)

def vmemset(ctx, ap, val, sem=None):
    return ctx.vop(lambda e, a=ap, v=val: e.memset(a, v), [], [ap])

def vrecip(ctx, out, in_, sem=None):
    return ctx.vop(lambda e, o=out, i=in_: e.reciprocal(o, i), [in_], [out])

def vmax8(ctx, out, in_, sem=None):
    return ctx.vop(lambda e, o=out, i=in_: e.max(o, i), [in_], [out])

def vreduce(ctx, out, in_, alu, sem=None):
    return ctx.vop(lambda e, o=out, i=in_, p=alu:
                   e.tensor_reduce(out=o, in_=i, axis=mybir.AxisListType.X, op=p), [in_], [out])


def build_nc(dbg=()):
    import contextlib
    nc = bass.Bass("TRN2", target_bir_lowering=False, debug=False, detect_race_conditions=False)
    stack = contextlib.ExitStack()
    c = Ctx(nc, stack)

    def din(name, shape, dtype=F32):
        c.dr[name] = nc.dram_tensor(name, list(shape), dtype, kind="ExternalInput").ap()

    def dtmp(name, shape, dtype=F32):
        kind = "ExternalOutput" if name in dbg else "Internal"
        c.dr[name] = nc.dram_tensor(name, list(shape), dtype, kind=kind).ap()

    def sb(name, shape, dtype=F32):
        c.sb[name] = stack.enter_context(nc.sbuf_tensor(name, list(shape), dtype))

    for n, s in [('x_tok', (TC, D)), ('xT', (D, TC)), ('wmod', (D, 6144)), ('c_col', (128, 8)),
                 ('wqT', (D, D)), ('wzT', (D, DZP)), ('woutT', (DIN, D)),
                 ('conv_w', (128, NCT * 4)), ('conv_b', (128, NCT)),
                 ('dtb_col', (32, 1)), ('alog_col', (32, 1)), ('d_hp', (128, 16)),
                 ('npw_col', (128, 8)), ('ngw_col', (128, 16)),
                 ('w1', (DIN, DH)), ('w2', (DH, NS)), ('wihT', (D, 3072)), ('whhT', (D, 3072)),
                 ('slot_init', (NS, D)), ('slot_initT', (D, NS)),
                 ('wqf', (D, DQ)), ('wkf', (D, DQ)), ('wvf', (D, D)), ('wvp', (D, D)), ('wof', (D, D)),
                 ('b1_col', (128, 4)), ('b2_row', (1, NS)), ('tau', (1, 1)),
                 ('bih_row', (1, 3072)), ('bhh_row', (1, 3072)),
                 ('bq_col', (128, 2)), ('bk_col', (128, 2)), ('bv_row', (1, D)),
                 ('lng_col', (128, 8)), ('lnb_col', (128, 8)), ('bvp_col', (128, 8)), ('bo_row', (1, D)),
                 ('rope_cos', (128, NTT * 64)), ('rope_sin', (128, NTT * 64)), ('ones_rowf', (1, 128)), ('identf', (128, 128)),
                 ('g_state', (128, 1)), ('g_use', (128, 1))]:
        din(n, s, F32)
    for n, s in [('sel_h0', (32, 2048)), ('sel_h1', (32, 2048)), ('sel32', (32, 2048)),
                 ('trilL', (127, 128)), ('trilR', (127, 2048)), ('ones_colb', (128, 1)),
                 ('sel64', (64, 2048)), ('ones32b', (32, 128)), ('ident', (128, 128))]:
        din(n, s, BF16)

    c.dr['y'] = nc.dram_tensor('y', [TC, D], F32, kind="ExternalOutput").ap()

    dtmp('qhat_d', (TC, D), BF16)
    dtmp('resid_d', (TC, D), F32)
    dtmp('xconv_d', (CONVD, TC), BF16)
    dtmp('zsilu_d', (DIN, TC), BF16)
    dtmp('ypre_d', (DIN, TC), BF16)
    dtmp('xmid_d', (TC, D), F32)
    dtmp('h2_d', (TC, D), BF16)
    dtmp('acshi_d', (32, TC), BF16)
    dtmp('acslo_d', (32, TC), BF16)
    dtmp('nahi_d', (32, TC), BF16)
    dtmp('nalo_d', (32, TC), BF16)
    dtmp('acsghi_d', (32, TC), BF16)
    dtmp('acsglo_d', (32, TC), BF16)
    dtmp('acs_d', (32, TC), F32)
    dtmp('dt_d', (32, TC), F32)
    dtmp('st_in', (NST, DIN), F32)
    dtmp('st_out', (NST, DIN), F32)
    dtmp('sl_in', (NS, 1025), F32)
    dtmp('sl_out', (NS, 1025), F32)
    dtmp('brow_d', (1, 8192), F32)

    sb('onesf', (1, 128), F32)
    sb('onesb', (128, 1), BF16)
    sb('c_sb', (128, 8), F32)
    sb('siluc', (128, 8), BF16)
    sb('silucf', (128, 8), F32)
    sb('modrow', (1, 6144), F32)
    sb('garow', (1, 2048), F32)
    sb('mcol', (128, 48), F32)
    sb('gab', (128, D), F32); sb('cb1b', (128, D), F32); sb('sbb', (128, D), F32)
    sb('stat_sum', (128, NTT + 16), F32); sb('stat_sq', (128, NTT + 16), F32)
    sb('r1', (128, NTT + 16), F32); sb('nmr1', (128, NTT + 16), F32)
    sb('r2', (128, NTT + 16), F32); sb('sq2', (128, NTT + 16), F32)
    sb('r3', (128, NTT + 16), F32); sb('sq3', (128, NTT + 16), F32)
    sb('tiny', (128, 64), F32)
    sb('tcol', (32, 16), F32)
    sb('alpha', (128, NTT * 64), BF16)
    sb('alphaT', (64, TC), BF16)

    for i in range(8):
        c.ps['pb%d' % i] = stack.enter_context(nc.psum_tensor('pb%d' % i, [128, 512], F32))
    P = [c.ps['pb%d' % i] for i in range(8)]

    A = lambda n: c.sb[n]
    Dr = lambda n: c.dr[n]

    c.dma('sync', A('onesf')[:, :], Dr('ones_rowf')[:, :], 'c0')
    c.dma('sync', A('onesb')[:, :], Dr('ones_colb')[:, :], 'c0')

    def row_bcast(dst_ap, row_ap, n, banks=(4, 5)):
        last = 0
        evs = {b: 0 for b in banks}
        for idx, j in enumerate(range(0, n, 512)):
            w = min(512, n - j)
            bk = banks[idx % len(banks)]
            c.wait('tensor', 'vv', evs[bk])
            vm = mm(c, P[bk][:, 0:w], A('onesf')[0:1, :], row_ap[0:1, j:j + w], start=True, stop=True, sem='mm')
            c.wait('vector', 'mm', vm)
            last = vcopy(c, dst_ap[:, j:j + w], P[bk][:, 0:w], sem='vv')
            evs[bk] = last
        return last

    ph = c.phase('p0')
    ph.sb('wmod_sb', (128, 2 * 512), BF16)
    ph.sb('wmodf_sb', (128, 2 * 512), F32)
    c.dma('sync', A('c_sb')[:, :], Dr('c_col')[:, :], 'c0')
    v_c0 = c.cnt['c0']
    c.wait('scalar', 'c0', v_c0)
    act(c, A('siluc')[:, :], A('c_sb')[:, :], AF.Silu)
    v_siluc = act(c, A('silucf')[:, :], A('c_sb')[:, :], AF.Silu, sem='ss')

    half_free = [0, 0]
    bank_ev = [0, 0]
    vmm = 0
    for n in range(12):
        fp32n = n in (4, 5, 10, 11)
        wmb = A('wmodf_sb') if fp32n else A('wmod_sb')
        lhs = A('silucf') if fp32n else A('siluc')
        bank = P[n % 2]
        for k in range(8):
            hb = (n * 8 + k) % 2
            c.wait('gpsimd', 'mm', half_free[hb])
            vld = c.dma('gpsimd', wmb[:, hb * 512:(hb + 1) * 512],
                        Dr('wmod')[k * 128:(k + 1) * 128, n * 512:(n + 1) * 512], 'wm%d' % hb)
            c.wait('tensor', 'wm%d' % hb, vld)
            if k == 0:
                c.wait('tensor', 'ss', v_siluc)
                c.wait('tensor', 'vv', bank_ev[n % 2])
            vmm = mm(c, bank[0:1, :], lhs[:, k:k + 1], wmb[:, hb * 512:(hb + 1) * 512],
                     start=(k == 0), stop=(k == 7), sem='mm')
            half_free[hb] = vmm
        c.wait('vector', 'mm', vmm)
        bank_ev[n % 2] = vcopy(c, A('modrow')[0:1, n * 512:(n + 1) * 512], bank[0:1, :], sem='vv')
    v_modrow = max(bank_ev)

    c.wait('tensor', 'vv', v_modrow)
    c.wait('tensor', 'c0', v_c0)
    ev23 = {2: 0, 3: 0}
    for a in range(48):
        bk = 2 + (a % 2)
        c.wait('tensor', 'vv', ev23[bk])
        vm = mm(c, P[bk][:, 0:1], A('modrow').ap()[0:1, a * 128:(a + 1) * 128],
                A('onesf')[0:1, 0:1], start=True, stop=True, sem='mm')
        c.wait('vector', 'mm', vm)
        ev23[bk] = vcopy(c, A('mcol')[:, a:a + 1], P[bk][:, 0:1], sem='vv')

    c.wait('scalar', 'vv', max(ev23.values()))
    v_sg = act(c, A('tiny')[:, 0:8], A('mcol')[:, 16:24], AF.Sign, sem='ss')
    v_sg2 = act(c, A('tiny')[:, 8:16], A('mcol')[:, 40:48], AF.Sign, sem='ss')
    v_ab = act(c, A('tiny')[:, 16:24], A('mcol')[:, 16:24], AF.Abs, sem='ss')
    v_ab2 = act(c, A('tiny')[:, 24:32], A('mcol')[:, 40:48], AF.Abs, sem='ss')
    c.wait('vector', 'ss', v_ab2)
    ts(c, A('tiny')[:, 16:24], A('tiny')[:, 16:24], 0.25, ALU.max)
    ts(c, A('tiny')[:, 24:32], A('tiny')[:, 24:32], 0.25, ALU.max)
    tt(c, A('mcol')[:, 16:24], A('tiny')[:, 0:8], A('tiny')[:, 16:24], ALU.mult)
    v_gclamp = tt(c, A('mcol')[:, 40:48], A('tiny')[:, 8:16], A('tiny')[:, 24:32], ALU.mult, sem='vv')

    c.wait('gpsimd', 'vv', v_gclamp)
    c.dma_slow('gpsimd', Dr('brow_d')[0:1, 0:1024].squeeze(0).rearrange("(b p) -> p b", p=128), A('mcol')[:, 16:24], 'gw0')
    vw2 = c.dma_slow('gpsimd', Dr('brow_d')[0:1, 1024:2048].squeeze(0).rearrange("(b p) -> p b", p=128), A('mcol')[:, 40:48], 'gw0')
    c.wait('sync', 'gw0', vw2)
    v_gar = c.dma('sync', A('garow')[0:1, :], Dr('brow_d')[0:1, 0:2048], 'gr0')
    c.wait('tensor', 'gr0', v_gar)

    row_bcast(A('gab').ap(), A('garow').ap()[0:1, 0:1024], D)
    row_bcast(A('cb1b').ap(), A('modrow').ap()[0:1, 4 * 1024:5 * 1024], D)
    ts(c, A('cb1b')[:, :], A('cb1b')[:, :], 1.0, ALU.add)
    row_bcast(A('sbb').ap(), A('modrow').ap()[0:1, 3 * 1024:4 * 1024], D)
    ph.close()

    if STAGE >= 1:
        ph = c.phase('p1')
        _p1(c, P, A, Dr, row_bcast, ph)
        ph.close()
    if STAGE >= 2:
        ph = c.phase('p2')
        _p2(c, P, A, Dr, ph)
        ph.close()
    if STAGE >= 3:
        ph = c.phase('p3')
        _p3(c, P, A, Dr, ph)
        ph.close()
    if STAGE >= 4:
        ph = c.phase('p4')
        _p4(c, P, A, Dr, ph)
        ph.close()
    if STAGE >= 5:
        ph = c.phase('p5')
        _p5(c, P, A, Dr, ph)
        ph.close()
    if STAGE >= 6:
        ph = c.phase('p6')
        _p6(c, P, A, Dr, ph)
        ph.close()
    return _finish(nc, c, stack)


def _p1(c, P, A, Dr, row_bcast, ph):
    ph.sb('v1row', (1, D), F32); ph.sb('v2row', (1, D), F32)
    ph.sb('v1b', (128, D), F32); ph.sb('v2b', (128, D), F32)
    ph.sb('x_sb', (128, 2 * D), F32)
    ph.sb('xT_sb', (128, 2 * 8 * SLAB), BF16)
    ph.sb('wq_sb', (128, 8 * D), BF16)
    ph.sb('sacol', (128, 8), BF16)
    ph.sb('rope_c', (128, NTT * 64), F32)
    ph.sb('rope_s', (128, NTT * 64), F32)
    ph.sb('muv', (128, D), F32)
    ph.sb('qpre', (128, 2 * D), F32)
    ph.sb('ropet', (128, 2 * D), F32)
    ph.sb('qhat_t', (128, 2 * D), BF16)
    ph.sb('resid_t', (128, 2 * D), F32)
    for k in range(8):
        c.dma('gpsimd', A('wq_sb')[:, k * D:(k + 1) * D], Dr('wqT')[k * 128:(k + 1) * 128, :], 'wql')
    v_wq = c.cnt['wql']
    v_sab = vcopy(c, A('sacol')[:, :], A('mcol')[:, 0:8], sem='vv')
    c.wait('tensor', 'vv', v_sab)
    c.wait('tensor', 'wql', v_wq)
    v_v2r = 0
    for j in range(2):
        c.wait('tensor', 'vv', v_v2r)
        for k in range(8):
            vmm = mm(c, P[6][0:1, :], A('sacol')[:, k:k + 1], A('wq_sb')[:, k * D + j * 512:k * D + j * 512 + 512],
                     start=(k == 0), stop=(k == 7), sem='mm')
        c.wait('vector', 'mm', vmm)
        v_v2r = vcopy(c, A('v2row')[0:1, j * 512:(j + 1) * 512], P[6][0:1, :], sem='vv')
    ts(c, A('tiny')[:, 32:40], A('mcol')[:, 8:16], 1.0, ALU.add)
    v_fold = 0
    for k in range(8):
        v_fold = ts(c, A('wq_sb')[:, k * D:(k + 1) * D], A('wq_sb')[:, k * D:(k + 1) * D],
                    A('tiny').ap()[:, 32 + k:33 + k], ALU.mult, sem='vv')
    c.wait('tensor', 'vv', v_fold)
    v_v1r = 0
    for j in range(2):
        c.wait('tensor', 'vv', v_v1r)
        for k in range(8):
            vmm = mm(c, P[7][0:1, :], A('onesb')[:, 0:1], A('wq_sb')[:, k * D + j * 512:k * D + j * 512 + 512],
                     start=(k == 0), stop=(k == 7), sem='mm')
        c.wait('vector', 'mm', vmm)
        v_v1r = vcopy(c, A('v1row')[0:1, j * 512:(j + 1) * 512], P[7][0:1, :], sem='vv')
    row_bcast(A('v1b').ap(), A('v1row').ap(), D, banks=(4, 5))
    v_v2b = row_bcast(A('v2b').ap(), A('v2row').ap(), D, banks=(4, 5))

    c.dma('sync', A('rope_c')[:, :], Dr('rope_cos')[:, :], 'rt')
    v_ropetab = c.dma('sync', A('rope_s')[:, :], Dr('rope_sin')[:, :], 'rt')

    xfree = [0, 0]
    spill_q = [0, 0]
    spill_r = [0, 0]
    qbank_free = [0, 0]
    slab_mm = [0, 0]
    for s in range(NSLAB):
        c.wait('gpsimd', 'mm', slab_mm[s % 2])
        for k in range(8):
            c.dma('gpsimd', A('xT_sb')[:, (s % 2) * 8 * SLAB + k * SLAB:(s % 2) * 8 * SLAB + (k + 1) * SLAB],
                  Dr('xT')[k * 128:(k + 1) * 128, s * SLAB:(s + 1) * SLAB], 'xt%d' % (s % 2))
        v_xT = c.cnt['xt%d' % (s % 2)]
        for tl in range(4):
            t = s * 4 + tl
            hb = t % 2
            c.wait('sync', 'ss', xfree[hb])
            vx = c.dma('sync', A('x_sb')[:, hb * D:(hb + 1) * D], Dr('x_tok')[t * 128:(t + 1) * 128, :], 'xk%d' % hb)
            c.wait('scalar', 'xk%d' % hb, vx)
            act(c, A('x_sb')[:, hb * D:(hb + 1) * D], A('x_sb')[:, hb * D:(hb + 1) * D], AF.Copy,
                accum_out=A('stat_sum').ap()[:, t:t + 1])
            v_stat = act(c, A('x_sb')[:, hb * D:(hb + 1) * D], A('x_sb')[:, hb * D:(hb + 1) * D], AF.Square,
                         accum_out=A('stat_sq').ap()[:, t:t + 1], sem='ss')
            c.wait('vector', 'ss', v_stat)
            ts(c, A('tiny')[:, 40:41], A('stat_sum').ap()[:, t:t + 1], 1.0 / D, ALU.mult)
            ts(c, A('tiny')[:, 41:42], A('stat_sq').ap()[:, t:t + 1], 1.0 / D, ALU.mult)
            tt(c, A('tiny')[:, 42:43], A('tiny')[:, 40:41], A('tiny')[:, 40:41], ALU.mult)
            tt(c, A('tiny')[:, 43:44], A('tiny')[:, 41:42], A('tiny')[:, 42:43], ALU.subtract)
            v_var = ts(c, A('tiny')[:, 43:44], A('tiny')[:, 43:44], EPS, ALU.add, sem='vv')
            c.wait('scalar', 'vv', v_var)
            v_sq = sinv(c, A('r1').ap()[:, t:t + 1], A('tiny').ap()[:, 43:44], -0.5, sem='ss')
            c.wait('vector', 'ss', v_sq)
            stt(c, A('nmr1').ap()[:, t:t + 1], A('tiny').ap()[:, 40:41], -1.0,
                A('r1').ap()[:, t:t + 1], ALU.mult, ALU.mult)
            c.wait('tensor', 'xt%d' % (s % 2), v_xT)
            if t == 0:
                c.wait('tensor', 'vv', v_v2b)
            c.wait('tensor', 'vv', qbank_free[hb])
            pu = [P[0 + 2 * hb], P[1 + 2 * hb]]
            for j in range(2):
                for k in range(8):
                    vmm = mm(c, pu[j][:, :],
                             A('xT_sb').ap()[:, (s % 2) * 8 * SLAB + k * SLAB + tl * 128:
                                             (s % 2) * 8 * SLAB + k * SLAB + tl * 128 + 128],
                             A('wq_sb')[:, k * D + j * 512:k * D + j * 512 + 512],
                             start=(k == 0), stop=(k == 7), sem='mm')
            v_qmm = vmm
            c.wait('vector', 'mm', v_qmm)
            c.wait('vector', 'sq%d' % hb, spill_q[hb])
            c.wait('vector', 'sr%d' % hb, spill_r[hb])
            stt(c, A('muv')[:, :], A('v1b')[:, :], A('nmr1').ap()[:, t:t + 1], A('v2b')[:, :],
                ALU.mult, ALU.add)
            qp = A('qpre').ap()[:, hb * D:(hb + 1) * D]
            for j in range(2):
                qbank_free[hb] = stt(c, qp[:, j * 512:(j + 1) * 512], pu[j][:, :], A('r1').ap()[:, t:t + 1],
                                     A('muv').ap()[:, j * 512:(j + 1) * 512], ALU.mult, ALU.add, sem='vv')
            if t == 0:
                c.wait('vector', 'rt', v_ropetab)
            cosap = A('rope_c').ap()[:, t * 64:(t + 1) * 64].unsqueeze(1).broadcast_to([128, 16, 64])
            qpv = qp.rearrange("p (h f) -> p h f", h=16)
            rp = A('ropet').ap()[:, hb * D:(hb + 1) * D]
            rpv = rp.rearrange("p (h f) -> p h f", h=16)
            tt(c, rpv, qpv, cosap, ALU.mult)
            qp4 = qp.rearrange("p (h a f) -> p h a f", h=16, a=2)
            crv = A('muv').ap().rearrange("p (h a f) -> p h a f", h=16, a=2)
            sin0 = A('rope_s').ap()[:, t * 64:t * 64 + 32].unsqueeze(1).broadcast_to([128, 16, 32]).unsqueeze(2)
            sin1 = A('rope_s').ap()[:, t * 64 + 32:t * 64 + 64].unsqueeze(1).broadcast_to([128, 16, 32]).unsqueeze(2)
            tt(c, crv[:, :, 0:1, :], qp4[:, :, 1:2, :], sin0, ALU.mult)
            tt(c, crv[:, :, 1:2, :], qp4[:, :, 0:1, :], sin1, ALU.mult)
            v_rope = tt(c, rp, rp, A('muv').ap(), ALU.add, sem='vv')
            c.wait('scalar', 'vv', v_rope)
            v_sq2 = act(c, A('x_sb')[:, hb * D:(hb + 1) * D], rp, AF.Square,
                        accum_out=A('sq2').ap()[:, t:t + 1], sem='ss')
            xfree[hb] = v_sq2
            c.wait('vector', 'ss', v_sq2)
            v_t = ts(c, A('tiny')[:, 45:46], A('sq2').ap()[:, t:t + 1], 1.0 / D, ALU.mult, EPS, ALU.add, sem='vv')
            c.wait('scalar', 'vv', v_t)
            v_sqr = sinv(c, A('r2').ap()[:, t:t + 1], A('tiny').ap()[:, 45:46], -0.5, sem='ss')
            c.wait('vector', 'ss', v_sqr)
            qh = A('qhat_t').ap()[:, hb * D:(hb + 1) * D]
            v_qh = ts(c, qh, rp, A('r2').ap()[:, t:t + 1], ALU.mult, sem='vv')
            rs_ = A('resid_t').ap()[:, hb * D:(hb + 1) * D]
            v_res = tt(c, rs_, rp, A('gab').ap(), ALU.mult, sem='vv')
            slab_mm[s % 2] = v_qmm
            c.wait('gpsimd', 'vv', v_qh)
            spill_q[hb] = c.dma('gpsimd', Dr('qhat_d')[t * 128:(t + 1) * 128, :], qh, 'sq%d' % hb)
            c.wait('sync', 'vv', v_res)
            spill_r[hb] = c.dma('sync', Dr('resid_d')[t * 128:(t + 1) * 128, :], rs_, 'sr%d' % hb)
    if DEBUG_P1:
        c.wait('sync', 'vv', c.cnt['vv'])
        c.wait('sync', 'ss', c.cnt['ss'])
        c.wait('sync', 'mm', c.cnt['mm'])
        c.dma('sync', Dr('xmid_d')[0:128, :], A('v1b')[:, :], 'dbg1')
        c.dma('sync', Dr('xmid_d')[128:256, :], A('v2b')[:, :], 'dbg1')
        c.dma('sync', Dr('xmid_d')[256:384, :], A('gab')[:, :], 'dbg1')
        c.dma('sync', Dr('xmid_d')[384:385, :], A('v1row')[0:1, :], 'dbg1')
        c.dma('sync', Dr('xmid_d')[385:386, :], A('v2row')[0:1, :], 'dbg1')
        c.dma('sync', Dr('xmid_d')[386:387, :], A('garow').ap()[0:1, 0:1024], 'dbg1')
        c.dma('sync', Dr('xmid_d')[387:388, :], A('garow').ap()[0:1, 1024:2048], 'dbg1')
        c.dma('sync', Dr('xmid_d')[388:389, :], A('modrow').ap()[0:1, 1024:2048], 'dbg1')
        c.dma('sync', Dr('xmid_d')[400:528, :], A('rope_c')[:, :], 'dbg1')
        c.dma('sync', Dr('xmid_d')[528:656, :], A('rope_s')[:, :], 'dbg1')
        c.dma('sync', Dr('xmid_d')[656:784, :], A('muv')[:, :], 'dbg1')
        c.dma('sync', Dr('xmid_d')[784:912, :], A('qpre')[:, 0:1024], 'dbg1')
        c.dma('sync', Dr('xmid_d')[912:1040, :], A('ropet')[:, 0:1024], 'dbg1')
        c.dma('sync', Dr('xmid_d')[1040:1168, 0:16], A('stat_sum')[:, 0:16], 'dbg1')
        c.dma('sync', Dr('xmid_d')[1168:1296, 0:16], A('stat_sq')[:, 0:16], 'dbg1')
        c.dma('sync', Dr('xmid_d')[1296:1424, 0:16], A('r1')[:, 0:16], 'dbg1')
        c.dma('sync', Dr('xmid_d')[1424:1552, 0:16], A('nmr1')[:, 0:16], 'dbg1')
        c.dma('sync', Dr('xmid_d')[1552:1680, 0:64], A('tiny')[:, :], 'dbg1')
        c.dma('sync', Dr('xmid_d')[394:395, 0:48], A('mcol')[0:1, :], 'dbg1')




def _p2(c, P, A, Dr, ph):
    ph.sb('wzres', (128, 8 * DZP), BF16)
    ph.sb('qhT', (128, 2 * 8 * SLAB), BF16)
    ph.sb('xbcT', (128, NCT * 516), BF16)
    ph.sb('xcv', (128, 2 * 512), BF16)
    ph.sb('zsl', (128, 2 * 512), BF16)
    ph.sb('cacc', (128, 2 * 512), BF16)
    ph.sb('dtr', (32, 512), F32)
    ph.sb('dtv', (32, 512), F32)
    ph.sb('lndt', (32, 512), F32)
    ph.sb('adt', (32, 512), F32)
    ph.sb('acsf', (32, 512), F32)
    ph.sb('acsg', (32, 512), F32)
    ph.sb('bhalf', (32, 2 * 512), BF16)
    ph.sb('blow', (32, 2 * 512), BF16)
    ph.sb('acol', (32, 1), F32)
    ph.sb('gcar', (32, 1), F32)
    ph.sb('convw', (128, NCT * 4), F32)
    ph.sb('convb', (128, NCT), F32)
    ph.sb('npw', (128, 8), F32)
    ph.sb('dtbc', (32, 1), F32)
    ph.sb('t32', (32, 512), F32)

    for k in range(8):
        c.dma('gpsimd', A('wzres')[:, k * DZP:(k + 1) * DZP], Dr('wzT')[k * 128:(k + 1) * 128, :], 'wzl')
    v_wz = c.cnt['wzl']
    c.dma('sync', A('convw')[:, :], Dr('conv_w')[:, :], 'c2')
    c.dma('sync', A('convb')[:, :], Dr('conv_b')[:, :], 'c2')
    c.dma('sync', A('npw')[:, :], Dr('npw_col')[:, :], 'c2')
    c.dma('sync', A('dtbc')[:, :], Dr('dtb_col')[:, :], 'c2')
    c.dma('sync', A('acol')[:, :], Dr('alog_col')[:, :], 'c2')
    v_c2 = c.cnt['c2']
    c.wait('scalar', 'c2', v_c2)
    v_a2 = act(c, A('acol')[:, :], A('acol')[:, :], AF.Exp, sem='ss')
    c.wait('vector', 'ss', v_a2)
    v_a3 = ts(c, A('acol')[:, :], A('acol')[:, :], -1.0, ALU.mult)

    vmemset(c, A('xbcT')[:, :], 0.0)
    v_ms = vmemset(c, A('gcar')[:, :], 0.0)

    zslv = [0, 0]
    v_dtr_used = 0
    caccv = [0, 0]
    bank_ev = [0, 0, 0, 0]
    v_dtev = 0
    v_lastmm = 0
    slab_mm = [0, 0]
    v_halo = v_ms
    v_sp = 0
    for s in range(NSLAB):
        sh = s % 2
        qbase = sh * 8 * SLAB
        c.wait('sync', 'mm', slab_mm[sh])
        vq = 0
        for k in range(8):
            vq = c.dmat('sync', A('qhT')[:, qbase + k * SLAB:qbase + (k + 1) * SLAB],
                        Dr('qhat_d')[s * SLAB:(s + 1) * SLAB, k * 128:(k + 1) * 128], 'qh%d' % sh)
        c.wait('vector', 'qh%d' % sh, vq)
        v_fold = 0
        for k in range(8):
            v_fold = ts(c, A('qhT')[:, qbase + k * SLAB:qbase + (k + 1) * SLAB],
                        A('qhT')[:, qbase + k * SLAB:qbase + (k + 1) * SLAB],
                        A('npw').ap()[:, k:k + 1], ALU.mult)
        if s > 0:
            c.wait('vector', 'ss', v_dtev)
            for ct in range(NCT):
                base = ct * 516
                v_halo = vcopy(c, A('xbcT')[:, base:base + 3], A('xbcT')[:, base + 512:base + 515])
        if s == 0:
            c.wait('tensor', 'wzl', v_wz)
        for m in range(MZ):
            bk = m % 4
            if m == 0:
                c.wait('tensor', 'vv', v_fold)
            c.wait('tensor', 'ss', bank_ev[bk])
            for k in range(8):
                vmm = mm(c, P[bk][:, :], A('wzres')[:, k * DZP + m * 128:k * DZP + (m + 1) * 128],
                         A('qhT')[:, qbase + k * SLAB:qbase + (k + 1) * SLAB],
                         start=(k == 0), stop=(k == 7), sem='mm')
            v_lastmm = vmm
            c.wait('scalar', 'mm', vmm)
            if m < 16:
                zh = m % 2
                c.wait('scalar', 'zs%d' % zh, zslv[zh])
                v_ev = act(c, A('zsl')[:, zh * 512:(zh + 1) * 512], P[bk][:, :], AF.Silu, sem='ss')
                c.wait('gpsimd', 'ss', v_ev)
                zslv[zh] = c.dma('gpsimd', Dr('zsilu_d')[m * 128:(m + 1) * 128, s * 512:(s + 1) * 512],
                                 A('zsl')[:, zh * 512:(zh + 1) * 512], 'zs%d' % zh)
                bank_ev[bk] = v_ev
            elif m < 34:
                ct = m - 16
                if s > 0 and m == 16:
                    c.wait('scalar', 'vv', v_halo)
                v_ev = scopy(c, A('xbcT')[:, ct * 516 + 3:ct * 516 + 515], P[bk][:, :], sem='ss')
                bank_ev[bk] = v_ev
                base = ct * 516
                hb = ct % 2
                ac = A('cacc').ap()[:, hb * 512:(hb + 1) * 512]
                c.wait('vector', 'ss', v_ev)
                ts(c, ac, A('xbcT')[:, base + 0:base + 512], A('convw').ap()[:, ct * 4:ct * 4 + 1], ALU.mult)
                for kk in range(1, 4):
                    v_cv = stt(c, ac, A('xbcT')[:, base + kk:base + kk + 512],
                               A('convw').ap()[:, ct * 4 + kk:ct * 4 + kk + 1], ac, ALU.mult, ALU.add)
                c.wait('scalar', 'vv', v_cv)
                c.wait('scalar', 'xc%d' % hb, caccv[hb])
                v_sl = act(c, A('xcv')[:, hb * 512:(hb + 1) * 512], ac, AF.Silu,
                           bias=A('convb').ap()[:, ct:ct + 1], sem='ss')
                c.wait('gpsimd', 'ss', v_sl)
                caccv[hb] = c.dma('gpsimd', Dr('xconv_d')[ct * 128:(ct + 1) * 128, s * 512:(s + 1) * 512],
                                  A('xcv')[:, hb * 512:(hb + 1) * 512], 'xc%d' % hb)
            else:
                c.wait('scalar', 'vv', v_dtr_used)
                v_ev = scopy(c, A('dtr')[:, :], P[bk][0:32, :], sem='ss')
                bank_ev[bk] = v_ev
                v_dtev = v_ev
        slab_mm[sh] = v_lastmm
        c.wait('vector', 'ss', v_dtev)
        v_v = ts(c, A('t32')[:, :], A('dtr')[:, :], A('dtbc').ap()[:, 0:1], ALU.add)
        v_dtr_used = v_v
        c.wait('scalar', 'vv', v_v)
        v_av = act(c, A('dtv')[:, :], A('t32')[:, :], AF.Abs, sem='ss')
        c.wait('scalar', 'ss', v_av)
        v_e = act(c, A('dtv')[:, :], A('dtv')[:, :], AF.Exp, scale=-1.0, sem='ss')
        c.wait('scalar', 'ss', v_e)
        v_l = act(c, A('dtv')[:, :], A('dtv')[:, :], AF.Ln, bias=1.0, sem='ss')
        c.wait('vector', 'ss', v_l)
        ts(c, A('t32')[:, :], A('t32')[:, :], 0.0, ALU.max)
        v_dt = tt(c, A('dtv')[:, :], A('dtv')[:, :], A('t32')[:, :], ALU.add)
        c.wait('scalar', 'vv', v_dt)
        v_ln = act(c, A('lndt')[:, :], A('dtv')[:, :], AF.Ln, sem='ss')
        c.wait('vector', 'ss', v_ln)
        v_adt = ts(c, A('adt')[:, :], A('dtv')[:, :], A('acol').ap()[:, 0:1], ALU.mult)
        for cl in range(4):
            sl = slice(cl * 128, (cl + 1) * 128)
            vs = vscan(c, A('acsf').ap()[:, sl], A('adt').ap()[:, sl], A('adt').ap()[:, sl], 0.0,
                       ALU.add, ALU.bypass)
        vg = vscan(c, A('acsg')[:, :], A('adt')[:, :], A('adt')[:, :], A('gcar').ap()[:, 0:1],
                   ALU.add, ALU.bypass)
        v_car = vcopy(c, A('gcar')[:, :], A('acsg').ap()[:, 511:512])
        bh = A('bhalf').ap()[:, sh * 512:(sh + 1) * 512]
        bl = A('blow').ap()[:, sh * 512:(sh + 1) * 512]
        c.wait('vector', 'sp%d' % sh, c.cnt.get('sp%d' % sh, 0))
        v1 = vcopy(c, bh, A('acsf')[:, :])
        v2 = tt(c, bl, A('acsf')[:, :], bh, ALU.subtract)
        c.wait('gpsimd', 'vv', v2)
        c.dma('gpsimd', Dr('acshi_d')[:, s * 512:(s + 1) * 512], bh, 'sp%d' % sh)
        vsp1 = c.dma('gpsimd', Dr('acslo_d')[:, s * 512:(s + 1) * 512], bl, 'sp%d' % sh)
        v3 = tt(c, A('t32')[:, :], A('lndt')[:, :], A('acsf')[:, :], ALU.subtract)
        c.wait('vector', 'sp%d' % sh, vsp1)
        v4 = vcopy(c, bh, A('t32')[:, :])
        v5 = tt(c, bl, A('t32')[:, :], bh, ALU.subtract)
        c.wait('gpsimd', 'vv', v5)
        c.dma('gpsimd', Dr('nahi_d')[:, s * 512:(s + 1) * 512], bh, 'sp%d' % sh)
        vsp2 = c.dma('gpsimd', Dr('nalo_d')[:, s * 512:(s + 1) * 512], bl, 'sp%d' % sh)
        c.wait('vector', 'sp%d' % sh, vsp2)
        v6 = vcopy(c, bh, A('acsg')[:, :])
        v7 = tt(c, bl, A('acsg')[:, :], bh, ALU.subtract)
        c.wait('gpsimd', 'vv', v7)
        c.dma('gpsimd', Dr('acsghi_d')[:, s * 512:(s + 1) * 512], bh, 'sp%d' % sh)
        c.dma('gpsimd', Dr('acsglo_d')[:, s * 512:(s + 1) * 512], bl, 'sp%d' % sh)
        c.dma('gpsimd', Dr('acs_d')[:, s * 512:(s + 1) * 512], A('acsf')[:, :], 'sp%d' % sh)
        v_sp = c.dma('gpsimd', Dr('dt_d')[:, s * 512:(s + 1) * 512], A('dtv')[:, :], 'sp%d' % sh)
        c.wait('vector', 'sp%d' % sh, v_sp)


def _p3(c, P, A, Dr, ph):
    ph.sb('bhi', (33, 2048), BF16)
    ph.sb('blo', (33, 2048), BF16)
    ph.sb('r00', (33, 2048), BF16)
    ph.sb('r01', (33, 2048), BF16)
    ph.sb('r10', (33, 2048), BF16)
    ph.sb('r11', (33, 2048), BF16)
    ph.sb('eacs', (64, 2048), BF16)
    ph.sb('acsf2', (32, 2048), F32)
    ph.sb('dtf2', (32, 2048), F32)
    ph.sb('xcf', (128, NCT * 128), BF16)
    ph.sb('xtok', (128, 2176), BF16)
    ph.sb('scob', (128, 128), BF16)
    ph.sb('gbuf', (128, 2048), BF16)
    ph.sb('eeb', (128, 2048), BF16)
    ph.sb('v2b2', (128, 2048), BF16)
    ph.sb('yoe', (128, 2048), BF16)
    ph.sb('dxt', (128, 2048), BF16)
    ph.sb('dxm', (128, 2048), BF16)
    ph.sb('ya', (128, 2048), F32)
    ph.sb('ypb', (128, 2048), BF16)
    ph.sb('srun', (128, 2048), F32)
    ph.sb('srunb', (128, 2048), BF16)
    ph.sb('sd', (128, 2048), F32)
    ph.sb('dsel', (32, 2048), BF16)
    ph.sb('dexpt', (128, 2048), BF16)
    ph.sb('f2b', (32, 128), BF16)
    ph.sb('e32', (32, 128), F32)
    ph.sb('dcol', (32, 1), F32)
    ph.sb('trilRs', (127, 2048), BF16)
    ph.sb('trilLs', (127, 128), BF16)
    ph.sb('sel32s', (32, 2048), BF16)
    ph.sb('sel64s', (64, 2048), BF16)
    ph.sb('ones32', (32, 128), BF16)
    ph.sb('dhp', (128, 16), F32)
    ph.sb('gcol', (128, 1), F32)

    c.dma('sync', A('trilRs')[:, :], Dr('trilR')[:, :], 'c3')
    c.dma('sync', A('trilLs')[:, :], Dr('trilL')[:, :], 'c3')
    c.dma('sync', A('sel32s')[:, :], Dr('sel32')[:, :], 'c3')
    c.dma('sync', A('sel64s')[:, :], Dr('sel64')[:, :], 'c3')
    c.dma('sync', A('ones32')[:, :], Dr('ones32b')[:, :], 'c3')
    c.dma('sync', A('dhp')[:, :], Dr('d_hp')[:, :], 'c3')
    c.dma('sync', A('gcol')[:, :], Dr('g_state')[:, :], 'c3')
    c.dma('sync', A('r00')[0:32, :], Dr('sel_h0')[:, :], 'c3')
    c.dma('sync', A('r01')[0:32, :], Dr('sel_h0')[:, :], 'c3')
    c.dma('sync', A('r10')[0:32, :], Dr('sel_h1')[:, :], 'c3')
    c.dma('sync', A('r11')[0:32, :], Dr('sel_h1')[:, :], 'c3')
    c.dma('sync', A('bhi')[0:32, :], Dr('nahi_d')[:, :], 'c3')
    c.dma('sync', A('blo')[0:32, :], Dr('nalo_d')[:, :], 'c3')
    c.dma('sync', A('eacs')[0:32, :], Dr('acshi_d')[:, :], 'c3')
    c.dma('sync', A('eacs')[32:64, :], Dr('acslo_d')[:, :], 'c3')
    c.dma('sync', A('acsf2')[:, :], Dr('acs_d')[:, :], 'c3')
    v_c3 = c.dma('sync', A('dtf2')[:, :], Dr('dt_d')[:, :], 'c3')
    vmemset(c, A('bhi')[32:33, :], 1.0)
    vmemset(c, A('blo')[32:33, :], 1.0)
    vmemset(c, A('srun')[:, :], 0.0)
    vmemset(c, A('srunb')[:, :], 0.0)
    c.wait('vector', 'c3', v_c3)
    v_dx = vcopy(c, A('dxt').ap().rearrange("p (a l) -> p a l", a=16),
                 A('dhp').ap().unsqueeze(2).broadcast_to([128, 16, 128]), sem='vv')
    c.wait('tensor', 'c3', v_c3)
    c.wait('scalar', 'c3', v_c3)

    rhs_free = {('r00'): 0, ('r01'): 0, ('r10'): 0, ('r11'): 0}
    v_xcf = 0; v_xtok = 0
    v_scan = 0; v_a1 = 0; v_a2 = 0; v_v2c = 0; v_sts = 0
    v_scev = 0; v_gev = [0, 0]; v_eeev = 0; v_ypsp = 0; v_dexpmm = 0
    v_diag = [0, 0]; v_yoff = 0; v_dxm = 0; v_f2p = 0; v_dsp = 0
    for ch in range(NCH):
        sl = slice(ch * 128, (ch + 1) * 128)
        c.wait('sync', 'vv', max(v_dxm if ch > 0 else 0, v_dx))
        c.wait('sync', 'mm', v_yoff)
        v_xcf = c.dma('sync', A('xcf').ap().rearrange("p (a l) -> p a l", a=NCT),
                      Dr('xconv_d').rearrange("(a p) t -> a p t", p=128)[:, :, sl].transpose([1, 0, 2]), 'xf')
        c.wait('sync', 'mm', max(v_sts, v_diag[1]))
        v_xtok = c.dmat('sync', A('xtok')[:, :], Dr('xconv_d')[0:2176, sl], 'xk2')
        for half, (rh, rl) in enumerate((('r00', 'r01'), ('r10', 'r11'))):
            h0 = half * 16
            c.wait('sync', 'mm', rhs_free[rh])
            c.dma('sync', A(rh).ap()[32:33, :].rearrange("p (a b) -> p a b", a=16),
                  Dr('acshi_d')[h0:h0 + 16, sl].unsqueeze(0), 'fl')
            c.dma('sync', A(rl).ap()[32:33, :].rearrange("p (a b) -> p a b", a=16),
                  Dr('acslo_d')[h0:h0 + 16, sl].unsqueeze(0), 'fl')
        v_fl = c.cnt['fl']
        c.wait('tensor', 'xf', v_xcf)
        c.wait('tensor', 'ss', v_eeev)
        c.wait('tensor', 'vv', max(v_a1, v_scan))
        vsc = mm(c, P[4][:, 0:128], A('xcf')[:, 16 * 128:17 * 128], A('xcf')[:, 17 * 128:18 * 128],
                 start=True, stop=True, sem='mm')
        c.wait('vector', 'mm', vsc)
        v_scev = vcopy(c, A('scob')[:, :], P[4][:, 0:128], sem='vv')
        for half in range(2):
            rh, rl = (('r00', 'r01'), ('r10', 'r11'))[half]
            c.wait('tensor', 'fl', v_fl)
            if half == 0:
                c.wait('tensor', 'vv', v_scev)
            for j in range(4):
                mm(c, P[4 + j][:, :], A('bhi').ap()[:, sl], A(rh)[:, j * 512:(j + 1) * 512],
                   start=True, stop=False, sem=None)
                mm(c, P[4 + j][:, :], A('blo').ap()[:, sl], A(rl)[:, j * 512:(j + 1) * 512],
                   start=False, stop=False, sem=None)
                vgb = mm(c, P[4 + j][:, :], A('trilLs')[:, :], A('trilRs')[:, j * 512:(j + 1) * 512],
                         start=False, stop=True, sem='mm')
            rhs_free[rh] = vgb
            c.wait('scalar', 'mm', vgb)
            c.wait('scalar', 'mm', v_diag[1 - half] if half == 1 else v_diag[1])
            for j in range(4):
                v_gev[half] = act(c, A('gbuf')[:, j * 512:(j + 1) * 512], P[4 + j][:, :], AF.Exp, sem='ss')
            c.wait('vector', 'ss', v_gev[half])
            v_gtt = tt(c, A('gbuf').ap().rearrange("p (h l) -> p h l", h=16),
                       A('gbuf').ap().rearrange("p (h l) -> p h l", h=16),
                       A('scob').ap().unsqueeze(1).broadcast_to([128, 16, 128]), ALU.mult, sem='vv')
            c.wait('tensor', 'vv', v_gtt)
            c.wait('tensor', 'xk2', v_xtok)
            if half == 0:
                c.wait('tensor', 'vv', max(v_a2, v_v2c))
                c.wait('tensor', 'vv', v_scan)
            for hl in range(16):
                h = half * 16 + hl
                ct = h // 2
                po = (ct % 4) * 128
                rows = slice(0, 64) if h % 2 == 0 else slice(64, 128)
                v_dg = mm(c, P[ct // 4][rows, po:po + 128], A('xtok')[:, h * 64:(h + 1) * 64],
                          A('gbuf')[:, hl * 128:(hl + 1) * 128],
                          start=True, stop=True, sem='mm')
            v_diag[half] = v_dg
        c.wait('tensor', 'ss', v_gev[1])
        for ct in range(16):
            v_ee = mm(c, P[4 + ct // 4][:, (ct % 4) * 128:(ct % 4) * 128 + 128],
                      A('sel64s')[:, ct * 128:(ct + 1) * 128], A('eacs').ap()[:, sl],
                      start=True, stop=True, sem='mm')
        c.wait('scalar', 'mm', v_ee)
        c.wait('scalar', 'vv', v_a1)
        for j in range(4):
            v_eeev = act(c, A('eeb')[:, j * 512:(j + 1) * 512], P[4 + j][:, :], AF.Exp, sem='ss')
        c.wait('tensor', 'ss', v_eeev)
        for ct in range(16):
            v_yoff = mm(c, P[4 + ct // 4][:, (ct % 4) * 128:(ct % 4) * 128 + 128],
                        A('srunb').ap()[:, ct * 128:(ct + 1) * 128], A('xcf')[:, 17 * 128:18 * 128],
                        start=True, stop=True, sem='mm')
        c.wait('vector', 'mm', v_yoff)
        for j in range(4):
            v_a1 = stt(c, A('yoe')[:, j * 512:(j + 1) * 512], A('eeb').ap()[:, j * 512:(j + 1) * 512],
                       1.0, P[4 + j][:, :], ALU.mult, ALU.mult, sem='vv')
        v_dxm = tt(c, A('dxm')[:, :], A('xcf').ap()[:, 0:2048], A('dxt').ap()[:, :], ALU.mult, sem='vv')
        for j in range(4):
            v_a2 = stt(c, A('ya')[:, j * 512:(j + 1) * 512], A('dxm').ap()[:, j * 512:(j + 1) * 512],
                       1.0, P[j][:, :], ALU.mult, ALU.add, sem='vv')
        c.wait('vector', 'yp', v_ypsp)
        v_a3 = tt(c, A('ypb')[:, :], A('ya')[:, :], A('yoe')[:, :], ALU.add, sem='vv')
        c.wait('gpsimd', 'vv', v_a3)
        v_ypsp = c.dma('gpsimd', Dr('ypre_d').rearrange("(a p) t -> a p t", p=128)[:, :, sl].transpose([1, 0, 2]),
                       A('ypb').ap().rearrange("p (a l) -> p a l", a=16), 'yp')
        aend = A('acsf2').ap()[:, ch * 128 + 127:ch * 128 + 128]
        c.wait('scalar', 'vv', v_f2p)
        v_f2e = act(c, A('e32')[:, :], A('acsf2').ap()[:, sl], AF.Exp, scale=-1.0, bias=aend, sem='ss')
        c.wait('vector', 'ss', v_f2e)
        v_f2 = tt(c, A('f2b')[:, :], A('e32')[:, :], A('dtf2').ap()[:, sl], ALU.mult, sem='vv')
        v_f2p = v_f2
        c.wait('tensor', 'vv', max(v_f2, v_a2))
        for j in range(4):
            v_f2mm = mm(c, P[j][:, :], A('f2b')[:, :], A('sel32s')[:, j * 512:(j + 1) * 512],
                        start=True, stop=True, sem='mm')
        c.wait('vector', 'mm', v_f2mm)
        for j in range(4):
            v_v2c = stt(c, A('v2b2')[:, j * 512:(j + 1) * 512], A('xtok').ap()[:, j * 512:(j + 1) * 512],
                        1.0, P[j][:, :], ALU.mult, ALU.mult, sem='vv')
        c.wait('tensor', 'vv', max(v_a1, v_v2c))
        for j in range(4):
            v_sts = mm(c, P[4 + j][:, :], A('xtok')[:, 2048:2176], A('v2b2')[:, j * 512:(j + 1) * 512],
                       start=True, stop=True, sem='mm')
        c.wait('scalar', 'vv', v_dsp)
        v_de = act(c, A('dcol')[:, :], aend, AF.Exp, sem='ss')
        c.wait('vector', 'ss', v_de)
        v_ds = ts(c, A('dsel')[:, :], A('sel32s')[:, :], A('dcol').ap()[:, 0:1], ALU.mult, sem='vv')
        v_dsp = v_ds
        c.wait('tensor', 'vv', max(v_ds, v_v2c))
        for j in range(4):
            v_dexpmm = mm(c, P[j][:, :], A('ones32')[:, :], A('dsel')[:, j * 512:(j + 1) * 512],
                          start=True, stop=True, sem='mm')
        c.wait('vector', 'mm', max(v_dexpmm, v_sts))
        for j in range(4):
            tt(c, A('sd')[:, j * 512:(j + 1) * 512], A('srun')[:, j * 512:(j + 1) * 512], P[j][:, :], ALU.mult)
            v_scan = stt(c, A('srun')[:, j * 512:(j + 1) * 512], A('sd').ap()[:, j * 512:(j + 1) * 512],
                         1.0, P[4 + j][:, :], ALU.mult, ALU.add, sem='vv')
        v_scan = vcopy(c, A('srunb')[:, :], A('srun')[:, :], sem='vv')
    v_g = ts(c, A('sd')[:, :], A('srun')[:, :], A('gcol').ap()[:, 0:1], ALU.mult, sem='vv')
    c.wait('gpsimd', 'vv', v_g)
    c.dma('gpsimd', Dr('st_in')[:, :], A('sd')[:, :], 'stw')


def _p4(c, P, A, Dr, ph):
    ph.sb('sinit', (128, 2048), F32)
    ph.sb('sinitb', (128, 2048), BF16)
    ph.sb('ws', (128, 16 * 1024), BF16)
    ph.sb('eag', (64, 2048), BF16)
    ph.sb('sel64s2', (64, 2048), BF16)
    ph.sb('ypt', (128, 2048), BF16)
    ph.sb('zst', (128, 2048), BF16)
    ph.sb('ctb', (128, 128), BF16)
    ph.sb('egb', (128, 2048), BF16)
    ph.sb('ycb', (128, 2048), BF16)
    ph.sb('ygz', (128, 2048), BF16)
    ph.sb('sqb', (128, 2048), BF16)
    ph.sb('xmt', (128, 1024), F32)
    ph.sb('rst', (128, 1024), F32)
    ph.sb('xdl', (128, 1024), F32)
    ph.sb('guse', (128, 1), F32)
    ph.sb('ngw', (128, 16), F32)
    ph.sb('mrow', (1, 128), F32)

    v_cc = c.op('gpsimd', lambda e: e.collective_compute(
        "AllReduce", ALU.add, replica_groups=GROUPS,
        ins=[c.dr['st_in'].opt()], outs=[c.dr['st_out'].opt()]), sem='cc')
    c.wait('gpsimd', 'cc', v_cc)
    v_si = c.dma('gpsimd', A('sinit')[:, :], Dr('st_out')[:, :], 'si')
    c.dma('sync', A('guse')[:, :], Dr('g_use')[:, :], 'c4')
    c.dma('sync', A('ngw')[:, :], Dr('ngw_col')[:, :], 'c4')
    c.dma('sync', A('sel64s2')[:, :], Dr('sel64')[:, :], 'c4')
    c.dma('sync', A('eag')[0:32, :], Dr('acsghi_d')[:, :], 'c4')
    v_c4 = c.dma('sync', A('eag')[32:64, :], Dr('acsglo_d')[:, :], 'c4')
    for k in range(16):
        c.dma('gpsimd', A('ws')[:, k * 1024:(k + 1) * 1024], Dr('woutT')[k * 128:(k + 1) * 128, :], 'wo')
    v_wo = c.cnt['wo']
    c.wait('vector', 'si', v_si)
    c.wait('vector', 'c4', v_c4)
    ts(c, A('sinit')[:, :], A('sinit')[:, :], A('guse').ap()[:, 0:1], ALU.mult)
    v_sib = vcopy(c, A('sinitb')[:, :], A('sinit')[:, :], sem='vv')
    c.wait('vector', 'wo', v_wo)
    for k in range(16):
        ts(c, A('ws')[:, k * 1024:(k + 1) * 1024], A('ws')[:, k * 1024:(k + 1) * 1024],
           A('ngw').ap()[:, k:k + 1], ALU.mult)
        v_wf = tt(c, A('ws')[:, k * 1024:(k + 1) * 1024], A('ws')[:, k * 1024:(k + 1) * 1024],
                  A('gab').ap(), ALU.mult, sem='vv')

    v_b1 = 0; v_b3 = 0; v_egev = 0; v_yc = 0; v_o2 = 0; v_wmm = 0; v_sqmm = 0; v_r3 = 0
    v_ld = {}
    for t in range(NTT):
        tsl = slice(t * 128, (t + 1) * 128)
        c.wait('sync', 'vv', v_b3)
        v_yp = c.dma('sync', A('ypt').ap().rearrange("p (a l) -> p a l", a=16),
                     Dr('ypre_d').rearrange("(a p) t -> a p t", p=128)[:, :, tsl].transpose([1, 0, 2]), 'yp2')
        v_zs = c.dma('sync', A('zst').ap().rearrange("p (a l) -> p a l", a=16),
                     Dr('zsilu_d').rearrange("(a p) t -> a p t", p=128)[:, :, tsl].transpose([1, 0, 2]), 'zs2')
        c.wait('sync', 'mm', v_yc)
        v_ct = c.dma('sync', A('ctb')[:, :], Dr('xconv_d')[2176:2304, tsl], 'ct2')
        c.wait('sync', 'vv', v_o2)
        v_xm = c.dma('sync', A('xmt')[:, :], Dr('x_tok')[tsl, :], 'xk4')
        v_rr = c.dma('sync', A('rst')[:, :], Dr('resid_d')[tsl, :], 'rr')
        c.wait('tensor', 'c4', v_c4)
        c.wait('tensor', 'vv', v_r3)
        c.wait('tensor', 'ss', v_egev)
        for ct in range(16):
            v_eg = mm(c, P[4 + ct // 4][:, (ct % 4) * 128:(ct % 4) * 128 + 128],
                      A('sel64s2')[:, ct * 128:(ct + 1) * 128], A('eag').ap()[:, tsl],
                      start=True, stop=True, sem='mm')
        c.wait('scalar', 'mm', v_eg)
        c.wait('scalar', 'vv', v_b1)
        for j in range(4):
            v_egev = act(c, A('egb')[:, j * 512:(j + 1) * 512], P[4 + j][:, :], AF.Exp, sem='ss')
        c.wait('tensor', 'ct2', v_ct)
        c.wait('tensor', 'vv', max(v_sib, v_wf))
        c.wait('tensor', 'vv', v_o2)
        for ct in range(16):
            v_yc = mm(c, P[ct // 4][:, (ct % 4) * 128:(ct % 4) * 128 + 128],
                      A('sinitb')[:, ct * 128:(ct + 1) * 128], A('ctb')[:, :],
                      start=True, stop=True, sem='mm')
        c.wait('vector', 'mm', v_yc)
        c.wait('vector', 'ss', v_egev)
        c.wait('vector', 'yp2', v_yp)
        c.wait('vector', 'zs2', v_zs)
        for j in range(4):
            v_b1 = stt(c, A('ycb')[:, j * 512:(j + 1) * 512], A('egb').ap()[:, j * 512:(j + 1) * 512],
                       1.0, P[j][:, :], ALU.mult, ALU.mult, sem='vv')
        tt(c, A('ypt')[:, :], A('ypt')[:, :], A('ycb')[:, :], ALU.add)
        v_b3 = tt(c, A('ygz')[:, :], A('ypt')[:, :], A('zst')[:, :], ALU.mult, sem='vv')
        c.wait('scalar', 'vv', v_b3)
        c.wait('scalar', 'mm', v_sqmm)
        v_sq = act(c, A('sqb')[:, :], A('ygz')[:, :], AF.Square, sem='ss')
        c.wait('tensor', 'ss', max(v_sq, v_egev))
        for ct in range(16):
            v_sqmm = mm(c, P[4][0:1, 0:128], A('onesb')[:, 0:1], A('sqb')[:, ct * 128:(ct + 1) * 128],
                        start=(ct == 0), stop=(ct == 15), sem='mm')
        c.wait('vector', 'mm', v_sqmm)
        v_mr = vcopy(c, A('mrow')[0:1, :], P[4][0:1, 0:128], sem='vv')
        v_m2 = ts(c, A('mrow')[0:1, :], A('mrow')[0:1, :], 1.0 / DIN, ALU.mult, EPS, ALU.add, sem='vv')
        c.wait('scalar', 'vv', v_m2)
        v_rc = sinv(c, A('mrow').ap()[0:1, :], A('mrow').ap()[0:1, :], -0.5, sem='ss')
        c.wait('tensor', 'ss', v_rc)
        v_r3t = mm(c, P[4][:, 256:257], A('mrow')[0:1, :], A('onesf')[0:1, 0:1], start=True, stop=True, sem='mm')
        c.wait('vector', 'mm', v_r3t)
        v_r3 = vcopy(c, A('r3').ap()[:, t:t + 1], P[4][:, 256:257], sem='vv')
        c.wait('tensor', 'vv', v_b1)
        for j in range(2):
            for ct in range(16):
                v_wmm = mm(c, P[j][:, :], A('ygz')[:, ct * 128:(ct + 1) * 128],
                           A('ws')[:, ct * 1024 + j * 512:ct * 1024 + j * 512 + 512],
                           start=(ct == 0), stop=(ct == 15), sem='mm')
        c.wait('vector', 'mm', v_wmm)
        c.wait('vector', 'xk4', v_xm)
        c.wait('vector', 'rr', v_rr)
        c.wait('vector', 'vv', v_r3)
        for j in range(2):
            stt(c, A('xdl')[:, j * 512:(j + 1) * 512], P[j][:, :], A('r3').ap()[:, t:t + 1],
                A('rst').ap()[:, j * 512:(j + 1) * 512], ALU.mult, ALU.add)
        v_o2 = tt(c, A('xdl')[:, :], A('xmt')[:, :], A('xdl')[:, :], ALU.add, sem='vv')
        c.wait('gpsimd', 'vv', v_o2)
        c.dma('gpsimd', Dr('xmid_d')[tsl, :], A('xdl')[:, :], 'xm')


def _p5(c, P, A, Dr, ph):
    ph.sb('xmt2', (128, 1024), F32)
    ph.sb('h2f', (128, 1024), F32)
    ph.sb('h2t', (128, 1024), BF16)
    ph.sb('w1s', (128, 8 * 512), BF16)
    ph.sb('w1Ls', (128, 8 * 512), BF16)
    ph.sb('w2s', (128, 4 * 64), BF16)
    ph.sb('sib', (64, 1024), BF16)
    ph.sb('o64', (64, 1), BF16)
    ph.sb('meanT', (128, 8), BF16)
    ph.sb('b1e', (128, 4), F32)
    ph.sb('b2b', (128, 64), F32)
    ph.sb('itc', (128, 1), F32)
    ph.sb('taus', (1, 1), F32)
    ph.sb('b2r', (1, 64), F32)
    ph.sb('hro', (128, 4 * 512), BF16)
    ph.sb('h2T', (128, 8 * 512), BF16)
    ph.sb('rlog', (128, 64), F32)
    ph.sb('m8', (128, 8), F32)
    ph.sb('msk', (128, 64), F32)
    ph.sb('erow', (128, 64), F32)
    ph.sb('alf', (128, 64), F32)
    ph.sb('identS', (128, 128), F32)
    ph.sb('h2r', (128, 1024), BF16)
    ph.sb('slin', (64, 1040), F32)

    for k in range(8):
        c.dma('gpsimd', A('w1s')[:, k * 512:(k + 1) * 512], Dr('w1')[k * 128:(k + 1) * 128, :], 'w1l')
        c.dma('gpsimd', A('w1Ls')[:, k * 512:(k + 1) * 512], Dr('w1')[1024 + k * 128:1024 + (k + 1) * 128, :], 'w1l')
    for k in range(4):
        c.dma('gpsimd', A('w2s')[:, k * 64:(k + 1) * 64], Dr('w2')[k * 128:(k + 1) * 128, :], 'w1l')
    c.dma('gpsimd', A('sib')[:, :], Dr('slot_init')[:, :], 'w1l')
    v_w1 = c.cnt['w1l']
    c.dma('sync', A('taus')[:, :], Dr('tau')[:, :], 'c5')
    c.dma('sync', A('b2r')[:, :], Dr('b2_row')[:, :], 'c5')
    c.dma('sync', A('identS')[:, :], Dr('identf')[:, :], 'c5')
    v_c5 = c.dma('sync', A('b1e')[:, 0:4], Dr('b1_col')[:, :], 'c5')
    v_o64 = vmemset(c, A('o64')[:, :], 1.0 / 64, sem='vv')
    c.wait('scalar', 'c5', v_c5)
    v_at = act(c, A('taus')[:, :], A('taus')[:, :], AF.Abs, sem='ss')
    c.wait('vector', 'ss', v_at)
    v_ta = ts(c, A('taus')[:, :], A('taus')[:, :], 0.1, ALU.add, sem='vv')
    c.wait('scalar', 'vv', v_ta)
    v_it = sinv(c, A('taus').ap()[:, :], A('taus').ap()[:, :], -1.0, sem='ss')
    c.wait('tensor', 'ss', v_it)
    v_itmm = mm(c, P[4][:, 300:301], A('onesf')[0:1, :], A('taus')[0:1, 0:1], start=True, stop=True, sem='mm')
    c.wait('vector', 'mm', v_itmm)
    v_itc = vcopy(c, A('itc')[:, :], P[4][:, 300:301], sem='vv')
    c.wait('tensor', 'vv', v_itc)
    v_b2mm = mm(c, P[4][:, 320:384], A('onesf')[0:1, :], A('b2r')[0:1, :], start=True, stop=True, sem='mm')
    c.wait('vector', 'mm', v_b2mm)
    vcopy(c, A('b2b')[:, :], P[4][:, 320:384])
    v_b2b = ts(c, A('b2b')[:, :], A('b2b')[:, :], A('itc').ap()[:, 0:1], ALU.mult, sem='vv')
    c.wait('tensor', 'w1l', v_w1)
    c.wait('tensor', 'vv', max(v_b2b, v_o64))
    for k in range(8):
        v_mmn = mm(c, P[4][:, 400 + k:401 + k], A('sib')[:, k * 128:(k + 1) * 128], A('o64')[:, 0:1],
                   start=True, stop=True, sem='mm')
    c.wait('vector', 'mm', v_mmn)
    v_mt = vcopy(c, A('meanT')[:, :], P[4][:, 400:408], sem='vv')
    c.wait('tensor', 'vv', v_mt)
    for m in range(4):
        for k in range(8):
            v_b1mm = mm(c, P[5][:, 400 + m:401 + m], A('w1Ls')[:, k * 512 + m * 128:k * 512 + (m + 1) * 128],
                        A('meanT')[:, k:k + 1], start=(k == 0), stop=(k == 7), sem='mm')
    c.wait('vector', 'mm', v_b1mm)
    c.wait('vector', 'c5', v_c5)
    v_b1e = stt(c, A('b1e')[:, 0:4], P[5][:, 400:404], 1.0, A('b1e').ap()[:, 0:4], ALU.mult, ALU.add, sem='vv')

    v_h2sp = [0, 0]
    v_gel = 0; v_slmm = 0; v_h2rd = 0; v_alt = 0; v_lnt = 0
    gel_ev = {}; gel_used = {}
    for s in range(NSLAB):
        for tl in range(4):
            t = s * 4 + tl
            tsl = slice(t * 128, (t + 1) * 128)
            hb = t % 2
            c.wait('sync', 'vv', v_lnt)
            v_xm = c.dma('sync', A('xmt2')[:, :], Dr('xmid_d')[tsl, :], 'xm2')
            c.wait('scalar', 'xm2', v_xm)
            act(c, A('xmt2')[:, :], A('xmt2')[:, :], AF.Copy, accum_out=A('stat_sum').ap()[:, t:t + 1])
            v_st = act(c, A('h2f')[:, :], A('xmt2')[:, :], AF.Square, accum_out=A('stat_sq').ap()[:, t:t + 1], sem='ss')
            c.wait('vector', 'ss', v_st)
            ts(c, A('tiny')[:, 40:41], A('stat_sum').ap()[:, t:t + 1], 1.0 / D, ALU.mult)
            ts(c, A('tiny')[:, 41:42], A('stat_sq').ap()[:, t:t + 1], 1.0 / D, ALU.mult)
            tt(c, A('tiny')[:, 42:43], A('tiny')[:, 40:41], A('tiny')[:, 40:41], ALU.mult)
            tt(c, A('tiny')[:, 43:44], A('tiny')[:, 41:42], A('tiny')[:, 42:43], ALU.subtract)
            v_var = ts(c, A('tiny')[:, 43:44], A('tiny')[:, 43:44], EPS, ALU.add, sem='vv')
            c.wait('scalar', 'vv', v_var)
            v_sq = sinv(c, A('tiny').ap()[:, 45:46], A('tiny').ap()[:, 43:44], -0.5, sem='ss')
            c.wait('vector', 'ss', v_sq)
            stt(c, A('tiny')[:, 46:47], A('tiny').ap()[:, 40:41], -1.0, A('tiny').ap()[:, 45:46],
                ALU.mult, ALU.mult)
            ts(c, A('h2f')[:, :], A('xmt2')[:, :], A('tiny').ap()[:, 45:46], ALU.mult,
               A('tiny').ap()[:, 46:47], ALU.add)
            tt(c, A('h2f')[:, :], A('h2f')[:, :], A('cb1b').ap(), ALU.mult)
            v_lnt = tt(c, A('h2t')[:, :], A('h2f')[:, :], A('sbb').ap(), ALU.add, sem='vv')
            c.wait('gpsimd', 'vv', v_lnt)
            v_h2sp[hb] = c.dma('gpsimd', Dr('h2_d')[tsl, :], A('h2t')[:, :], 'h2s%d' % hb)
        c.wait('sync', 'h2s0', v_h2sp[0])
        c.wait('sync', 'h2s1', v_h2sp[1])
        c.wait('sync', 'mm', v_gel)
        vq = 0
        for k in range(8):
            vq = c.dmat('sync', A('h2T')[:, k * SLAB:(k + 1) * SLAB],
                        Dr('h2_d')[s * SLAB:(s + 1) * SLAB, k * 128:(k + 1) * 128], 'h2t2')
        c.wait('tensor', 'h2t2', vq)
        if s == 0:
            c.wait('tensor', 'vv', v_b1e)
            c.wait('tensor', 'c5', v_c5)
        for m in range(4):
            bk = m % 4
            c.wait('tensor', 'ss', gel_ev.get(bk, 0))
            for k in range(8):
                v_w1mm = mm(c, P[bk][:, :], A('w1s')[:, k * 512 + m * 128:k * 512 + (m + 1) * 128],
                            A('h2T')[:, k * SLAB:(k + 1) * SLAB], start=(k == 0), stop=(k == 7), sem='mm')
            c.wait('scalar', 'mm', v_w1mm)
            v_gl = act(c, A('hro')[:, m * 512:(m + 1) * 512], P[bk][:, :], AF.Gelu,
                       bias=A('b1e').ap()[:, m:m + 1], sem='ss')
            gel_ev[bk] = v_gl
        v_gel = v_w1mm
        for tl in range(4):
            t = s * 4 + tl
            tsl = slice(t * 128, (t + 1) * 128)
            c.wait('tensor', 'ss', v_gl)
            c.wait('tensor', 'vv', v_alt)
            for m in range(4):
                v_w2mm = mm(c, P[4][:, 0:64], A('hro')[:, m * 512 + tl * 128:m * 512 + (tl + 1) * 128],
                            A('w2s')[:, m * 64:(m + 1) * 64], start=(m == 0), stop=(m == 3), sem='mm')

            c.wait('vector', 'mm', v_w2mm)
            c.wait('vector', 'vv', v_b2b)
            v_rl = stt(c, A('rlog')[:, :], P[4][:, 0:64], A('itc').ap()[:, 0:1], A('b2b').ap()[:, :],
                       ALU.mult, ALU.add, sem='vv')
            vmax8(c, A('m8')[:, :], A('rlog')[:, :])
            ts(c, A('tiny')[:, 48:49], A('m8').ap()[:, 0:1], -1.0, ALU.mult)
            v_msk = ts(c, A('msk')[:, :], A('rlog')[:, :], A('m8').ap()[:, 7:8], ALU.is_ge, sem='vv')
            c.wait('scalar', 'vv', v_msk)
            v_er = act(c, A('erow')[:, :], A('rlog')[:, :], AF.Exp, bias=A('tiny').ap()[:, 48:49], sem='ss')
            c.wait('vector', 'ss', v_er)
            tt(c, A('erow')[:, :], A('erow')[:, :], A('msk')[:, :], ALU.mult)
            v_sumr = vreduce(c, A('tiny').ap()[:, 49:50], A('erow').ap()[:, :], ALU.add, sem='vv')
            c.wait('scalar', 'vv', v_sumr)
            v_rs = sinv(c, A('tiny').ap()[:, 50:51], A('tiny').ap()[:, 49:50], -1.0, sem='ss')
            c.wait('vector', 'ss', v_rs)
            ts(c, A('alf')[:, :], A('erow')[:, :], A('tiny').ap()[:, 50:51], ALU.mult)
            v_al = vcopy(c, A('alpha').ap()[:, t * 64:(t + 1) * 64], A('alf')[:, :], sem='vv')
            c.wait('tensor', 'vv', v_al)
            v_atr = c.op('tensor', lambda e, o=P[4][0:64, 64:192], i=A('alf').ap()[:, :],
                         idt=A('identS')[:, :]: e.transpose(o, i, idt), sem='mm')
            c.wait('vector', 'mm', v_atr)
            v_alt = vcopy(c, A('alphaT').ap()[:, tsl], P[4][0:64, 64:192], sem='vv')
            c.wait('sync', 'mm', v_slmm)
            v_h2rd = c.dma('sync', A('h2r')[:, :], Dr('h2_d')[tsl, :], 'h2r')
            c.wait('tensor', 'h2r', v_h2rd)
            mm(c, P[5][0:64, :], A('alpha').ap()[:, t * 64:(t + 1) * 64], A('h2r')[:, 0:512],
               start=(t == 0), stop=(t == NTT - 1), sem=None)
            mm(c, P[6][0:64, :], A('alpha').ap()[:, t * 64:(t + 1) * 64], A('h2r')[:, 512:1024],
               start=(t == 0), stop=(t == NTT - 1), sem=None)
            v_slmm = mm(c, P[7][0:64, 0:1], A('alpha').ap()[:, t * 64:(t + 1) * 64], A('onesb')[:, 0:1],
                        start=(t == 0), stop=(t == NTT - 1), sem='mm')
    c.wait('vector', 'mm', v_slmm)
    vcopy(c, A('slin')[:, 0:512], P[5][0:64, :])
    vcopy(c, A('slin')[:, 512:1024], P[6][0:64, :])
    v_sl = vcopy(c, A('slin')[:, 1024:1025], P[7][0:64, 0:1], sem='vv')
    c.wait('gpsimd', 'vv', v_sl)
    v_slw = c.dma('gpsimd', Dr('sl_in')[:, :], A('slin')[:, 0:1025], 'slw')
    c.wait('gpsimd', 'slw', v_slw)
    v_cc2 = c.op('gpsimd', lambda e: e.collective_compute(
        "AllReduce", ALU.add, replica_groups=GROUPS,
        ins=[c.dr['sl_in'].opt()], outs=[c.dr['sl_out'].opt()]), sem='cc')
    c.wait('gpsimd', 'cc', v_cc2)


def _p6(c, P, A, Dr, ph):
    ph.sb('slot', (64, 1040), F32)
    ph.sb('sif', (64, 1024), F32)
    ph.sb('sibb', (64, 1024), BF16)
    ph.sb('sitT', (128, 8 * 64), BF16)
    ph.sb('siT', (128, 8 * 64), BF16)
    ph.sb('wres', (128, 8 * 3072), BF16)
    ph.sb('wqs', (128, 8 * 256), BF16)
    ph.sb('wks', (128, 8 * 256), BF16)
    ph.sb('gi', (64, 3072), F32)
    ph.sb('gh', (64, 3072), F32)
    ph.sb('gt', (64, 1024), F32)
    ph.sb('snew', (64, 1024), F32)
    ph.sb('snb', (64, 1024), BF16)
    ph.sb('snT', (128, 8 * 64), BF16)
    ph.sb('qf', (128, 2 * 64), BF16)
    ph.sb('kf', (128, 2 * 64), BF16)
    ph.sb('vtok', (64, 1024), BF16)
    ph.sb('att', (64, 64), F32)
    ph.sb('attT', (64, 64), BF16)
    ph.sb('sc2', (64, 1024), F32)
    ph.sb('yv', (64, 1024), F32)
    ph.sb('ylnT', (128, 8 * 64), BF16)
    ph.sb('svf', (128, 8 * 64), BF16)
    ph.sb('svw', (64, 1024), BF16)
    ph.sb('gb64', (64, 1024), F32)
    ph.sb('bo64', (64, 1024), F32)
    ph.sb('bihb', (1, 3072), BF16)
    ph.sb('bhhb', (1, 3072), BF16)
    ph.sb('bvb', (1, 1024), BF16)
    ph.sb('o64r', (1, 64), BF16)
    ph.sb('lnbb', (128, 8), BF16)
    ph.sb('bcol2', (128, 8), F32)
    ph.sb('bvpc', (128, 8), F32)
    ph.sb('lngc', (128, 8), F32)
    ph.sb('bqc', (128, 2), F32)
    ph.sb('bkc', (128, 2), F32)
    ph.sb('bias2', (128, 8), F32)
    ph.sb('xmt3', (128, 1024), F32)
    ph.sb('yout', (128, 1024), F32)
    ph.sb('t64', (64, 16), F32)
    ph.sb('borow', (1, 1024), F32)
    ph.sb('identS2', (128, 128), F32)

    c.dma('sync', A('slot')[:, 0:1025], Dr('sl_out')[:, :], 'c6')
    c.dma('sync', A('sif')[:, :], Dr('slot_init')[:, :], 'c6')
    c.dma('sync', A('bcol2')[:, 0:8], Dr('lnb_col')[:, :], 'c6')
    c.dma('sync', A('bvpc')[:, 0:8], Dr('bvp_col')[:, :], 'c6')
    c.dma('sync', A('lngc')[:, 0:8], Dr('lng_col')[:, :], 'c6')
    c.dma('sync', A('bqc')[:, 0:2], Dr('bq_col')[:, :], 'c6')
    c.dma('sync', A('bkc')[:, 0:2], Dr('bk_col')[:, :], 'c6')
    c.dma('sync', A('identS2')[:, :], Dr('identf')[:, :], 'c6')
    c.dma('sync', A('borow')[:, :], Dr('bo_row')[:, :], 'c6')
    for k in range(8):
        c.dma('gpsimd', A('sitT')[:, k * 64:(k + 1) * 64], Dr('slot_initT')[k * 128:(k + 1) * 128, :], 'c6g')
    for k in range(8):
        c.dma('gpsimd', A('wqs')[:, k * 256:(k + 1) * 256], Dr('wqf')[k * 128:(k + 1) * 128, :], 'c6g')
        c.dma('gpsimd', A('wks')[:, k * 256:(k + 1) * 256], Dr('wkf')[k * 128:(k + 1) * 128, :], 'c6g')
    for k in range(8):
        c.dma('gpsimd', A('wres')[:, k * 3072:(k + 1) * 3072], Dr('wihT')[k * 128:(k + 1) * 128, :], 'wih')
    v_wih = c.cnt['wih']
    v_o64r = vmemset(c, A('o64r')[:, :], 1.0, sem='vv')

    c.dma('gpsimd', A('bihb')[:, :], Dr('bih_row')[:, :], 'c6g')
    c.dma('gpsimd', A('bhhb')[:, :], Dr('bhh_row')[:, :], 'c6g')
    v_c6g = c.dma('gpsimd', A('bvb')[:, :], Dr('bv_row')[:, :], 'c6g')
    v_bihb = v_c6g; v_bhhb = v_c6g; v_bvb = 0
    c.wait('vector', 'c6', c.cnt['c6'])
    v_lnbb = vcopy(c, A('lnbb')[:, :], A('bcol2')[:, 0:8], sem='vv')

    v_dn0 = ts(c, A('slot')[:, 1024:1025], A('slot')[:, 1024:1025], 1e-8, ALU.add, sem='vv')
    c.wait('scalar', 'vv', v_dn0)
    v_dn1 = sinv(c, A('slot').ap()[:, 1025:1026], A('slot').ap()[:, 1024:1025], -1.0, sem='ss')
    c.wait('vector', 'ss', v_dn1)
    ts(c, A('slot')[:, 0:1024], A('slot')[:, 0:1024], A('slot').ap()[:, 1025:1026], ALU.mult)
    v_sib = vcopy(c, A('sibb')[:, :], A('slot')[:, 0:1024], sem='vv')
    c.wait('tensor', 'vv', v_sib)
    v_tr = 0
    for k in range(8):
        v_trm = c.op('tensor', lambda e, o=P[4][:, 0:64], i=A('slot').ap()[:, k * 128:(k + 1) * 128],
                     idt=A('identS2')[0:64, 0:64]: e.transpose(o, i, idt), sem='mm')
        c.wait('vector', 'mm', v_trm)
        v_tr = vcopy(c, A('siT')[:, k * 64:(k + 1) * 64], P[4][:, 0:64], sem='vv')
        c.wait('tensor', 'vv', v_tr)
    c.wait('tensor', 'wih', v_wih)
    c.wait('tensor', 'c6g', v_bihb)
    c.wait('tensor', 'vv', v_o64r)
    v_giev = 0
    for n in range(6):
        bk = n % 2
        for k in range(8):
            mm(c, P[bk][0:64, :], A('siT')[:, k * 64:(k + 1) * 64],
               A('wres')[:, k * 3072 + n * 512:k * 3072 + (n + 1) * 512],
               start=(k == 0), stop=False, sem=None)
        v_gimm = mm(c, P[bk][0:64, :], A('o64r')[0:1, 0:64],
                    A('bihb')[0:1, n * 512:(n + 1) * 512], start=False, stop=True, sem='mm')
        c.wait('vector', 'mm', v_gimm)
        v_giev = vcopy(c, A('gi')[:, n * 512:(n + 1) * 512], P[bk][0:64, :], sem='vv')
        c.wait('tensor', 'vv', v_giev)
    c.wait('gpsimd', 'mm', v_gimm)
    for k in range(8):
        c.dma('gpsimd', A('wres')[:, k * 3072:(k + 1) * 3072], Dr('whhT')[k * 128:(k + 1) * 128, :], 'wih')
    v_whh = c.cnt['wih']
    c.wait('tensor', 'wih', v_whh)
    c.wait('tensor', 'c6g', v_c6g)
    for n in range(6):
        bk = n % 2
        for k in range(8):
            mm(c, P[bk][0:64, :], A('sitT')[:, k * 64:(k + 1) * 64],
               A('wres')[:, k * 3072 + n * 512:k * 3072 + (n + 1) * 512],
               start=(k == 0), stop=False, sem=None)
        v_ghmm = mm(c, P[bk][0:64, :], A('o64r')[0:1, 0:64],
                    A('bhhb')[0:1, n * 512:(n + 1) * 512], start=False, stop=True, sem='mm')
        c.wait('vector', 'mm', v_ghmm)
        v_ghev = vcopy(c, A('gh')[:, n * 512:(n + 1) * 512], P[bk][0:64, :], sem='vv')
        c.wait('tensor', 'vv', v_ghev)
    c.wait('gpsimd', 'mm', v_ghmm)
    for k in range(8):
        c.dma('gpsimd', A('wres')[:, k * 1024:(k + 1) * 1024], Dr('wvp')[k * 128:(k + 1) * 128, :], 'wih')
        c.dma('gpsimd', A('wres')[:, 8192 + k * 1024:8192 + (k + 1) * 1024], Dr('wvf')[k * 128:(k + 1) * 128, :], 'wih')
        c.dma('gpsimd', A('wres')[:, 16384 + k * 1024:16384 + (k + 1) * 1024], Dr('wof')[k * 128:(k + 1) * 128, :], 'wih')
    v_wv3 = c.cnt['wih']
    tt(c, A('gt')[:, :], A('gi')[:, 0:1024], A('gh')[:, 0:1024], ALU.add)
    v_g1 = c.op('vector', lambda e: e.nop(), sem='vv')
    c.wait('scalar', 'vv', v_g1)
    v_r = act(c, A('gh')[:, 0:1024], A('gt')[:, :], AF.Sigmoid, sem='ss')
    tt(c, A('gt')[:, :], A('gi')[:, 1024:2048], A('gh')[:, 1024:2048], ALU.add)
    v_g2 = c.op('vector', lambda e: e.nop(), sem='vv')
    c.wait('scalar', 'vv', v_g2)
    v_z = act(c, A('gh')[:, 1024:2048], A('gt')[:, :], AF.Sigmoid, sem='ss')
    c.wait('vector', 'ss', v_z)
    tt(c, A('gt')[:, :], A('gh')[:, 0:1024], A('gh')[:, 2048:3072], ALU.mult)
    v_g3 = tt(c, A('gt')[:, :], A('gt')[:, :], A('gi')[:, 2048:3072], ALU.add, sem='vv')
    c.wait('scalar', 'vv', v_g3)
    v_n = act(c, A('gt')[:, :], A('gt')[:, :], AF.Tanh, sem='ss')
    c.wait('vector', 'ss', max(v_n, v_z))
    tt(c, A('snew')[:, :], A('sif')[:, :], A('gt')[:, :], ALU.subtract)
    tt(c, A('snew')[:, :], A('gh')[:, 1024:2048], A('snew')[:, :], ALU.mult)
    tt(c, A('snew')[:, :], A('gt')[:, :], A('snew')[:, :], ALU.add)
    v_snb = vcopy(c, A('snb')[:, :], A('snew')[:, :], sem='vv')
    c.wait('tensor', 'vv', v_snb)
    for k in range(8):
        v_trm = c.op('tensor', lambda e, o=P[4][:, 0:64], i=A('snew').ap()[:, k * 128:(k + 1) * 128],
                     idt=A('identS2')[0:64, 0:64]: e.transpose(o, i, idt), sem='mm')
        c.wait('vector', 'mm', v_trm)
        v_tr = vcopy(c, A('snT')[:, k * 64:(k + 1) * 64], P[4][:, 0:64], sem='vv')
        c.wait('tensor', 'vv', v_tr)
    c.wait('scalar', 'c6', c.cnt['c6'])
    v_qf = 0; v_kf = 0
    for m in range(2):
        c.wait('tensor', 'vv', max(v_qf, v_kf))
        for k in range(8):
            v_qmm = mm(c, P[0][:, 0:64], A('wqs')[:, k * 256 + m * 128:k * 256 + (m + 1) * 128],
                       A('snT')[:, k * 64:(k + 1) * 64], start=(k == 0), stop=(k == 7), sem='mm')
        c.wait('vector', 'mm', v_qmm)
        v_qf = ts(c, A('qf')[:, m * 64:(m + 1) * 64], P[0][:, 0:64],
                  A('bqc').ap()[:, m:m + 1], ALU.add, sem='vv')
        for k in range(8):
            v_kmm = mm(c, P[1][:, 0:64], A('wks')[:, k * 256 + m * 128:k * 256 + (m + 1) * 128],
                       A('snT')[:, k * 64:(k + 1) * 64], start=(k == 0), stop=(k == 7), sem='mm')
        c.wait('vector', 'mm', v_kmm)
        v_kf = ts(c, A('kf')[:, m * 64:(m + 1) * 64], P[1][:, 0:64],
                  A('bkc').ap()[:, m:m + 1], ALU.add, sem='vv')
    c.wait('tensor', 'wih', v_wv3)
    for n in range(2):
        for k in range(8):
            mm(c, P[2][0:64, :], A('snT')[:, k * 64:(k + 1) * 64],
               A('wres')[:, 8192 + k * 1024 + n * 512:8192 + k * 1024 + (n + 1) * 512],
               start=(k == 0), stop=False, sem=None)
        v_vmm = mm(c, P[2][0:64, :], A('o64r')[0:1, 0:64], A('bvb')[0:1, n * 512:(n + 1) * 512],
                   start=False, stop=True, sem='mm')
        c.wait('vector', 'mm', v_vmm)
        v_vt = vcopy(c, A('vtok')[:, n * 512:(n + 1) * 512], P[2][0:64, :], sem='vv')
        c.wait('tensor', 'vv', v_vt)
    c.wait('tensor', 'vv', max(v_qf, v_kf))
    for k in range(2):
        v_amm = mm(c, P[3][0:64, 0:64], A('qf')[:, k * 64:(k + 1) * 64], A('kf')[:, k * 64:(k + 1) * 64],
                   start=(k == 0), stop=(k == 1), sem='mm')
    c.wait('vector', 'mm', v_amm)
    ts(c, A('att')[:, :], P[3][0:64, 0:64], DQ ** (-0.5), ALU.mult)
    vreduce(c, A('t64').ap()[:, 0:1], A('att').ap()[:, :], ALU.max)
    v_ng = ts(c, A('t64')[:, 0:1], A('t64')[:, 0:1], -1.0, ALU.mult, sem='vv')
    c.wait('scalar', 'vv', v_ng)
    v_ex = act(c, A('att')[:, :], A('att')[:, :], AF.Exp, bias=A('t64').ap()[:, 0:1], sem='ss')
    c.wait('vector', 'ss', v_ex)
    v_sm = vreduce(c, A('t64').ap()[:, 1:2], A('att').ap()[:, :], ALU.add, sem='vv')
    c.wait('scalar', 'vv', v_sm)
    v_rq = sinv(c, A('t64').ap()[:, 2:3], A('t64').ap()[:, 1:2], -1.0, sem='ss')
    c.wait('vector', 'ss', v_rq)
    v_ab = ts(c, A('att')[:, :], A('att')[:, :], A('t64').ap()[:, 2:3], ALU.mult, sem='vv')
    c.wait('tensor', 'vv', v_ab)
    v_att = c.op('tensor', lambda e, o=P[3][0:64, 64:128], i=A('att')[:, :],
                 idt=A('identS2')[0:64, 0:64]: e.transpose(o, i, idt), sem='mm')
    c.wait('vector', 'mm', v_att)
    v_atT = vcopy(c, A('attT')[:, :], P[3][0:64, 64:128], sem='vv')
    c.wait('tensor', 'vv', v_atT)
    v_scv = 0
    for n in range(2):
        c.wait('tensor', 'vv', v_scv)
        v_scm = mm(c, P[5][0:64, :], A('attT')[:, :], A('vtok')[:, n * 512:(n + 1) * 512],
                   start=True, stop=True, sem='mm')
        c.wait('vector', 'mm', v_scm)
        v_scv = vcopy(c, A('sc2')[:, n * 512:(n + 1) * 512], P[5][0:64, :], sem='vv')
    v_y = tt(c, A('yv')[:, :], A('snew')[:, :], A('sc2')[:, :], ALU.add, sem='vv')
    c.wait('scalar', 'vv', v_y)
    act(c, A('yv')[:, :], A('yv')[:, :], AF.Copy, accum_out=A('t64').ap()[:, 3:4])
    v_sacc = act(c, A('gt')[:, :], A('yv')[:, :], AF.Square, accum_out=A('t64').ap()[:, 4:5], sem='ss')
    c.wait('vector', 'ss', v_sacc)
    ts(c, A('t64')[:, 5:6], A('t64').ap()[:, 3:4], 1.0 / D, ALU.mult)
    ts(c, A('t64')[:, 6:7], A('t64').ap()[:, 4:5], 1.0 / D, ALU.mult)
    tt(c, A('t64')[:, 7:8], A('t64')[:, 5:6], A('t64')[:, 5:6], ALU.mult)
    tt(c, A('t64')[:, 7:8], A('t64')[:, 6:7], A('t64')[:, 7:8], ALU.subtract)
    v_v6 = ts(c, A('t64')[:, 7:8], A('t64')[:, 7:8], EPS, ALU.add, sem='vv')
    c.wait('scalar', 'vv', v_v6)
    v_s6 = sinv(c, A('t64').ap()[:, 10:11], A('t64').ap()[:, 7:8], -0.5, sem='ss')
    c.wait('vector', 'ss', v_s6)
    stt(c, A('t64')[:, 11:12], A('t64').ap()[:, 5:6], -1.0, A('t64').ap()[:, 10:11], ALU.mult, ALU.mult)
    v_yln = ts(c, A('sc2')[:, :], A('yv')[:, :], A('t64').ap()[:, 11:12], ALU.add,
               A('t64').ap()[:, 10:11], ALU.mult, sem='vv')
    c.wait('vector', 'wih', v_wv3)
    c.wait('vector', 'c6', c.cnt['c6'])
    for k in range(8):
        v_wvf = ts(c, A('wres')[:, k * 1024:(k + 1) * 1024], A('wres')[:, k * 1024:(k + 1) * 1024],
                   A('lngc').ap()[:, k:k + 1], ALU.mult, sem='vv')
    c.wait('tensor', 'vv', max(v_wvf, v_lnbb, v_yln))
    for m in range(8):
        for k in range(8):
            v_b2mm = mm(c, P[6][:, m:m + 1], A('wres')[:, k * 1024 + m * 128:k * 1024 + (m + 1) * 128],
                        A('lnbb')[:, k:k + 1], start=(k == 0), stop=(k == 7), sem='mm')
    c.wait('vector', 'mm', v_b2mm)
    vcopy(c, A('bias2')[:, 0:8], P[6][:, 0:8])
    v_bi2 = tt(c, A('bias2')[:, 0:8], A('bias2')[:, 0:8], A('bvpc')[:, 0:8], ALU.add, sem='vv')
    c.wait('tensor', 'vv', v_yln)
    for k in range(8):
        v_trm = c.op('tensor', lambda e, o=P[4][:, 0:64], i=A('sc2').ap()[:, k * 128:(k + 1) * 128],
                     idt=A('identS2')[0:64, 0:64]: e.transpose(o, i, idt), sem='mm')
        c.wait('vector', 'mm', v_trm)
        v_tr = vcopy(c, A('ylnT')[:, k * 64:(k + 1) * 64], P[4][:, 0:64], sem='vv')
        c.wait('tensor', 'vv', v_tr)
    c.wait('scalar', 'vv', v_bi2)
    v_sv = 0
    for m in range(8):
        c.wait('tensor', 'vv', v_sv)
        for k in range(8):
            v_svm = mm(c, P[7][:, 0:64], A('wres')[:, k * 1024 + m * 128:k * 1024 + (m + 1) * 128],
                       A('ylnT')[:, k * 64:(k + 1) * 64], start=(k == 0), stop=(k == 7), sem='mm')
        c.wait('vector', 'mm', v_svm)
        v_sv = ts(c, A('svf')[:, m * 64:(m + 1) * 64], P[7][:, 0:64],
                  A('bias2').ap()[:, m:m + 1], ALU.add, sem='vv')
    c.wait('tensor', 'gr0', c.cnt.get('gr0', 0))
    c.wait('tensor', 'vv', v_bi2)
    v_bom = mm(c, P[6][0:64, 0:512], A('onesf')[0:1, 0:64], A('borow')[0:1, 0:512], start=True, stop=True, sem='mm')
    v_bom2 = mm(c, P[6][0:64, 0:512] if False else P[5][0:64, 0:512], A('onesf')[0:1, 0:64], A('borow')[0:1, 512:1024], start=True, stop=True, sem='mm')
    c.wait('vector', 'mm', v_bom2)
    vcopy(c, A('bo64')[:, 0:512], P[6][0:64, 0:512])
    v_bo = vcopy(c, A('bo64')[:, 512:1024], P[5][0:64, 0:512], sem='vv')
    c.wait('tensor', 'vv', v_bo)
    v_gbm = mm(c, P[6][0:64, 0:512], A('onesf')[0:1, 0:64], A('garow')[0:1, 1024:1536], start=True, stop=True, sem='mm')
    v_gbm2 = mm(c, P[5][0:64, 0:512], A('onesf')[0:1, 0:64], A('garow')[0:1, 1536:2048], start=True, stop=True, sem='mm')
    c.wait('vector', 'mm', v_gbm2)
    vcopy(c, A('gb64')[:, 0:512], P[6][0:64, 0:512])
    v_gb = vcopy(c, A('gb64')[:, 512:1024], P[5][0:64, 0:512], sem='vv')
    c.wait('tensor', 'vv', max(v_sv, v_gb))
    for n in range(2):
        for k in range(8):
            v_svwm = mm(c, P[3][0:64, :], A('svf')[:, k * 64:(k + 1) * 64],
                        A('wres')[:, 16384 + k * 1024 + n * 512:16384 + k * 1024 + (n + 1) * 512],
                        start=(k == 0), stop=(k == 7), sem='mm')
        c.wait('vector', 'mm', v_svwm)
        stt(c, A('sc2')[:, n * 512:(n + 1) * 512], P[3][0:64, :], 1.0,
            A('bo64').ap()[:, n * 512:(n + 1) * 512], ALU.mult, ALU.add)
        v_svw = tt(c, A('svw')[:, n * 512:(n + 1) * 512], A('sc2')[:, n * 512:(n + 1) * 512],
                   A('gb64')[:, n * 512:(n + 1) * 512], ALU.mult, sem='vv')
        c.wait('tensor', 'vv', v_svw)
    v_om = 0; v_ost = 0
    for t in range(NTT):
        tsl = slice(t * 128, (t + 1) * 128)
        c.wait('sync', 'vv', v_ost)
        v_xm3 = c.dma('sync', A('xmt3')[:, :], Dr('xmid_d')[tsl, :], 'xm3')
        c.wait('tensor', 'vv', v_ost)
        for j in range(2):
            v_om = mm(c, P[j][:, :], A('alphaT').ap()[:, tsl], A('svw')[:, j * 512:(j + 1) * 512],
                      start=True, stop=True, sem='mm')
        c.wait('vector', 'mm', v_om)
        c.wait('vector', 'xm3', v_xm3)
        for j in range(2):
            v_ost = stt(c, A('yout')[:, j * 512:(j + 1) * 512], P[j][:, :], 1.0,
                        A('xmt3').ap()[:, j * 512:(j + 1) * 512], ALU.mult, ALU.add, sem='vv')
        c.wait('sync', 'vv', v_ost)
        c.dma('sync', Dr('y')[tsl, :], A('yout')[:, :], 'yo')

def _finish(nc, c, stack):
    for sname in list(c.cnt.keys()):
        c.wait('sync', sname, c.cnt.get(sname, 0))
    with nc.Block() as block:
        for name in ('sync', 'tensor', 'vector', 'scalar', 'gpsimd'):
            ops = c.ops[name]
            def runner(e, ops=ops):
                for f in ops:
                    f(e)
            getattr(block, name)(runner)
    stack.close()
    return nc, c


def kernel(**inputs):
    nc, _ = build_nc(dbg=tuple(DBG))
    in_maps = [_per_core_inputs(core, inputs) for core in range(8)]
    res = run_bass_kernel_spmd(nc, in_maps, core_ids=list(range(8)))
    out = np.zeros((4, 4096, 1024), np.float32)
    for core in range(8):
        b, u = core // 2, core % 2
        out[b, u * TC:(u + 1) * TC] = res.results[core]['y']
    return out


# version 06
# speedup vs baseline: 1.0612x; 1.0612x over previous
import numpy as np
import ml_dtypes
import concourse.bass as bass
import concourse.mybir as mybir
from concourse.bass_utils import run_bass_kernel_spmd

F32 = mybir.dt.float32
BF16 = mybir.dt.bfloat16
AF = mybir.ActivationFunctionType
ALU = mybir.AluOpType
BF = ml_dtypes.bfloat16

D = 1024; TC = 2048; SLAB = 512; NSLAB = 4; TT = 128; NTT = 16
CH = 128; NCH = 16
H = 32; PH = 64; NST = 128; DIN = 2048; CONVD = 2304; NCT = 18
DZ = 4384; DZP = 4480; MZ = 35
NS = 64; DH = 512; DQ = 256
EPS = 1e-5

DBG = []
STAGE = 99
DEBUG_P1 = False
GROUPS = [[0, 1], [2, 3], [4, 5], [6, 7]]


def _core_constants(u):
    t0 = u * TC
    pos = np.arange(t0, t0 + TC, dtype=np.float64)
    inv_freq = 1.0 / 10000.0 ** (np.arange(0, 64, 2, dtype=np.float64) / 64)
    fr = pos[:, None] * inv_freq[None, :]
    cos = np.cos(fr); sin = np.sin(fr)
    rope_cos = np.concatenate([cos, cos], axis=1).astype(np.float32)
    rope_sin = np.concatenate([-sin, sin], axis=1).astype(np.float32)
    return rope_cos, rope_sin

def _shared_constants():
    c = {}
    sel0 = np.zeros((32, 2048), np.float32); sel1 = np.zeros((32, 2048), np.float32)
    for h in range(16):
        sel0[h, h * 128:(h + 1) * 128] = 1.0
        sel1[16 + h, h * 128:(h + 1) * 128] = 1.0
    c['sel_h0'] = sel0.astype(BF); c['sel_h1'] = sel1.astype(BF)
    s32 = np.zeros((32, 2048), np.float32)
    for h in range(32):
        s32[h, h * 64:(h + 1) * 64] = 1.0
    c['sel32'] = s32.astype(BF)
    tl = np.zeros((127, 128), np.float32); tr = np.zeros((127, 2048), np.float32)
    for j in range(127):
        tl[j, j + 1:] = 1.0
        tr[j, j::128] = -1e30
    c['trilL'] = tl.astype(BF); c['trilR'] = tr.astype(BF)
    c['sel64'] = np.vstack([s32, s32]).astype(BF)
    c['ident'] = np.eye(128, dtype=np.float32).astype(BF)
    c['identf'] = np.eye(128, dtype=np.float32)
    c['ones32b'] = np.ones((32, 128), np.float32).astype(BF)
    c['ones_colb'] = np.ones((128, 1), np.float32).astype(BF)
    c['ones_rowf'] = np.ones((1, 128), np.float32)
    return c

def _per_core_inputs(core, inp):
    b, u = core // 2, core % 2
    t0 = u * TC
    m = {}
    x = np.asarray(inp['L0_attn_x'])[b, t0:t0 + TC]
    m['x_tok'] = np.ascontiguousarray(x)
    m['xT'] = np.ascontiguousarray(x.T)
    m['wmod'] = np.ascontiguousarray(np.asarray(inp['L0_wmod']))
    m['c_col'] = np.ascontiguousarray(np.asarray(inp['L0_c'])[b].reshape(8, 128).T)
    m['wqT'] = np.ascontiguousarray(np.asarray(inp['L0_attn_W_in'])[:1024, :].T)
    wz = np.asarray(inp['L0_attn_W_zxbcdt'])
    wzT = np.zeros((1024, DZP), np.float32); wzT[:, :DZ] = wz.T
    m['wzT'] = wzT
    m['woutT'] = np.ascontiguousarray(np.asarray(inp['L0_attn_W_out']).T)
    cw = np.asarray(inp['L0_attn_conv_w'])
    m['conv_w'] = np.ascontiguousarray(cw.reshape(NCT, 128, 4).transpose(1, 0, 2).reshape(128, NCT * 4))
    m['conv_b'] = np.ascontiguousarray(np.asarray(inp['L0_attn_conv_b']).reshape(NCT, 128).T)
    m['dtb_col'] = np.asarray(inp['L0_attn_dt_bias']).reshape(32, 1).astype(np.float32)
    m['alog_col'] = np.asarray(inp['L0_attn_A_log']).reshape(32, 1).astype(np.float32)
    m['d_hp'] = np.ascontiguousarray(np.repeat(np.asarray(inp['L0_attn_D_param']), 64).reshape(16, 128).T)
    m['npw_col'] = np.ascontiguousarray(np.asarray(inp['L0_attn_norm_pre_w']).reshape(8, 128).T)
    m['ngw_col'] = np.ascontiguousarray(np.asarray(inp['L0_attn_norm_gated_w']).reshape(16, 128).T)
    m['w1'] = np.ascontiguousarray(np.asarray(inp['L0_ffn_w1']))
    m['w2'] = np.ascontiguousarray(np.asarray(inp['L0_ffn_w2']))
    m['wihT'] = np.ascontiguousarray(np.asarray(inp['L0_ffn_Wih']).T)
    m['whhT'] = np.ascontiguousarray(np.asarray(inp['L0_ffn_Whh']).T)
    m['slot_init'] = np.ascontiguousarray(np.asarray(inp['L0_ffn_slot_init']))
    m['slot_initT'] = np.ascontiguousarray(np.asarray(inp['L0_ffn_slot_init']).T)
    m['wqf'] = np.ascontiguousarray(np.asarray(inp['L0_ffn_wq']))
    m['wkf'] = np.ascontiguousarray(np.asarray(inp['L0_ffn_wk']))
    m['wvf'] = np.ascontiguousarray(np.asarray(inp['L0_ffn_wv']))
    m['wvp'] = np.ascontiguousarray(np.asarray(inp['L0_ffn_wvp']))
    m['wof'] = np.ascontiguousarray(np.asarray(inp['L0_ffn_wo']))
    m['b1_col'] = np.ascontiguousarray(np.asarray(inp['L0_ffn_b1']).reshape(4, 128).T)
    m['b2_row'] = np.asarray(inp['L0_ffn_b2']).reshape(1, 64).astype(np.float32)
    m['tau'] = np.asarray(inp['L0_ffn_tau']).reshape(1, 1).astype(np.float32)
    m['bih_row'] = np.asarray(inp['L0_ffn_bih']).reshape(1, 3072).astype(np.float32)
    m['bhh_row'] = np.asarray(inp['L0_ffn_bhh']).reshape(1, 3072).astype(np.float32)
    m['bq_col'] = np.ascontiguousarray(np.asarray(inp['L0_ffn_bq']).reshape(2, 128).T)
    m['bk_col'] = np.ascontiguousarray(np.asarray(inp['L0_ffn_bk']).reshape(2, 128).T)
    m['bv_row'] = np.asarray(inp['L0_ffn_bv']).reshape(1, 1024).astype(np.float32)
    m['lng_col'] = np.ascontiguousarray(np.asarray(inp['L0_ffn_ln_g']).reshape(8, 128).T)
    m['lnb_col'] = np.ascontiguousarray(np.asarray(inp['L0_ffn_ln_b']).reshape(8, 128).T)
    m['bvp_col'] = np.ascontiguousarray(np.asarray(inp['L0_ffn_bvp']).reshape(8, 128).T)
    m['bo_row'] = np.asarray(inp['L0_ffn_bo']).reshape(1, 1024).astype(np.float32)
    rc, rs = _core_constants(u)
    m['rope_cos'] = np.ascontiguousarray(rc.reshape(16, 128, 64).transpose(1, 0, 2).reshape(128, 1024))
    m['rope_sin'] = np.ascontiguousarray(rs.reshape(16, 128, 64).transpose(1, 0, 2).reshape(128, 1024))
    m.update(_shared_constants())
    m['g_state'] = np.full((128, 1), 1.0 if u == 0 else 0.0, np.float32)
    m['g_use'] = np.full((128, 1), 0.0 if u == 0 else 1.0, np.float32)
    return m


class Ctx:
    def __init__(self, nc, stack):
        self.nc = nc
        self.ops = {k: [] for k in ('sync', 'tensor', 'vector', 'scalar', 'gpsimd')}
        self.cnt = {}
        self.sems = {}
        self.sb = {}
        self.ps = {}
        self.dr = {}
        self.dma_sems = set()
        self.vlast_w = {}
        self.vlast_r = {}
        self.vwaited = 0
        self._stack = stack

    def sem(self, name):
        if name not in self.sems:
            self.sems[name] = self._stack.enter_context(self.nc.semaphore(name))
            self.cnt[name] = 0
        return self.sems[name]

    def op(self, eng, builder, sem=None, amt=1):
        v = None
        if sem is not None:
            self.sem(sem)
            self.cnt[sem] += amt
            v = self.cnt[sem]
        def f(e, b=builder, s=sem, a=amt):
            ins = b(e)
            if s is not None:
                ins.then_inc(self.sems[s], a)
        self.ops[eng].append(f)
        return v

    def wait(self, eng, sem, val):
        if val is None or val <= 0:
            return
        self.sem(sem)
        assert val <= self.cnt[sem], (eng, sem, val, self.cnt[sem])
        def f(e, s=sem, v=val):
            e.wait_ge(self.sems[s], v)
        self.ops[eng].append(f)

    def phase(self, tag):
        import contextlib
        ctx = self
        class _Ph:
            def __init__(self):
                self.stack = contextlib.ExitStack()
                self.names = []
            def sb(self, name, shape, dtype=F32):
                h = self.stack.enter_context(ctx.nc.sbuf_tensor(name, list(shape), dtype))
                ctx.sb[name] = h
                self.names.append(name)
                return h
            def close(self):
                ctx.barrier(tag)
                for n in self.names:
                    ctx.sb.pop(n, None)
                self.stack.close()
        return _Ph()

    def barrier(self, tag):
        engs = ('sync', 'tensor', 'vector', 'scalar', 'gpsimd')
        for sname in sorted(self.dma_sems):
            self.wait('sync', sname, self.cnt.get(sname, 0))
        for e in engs:
            self.op(e, lambda x: x.nop(), sem='bar_' + e)
        tgt = {e: self.cnt['bar_' + e] for e in engs}
        for e in engs:
            for o in engs:
                if o != e:
                    self.wait(e, 'bar_' + o, tgt[o])

    def vop(self, builder, reads, writes, sem=None, amt=1):
        def names(aps):
            out = []
            for a in aps:
                if a is None:
                    continue
                t = getattr(a, 'tensor', None)
                if t is None and hasattr(a, 'ap'):
                    t = getattr(a.ap(), 'tensor', None)
                if t is not None:
                    out.append(t.name)
            return out
        rn = names(reads); wn = names(writes)
        need = 0
        for t in rn:
            need = max(need, self.vlast_w.get(t, 0))
        for t in wn:
            need = max(need, self.vlast_w.get(t, 0), self.vlast_r.get(t, 0))
        if need > self.vwaited:
            self.wait('vector', 'vv', need)
            self.vwaited = need
        v = self.op('vector', builder, sem='vv', amt=amt)
        for t in wn:
            self.vlast_w[t] = v
        for t in rn:
            self.vlast_r[t] = max(self.vlast_r.get(t, 0), v)
        return v

    def dma(self, q, out, in_, sem):
        self.dma_sems.add(sem)
        return self.op(q, lambda e, o=out, i=in_: e.dma_start(out=o, in_=i), sem=sem, amt=16)

    def dma_slow(self, q, out, in_, sem):
        self.dma_sems.add(sem)
        return self.op(q, lambda e, o=out, i=in_:
                       e.dma_start(out=o, in_=i, allow_slow_non_contiguous=True), sem=sem, amt=16)

    def dmat(self, q, out, in_, sem):
        self.dma_sems.add(sem)
        return self.op(q, lambda e, o=out, i=in_: e.dma_start_transpose(o, i), sem=sem, amt=16)

def mm(ctx, out, lhsT, rhs, start, stop, sem=None):
    return ctx.op('tensor', lambda e, o=out, l=lhsT, r=rhs, st=start, sp=stop:
                  e.matmul(o, l, r, start=st, stop=sp), sem=sem)

def act(ctx, out, in_, func, bias=None, scale=None, accum_out=None, sem=None):
    kw = {}
    if bias is not None: kw['bias'] = bias
    if scale is not None: kw['scale'] = scale
    if accum_out is not None: kw['accum_out'] = accum_out
    return ctx.op('scalar', lambda e, o=out, i=in_, f=func, k=dict(kw):
                  e.activation(out=o, in_=i, func=f, **k), sem=sem)

def tt(ctx, out, in0, in1, alu, sem=None):
    return ctx.vop(lambda e, o=out, a=in0, b=in1, p=alu:
                   e.tensor_tensor(out=o, in0=a, in1=b, op=p), [in0, in1], [out])

def ts(ctx, out, in0, s1, op0, s2=None, op1=None, sem=None):
    def b(e, o=out, i=in0, x1=s1, x2=s2, p0=op0, p1=op1):
        if p1 is None:
            return e.tensor_scalar(out=o, in0=i, scalar1=x1, scalar2=None, op0=p0)
        return e.tensor_scalar(out=o, in0=i, scalar1=x1, scalar2=x2, op0=p0, op1=p1)
    rd = [in0] + [x for x in (s1, s2) if not isinstance(x, (int, float, type(None)))]
    return ctx.vop(b, rd, [out])

def stt(ctx, out, in0, scalar, in1, op0, op1, sem=None):
    rd = [in0, in1] + ([scalar] if not isinstance(scalar, (int, float)) else [])
    return ctx.vop(lambda e, o=out, a=in0, s=scalar, b=in1, p0=op0, p1=op1:
                   e.scalar_tensor_tensor(out=o, in0=a, scalar=s, in1=b, op0=p0, op1=p1), rd, [out])

def vcopy(ctx, out, in_, sem=None):
    return ctx.vop(lambda e, o=out, i=in_: e.tensor_copy(o, i), [in_], [out])

def scopy(ctx, out, in_, sem=None):
    return ctx.op('scalar', lambda e, o=out, i=in_: e.copy(o, i), sem=sem)

def vscan(ctx, out, d0, d1, initial, op0, op1, sem=None):
    rd = [d0, d1] + ([initial] if not isinstance(initial, (int, float)) else [])
    return ctx.vop(lambda e, o=out, a=d0, b=d1, i=initial, p0=op0, p1=op1:
                   e.tensor_tensor_scan(o, a, b, i, p0, p1), rd, [out])


def sinv(ctx, out, in_, power=-1.0, sem=None):
    act(ctx, out, in_, AF.Ln)
    return act(ctx, out, out, AF.Exp, scale=power, sem=sem)

def vmemset(ctx, ap, val, sem=None):
    return ctx.vop(lambda e, a=ap, v=val: e.memset(a, v), [], [ap])

def vrecip(ctx, out, in_, sem=None):
    return ctx.vop(lambda e, o=out, i=in_: e.reciprocal(o, i), [in_], [out])

def vmax8(ctx, out, in_, sem=None):
    return ctx.vop(lambda e, o=out, i=in_: e.max(o, i), [in_], [out])

def vreduce(ctx, out, in_, alu, sem=None):
    return ctx.vop(lambda e, o=out, i=in_, p=alu:
                   e.tensor_reduce(out=o, in_=i, axis=mybir.AxisListType.X, op=p), [in_], [out])


def build_nc(dbg=()):
    import contextlib
    nc = bass.Bass("TRN2", target_bir_lowering=False, debug=False, detect_race_conditions=False)
    stack = contextlib.ExitStack()
    c = Ctx(nc, stack)

    def din(name, shape, dtype=F32):
        c.dr[name] = nc.dram_tensor(name, list(shape), dtype, kind="ExternalInput").ap()

    def dtmp(name, shape, dtype=F32):
        kind = "ExternalOutput" if name in dbg else "Internal"
        c.dr[name] = nc.dram_tensor(name, list(shape), dtype, kind=kind).ap()

    def sb(name, shape, dtype=F32):
        c.sb[name] = stack.enter_context(nc.sbuf_tensor(name, list(shape), dtype))

    for n, s in [('x_tok', (TC, D)), ('xT', (D, TC)), ('wmod', (D, 6144)), ('c_col', (128, 8)),
                 ('wqT', (D, D)), ('wzT', (D, DZP)), ('woutT', (DIN, D)),
                 ('conv_w', (128, NCT * 4)), ('conv_b', (128, NCT)),
                 ('dtb_col', (32, 1)), ('alog_col', (32, 1)), ('d_hp', (128, 16)),
                 ('npw_col', (128, 8)), ('ngw_col', (128, 16)),
                 ('w1', (DIN, DH)), ('w2', (DH, NS)), ('wihT', (D, 3072)), ('whhT', (D, 3072)),
                 ('slot_init', (NS, D)), ('slot_initT', (D, NS)),
                 ('wqf', (D, DQ)), ('wkf', (D, DQ)), ('wvf', (D, D)), ('wvp', (D, D)), ('wof', (D, D)),
                 ('b1_col', (128, 4)), ('b2_row', (1, NS)), ('tau', (1, 1)),
                 ('bih_row', (1, 3072)), ('bhh_row', (1, 3072)),
                 ('bq_col', (128, 2)), ('bk_col', (128, 2)), ('bv_row', (1, D)),
                 ('lng_col', (128, 8)), ('lnb_col', (128, 8)), ('bvp_col', (128, 8)), ('bo_row', (1, D)),
                 ('rope_cos', (128, NTT * 64)), ('rope_sin', (128, NTT * 64)), ('ones_rowf', (1, 128)), ('identf', (128, 128)),
                 ('g_state', (128, 1)), ('g_use', (128, 1))]:
        din(n, s, F32)
    for n, s in [('sel_h0', (32, 2048)), ('sel_h1', (32, 2048)), ('sel32', (32, 2048)),
                 ('trilL', (127, 128)), ('trilR', (127, 2048)), ('ones_colb', (128, 1)),
                 ('sel64', (64, 2048)), ('ones32b', (32, 128)), ('ident', (128, 128))]:
        din(n, s, BF16)

    c.dr['y'] = nc.dram_tensor('y', [TC, D], F32, kind="ExternalOutput").ap()

    dtmp('qhat_d', (TC, D), BF16)
    dtmp('resid_d', (TC, D), F32)
    dtmp('xconv_d', (CONVD, TC), BF16)
    dtmp('zsilu_d', (DIN, TC), BF16)
    dtmp('ypre_d', (DIN, TC), BF16)
    dtmp('xmid_d', (TC, D), F32)
    dtmp('h2_d', (TC, D), BF16)
    dtmp('acshi_d', (32, TC), BF16)
    dtmp('acslo_d', (32, TC), BF16)
    dtmp('nahi_d', (32, TC), BF16)
    dtmp('nalo_d', (32, TC), BF16)
    dtmp('acsghi_d', (32, TC), BF16)
    dtmp('acsglo_d', (32, TC), BF16)
    dtmp('acs_d', (32, TC), F32)
    dtmp('dt_d', (32, TC), F32)
    dtmp('st_in', (NST, DIN), F32)
    dtmp('st_out', (NST, DIN), F32)
    dtmp('sl_in', (NS, 1025), F32)
    dtmp('sl_out', (NS, 1025), F32)
    dtmp('brow_d', (1, 8192), F32)

    sb('onesf', (1, 128), F32)
    sb('onesb', (128, 1), BF16)
    sb('c_sb', (128, 8), F32)
    sb('siluc', (128, 8), BF16)
    sb('silucf', (128, 8), F32)
    sb('modrow', (1, 6144), F32)
    sb('garow', (1, 2048), F32)
    sb('mcol', (128, 48), F32)
    sb('gab', (128, D), F32); sb('cb1b', (128, D), F32); sb('sbb', (128, D), F32)
    sb('stat_sum', (128, NTT + 16), F32); sb('stat_sq', (128, NTT + 16), F32)
    sb('r1', (128, NTT + 16), F32); sb('nmr1', (128, NTT + 16), F32)
    sb('r2', (128, NTT + 16), F32); sb('sq2', (128, NTT + 16), F32)
    sb('r3', (128, NTT + 16), F32); sb('sq3', (128, NTT + 16), F32)
    sb('tiny', (128, 64), F32)
    sb('tcol', (32, 16), F32)
    sb('alpha', (128, NTT * 64), BF16)
    sb('alphaT', (64, TC), BF16)

    for i in range(8):
        c.ps['pb%d' % i] = stack.enter_context(nc.psum_tensor('pb%d' % i, [128, 512], F32))
    P = [c.ps['pb%d' % i] for i in range(8)]

    A = lambda n: c.sb[n]
    Dr = lambda n: c.dr[n]

    c.dma('sync', A('onesf')[:, :], Dr('ones_rowf')[:, :], 'c0')
    c.dma('sync', A('onesb')[:, :], Dr('ones_colb')[:, :], 'c0')

    def row_bcast(dst_ap, row_ap, n, banks=(4, 5)):
        last = 0
        evs = {b: 0 for b in banks}
        for idx, j in enumerate(range(0, n, 512)):
            w = min(512, n - j)
            bk = banks[idx % len(banks)]
            c.wait('tensor', 'vv', evs[bk])
            vm = mm(c, P[bk][:, 0:w], A('onesf')[0:1, :], row_ap[0:1, j:j + w], start=True, stop=True, sem='mm')
            c.wait('vector', 'mm', vm)
            last = vcopy(c, dst_ap[:, j:j + w], P[bk][:, 0:w], sem='vv')
            evs[bk] = last
        return last

    ph = c.phase('p0')
    ph.sb('wmodb', (128, 8 * 4096), BF16)
    ph.sb('wmodf', (128, 8 * 2048), F32)
    ph.sb('identP', (128, 128), F32)
    c.dma('sync', A('c_sb')[:, :], Dr('c_col')[:, :], 'c0')
    c.dma('sync', A('identP')[:, :], Dr('identf')[:, :], 'c0')
    v_c0 = c.cnt['c0']
    vbf = []
    vfp = []
    for k in range(8):
        c.dma('gpsimd', A('wmodb')[:, k * 4096:k * 4096 + 2048],
              Dr('wmod')[k * 128:(k + 1) * 128, 0:2048], 'wma')
        vbf.append(c.dma('gpsimd', A('wmodb')[:, k * 4096 + 2048:k * 4096 + 4096],
                         Dr('wmod')[k * 128:(k + 1) * 128, 3072:5120], 'wma'))
        c.dma('sync', A('wmodf')[:, k * 2048:k * 2048 + 1024],
              Dr('wmod')[k * 128:(k + 1) * 128, 2048:3072], 'wmb')
        vfp.append(c.dma('sync', A('wmodf')[:, k * 2048 + 1024:k * 2048 + 2048],
                         Dr('wmod')[k * 128:(k + 1) * 128, 5120:6144], 'wmb'))
    c.wait('scalar', 'c0', v_c0)
    act(c, A('siluc')[:, :], A('c_sb')[:, :], AF.Silu)
    v_siluc = act(c, A('silucf')[:, :], A('c_sb')[:, :], AF.Silu, sem='ss')

    bank_ev = [0, 0]
    c.wait('tensor', 'ss', v_siluc)
    for n in range(12):
        fp32n = n in (4, 5, 10, 11)
        lhs = A('silucf') if fp32n else A('siluc')
        bank = P[n % 2]
        c.wait('tensor', 'vv', bank_ev[n % 2])
        for k in range(8):
            if fp32n:
                c.wait('tensor', 'wmb', vfp[k])
                j0 = k * 2048 + (1024 if n >= 10 else 0) + (n % 2) * 512
                rhs = A('wmodf')[:, j0:j0 + 512]
            else:
                c.wait('tensor', 'wma', vbf[k])
                j0 = k * 4096 + (2048 if n >= 6 else 0) + ((n - 6 if n >= 6 else n) % 4) * 512
                rhs = A('wmodb')[:, j0:j0 + 512]
            vmm = mm(c, bank[0:1, :], lhs[:, k:k + 1], rhs,
                     start=(k == 0), stop=(k == 7), sem='mm')
        c.wait('vector', 'mm', vmm)
        bank_ev[n % 2] = vcopy(c, A('modrow')[0:1, n * 512:(n + 1) * 512], bank[0:1, :], sem='vv')
    v_modrow = max(bank_ev)

    c.wait('tensor', 'vv', v_modrow)
    c.wait('tensor', 'c0', v_c0)
    for a in range(48):
        bk, col = (2, a) if a < 24 else (3, a - 24)
        vm = mm(c, P[bk][:, col:col + 1], A('modrow').ap()[0:1, a * 128:(a + 1) * 128],
                A('onesf')[0:1, 0:1], start=True, stop=True, sem='mm')
    c.wait('vector', 'mm', vm)
    vcopy(c, A('mcol')[:, 0:24], P[2][:, 0:24], sem='vv')
    v_mcol = vcopy(c, A('mcol')[:, 24:48], P[3][:, 0:24], sem='vv')

    c.wait('scalar', 'vv', v_mcol)
    v_sg = act(c, A('tiny')[:, 0:8], A('mcol')[:, 16:24], AF.Sign, sem='ss')
    v_sg2 = act(c, A('tiny')[:, 8:16], A('mcol')[:, 40:48], AF.Sign, sem='ss')
    v_ab = act(c, A('tiny')[:, 16:24], A('mcol')[:, 16:24], AF.Abs, sem='ss')
    v_ab2 = act(c, A('tiny')[:, 24:32], A('mcol')[:, 40:48], AF.Abs, sem='ss')
    c.wait('vector', 'ss', v_ab2)
    ts(c, A('tiny')[:, 16:24], A('tiny')[:, 16:24], 0.25, ALU.max)
    ts(c, A('tiny')[:, 24:32], A('tiny')[:, 24:32], 0.25, ALU.max)
    tt(c, A('mcol')[:, 16:24], A('tiny')[:, 0:8], A('tiny')[:, 16:24], ALU.mult)
    v_gclamp = tt(c, A('mcol')[:, 40:48], A('tiny')[:, 8:16], A('tiny')[:, 24:32], ALU.mult, sem='vv')

    c.wait('tensor', 'vv', v_gclamp)
    ev67 = 0
    for g in range(2):
        c.wait('tensor', 'vv', ev67)
        for j in range(8):
            bk, col = (6, j) if j < 4 else (7, j - 4)
            vm = mm(c, P[bk][0:1, col * 128:(col + 1) * 128], A('mcol')[:, (16 if g == 0 else 40) + j:(17 if g == 0 else 41) + j],
                    A('identP')[:, :], start=True, stop=True, sem='mm')
        c.wait('vector', 'mm', vm)
        vcopy(c, A('garow')[0:1, g * 1024:g * 1024 + 512], P[6][0:1, 0:512], sem='vv')
        ev67 = vcopy(c, A('garow')[0:1, g * 1024 + 512:g * 1024 + 1024], P[7][0:1, 0:512], sem='vv')
    c.wait('tensor', 'vv', ev67)

    row_bcast(A('gab').ap(), A('garow').ap()[0:1, 0:1024], D)
    row_bcast(A('cb1b').ap(), A('modrow').ap()[0:1, 4 * 1024:5 * 1024], D)
    ts(c, A('cb1b')[:, :], A('cb1b')[:, :], 1.0, ALU.add)
    row_bcast(A('sbb').ap(), A('modrow').ap()[0:1, 3 * 1024:4 * 1024], D)
    ph.close()

    if STAGE >= 1:
        ph = c.phase('p1')
        _p1(c, P, A, Dr, row_bcast, ph)
        ph.close()
    if STAGE >= 2:
        ph = c.phase('p2')
        _p2(c, P, A, Dr, ph)
        ph.close()
    if STAGE >= 3:
        ph = c.phase('p3')
        _p3(c, P, A, Dr, ph)
        ph.close()
    if STAGE >= 4:
        ph = c.phase('p4')
        _p4(c, P, A, Dr, ph)
        ph.close()
    if STAGE >= 5:
        ph = c.phase('p5')
        _p5(c, P, A, Dr, ph)
        ph.close()
    if STAGE >= 6:
        ph = c.phase('p6')
        _p6(c, P, A, Dr, ph)
        ph.close()
    return _finish(nc, c, stack)


def _p1(c, P, A, Dr, row_bcast, ph):
    ph.sb('v1row', (1, D), F32); ph.sb('v2row', (1, D), F32)
    ph.sb('v1b', (128, D), F32); ph.sb('v2b', (128, D), F32)
    ph.sb('x_sb', (128, 4 * D), F32)
    ph.sb('xT_sb', (128, 2 * 8 * SLAB), BF16)
    ph.sb('wq_sb', (128, 8 * D), BF16)
    ph.sb('sacol', (128, 8), BF16)
    ph.sb('rope_c', (128, NTT * 64), F32)
    ph.sb('rope_s', (128, NTT * 64), F32)
    ph.sb('muv', (128, 4 * D), F32)
    ph.sb('qpre', (128, 4 * D), F32)
    ph.sb('ropet', (128, 4 * D), F32)
    ph.sb('qhat_t', (128, 4 * D), BF16)
    ph.sb('resid_t', (128, 4 * D), F32)
    for k in range(8):
        c.dma('gpsimd', A('wq_sb')[:, k * D:(k + 1) * D], Dr('wqT')[k * 128:(k + 1) * 128, :], 'wql')
    v_wq = c.cnt['wql']
    v_sab = vcopy(c, A('sacol')[:, :], A('mcol')[:, 0:8], sem='vv')
    c.wait('tensor', 'vv', v_sab)
    c.wait('tensor', 'wql', v_wq)
    v_v2r = 0
    for j in range(2):
        c.wait('tensor', 'vv', v_v2r)
        for k in range(8):
            vmm = mm(c, P[6][0:1, :], A('sacol')[:, k:k + 1], A('wq_sb')[:, k * D + j * 512:k * D + j * 512 + 512],
                     start=(k == 0), stop=(k == 7), sem='mm')
        c.wait('vector', 'mm', vmm)
        v_v2r = vcopy(c, A('v2row')[0:1, j * 512:(j + 1) * 512], P[6][0:1, :], sem='vv')
    ts(c, A('tiny')[:, 32:40], A('mcol')[:, 8:16], 1.0, ALU.add)
    v_fold = 0
    for k in range(8):
        v_fold = ts(c, A('wq_sb')[:, k * D:(k + 1) * D], A('wq_sb')[:, k * D:(k + 1) * D],
                    A('tiny').ap()[:, 32 + k:33 + k], ALU.mult, sem='vv')
    c.wait('tensor', 'vv', v_fold)
    v_v1r = 0
    for j in range(2):
        c.wait('tensor', 'vv', v_v1r)
        for k in range(8):
            vmm = mm(c, P[7][0:1, :], A('onesb')[:, 0:1], A('wq_sb')[:, k * D + j * 512:k * D + j * 512 + 512],
                     start=(k == 0), stop=(k == 7), sem='mm')
        c.wait('vector', 'mm', vmm)
        v_v1r = vcopy(c, A('v1row')[0:1, j * 512:(j + 1) * 512], P[7][0:1, :], sem='vv')
    row_bcast(A('v1b').ap(), A('v1row').ap(), D, banks=(4, 5))
    v_v2b = row_bcast(A('v2b').ap(), A('v2row').ap(), D, banks=(4, 5))

    c.dma('sync', A('rope_c')[:, :], Dr('rope_cos')[:, :], 'rt')
    v_ropetab = c.dma('sync', A('rope_s')[:, :], Dr('rope_sin')[:, :], 'rt')

    xfree = 0
    spill_q = 0
    spill_r = 0
    qbank_free = 0
    sq2_done = 0
    slab_mm = [0, 0]
    for s in range(NSLAB):
        t0 = s * 4
        c.wait('gpsimd', 'mm', slab_mm[s % 2])
        for k in range(8):
            c.dma('gpsimd', A('xT_sb')[:, (s % 2) * 8 * SLAB + k * SLAB:(s % 2) * 8 * SLAB + (k + 1) * SLAB],
                  Dr('xT')[k * 128:(k + 1) * 128, s * SLAB:(s + 1) * SLAB], 'xt%d' % (s % 2))
        v_xT = c.cnt['xt%d' % (s % 2)]
        c.wait('sync', 'ss', xfree)
        for tl in range(4):
            vx = c.dma('sync', A('x_sb')[:, tl * D:(tl + 1) * D],
                       Dr('x_tok')[(t0 + tl) * 128:(t0 + tl + 1) * 128, :], 'xk')
        c.wait('scalar', 'xk', vx)
        for tl in range(4):
            xs = A('x_sb')[:, tl * D:(tl + 1) * D]
            act(c, xs, xs, AF.Copy, accum_out=A('stat_sum').ap()[:, t0 + tl:t0 + tl + 1])
            stats_done = act(c, xs, xs, AF.Square, accum_out=A('stat_sq').ap()[:, t0 + tl:t0 + tl + 1], sem='ss')
        xfree = stats_done
        c.wait('vector', 'ss', stats_done)
        ts(c, A('tiny')[:, 40:44], A('stat_sum').ap()[:, t0:t0 + 4], 1.0 / D, ALU.mult)
        ts(c, A('tiny')[:, 44:48], A('stat_sq').ap()[:, t0:t0 + 4], 1.0 / D, ALU.mult)
        tt(c, A('tiny')[:, 48:52], A('tiny')[:, 40:44], A('tiny')[:, 40:44], ALU.mult)
        tt(c, A('tiny')[:, 52:56], A('tiny')[:, 44:48], A('tiny')[:, 48:52], ALU.subtract)
        v_var = ts(c, A('tiny')[:, 52:56], A('tiny')[:, 52:56], EPS, ALU.add, sem='vv')
        c.wait('scalar', 'vv', v_var)
        v_r1 = sinv(c, A('r1').ap()[:, t0:t0 + 4], A('tiny').ap()[:, 52:56], -0.5, sem='ss')
        c.wait('vector', 'ss', v_r1)
        stt(c, A('nmr1').ap()[:, t0:t0 + 4], A('tiny').ap()[:, 40:44], -1.0,
            A('r1').ap()[:, t0:t0 + 4], ALU.mult, ALU.mult)
        c.wait('tensor', 'xt%d' % (s % 2), v_xT)
        if s == 0:
            c.wait('tensor', 'vv', v_v2b)
        c.wait('tensor', 'vv', qbank_free)
        for tl in range(4):
            for j in range(2):
                for k in range(8):
                    v_qmm = mm(c, P[2 * tl + j][:, :],
                               A('xT_sb').ap()[:, (s % 2) * 8 * SLAB + k * SLAB + tl * 128:
                                               (s % 2) * 8 * SLAB + k * SLAB + tl * 128 + 128],
                               A('wq_sb')[:, k * D + j * 512:k * D + j * 512 + 512],
                               start=(k == 0), stop=(k == 7), sem='mm')
        slab_mm[s % 2] = v_qmm
        c.wait('vector', 'mm', v_qmm)
        c.wait('vector', 'ss', sq2_done)
        c.wait('vector', 'sq0', spill_q)
        c.wait('vector', 'sr0', spill_r)
        for tl in range(4):
            stt(c, A('muv')[:, tl * D:(tl + 1) * D], A('v1b')[:, :], A('nmr1').ap()[:, t0 + tl:t0 + tl + 1],
                A('v2b')[:, :], ALU.mult, ALU.add)
        for tl in range(4):
            qp = A('qpre').ap()[:, tl * D:(tl + 1) * D]
            for j in range(2):
                qbank_free = stt(c, qp[:, j * 512:(j + 1) * 512], P[2 * tl + j][:, :],
                                 A('r1').ap()[:, t0 + tl:t0 + tl + 1],
                                 A('muv').ap()[:, tl * D + j * 512:tl * D + j * 512 + 512],
                                 ALU.mult, ALU.add, sem='vv')
        if s == 0:
            c.wait('vector', 'rt', v_ropetab)
        for tl in range(4):
            t = t0 + tl
            qp = A('qpre').ap()[:, tl * D:(tl + 1) * D]
            cosap = A('rope_c').ap()[:, t * 64:(t + 1) * 64].unsqueeze(1).broadcast_to([128, 16, 64])
            qpv = qp.rearrange("p (h f) -> p h f", h=16)
            rpv = A('ropet').ap()[:, tl * D:(tl + 1) * D].rearrange("p (h f) -> p h f", h=16)
            tt(c, rpv, qpv, cosap, ALU.mult)
        for tl in range(4):
            t = t0 + tl
            qp4 = A('qpre').ap()[:, tl * D:(tl + 1) * D].rearrange("p (h a f) -> p h a f", h=16, a=2)
            crv = A('muv').ap()[:, tl * D:(tl + 1) * D].rearrange("p (h a f) -> p h a f", h=16, a=2)
            sin0 = A('rope_s').ap()[:, t * 64:t * 64 + 32].unsqueeze(1).broadcast_to([128, 16, 32]).unsqueeze(2)
            sin1 = A('rope_s').ap()[:, t * 64 + 32:t * 64 + 64].unsqueeze(1).broadcast_to([128, 16, 32]).unsqueeze(2)
            tt(c, crv[:, :, 0:1, :], qp4[:, :, 1:2, :], sin0, ALU.mult)
            tt(c, crv[:, :, 1:2, :], qp4[:, :, 0:1, :], sin1, ALU.mult)
        for tl in range(4):
            rp = A('ropet').ap()[:, tl * D:(tl + 1) * D]
            v_rope = tt(c, rp, rp, A('muv').ap()[:, tl * D:(tl + 1) * D], ALU.add, sem='vv')
        c.wait('scalar', 'vv', v_rope)
        for tl in range(4):
            sq2_done = act(c, A('muv')[:, tl * D:(tl + 1) * D], A('ropet').ap()[:, tl * D:(tl + 1) * D],
                           AF.Square, accum_out=A('sq2').ap()[:, t0 + tl:t0 + tl + 1], sem='ss')
        c.wait('vector', 'ss', sq2_done)
        v_t = ts(c, A('tiny')[:, 56:60], A('sq2').ap()[:, t0:t0 + 4], 1.0 / D, ALU.mult, EPS, ALU.add, sem='vv')
        c.wait('scalar', 'vv', v_t)
        v_r2 = sinv(c, A('r2').ap()[:, t0:t0 + 4], A('tiny').ap()[:, 56:60], -0.5, sem='ss')
        c.wait('vector', 'ss', v_r2)
        v_qh = 0
        for tl in range(4):
            v_qh = ts(c, A('qhat_t').ap()[:, tl * D:(tl + 1) * D], A('ropet').ap()[:, tl * D:(tl + 1) * D],
                      A('r2').ap()[:, t0 + tl:t0 + tl + 1], ALU.mult, sem='vv')
        v_res = 0
        for tl in range(4):
            v_res = tt(c, A('resid_t').ap()[:, tl * D:(tl + 1) * D], A('ropet').ap()[:, tl * D:(tl + 1) * D],
                       A('gab').ap(), ALU.mult, sem='vv')
        c.wait('gpsimd', 'vv', v_qh)
        for tl in range(4):
            spill_q = c.dma('gpsimd', Dr('qhat_d')[(t0 + tl) * 128:(t0 + tl + 1) * 128, :],
                            A('qhat_t').ap()[:, tl * D:(tl + 1) * D], 'sq0')
        c.wait('sync', 'vv', v_res)
        for tl in range(4):
            spill_r = c.dma('sync', Dr('resid_d')[(t0 + tl) * 128:(t0 + tl + 1) * 128, :],
                            A('resid_t').ap()[:, tl * D:(tl + 1) * D], 'sr0')
    if DEBUG_P1:
        c.wait('sync', 'vv', c.cnt['vv'])
        c.wait('sync', 'ss', c.cnt['ss'])
        c.wait('sync', 'mm', c.cnt['mm'])
        c.dma('sync', Dr('xmid_d')[0:128, :], A('v1b')[:, :], 'dbg1')
        c.dma('sync', Dr('xmid_d')[128:256, :], A('v2b')[:, :], 'dbg1')
        c.dma('sync', Dr('xmid_d')[256:384, :], A('gab')[:, :], 'dbg1')
        c.dma('sync', Dr('xmid_d')[384:385, :], A('v1row')[0:1, :], 'dbg1')
        c.dma('sync', Dr('xmid_d')[385:386, :], A('v2row')[0:1, :], 'dbg1')
        c.dma('sync', Dr('xmid_d')[386:387, :], A('garow').ap()[0:1, 0:1024], 'dbg1')
        c.dma('sync', Dr('xmid_d')[387:388, :], A('garow').ap()[0:1, 1024:2048], 'dbg1')
        c.dma('sync', Dr('xmid_d')[388:389, :], A('modrow').ap()[0:1, 1024:2048], 'dbg1')
        c.dma('sync', Dr('xmid_d')[400:528, :], A('rope_c')[:, :], 'dbg1')
        c.dma('sync', Dr('xmid_d')[528:656, :], A('rope_s')[:, :], 'dbg1')
        c.dma('sync', Dr('xmid_d')[656:784, :], A('muv')[:, 0:1024], 'dbg1')
        c.dma('sync', Dr('xmid_d')[784:912, :], A('qpre')[:, 0:1024], 'dbg1')
        c.dma('sync', Dr('xmid_d')[912:1040, :], A('ropet')[:, 0:1024], 'dbg1')
        c.dma('sync', Dr('xmid_d')[1040:1168, 0:16], A('stat_sum')[:, 0:16], 'dbg1')
        c.dma('sync', Dr('xmid_d')[1168:1296, 0:16], A('stat_sq')[:, 0:16], 'dbg1')
        c.dma('sync', Dr('xmid_d')[1296:1424, 0:16], A('r1')[:, 0:16], 'dbg1')
        c.dma('sync', Dr('xmid_d')[1424:1552, 0:16], A('nmr1')[:, 0:16], 'dbg1')
        c.dma('sync', Dr('xmid_d')[1552:1680, 0:64], A('tiny')[:, :], 'dbg1')
        c.dma('sync', Dr('xmid_d')[394:395, 0:48], A('mcol')[0:1, :], 'dbg1')




def _p2(c, P, A, Dr, ph):
    ph.sb('wzres', (128, 8 * DZP), BF16)
    ph.sb('qhT', (128, 2 * 8 * SLAB), BF16)
    ph.sb('xbcT', (128, NCT * 516), BF16)
    ph.sb('xcv', (128, 2 * 512), BF16)
    ph.sb('zsl', (128, 2 * 512), BF16)
    ph.sb('cacc', (128, 2 * 512), BF16)
    ph.sb('dtr', (32, 512), F32)
    ph.sb('dtv', (32, 512), F32)
    ph.sb('lndt', (32, 512), F32)
    ph.sb('adt', (32, 512), F32)
    ph.sb('acsf', (32, 512), F32)
    ph.sb('acsg', (32, 512), F32)
    ph.sb('bhalf', (32, 2 * 512), BF16)
    ph.sb('blow', (32, 2 * 512), BF16)
    ph.sb('acol', (32, 1), F32)
    ph.sb('gcar', (32, 1), F32)
    ph.sb('convw', (128, NCT * 4), F32)
    ph.sb('convb', (128, NCT), F32)
    ph.sb('npw', (128, 8), F32)
    ph.sb('dtbc', (32, 1), F32)
    ph.sb('t32', (32, 512), F32)

    for k in range(8):
        c.dma('gpsimd', A('wzres')[:, k * DZP:(k + 1) * DZP], Dr('wzT')[k * 128:(k + 1) * 128, :], 'wzl')
    v_wz = c.cnt['wzl']
    c.dma('sync', A('convw')[:, :], Dr('conv_w')[:, :], 'c2')
    c.dma('sync', A('convb')[:, :], Dr('conv_b')[:, :], 'c2')
    c.dma('sync', A('npw')[:, :], Dr('npw_col')[:, :], 'c2')
    c.dma('sync', A('dtbc')[:, :], Dr('dtb_col')[:, :], 'c2')
    c.dma('sync', A('acol')[:, :], Dr('alog_col')[:, :], 'c2')
    v_c2 = c.cnt['c2']
    c.wait('scalar', 'c2', v_c2)
    v_a2 = act(c, A('acol')[:, :], A('acol')[:, :], AF.Exp, sem='ss')
    c.wait('vector', 'ss', v_a2)
    v_a3 = ts(c, A('acol')[:, :], A('acol')[:, :], -1.0, ALU.mult)

    vmemset(c, A('xbcT')[:, :], 0.0)
    v_ms = vmemset(c, A('gcar')[:, :], 0.0)

    zslv = [0, 0]
    v_dtr_used = 0
    caccv = [0, 0]
    bank_ev = [0, 0, 0, 0]
    v_dtev = 0
    v_lastmm = 0
    slab_mm = [0, 0]
    v_halo = v_ms
    v_sp = 0
    for s in range(NSLAB):
        sh = s % 2
        qbase = sh * 8 * SLAB
        c.wait('sync', 'mm', slab_mm[sh])
        vq = 0
        for k in range(8):
            vq = c.dmat('sync', A('qhT')[:, qbase + k * SLAB:qbase + (k + 1) * SLAB],
                        Dr('qhat_d')[s * SLAB:(s + 1) * SLAB, k * 128:(k + 1) * 128], 'qh%d' % sh)
        c.wait('vector', 'qh%d' % sh, vq)
        v_fold = 0
        for k in range(8):
            v_fold = ts(c, A('qhT')[:, qbase + k * SLAB:qbase + (k + 1) * SLAB],
                        A('qhT')[:, qbase + k * SLAB:qbase + (k + 1) * SLAB],
                        A('npw').ap()[:, k:k + 1], ALU.mult)
        if s > 0:
            c.wait('vector', 'ss', v_dtev)
            for ct in range(NCT):
                base = ct * 516
                v_halo = vcopy(c, A('xbcT')[:, base:base + 3], A('xbcT')[:, base + 512:base + 515])
        if s == 0:
            c.wait('tensor', 'wzl', v_wz)
        for m in range(MZ):
            bk = m % 4
            if m == 0:
                c.wait('tensor', 'vv', v_fold)
            c.wait('tensor', 'ss', bank_ev[bk])
            for k in range(8):
                vmm = mm(c, P[bk][:, :], A('wzres')[:, k * DZP + m * 128:k * DZP + (m + 1) * 128],
                         A('qhT')[:, qbase + k * SLAB:qbase + (k + 1) * SLAB],
                         start=(k == 0), stop=(k == 7), sem='mm')
            v_lastmm = vmm
            c.wait('scalar', 'mm', vmm)
            if m < 16:
                zh = m % 2
                c.wait('scalar', 'zs%d' % zh, zslv[zh])
                v_ev = act(c, A('zsl')[:, zh * 512:(zh + 1) * 512], P[bk][:, :], AF.Silu, sem='ss')
                c.wait('gpsimd', 'ss', v_ev)
                zslv[zh] = c.dma('gpsimd', Dr('zsilu_d')[m * 128:(m + 1) * 128, s * 512:(s + 1) * 512],
                                 A('zsl')[:, zh * 512:(zh + 1) * 512], 'zs%d' % zh)
                bank_ev[bk] = v_ev
            elif m < 34:
                ct = m - 16
                if s > 0 and m == 16:
                    c.wait('scalar', 'vv', v_halo)
                v_ev = scopy(c, A('xbcT')[:, ct * 516 + 3:ct * 516 + 515], P[bk][:, :], sem='ss')
                bank_ev[bk] = v_ev
                base = ct * 516
                hb = ct % 2
                ac = A('cacc').ap()[:, hb * 512:(hb + 1) * 512]
                c.wait('vector', 'ss', v_ev)
                ts(c, ac, A('xbcT')[:, base + 0:base + 512], A('convw').ap()[:, ct * 4:ct * 4 + 1], ALU.mult)
                for kk in range(1, 4):
                    v_cv = stt(c, ac, A('xbcT')[:, base + kk:base + kk + 512],
                               A('convw').ap()[:, ct * 4 + kk:ct * 4 + kk + 1], ac, ALU.mult, ALU.add)
                c.wait('scalar', 'vv', v_cv)
                c.wait('scalar', 'xc%d' % hb, caccv[hb])
                v_sl = act(c, A('xcv')[:, hb * 512:(hb + 1) * 512], ac, AF.Silu,
                           bias=A('convb').ap()[:, ct:ct + 1], sem='ss')
                c.wait('gpsimd', 'ss', v_sl)
                caccv[hb] = c.dma('gpsimd', Dr('xconv_d')[ct * 128:(ct + 1) * 128, s * 512:(s + 1) * 512],
                                  A('xcv')[:, hb * 512:(hb + 1) * 512], 'xc%d' % hb)
            else:
                c.wait('scalar', 'vv', v_dtr_used)
                v_ev = scopy(c, A('dtr')[:, :], P[bk][0:32, :], sem='ss')
                bank_ev[bk] = v_ev
                v_dtev = v_ev
        slab_mm[sh] = v_lastmm
        c.wait('vector', 'ss', v_dtev)
        v_v = ts(c, A('t32')[:, :], A('dtr')[:, :], A('dtbc').ap()[:, 0:1], ALU.add)
        v_dtr_used = v_v
        c.wait('scalar', 'vv', v_v)
        v_av = act(c, A('dtv')[:, :], A('t32')[:, :], AF.Abs, sem='ss')
        c.wait('scalar', 'ss', v_av)
        v_e = act(c, A('dtv')[:, :], A('dtv')[:, :], AF.Exp, scale=-1.0, sem='ss')
        c.wait('scalar', 'ss', v_e)
        v_l = act(c, A('dtv')[:, :], A('dtv')[:, :], AF.Ln, bias=1.0, sem='ss')
        c.wait('vector', 'ss', v_l)
        ts(c, A('t32')[:, :], A('t32')[:, :], 0.0, ALU.max)
        v_dt = tt(c, A('dtv')[:, :], A('dtv')[:, :], A('t32')[:, :], ALU.add)
        c.wait('scalar', 'vv', v_dt)
        v_ln = act(c, A('lndt')[:, :], A('dtv')[:, :], AF.Ln, sem='ss')
        c.wait('vector', 'ss', v_ln)
        v_adt = ts(c, A('adt')[:, :], A('dtv')[:, :], A('acol').ap()[:, 0:1], ALU.mult)
        for cl in range(4):
            sl = slice(cl * 128, (cl + 1) * 128)
            vs = vscan(c, A('acsf').ap()[:, sl], A('adt').ap()[:, sl], A('adt').ap()[:, sl], 0.0,
                       ALU.add, ALU.bypass)
        vg = vscan(c, A('acsg')[:, :], A('adt')[:, :], A('adt')[:, :], A('gcar').ap()[:, 0:1],
                   ALU.add, ALU.bypass)
        v_car = vcopy(c, A('gcar')[:, :], A('acsg').ap()[:, 511:512])
        bh = A('bhalf').ap()[:, sh * 512:(sh + 1) * 512]
        bl = A('blow').ap()[:, sh * 512:(sh + 1) * 512]
        c.wait('vector', 'sp%d' % sh, c.cnt.get('sp%d' % sh, 0))
        v1 = vcopy(c, bh, A('acsf')[:, :])
        v2 = tt(c, bl, A('acsf')[:, :], bh, ALU.subtract)
        c.wait('gpsimd', 'vv', v2)
        c.dma('gpsimd', Dr('acshi_d')[:, s * 512:(s + 1) * 512], bh, 'sp%d' % sh)
        vsp1 = c.dma('gpsimd', Dr('acslo_d')[:, s * 512:(s + 1) * 512], bl, 'sp%d' % sh)
        v3 = tt(c, A('t32')[:, :], A('lndt')[:, :], A('acsf')[:, :], ALU.subtract)
        c.wait('vector', 'sp%d' % sh, vsp1)
        v4 = vcopy(c, bh, A('t32')[:, :])
        v5 = tt(c, bl, A('t32')[:, :], bh, ALU.subtract)
        c.wait('gpsimd', 'vv', v5)
        c.dma('gpsimd', Dr('nahi_d')[:, s * 512:(s + 1) * 512], bh, 'sp%d' % sh)
        vsp2 = c.dma('gpsimd', Dr('nalo_d')[:, s * 512:(s + 1) * 512], bl, 'sp%d' % sh)
        c.wait('vector', 'sp%d' % sh, vsp2)
        v6 = vcopy(c, bh, A('acsg')[:, :])
        v7 = tt(c, bl, A('acsg')[:, :], bh, ALU.subtract)
        c.wait('gpsimd', 'vv', v7)
        c.dma('gpsimd', Dr('acsghi_d')[:, s * 512:(s + 1) * 512], bh, 'sp%d' % sh)
        c.dma('gpsimd', Dr('acsglo_d')[:, s * 512:(s + 1) * 512], bl, 'sp%d' % sh)
        c.dma('gpsimd', Dr('acs_d')[:, s * 512:(s + 1) * 512], A('acsf')[:, :], 'sp%d' % sh)
        v_sp = c.dma('gpsimd', Dr('dt_d')[:, s * 512:(s + 1) * 512], A('dtv')[:, :], 'sp%d' % sh)
        c.wait('vector', 'sp%d' % sh, v_sp)


def _p3(c, P, A, Dr, ph):
    ph.sb('bhi', (33, 2048), BF16)
    ph.sb('blo', (33, 2048), BF16)
    ph.sb('r00', (33, 2048), BF16)
    ph.sb('r01', (33, 2048), BF16)
    ph.sb('r10', (33, 2048), BF16)
    ph.sb('r11', (33, 2048), BF16)
    ph.sb('eacs', (64, 2048), BF16)
    ph.sb('acsf2', (32, 2048), F32)
    ph.sb('dtf2', (32, 2048), F32)
    ph.sb('xcf', (128, NCT * 128), BF16)
    ph.sb('xtok', (128, 2176), BF16)
    ph.sb('scob', (128, 128), BF16)
    ph.sb('gbuf', (128, 2048), BF16)
    ph.sb('eeb', (128, 2048), BF16)
    ph.sb('v2b2', (128, 2048), BF16)
    ph.sb('yoe', (128, 2048), BF16)
    ph.sb('dxt', (128, 2048), BF16)
    ph.sb('dxm', (128, 2048), BF16)
    ph.sb('ya', (128, 2048), F32)
    ph.sb('ypb', (128, 2048), BF16)
    ph.sb('srun', (128, 2048), F32)
    ph.sb('srunb', (128, 2048), BF16)
    ph.sb('sd', (128, 2048), F32)
    ph.sb('dsel', (32, 2048), BF16)
    ph.sb('dexpt', (128, 2048), BF16)
    ph.sb('f2b', (32, 128), BF16)
    ph.sb('e32', (32, 128), F32)
    ph.sb('dcol', (32, 1), F32)
    ph.sb('trilRs', (127, 2048), BF16)
    ph.sb('trilLs', (127, 128), BF16)
    ph.sb('sel32s', (32, 2048), BF16)
    ph.sb('sel64s', (64, 2048), BF16)
    ph.sb('ones32', (32, 128), BF16)
    ph.sb('dhp', (128, 16), F32)
    ph.sb('gcol', (128, 1), F32)

    c.dma('sync', A('trilRs')[:, :], Dr('trilR')[:, :], 'c3')
    c.dma('sync', A('trilLs')[:, :], Dr('trilL')[:, :], 'c3')
    c.dma('sync', A('sel32s')[:, :], Dr('sel32')[:, :], 'c3')
    c.dma('sync', A('sel64s')[:, :], Dr('sel64')[:, :], 'c3')
    c.dma('sync', A('ones32')[:, :], Dr('ones32b')[:, :], 'c3')
    c.dma('sync', A('dhp')[:, :], Dr('d_hp')[:, :], 'c3')
    c.dma('sync', A('gcol')[:, :], Dr('g_state')[:, :], 'c3')
    c.dma('sync', A('r00')[0:32, :], Dr('sel_h0')[:, :], 'c3')
    c.dma('sync', A('r01')[0:32, :], Dr('sel_h0')[:, :], 'c3')
    c.dma('sync', A('r10')[0:32, :], Dr('sel_h1')[:, :], 'c3')
    c.dma('sync', A('r11')[0:32, :], Dr('sel_h1')[:, :], 'c3')
    c.dma('sync', A('bhi')[0:32, :], Dr('nahi_d')[:, :], 'c3')
    c.dma('sync', A('blo')[0:32, :], Dr('nalo_d')[:, :], 'c3')
    c.dma('sync', A('eacs')[0:32, :], Dr('acshi_d')[:, :], 'c3')
    c.dma('sync', A('eacs')[32:64, :], Dr('acslo_d')[:, :], 'c3')
    c.dma('sync', A('acsf2')[:, :], Dr('acs_d')[:, :], 'c3')
    v_c3 = c.dma('sync', A('dtf2')[:, :], Dr('dt_d')[:, :], 'c3')
    vmemset(c, A('bhi')[32:33, :], 1.0)
    vmemset(c, A('blo')[32:33, :], 1.0)
    vmemset(c, A('srun')[:, :], 0.0)
    vmemset(c, A('srunb')[:, :], 0.0)
    c.wait('vector', 'c3', v_c3)
    v_dx = vcopy(c, A('dxt').ap().rearrange("p (a l) -> p a l", a=16),
                 A('dhp').ap().unsqueeze(2).broadcast_to([128, 16, 128]), sem='vv')
    c.wait('tensor', 'c3', v_c3)
    c.wait('scalar', 'c3', v_c3)

    rhs_free = {('r00'): 0, ('r01'): 0, ('r10'): 0, ('r11'): 0}
    v_xcf = 0; v_xtok = 0
    v_scan = 0; v_a1 = 0; v_a2 = 0; v_v2c = 0; v_sts = 0
    v_scev = 0; v_gev = [0, 0]; v_eeev = 0; v_ypsp = 0; v_dexpmm = 0
    v_diag = [0, 0]; v_yoff = 0; v_dxm = 0; v_f2p = 0; v_dsp = 0
    for ch in range(NCH):
        sl = slice(ch * 128, (ch + 1) * 128)
        c.wait('sync', 'vv', max(v_dxm if ch > 0 else 0, v_dx))
        c.wait('sync', 'mm', v_yoff)
        v_xcf = c.dma('sync', A('xcf').ap().rearrange("p (a l) -> p a l", a=NCT),
                      Dr('xconv_d').rearrange("(a p) t -> a p t", p=128)[:, :, sl].transpose([1, 0, 2]), 'xf')
        c.wait('sync', 'mm', max(v_sts, v_diag[1]))
        v_xtok = c.dmat('sync', A('xtok')[:, :], Dr('xconv_d')[0:2176, sl], 'xk2')
        for half, (rh, rl) in enumerate((('r00', 'r01'), ('r10', 'r11'))):
            h0 = half * 16
            c.wait('sync', 'mm', rhs_free[rh])
            c.dma('sync', A(rh).ap()[32:33, :].rearrange("p (a b) -> p a b", a=16),
                  Dr('acshi_d')[h0:h0 + 16, sl].unsqueeze(0), 'fl')
            c.dma('sync', A(rl).ap()[32:33, :].rearrange("p (a b) -> p a b", a=16),
                  Dr('acslo_d')[h0:h0 + 16, sl].unsqueeze(0), 'fl')
        v_fl = c.cnt['fl']
        c.wait('tensor', 'xf', v_xcf)
        c.wait('tensor', 'ss', v_eeev)
        c.wait('tensor', 'vv', max(v_a1, v_scan))
        vsc = mm(c, P[4][:, 0:128], A('xcf')[:, 16 * 128:17 * 128], A('xcf')[:, 17 * 128:18 * 128],
                 start=True, stop=True, sem='mm')
        c.wait('vector', 'mm', vsc)
        v_scev = vcopy(c, A('scob')[:, :], P[4][:, 0:128], sem='vv')
        for half in range(2):
            rh, rl = (('r00', 'r01'), ('r10', 'r11'))[half]
            c.wait('tensor', 'fl', v_fl)
            if half == 0:
                c.wait('tensor', 'vv', v_scev)
            for j in range(4):
                mm(c, P[4 + j][:, :], A('bhi').ap()[:, sl], A(rh)[:, j * 512:(j + 1) * 512],
                   start=True, stop=False, sem=None)
                mm(c, P[4 + j][:, :], A('blo').ap()[:, sl], A(rl)[:, j * 512:(j + 1) * 512],
                   start=False, stop=False, sem=None)
                vgb = mm(c, P[4 + j][:, :], A('trilLs')[:, :], A('trilRs')[:, j * 512:(j + 1) * 512],
                         start=False, stop=True, sem='mm')
            rhs_free[rh] = vgb
            c.wait('scalar', 'mm', vgb)
            c.wait('scalar', 'mm', v_diag[1 - half] if half == 1 else v_diag[1])
            for j in range(4):
                v_gev[half] = act(c, A('gbuf')[:, j * 512:(j + 1) * 512], P[4 + j][:, :], AF.Exp, sem='ss')
            c.wait('vector', 'ss', v_gev[half])
            v_gtt = tt(c, A('gbuf').ap().rearrange("p (h l) -> p h l", h=16),
                       A('gbuf').ap().rearrange("p (h l) -> p h l", h=16),
                       A('scob').ap().unsqueeze(1).broadcast_to([128, 16, 128]), ALU.mult, sem='vv')
            c.wait('tensor', 'vv', v_gtt)
            c.wait('tensor', 'xk2', v_xtok)
            if half == 0:
                c.wait('tensor', 'vv', max(v_a2, v_v2c))
                c.wait('tensor', 'vv', v_scan)
            for hl in range(16):
                h = half * 16 + hl
                ct = h // 2
                po = (ct % 4) * 128
                rows = slice(0, 64) if h % 2 == 0 else slice(64, 128)
                v_dg = mm(c, P[ct // 4][rows, po:po + 128], A('xtok')[:, h * 64:(h + 1) * 64],
                          A('gbuf')[:, hl * 128:(hl + 1) * 128],
                          start=True, stop=True, sem='mm')
            v_diag[half] = v_dg
        c.wait('tensor', 'ss', v_gev[1])
        for ct in range(16):
            v_ee = mm(c, P[4 + ct // 4][:, (ct % 4) * 128:(ct % 4) * 128 + 128],
                      A('sel64s')[:, ct * 128:(ct + 1) * 128], A('eacs').ap()[:, sl],
                      start=True, stop=True, sem='mm')
        c.wait('scalar', 'mm', v_ee)
        c.wait('scalar', 'vv', v_a1)
        for j in range(4):
            v_eeev = act(c, A('eeb')[:, j * 512:(j + 1) * 512], P[4 + j][:, :], AF.Exp, sem='ss')
        c.wait('tensor', 'ss', v_eeev)
        for ct in range(16):
            v_yoff = mm(c, P[4 + ct // 4][:, (ct % 4) * 128:(ct % 4) * 128 + 128],
                        A('srunb').ap()[:, ct * 128:(ct + 1) * 128], A('xcf')[:, 17 * 128:18 * 128],
                        start=True, stop=True, sem='mm')
        c.wait('vector', 'mm', v_yoff)
        for j in range(4):
            v_a1 = stt(c, A('yoe')[:, j * 512:(j + 1) * 512], A('eeb').ap()[:, j * 512:(j + 1) * 512],
                       1.0, P[4 + j][:, :], ALU.mult, ALU.mult, sem='vv')
        v_dxm = tt(c, A('dxm')[:, :], A('xcf').ap()[:, 0:2048], A('dxt').ap()[:, :], ALU.mult, sem='vv')
        for j in range(4):
            v_a2 = stt(c, A('ya')[:, j * 512:(j + 1) * 512], A('dxm').ap()[:, j * 512:(j + 1) * 512],
                       1.0, P[j][:, :], ALU.mult, ALU.add, sem='vv')
        c.wait('vector', 'yp', v_ypsp)
        v_a3 = tt(c, A('ypb')[:, :], A('ya')[:, :], A('yoe')[:, :], ALU.add, sem='vv')
        c.wait('gpsimd', 'vv', v_a3)
        v_ypsp = c.dma('gpsimd', Dr('ypre_d').rearrange("(a p) t -> a p t", p=128)[:, :, sl].transpose([1, 0, 2]),
                       A('ypb').ap().rearrange("p (a l) -> p a l", a=16), 'yp')
        aend = A('acsf2').ap()[:, ch * 128 + 127:ch * 128 + 128]
        c.wait('scalar', 'vv', v_f2p)
        v_f2e = act(c, A('e32')[:, :], A('acsf2').ap()[:, sl], AF.Exp, scale=-1.0, bias=aend, sem='ss')
        c.wait('vector', 'ss', v_f2e)
        v_f2 = tt(c, A('f2b')[:, :], A('e32')[:, :], A('dtf2').ap()[:, sl], ALU.mult, sem='vv')
        v_f2p = v_f2
        c.wait('tensor', 'vv', max(v_f2, v_a2))
        for j in range(4):
            v_f2mm = mm(c, P[j][:, :], A('f2b')[:, :], A('sel32s')[:, j * 512:(j + 1) * 512],
                        start=True, stop=True, sem='mm')
        c.wait('vector', 'mm', v_f2mm)
        for j in range(4):
            v_v2c = stt(c, A('v2b2')[:, j * 512:(j + 1) * 512], A('xtok').ap()[:, j * 512:(j + 1) * 512],
                        1.0, P[j][:, :], ALU.mult, ALU.mult, sem='vv')
        c.wait('tensor', 'vv', max(v_a1, v_v2c))
        for j in range(4):
            v_sts = mm(c, P[4 + j][:, :], A('xtok')[:, 2048:2176], A('v2b2')[:, j * 512:(j + 1) * 512],
                       start=True, stop=True, sem='mm')
        c.wait('scalar', 'vv', v_dsp)
        v_de = act(c, A('dcol')[:, :], aend, AF.Exp, sem='ss')
        c.wait('vector', 'ss', v_de)
        v_ds = ts(c, A('dsel')[:, :], A('sel32s')[:, :], A('dcol').ap()[:, 0:1], ALU.mult, sem='vv')
        v_dsp = v_ds
        c.wait('tensor', 'vv', max(v_ds, v_v2c))
        for j in range(4):
            v_dexpmm = mm(c, P[j][:, :], A('ones32')[:, :], A('dsel')[:, j * 512:(j + 1) * 512],
                          start=True, stop=True, sem='mm')
        c.wait('vector', 'mm', max(v_dexpmm, v_sts))
        for j in range(4):
            tt(c, A('sd')[:, j * 512:(j + 1) * 512], A('srun')[:, j * 512:(j + 1) * 512], P[j][:, :], ALU.mult)
            v_scan = stt(c, A('srun')[:, j * 512:(j + 1) * 512], A('sd').ap()[:, j * 512:(j + 1) * 512],
                         1.0, P[4 + j][:, :], ALU.mult, ALU.add, sem='vv')
        v_scan = vcopy(c, A('srunb')[:, :], A('srun')[:, :], sem='vv')
    v_g = ts(c, A('sd')[:, :], A('srun')[:, :], A('gcol').ap()[:, 0:1], ALU.mult, sem='vv')
    c.wait('gpsimd', 'vv', v_g)
    c.dma('gpsimd', Dr('st_in')[:, :], A('sd')[:, :], 'stw')


def _p4(c, P, A, Dr, ph):
    ph.sb('sinit', (128, 2048), F32)
    ph.sb('sinitb', (128, 2048), BF16)
    ph.sb('ws', (128, 16 * 1024), BF16)
    ph.sb('eag', (64, 2048), BF16)
    ph.sb('sel64s2', (64, 2048), BF16)
    ph.sb('ypt', (128, 2048), BF16)
    ph.sb('zst', (128, 2048), BF16)
    ph.sb('ctb', (128, 128), BF16)
    ph.sb('egb', (128, 2048), BF16)
    ph.sb('ycb', (128, 2048), BF16)
    ph.sb('ygz', (128, 2048), BF16)
    ph.sb('sqb', (128, 2048), BF16)
    ph.sb('xmt', (128, 1024), F32)
    ph.sb('rst', (128, 1024), F32)
    ph.sb('xdl', (128, 1024), F32)
    ph.sb('guse', (128, 1), F32)
    ph.sb('ngw', (128, 16), F32)
    ph.sb('mrow', (1, 128), F32)

    v_cc = c.op('gpsimd', lambda e: e.collective_compute(
        "AllReduce", ALU.add, replica_groups=GROUPS,
        ins=[c.dr['st_in'].opt()], outs=[c.dr['st_out'].opt()]), sem='cc')
    c.wait('gpsimd', 'cc', v_cc)
    v_si = c.dma('gpsimd', A('sinit')[:, :], Dr('st_out')[:, :], 'si')
    c.dma('sync', A('guse')[:, :], Dr('g_use')[:, :], 'c4')
    c.dma('sync', A('ngw')[:, :], Dr('ngw_col')[:, :], 'c4')
    c.dma('sync', A('sel64s2')[:, :], Dr('sel64')[:, :], 'c4')
    c.dma('sync', A('eag')[0:32, :], Dr('acsghi_d')[:, :], 'c4')
    v_c4 = c.dma('sync', A('eag')[32:64, :], Dr('acsglo_d')[:, :], 'c4')
    for k in range(16):
        c.dma('gpsimd', A('ws')[:, k * 1024:(k + 1) * 1024], Dr('woutT')[k * 128:(k + 1) * 128, :], 'wo')
    v_wo = c.cnt['wo']
    c.wait('vector', 'si', v_si)
    c.wait('vector', 'c4', v_c4)
    ts(c, A('sinit')[:, :], A('sinit')[:, :], A('guse').ap()[:, 0:1], ALU.mult)
    v_sib = vcopy(c, A('sinitb')[:, :], A('sinit')[:, :], sem='vv')
    c.wait('vector', 'wo', v_wo)
    for k in range(16):
        ts(c, A('ws')[:, k * 1024:(k + 1) * 1024], A('ws')[:, k * 1024:(k + 1) * 1024],
           A('ngw').ap()[:, k:k + 1], ALU.mult)
        v_wf = tt(c, A('ws')[:, k * 1024:(k + 1) * 1024], A('ws')[:, k * 1024:(k + 1) * 1024],
                  A('gab').ap(), ALU.mult, sem='vv')

    v_b1 = 0; v_b3 = 0; v_egev = 0; v_yc = 0; v_o2 = 0; v_wmm = 0; v_sqmm = 0; v_r3 = 0
    v_ld = {}
    for t in range(NTT):
        tsl = slice(t * 128, (t + 1) * 128)
        c.wait('sync', 'vv', v_b3)
        v_yp = c.dma('sync', A('ypt').ap().rearrange("p (a l) -> p a l", a=16),
                     Dr('ypre_d').rearrange("(a p) t -> a p t", p=128)[:, :, tsl].transpose([1, 0, 2]), 'yp2')
        v_zs = c.dma('sync', A('zst').ap().rearrange("p (a l) -> p a l", a=16),
                     Dr('zsilu_d').rearrange("(a p) t -> a p t", p=128)[:, :, tsl].transpose([1, 0, 2]), 'zs2')
        c.wait('sync', 'mm', v_yc)
        v_ct = c.dma('sync', A('ctb')[:, :], Dr('xconv_d')[2176:2304, tsl], 'ct2')
        c.wait('sync', 'vv', v_o2)
        v_xm = c.dma('sync', A('xmt')[:, :], Dr('x_tok')[tsl, :], 'xk4')
        v_rr = c.dma('sync', A('rst')[:, :], Dr('resid_d')[tsl, :], 'rr')
        c.wait('tensor', 'c4', v_c4)
        c.wait('tensor', 'vv', v_r3)
        c.wait('tensor', 'ss', v_egev)
        for ct in range(16):
            v_eg = mm(c, P[4 + ct // 4][:, (ct % 4) * 128:(ct % 4) * 128 + 128],
                      A('sel64s2')[:, ct * 128:(ct + 1) * 128], A('eag').ap()[:, tsl],
                      start=True, stop=True, sem='mm')
        c.wait('scalar', 'mm', v_eg)
        c.wait('scalar', 'vv', v_b1)
        for j in range(4):
            v_egev = act(c, A('egb')[:, j * 512:(j + 1) * 512], P[4 + j][:, :], AF.Exp, sem='ss')
        c.wait('tensor', 'ct2', v_ct)
        c.wait('tensor', 'vv', max(v_sib, v_wf))
        c.wait('tensor', 'vv', v_o2)
        for ct in range(16):
            v_yc = mm(c, P[ct // 4][:, (ct % 4) * 128:(ct % 4) * 128 + 128],
                      A('sinitb')[:, ct * 128:(ct + 1) * 128], A('ctb')[:, :],
                      start=True, stop=True, sem='mm')
        c.wait('vector', 'mm', v_yc)
        c.wait('vector', 'ss', v_egev)
        c.wait('vector', 'yp2', v_yp)
        c.wait('vector', 'zs2', v_zs)
        for j in range(4):
            v_b1 = stt(c, A('ycb')[:, j * 512:(j + 1) * 512], A('egb').ap()[:, j * 512:(j + 1) * 512],
                       1.0, P[j][:, :], ALU.mult, ALU.mult, sem='vv')
        tt(c, A('ypt')[:, :], A('ypt')[:, :], A('ycb')[:, :], ALU.add)
        v_b3 = tt(c, A('ygz')[:, :], A('ypt')[:, :], A('zst')[:, :], ALU.mult, sem='vv')
        c.wait('scalar', 'vv', v_b3)
        c.wait('scalar', 'mm', v_sqmm)
        v_sq = act(c, A('sqb')[:, :], A('ygz')[:, :], AF.Square, sem='ss')
        c.wait('tensor', 'ss', max(v_sq, v_egev))
        for ct in range(16):
            v_sqmm = mm(c, P[4][0:1, 0:128], A('onesb')[:, 0:1], A('sqb')[:, ct * 128:(ct + 1) * 128],
                        start=(ct == 0), stop=(ct == 15), sem='mm')
        c.wait('vector', 'mm', v_sqmm)
        v_mr = vcopy(c, A('mrow')[0:1, :], P[4][0:1, 0:128], sem='vv')
        v_m2 = ts(c, A('mrow')[0:1, :], A('mrow')[0:1, :], 1.0 / DIN, ALU.mult, EPS, ALU.add, sem='vv')
        c.wait('scalar', 'vv', v_m2)
        v_rc = sinv(c, A('mrow').ap()[0:1, :], A('mrow').ap()[0:1, :], -0.5, sem='ss')
        c.wait('tensor', 'ss', v_rc)
        v_r3t = mm(c, P[4][:, 256:257], A('mrow')[0:1, :], A('onesf')[0:1, 0:1], start=True, stop=True, sem='mm')
        c.wait('vector', 'mm', v_r3t)
        v_r3 = vcopy(c, A('r3').ap()[:, t:t + 1], P[4][:, 256:257], sem='vv')
        c.wait('tensor', 'vv', v_b1)
        for j in range(2):
            for ct in range(16):
                v_wmm = mm(c, P[j][:, :], A('ygz')[:, ct * 128:(ct + 1) * 128],
                           A('ws')[:, ct * 1024 + j * 512:ct * 1024 + j * 512 + 512],
                           start=(ct == 0), stop=(ct == 15), sem='mm')
        c.wait('vector', 'mm', v_wmm)
        c.wait('vector', 'xk4', v_xm)
        c.wait('vector', 'rr', v_rr)
        c.wait('vector', 'vv', v_r3)
        for j in range(2):
            stt(c, A('xdl')[:, j * 512:(j + 1) * 512], P[j][:, :], A('r3').ap()[:, t:t + 1],
                A('rst').ap()[:, j * 512:(j + 1) * 512], ALU.mult, ALU.add)
        v_o2 = tt(c, A('xdl')[:, :], A('xmt')[:, :], A('xdl')[:, :], ALU.add, sem='vv')
        c.wait('gpsimd', 'vv', v_o2)
        c.dma('gpsimd', Dr('xmid_d')[tsl, :], A('xdl')[:, :], 'xm')


def _p5(c, P, A, Dr, ph):
    ph.sb('xmt2', (128, 1024), F32)
    ph.sb('h2f', (128, 1024), F32)
    ph.sb('h2t', (128, 1024), BF16)
    ph.sb('w1s', (128, 8 * 512), BF16)
    ph.sb('w1Ls', (128, 8 * 512), BF16)
    ph.sb('w2s', (128, 4 * 64), BF16)
    ph.sb('sib', (64, 1024), BF16)
    ph.sb('o64', (64, 1), BF16)
    ph.sb('meanT', (128, 8), BF16)
    ph.sb('b1e', (128, 4), F32)
    ph.sb('b2b', (128, 64), F32)
    ph.sb('itc', (128, 1), F32)
    ph.sb('taus', (1, 1), F32)
    ph.sb('b2r', (1, 64), F32)
    ph.sb('hro', (128, 4 * 512), BF16)
    ph.sb('h2T', (128, 8 * 512), BF16)
    ph.sb('rlog', (128, 64), F32)
    ph.sb('m8', (128, 8), F32)
    ph.sb('msk', (128, 64), F32)
    ph.sb('erow', (128, 64), F32)
    ph.sb('alf', (128, 64), F32)
    ph.sb('identS', (128, 128), F32)
    ph.sb('h2r', (128, 1024), BF16)
    ph.sb('slin', (64, 1040), F32)

    for k in range(8):
        c.dma('gpsimd', A('w1s')[:, k * 512:(k + 1) * 512], Dr('w1')[k * 128:(k + 1) * 128, :], 'w1l')
        c.dma('gpsimd', A('w1Ls')[:, k * 512:(k + 1) * 512], Dr('w1')[1024 + k * 128:1024 + (k + 1) * 128, :], 'w1l')
    for k in range(4):
        c.dma('gpsimd', A('w2s')[:, k * 64:(k + 1) * 64], Dr('w2')[k * 128:(k + 1) * 128, :], 'w1l')
    c.dma('gpsimd', A('sib')[:, :], Dr('slot_init')[:, :], 'w1l')
    v_w1 = c.cnt['w1l']
    c.dma('sync', A('taus')[:, :], Dr('tau')[:, :], 'c5')
    c.dma('sync', A('b2r')[:, :], Dr('b2_row')[:, :], 'c5')
    c.dma('sync', A('identS')[:, :], Dr('identf')[:, :], 'c5')
    v_c5 = c.dma('sync', A('b1e')[:, 0:4], Dr('b1_col')[:, :], 'c5')
    v_o64 = vmemset(c, A('o64')[:, :], 1.0 / 64, sem='vv')
    c.wait('scalar', 'c5', v_c5)
    v_at = act(c, A('taus')[:, :], A('taus')[:, :], AF.Abs, sem='ss')
    c.wait('vector', 'ss', v_at)
    v_ta = ts(c, A('taus')[:, :], A('taus')[:, :], 0.1, ALU.add, sem='vv')
    c.wait('scalar', 'vv', v_ta)
    v_it = sinv(c, A('taus').ap()[:, :], A('taus').ap()[:, :], -1.0, sem='ss')
    c.wait('tensor', 'ss', v_it)
    v_itmm = mm(c, P[4][:, 300:301], A('onesf')[0:1, :], A('taus')[0:1, 0:1], start=True, stop=True, sem='mm')
    c.wait('vector', 'mm', v_itmm)
    v_itc = vcopy(c, A('itc')[:, :], P[4][:, 300:301], sem='vv')
    c.wait('tensor', 'vv', v_itc)
    v_b2mm = mm(c, P[4][:, 320:384], A('onesf')[0:1, :], A('b2r')[0:1, :], start=True, stop=True, sem='mm')
    c.wait('vector', 'mm', v_b2mm)
    vcopy(c, A('b2b')[:, :], P[4][:, 320:384])
    v_b2b = ts(c, A('b2b')[:, :], A('b2b')[:, :], A('itc').ap()[:, 0:1], ALU.mult, sem='vv')
    c.wait('tensor', 'w1l', v_w1)
    c.wait('tensor', 'vv', max(v_b2b, v_o64))
    for k in range(8):
        v_mmn = mm(c, P[4][:, 400 + k:401 + k], A('sib')[:, k * 128:(k + 1) * 128], A('o64')[:, 0:1],
                   start=True, stop=True, sem='mm')
    c.wait('vector', 'mm', v_mmn)
    v_mt = vcopy(c, A('meanT')[:, :], P[4][:, 400:408], sem='vv')
    c.wait('tensor', 'vv', v_mt)
    for m in range(4):
        for k in range(8):
            v_b1mm = mm(c, P[5][:, 400 + m:401 + m], A('w1Ls')[:, k * 512 + m * 128:k * 512 + (m + 1) * 128],
                        A('meanT')[:, k:k + 1], start=(k == 0), stop=(k == 7), sem='mm')
    c.wait('vector', 'mm', v_b1mm)
    c.wait('vector', 'c5', v_c5)
    v_b1e = stt(c, A('b1e')[:, 0:4], P[5][:, 400:404], 1.0, A('b1e').ap()[:, 0:4], ALU.mult, ALU.add, sem='vv')

    v_h2sp = [0, 0]
    v_gel = 0; v_slmm = 0; v_h2rd = 0; v_alt = 0; v_lnt = 0
    gel_ev = {}; gel_used = {}
    for s in range(NSLAB):
        for tl in range(4):
            t = s * 4 + tl
            tsl = slice(t * 128, (t + 1) * 128)
            hb = t % 2
            c.wait('sync', 'vv', v_lnt)
            v_xm = c.dma('sync', A('xmt2')[:, :], Dr('xmid_d')[tsl, :], 'xm2')
            c.wait('scalar', 'xm2', v_xm)
            act(c, A('xmt2')[:, :], A('xmt2')[:, :], AF.Copy, accum_out=A('stat_sum').ap()[:, t:t + 1])
            v_st = act(c, A('h2f')[:, :], A('xmt2')[:, :], AF.Square, accum_out=A('stat_sq').ap()[:, t:t + 1], sem='ss')
            c.wait('vector', 'ss', v_st)
            ts(c, A('tiny')[:, 40:41], A('stat_sum').ap()[:, t:t + 1], 1.0 / D, ALU.mult)
            ts(c, A('tiny')[:, 41:42], A('stat_sq').ap()[:, t:t + 1], 1.0 / D, ALU.mult)
            tt(c, A('tiny')[:, 42:43], A('tiny')[:, 40:41], A('tiny')[:, 40:41], ALU.mult)
            tt(c, A('tiny')[:, 43:44], A('tiny')[:, 41:42], A('tiny')[:, 42:43], ALU.subtract)
            v_var = ts(c, A('tiny')[:, 43:44], A('tiny')[:, 43:44], EPS, ALU.add, sem='vv')
            c.wait('scalar', 'vv', v_var)
            v_sq = sinv(c, A('tiny').ap()[:, 45:46], A('tiny').ap()[:, 43:44], -0.5, sem='ss')
            c.wait('vector', 'ss', v_sq)
            stt(c, A('tiny')[:, 46:47], A('tiny').ap()[:, 40:41], -1.0, A('tiny').ap()[:, 45:46],
                ALU.mult, ALU.mult)
            ts(c, A('h2f')[:, :], A('xmt2')[:, :], A('tiny').ap()[:, 45:46], ALU.mult,
               A('tiny').ap()[:, 46:47], ALU.add)
            tt(c, A('h2f')[:, :], A('h2f')[:, :], A('cb1b').ap(), ALU.mult)
            v_lnt = tt(c, A('h2t')[:, :], A('h2f')[:, :], A('sbb').ap(), ALU.add, sem='vv')
            c.wait('gpsimd', 'vv', v_lnt)
            v_h2sp[hb] = c.dma('gpsimd', Dr('h2_d')[tsl, :], A('h2t')[:, :], 'h2s%d' % hb)
        c.wait('sync', 'h2s0', v_h2sp[0])
        c.wait('sync', 'h2s1', v_h2sp[1])
        c.wait('sync', 'mm', v_gel)
        vq = 0
        for k in range(8):
            vq = c.dmat('sync', A('h2T')[:, k * SLAB:(k + 1) * SLAB],
                        Dr('h2_d')[s * SLAB:(s + 1) * SLAB, k * 128:(k + 1) * 128], 'h2t2')
        c.wait('tensor', 'h2t2', vq)
        if s == 0:
            c.wait('tensor', 'vv', v_b1e)
            c.wait('tensor', 'c5', v_c5)
        for m in range(4):
            bk = m % 4
            c.wait('tensor', 'ss', gel_ev.get(bk, 0))
            for k in range(8):
                v_w1mm = mm(c, P[bk][:, :], A('w1s')[:, k * 512 + m * 128:k * 512 + (m + 1) * 128],
                            A('h2T')[:, k * SLAB:(k + 1) * SLAB], start=(k == 0), stop=(k == 7), sem='mm')
            c.wait('scalar', 'mm', v_w1mm)
            v_gl = act(c, A('hro')[:, m * 512:(m + 1) * 512], P[bk][:, :], AF.Gelu,
                       bias=A('b1e').ap()[:, m:m + 1], sem='ss')
            gel_ev[bk] = v_gl
        v_gel = v_w1mm
        for tl in range(4):
            t = s * 4 + tl
            tsl = slice(t * 128, (t + 1) * 128)
            c.wait('tensor', 'ss', v_gl)
            c.wait('tensor', 'vv', v_alt)
            for m in range(4):
                v_w2mm = mm(c, P[4][:, 0:64], A('hro')[:, m * 512 + tl * 128:m * 512 + (tl + 1) * 128],
                            A('w2s')[:, m * 64:(m + 1) * 64], start=(m == 0), stop=(m == 3), sem='mm')

            c.wait('vector', 'mm', v_w2mm)
            c.wait('vector', 'vv', v_b2b)
            v_rl = stt(c, A('rlog')[:, :], P[4][:, 0:64], A('itc').ap()[:, 0:1], A('b2b').ap()[:, :],
                       ALU.mult, ALU.add, sem='vv')
            vmax8(c, A('m8')[:, :], A('rlog')[:, :])
            ts(c, A('tiny')[:, 48:49], A('m8').ap()[:, 0:1], -1.0, ALU.mult)
            v_msk = ts(c, A('msk')[:, :], A('rlog')[:, :], A('m8').ap()[:, 7:8], ALU.is_ge, sem='vv')
            c.wait('scalar', 'vv', v_msk)
            v_er = act(c, A('erow')[:, :], A('rlog')[:, :], AF.Exp, bias=A('tiny').ap()[:, 48:49], sem='ss')
            c.wait('vector', 'ss', v_er)
            tt(c, A('erow')[:, :], A('erow')[:, :], A('msk')[:, :], ALU.mult)
            v_sumr = vreduce(c, A('tiny').ap()[:, 49:50], A('erow').ap()[:, :], ALU.add, sem='vv')
            c.wait('scalar', 'vv', v_sumr)
            v_rs = sinv(c, A('tiny').ap()[:, 50:51], A('tiny').ap()[:, 49:50], -1.0, sem='ss')
            c.wait('vector', 'ss', v_rs)
            ts(c, A('alf')[:, :], A('erow')[:, :], A('tiny').ap()[:, 50:51], ALU.mult)
            v_al = vcopy(c, A('alpha').ap()[:, t * 64:(t + 1) * 64], A('alf')[:, :], sem='vv')
            c.wait('tensor', 'vv', v_al)
            v_atr = c.op('tensor', lambda e, o=P[4][0:64, 64:192], i=A('alf').ap()[:, :],
                         idt=A('identS')[:, :]: e.transpose(o, i, idt), sem='mm')
            c.wait('vector', 'mm', v_atr)
            v_alt = vcopy(c, A('alphaT').ap()[:, tsl], P[4][0:64, 64:192], sem='vv')
            c.wait('sync', 'mm', v_slmm)
            v_h2rd = c.dma('sync', A('h2r')[:, :], Dr('h2_d')[tsl, :], 'h2r')
            c.wait('tensor', 'h2r', v_h2rd)
            mm(c, P[5][0:64, :], A('alpha').ap()[:, t * 64:(t + 1) * 64], A('h2r')[:, 0:512],
               start=(t == 0), stop=(t == NTT - 1), sem=None)
            mm(c, P[6][0:64, :], A('alpha').ap()[:, t * 64:(t + 1) * 64], A('h2r')[:, 512:1024],
               start=(t == 0), stop=(t == NTT - 1), sem=None)
            v_slmm = mm(c, P[7][0:64, 0:1], A('alpha').ap()[:, t * 64:(t + 1) * 64], A('onesb')[:, 0:1],
                        start=(t == 0), stop=(t == NTT - 1), sem='mm')
    c.wait('vector', 'mm', v_slmm)
    vcopy(c, A('slin')[:, 0:512], P[5][0:64, :])
    vcopy(c, A('slin')[:, 512:1024], P[6][0:64, :])
    v_sl = vcopy(c, A('slin')[:, 1024:1025], P[7][0:64, 0:1], sem='vv')
    c.wait('gpsimd', 'vv', v_sl)
    v_slw = c.dma('gpsimd', Dr('sl_in')[:, :], A('slin')[:, 0:1025], 'slw')
    c.wait('gpsimd', 'slw', v_slw)
    v_cc2 = c.op('gpsimd', lambda e: e.collective_compute(
        "AllReduce", ALU.add, replica_groups=GROUPS,
        ins=[c.dr['sl_in'].opt()], outs=[c.dr['sl_out'].opt()]), sem='cc')
    c.wait('gpsimd', 'cc', v_cc2)


def _p6(c, P, A, Dr, ph):
    ph.sb('slot', (64, 1040), F32)
    ph.sb('sif', (64, 1024), F32)
    ph.sb('sibb', (64, 1024), BF16)
    ph.sb('sitT', (128, 8 * 64), BF16)
    ph.sb('siT', (128, 8 * 64), BF16)
    ph.sb('wres', (128, 8 * 3072), BF16)
    ph.sb('wqs', (128, 8 * 256), BF16)
    ph.sb('wks', (128, 8 * 256), BF16)
    ph.sb('gi', (64, 3072), F32)
    ph.sb('gh', (64, 3072), F32)
    ph.sb('gt', (64, 1024), F32)
    ph.sb('snew', (64, 1024), F32)
    ph.sb('snb', (64, 1024), BF16)
    ph.sb('snT', (128, 8 * 64), BF16)
    ph.sb('qf', (128, 2 * 64), BF16)
    ph.sb('kf', (128, 2 * 64), BF16)
    ph.sb('vtok', (64, 1024), BF16)
    ph.sb('att', (64, 64), F32)
    ph.sb('attT', (64, 64), BF16)
    ph.sb('sc2', (64, 1024), F32)
    ph.sb('yv', (64, 1024), F32)
    ph.sb('ylnT', (128, 8 * 64), BF16)
    ph.sb('svf', (128, 8 * 64), BF16)
    ph.sb('svw', (64, 1024), BF16)
    ph.sb('gb64', (64, 1024), F32)
    ph.sb('bo64', (64, 1024), F32)
    ph.sb('bihb', (1, 3072), BF16)
    ph.sb('bhhb', (1, 3072), BF16)
    ph.sb('bvb', (1, 1024), BF16)
    ph.sb('o64r', (1, 64), BF16)
    ph.sb('lnbb', (128, 8), BF16)
    ph.sb('bcol2', (128, 8), F32)
    ph.sb('bvpc', (128, 8), F32)
    ph.sb('lngc', (128, 8), F32)
    ph.sb('bqc', (128, 2), F32)
    ph.sb('bkc', (128, 2), F32)
    ph.sb('bias2', (128, 8), F32)
    ph.sb('xmt3', (128, 1024), F32)
    ph.sb('yout', (128, 1024), F32)
    ph.sb('t64', (64, 16), F32)
    ph.sb('borow', (1, 1024), F32)
    ph.sb('identS2', (128, 128), F32)

    c.dma('sync', A('slot')[:, 0:1025], Dr('sl_out')[:, :], 'c6')
    c.dma('sync', A('sif')[:, :], Dr('slot_init')[:, :], 'c6')
    c.dma('sync', A('bcol2')[:, 0:8], Dr('lnb_col')[:, :], 'c6')
    c.dma('sync', A('bvpc')[:, 0:8], Dr('bvp_col')[:, :], 'c6')
    c.dma('sync', A('lngc')[:, 0:8], Dr('lng_col')[:, :], 'c6')
    c.dma('sync', A('bqc')[:, 0:2], Dr('bq_col')[:, :], 'c6')
    c.dma('sync', A('bkc')[:, 0:2], Dr('bk_col')[:, :], 'c6')
    c.dma('sync', A('identS2')[:, :], Dr('identf')[:, :], 'c6')
    c.dma('sync', A('borow')[:, :], Dr('bo_row')[:, :], 'c6')
    for k in range(8):
        c.dma('gpsimd', A('sitT')[:, k * 64:(k + 1) * 64], Dr('slot_initT')[k * 128:(k + 1) * 128, :], 'c6g')
    for k in range(8):
        c.dma('gpsimd', A('wqs')[:, k * 256:(k + 1) * 256], Dr('wqf')[k * 128:(k + 1) * 128, :], 'c6g')
        c.dma('gpsimd', A('wks')[:, k * 256:(k + 1) * 256], Dr('wkf')[k * 128:(k + 1) * 128, :], 'c6g')
    for k in range(8):
        c.dma('gpsimd', A('wres')[:, k * 3072:(k + 1) * 3072], Dr('wihT')[k * 128:(k + 1) * 128, :], 'wih')
    v_wih = c.cnt['wih']
    v_o64r = vmemset(c, A('o64r')[:, :], 1.0, sem='vv')

    c.dma('gpsimd', A('bihb')[:, :], Dr('bih_row')[:, :], 'c6g')
    c.dma('gpsimd', A('bhhb')[:, :], Dr('bhh_row')[:, :], 'c6g')
    v_c6g = c.dma('gpsimd', A('bvb')[:, :], Dr('bv_row')[:, :], 'c6g')
    v_bihb = v_c6g; v_bhhb = v_c6g; v_bvb = 0
    c.wait('vector', 'c6', c.cnt['c6'])
    v_lnbb = vcopy(c, A('lnbb')[:, :], A('bcol2')[:, 0:8], sem='vv')

    v_dn0 = ts(c, A('slot')[:, 1024:1025], A('slot')[:, 1024:1025], 1e-8, ALU.add, sem='vv')
    c.wait('scalar', 'vv', v_dn0)
    v_dn1 = sinv(c, A('slot').ap()[:, 1025:1026], A('slot').ap()[:, 1024:1025], -1.0, sem='ss')
    c.wait('vector', 'ss', v_dn1)
    ts(c, A('slot')[:, 0:1024], A('slot')[:, 0:1024], A('slot').ap()[:, 1025:1026], ALU.mult)
    v_sib = vcopy(c, A('sibb')[:, :], A('slot')[:, 0:1024], sem='vv')
    c.wait('tensor', 'vv', v_sib)
    v_tr = 0
    for k in range(8):
        v_trm = c.op('tensor', lambda e, o=P[4][:, 0:64], i=A('slot').ap()[:, k * 128:(k + 1) * 128],
                     idt=A('identS2')[0:64, 0:64]: e.transpose(o, i, idt), sem='mm')
        c.wait('vector', 'mm', v_trm)
        v_tr = vcopy(c, A('siT')[:, k * 64:(k + 1) * 64], P[4][:, 0:64], sem='vv')
        c.wait('tensor', 'vv', v_tr)
    c.wait('tensor', 'wih', v_wih)
    c.wait('tensor', 'c6g', v_bihb)
    c.wait('tensor', 'vv', v_o64r)
    v_giev = 0
    for n in range(6):
        bk = n % 2
        for k in range(8):
            mm(c, P[bk][0:64, :], A('siT')[:, k * 64:(k + 1) * 64],
               A('wres')[:, k * 3072 + n * 512:k * 3072 + (n + 1) * 512],
               start=(k == 0), stop=False, sem=None)
        v_gimm = mm(c, P[bk][0:64, :], A('o64r')[0:1, 0:64],
                    A('bihb')[0:1, n * 512:(n + 1) * 512], start=False, stop=True, sem='mm')
        c.wait('vector', 'mm', v_gimm)
        v_giev = vcopy(c, A('gi')[:, n * 512:(n + 1) * 512], P[bk][0:64, :], sem='vv')
        c.wait('tensor', 'vv', v_giev)
    c.wait('gpsimd', 'mm', v_gimm)
    for k in range(8):
        c.dma('gpsimd', A('wres')[:, k * 3072:(k + 1) * 3072], Dr('whhT')[k * 128:(k + 1) * 128, :], 'wih')
    v_whh = c.cnt['wih']
    c.wait('tensor', 'wih', v_whh)
    c.wait('tensor', 'c6g', v_c6g)
    for n in range(6):
        bk = n % 2
        for k in range(8):
            mm(c, P[bk][0:64, :], A('sitT')[:, k * 64:(k + 1) * 64],
               A('wres')[:, k * 3072 + n * 512:k * 3072 + (n + 1) * 512],
               start=(k == 0), stop=False, sem=None)
        v_ghmm = mm(c, P[bk][0:64, :], A('o64r')[0:1, 0:64],
                    A('bhhb')[0:1, n * 512:(n + 1) * 512], start=False, stop=True, sem='mm')
        c.wait('vector', 'mm', v_ghmm)
        v_ghev = vcopy(c, A('gh')[:, n * 512:(n + 1) * 512], P[bk][0:64, :], sem='vv')
        c.wait('tensor', 'vv', v_ghev)
    c.wait('gpsimd', 'mm', v_ghmm)
    for k in range(8):
        c.dma('gpsimd', A('wres')[:, k * 1024:(k + 1) * 1024], Dr('wvp')[k * 128:(k + 1) * 128, :], 'wih')
        c.dma('gpsimd', A('wres')[:, 8192 + k * 1024:8192 + (k + 1) * 1024], Dr('wvf')[k * 128:(k + 1) * 128, :], 'wih')
        c.dma('gpsimd', A('wres')[:, 16384 + k * 1024:16384 + (k + 1) * 1024], Dr('wof')[k * 128:(k + 1) * 128, :], 'wih')
    v_wv3 = c.cnt['wih']
    tt(c, A('gt')[:, :], A('gi')[:, 0:1024], A('gh')[:, 0:1024], ALU.add)
    v_g1 = c.op('vector', lambda e: e.nop(), sem='vv')
    c.wait('scalar', 'vv', v_g1)
    v_r = act(c, A('gh')[:, 0:1024], A('gt')[:, :], AF.Sigmoid, sem='ss')
    tt(c, A('gt')[:, :], A('gi')[:, 1024:2048], A('gh')[:, 1024:2048], ALU.add)
    v_g2 = c.op('vector', lambda e: e.nop(), sem='vv')
    c.wait('scalar', 'vv', v_g2)
    v_z = act(c, A('gh')[:, 1024:2048], A('gt')[:, :], AF.Sigmoid, sem='ss')
    c.wait('vector', 'ss', v_z)
    tt(c, A('gt')[:, :], A('gh')[:, 0:1024], A('gh')[:, 2048:3072], ALU.mult)
    v_g3 = tt(c, A('gt')[:, :], A('gt')[:, :], A('gi')[:, 2048:3072], ALU.add, sem='vv')
    c.wait('scalar', 'vv', v_g3)
    v_n = act(c, A('gt')[:, :], A('gt')[:, :], AF.Tanh, sem='ss')
    c.wait('vector', 'ss', max(v_n, v_z))
    tt(c, A('snew')[:, :], A('sif')[:, :], A('gt')[:, :], ALU.subtract)
    tt(c, A('snew')[:, :], A('gh')[:, 1024:2048], A('snew')[:, :], ALU.mult)
    tt(c, A('snew')[:, :], A('gt')[:, :], A('snew')[:, :], ALU.add)
    v_snb = vcopy(c, A('snb')[:, :], A('snew')[:, :], sem='vv')
    c.wait('tensor', 'vv', v_snb)
    for k in range(8):
        v_trm = c.op('tensor', lambda e, o=P[4][:, 0:64], i=A('snew').ap()[:, k * 128:(k + 1) * 128],
                     idt=A('identS2')[0:64, 0:64]: e.transpose(o, i, idt), sem='mm')
        c.wait('vector', 'mm', v_trm)
        v_tr = vcopy(c, A('snT')[:, k * 64:(k + 1) * 64], P[4][:, 0:64], sem='vv')
        c.wait('tensor', 'vv', v_tr)
    c.wait('scalar', 'c6', c.cnt['c6'])
    v_qf = 0; v_kf = 0
    for m in range(2):
        c.wait('tensor', 'vv', max(v_qf, v_kf))
        for k in range(8):
            v_qmm = mm(c, P[0][:, 0:64], A('wqs')[:, k * 256 + m * 128:k * 256 + (m + 1) * 128],
                       A('snT')[:, k * 64:(k + 1) * 64], start=(k == 0), stop=(k == 7), sem='mm')
        c.wait('vector', 'mm', v_qmm)
        v_qf = ts(c, A('qf')[:, m * 64:(m + 1) * 64], P[0][:, 0:64],
                  A('bqc').ap()[:, m:m + 1], ALU.add, sem='vv')
        for k in range(8):
            v_kmm = mm(c, P[1][:, 0:64], A('wks')[:, k * 256 + m * 128:k * 256 + (m + 1) * 128],
                       A('snT')[:, k * 64:(k + 1) * 64], start=(k == 0), stop=(k == 7), sem='mm')
        c.wait('vector', 'mm', v_kmm)
        v_kf = ts(c, A('kf')[:, m * 64:(m + 1) * 64], P[1][:, 0:64],
                  A('bkc').ap()[:, m:m + 1], ALU.add, sem='vv')
    c.wait('tensor', 'wih', v_wv3)
    for n in range(2):
        for k in range(8):
            mm(c, P[2][0:64, :], A('snT')[:, k * 64:(k + 1) * 64],
               A('wres')[:, 8192 + k * 1024 + n * 512:8192 + k * 1024 + (n + 1) * 512],
               start=(k == 0), stop=False, sem=None)
        v_vmm = mm(c, P[2][0:64, :], A('o64r')[0:1, 0:64], A('bvb')[0:1, n * 512:(n + 1) * 512],
                   start=False, stop=True, sem='mm')
        c.wait('vector', 'mm', v_vmm)
        v_vt = vcopy(c, A('vtok')[:, n * 512:(n + 1) * 512], P[2][0:64, :], sem='vv')
        c.wait('tensor', 'vv', v_vt)
    c.wait('tensor', 'vv', max(v_qf, v_kf))
    for k in range(2):
        v_amm = mm(c, P[3][0:64, 0:64], A('qf')[:, k * 64:(k + 1) * 64], A('kf')[:, k * 64:(k + 1) * 64],
                   start=(k == 0), stop=(k == 1), sem='mm')
    c.wait('vector', 'mm', v_amm)
    ts(c, A('att')[:, :], P[3][0:64, 0:64], DQ ** (-0.5), ALU.mult)
    vreduce(c, A('t64').ap()[:, 0:1], A('att').ap()[:, :], ALU.max)
    v_ng = ts(c, A('t64')[:, 0:1], A('t64')[:, 0:1], -1.0, ALU.mult, sem='vv')
    c.wait('scalar', 'vv', v_ng)
    v_ex = act(c, A('att')[:, :], A('att')[:, :], AF.Exp, bias=A('t64').ap()[:, 0:1], sem='ss')
    c.wait('vector', 'ss', v_ex)
    v_sm = vreduce(c, A('t64').ap()[:, 1:2], A('att').ap()[:, :], ALU.add, sem='vv')
    c.wait('scalar', 'vv', v_sm)
    v_rq = sinv(c, A('t64').ap()[:, 2:3], A('t64').ap()[:, 1:2], -1.0, sem='ss')
    c.wait('vector', 'ss', v_rq)
    v_ab = ts(c, A('att')[:, :], A('att')[:, :], A('t64').ap()[:, 2:3], ALU.mult, sem='vv')
    c.wait('tensor', 'vv', v_ab)
    v_att = c.op('tensor', lambda e, o=P[3][0:64, 64:128], i=A('att')[:, :],
                 idt=A('identS2')[0:64, 0:64]: e.transpose(o, i, idt), sem='mm')
    c.wait('vector', 'mm', v_att)
    v_atT = vcopy(c, A('attT')[:, :], P[3][0:64, 64:128], sem='vv')
    c.wait('tensor', 'vv', v_atT)
    v_scv = 0
    for n in range(2):
        c.wait('tensor', 'vv', v_scv)
        v_scm = mm(c, P[5][0:64, :], A('attT')[:, :], A('vtok')[:, n * 512:(n + 1) * 512],
                   start=True, stop=True, sem='mm')
        c.wait('vector', 'mm', v_scm)
        v_scv = vcopy(c, A('sc2')[:, n * 512:(n + 1) * 512], P[5][0:64, :], sem='vv')
    v_y = tt(c, A('yv')[:, :], A('snew')[:, :], A('sc2')[:, :], ALU.add, sem='vv')
    c.wait('scalar', 'vv', v_y)
    act(c, A('yv')[:, :], A('yv')[:, :], AF.Copy, accum_out=A('t64').ap()[:, 3:4])
    v_sacc = act(c, A('gt')[:, :], A('yv')[:, :], AF.Square, accum_out=A('t64').ap()[:, 4:5], sem='ss')
    c.wait('vector', 'ss', v_sacc)
    ts(c, A('t64')[:, 5:6], A('t64').ap()[:, 3:4], 1.0 / D, ALU.mult)
    ts(c, A('t64')[:, 6:7], A('t64').ap()[:, 4:5], 1.0 / D, ALU.mult)
    tt(c, A('t64')[:, 7:8], A('t64')[:, 5:6], A('t64')[:, 5:6], ALU.mult)
    tt(c, A('t64')[:, 7:8], A('t64')[:, 6:7], A('t64')[:, 7:8], ALU.subtract)
    v_v6 = ts(c, A('t64')[:, 7:8], A('t64')[:, 7:8], EPS, ALU.add, sem='vv')
    c.wait('scalar', 'vv', v_v6)
    v_s6 = sinv(c, A('t64').ap()[:, 10:11], A('t64').ap()[:, 7:8], -0.5, sem='ss')
    c.wait('vector', 'ss', v_s6)
    stt(c, A('t64')[:, 11:12], A('t64').ap()[:, 5:6], -1.0, A('t64').ap()[:, 10:11], ALU.mult, ALU.mult)
    v_yln = ts(c, A('sc2')[:, :], A('yv')[:, :], A('t64').ap()[:, 11:12], ALU.add,
               A('t64').ap()[:, 10:11], ALU.mult, sem='vv')
    c.wait('vector', 'wih', v_wv3)
    c.wait('vector', 'c6', c.cnt['c6'])
    for k in range(8):
        v_wvf = ts(c, A('wres')[:, k * 1024:(k + 1) * 1024], A('wres')[:, k * 1024:(k + 1) * 1024],
                   A('lngc').ap()[:, k:k + 1], ALU.mult, sem='vv')
    c.wait('tensor', 'vv', max(v_wvf, v_lnbb, v_yln))
    for m in range(8):
        for k in range(8):
            v_b2mm = mm(c, P[6][:, m:m + 1], A('wres')[:, k * 1024 + m * 128:k * 1024 + (m + 1) * 128],
                        A('lnbb')[:, k:k + 1], start=(k == 0), stop=(k == 7), sem='mm')
    c.wait('vector', 'mm', v_b2mm)
    vcopy(c, A('bias2')[:, 0:8], P[6][:, 0:8])
    v_bi2 = tt(c, A('bias2')[:, 0:8], A('bias2')[:, 0:8], A('bvpc')[:, 0:8], ALU.add, sem='vv')
    c.wait('tensor', 'vv', v_yln)
    for k in range(8):
        v_trm = c.op('tensor', lambda e, o=P[4][:, 0:64], i=A('sc2').ap()[:, k * 128:(k + 1) * 128],
                     idt=A('identS2')[0:64, 0:64]: e.transpose(o, i, idt), sem='mm')
        c.wait('vector', 'mm', v_trm)
        v_tr = vcopy(c, A('ylnT')[:, k * 64:(k + 1) * 64], P[4][:, 0:64], sem='vv')
        c.wait('tensor', 'vv', v_tr)
    c.wait('scalar', 'vv', v_bi2)
    v_sv = 0
    for m in range(8):
        c.wait('tensor', 'vv', v_sv)
        for k in range(8):
            v_svm = mm(c, P[7][:, 0:64], A('wres')[:, k * 1024 + m * 128:k * 1024 + (m + 1) * 128],
                       A('ylnT')[:, k * 64:(k + 1) * 64], start=(k == 0), stop=(k == 7), sem='mm')
        c.wait('vector', 'mm', v_svm)
        v_sv = ts(c, A('svf')[:, m * 64:(m + 1) * 64], P[7][:, 0:64],
                  A('bias2').ap()[:, m:m + 1], ALU.add, sem='vv')
    c.wait('tensor', 'gr0', c.cnt.get('gr0', 0))
    c.wait('tensor', 'vv', v_bi2)
    v_bom = mm(c, P[6][0:64, 0:512], A('onesf')[0:1, 0:64], A('borow')[0:1, 0:512], start=True, stop=True, sem='mm')
    v_bom2 = mm(c, P[6][0:64, 0:512] if False else P[5][0:64, 0:512], A('onesf')[0:1, 0:64], A('borow')[0:1, 512:1024], start=True, stop=True, sem='mm')
    c.wait('vector', 'mm', v_bom2)
    vcopy(c, A('bo64')[:, 0:512], P[6][0:64, 0:512])
    v_bo = vcopy(c, A('bo64')[:, 512:1024], P[5][0:64, 0:512], sem='vv')
    c.wait('tensor', 'vv', v_bo)
    v_gbm = mm(c, P[6][0:64, 0:512], A('onesf')[0:1, 0:64], A('garow')[0:1, 1024:1536], start=True, stop=True, sem='mm')
    v_gbm2 = mm(c, P[5][0:64, 0:512], A('onesf')[0:1, 0:64], A('garow')[0:1, 1536:2048], start=True, stop=True, sem='mm')
    c.wait('vector', 'mm', v_gbm2)
    vcopy(c, A('gb64')[:, 0:512], P[6][0:64, 0:512])
    v_gb = vcopy(c, A('gb64')[:, 512:1024], P[5][0:64, 0:512], sem='vv')
    c.wait('tensor', 'vv', max(v_sv, v_gb))
    for n in range(2):
        for k in range(8):
            v_svwm = mm(c, P[3][0:64, :], A('svf')[:, k * 64:(k + 1) * 64],
                        A('wres')[:, 16384 + k * 1024 + n * 512:16384 + k * 1024 + (n + 1) * 512],
                        start=(k == 0), stop=(k == 7), sem='mm')
        c.wait('vector', 'mm', v_svwm)
        stt(c, A('sc2')[:, n * 512:(n + 1) * 512], P[3][0:64, :], 1.0,
            A('bo64').ap()[:, n * 512:(n + 1) * 512], ALU.mult, ALU.add)
        v_svw = tt(c, A('svw')[:, n * 512:(n + 1) * 512], A('sc2')[:, n * 512:(n + 1) * 512],
                   A('gb64')[:, n * 512:(n + 1) * 512], ALU.mult, sem='vv')
        c.wait('tensor', 'vv', v_svw)
    v_om = 0; v_ost = 0
    for t in range(NTT):
        tsl = slice(t * 128, (t + 1) * 128)
        c.wait('sync', 'vv', v_ost)
        v_xm3 = c.dma('sync', A('xmt3')[:, :], Dr('xmid_d')[tsl, :], 'xm3')
        c.wait('tensor', 'vv', v_ost)
        for j in range(2):
            v_om = mm(c, P[j][:, :], A('alphaT').ap()[:, tsl], A('svw')[:, j * 512:(j + 1) * 512],
                      start=True, stop=True, sem='mm')
        c.wait('vector', 'mm', v_om)
        c.wait('vector', 'xm3', v_xm3)
        for j in range(2):
            v_ost = stt(c, A('yout')[:, j * 512:(j + 1) * 512], P[j][:, :], 1.0,
                        A('xmt3').ap()[:, j * 512:(j + 1) * 512], ALU.mult, ALU.add, sem='vv')
        c.wait('sync', 'vv', v_ost)
        c.dma('sync', Dr('y')[tsl, :], A('yout')[:, :], 'yo')

def _finish(nc, c, stack):
    for sname in list(c.cnt.keys()):
        c.wait('sync', sname, c.cnt.get(sname, 0))
    with nc.Block() as block:
        for name in ('sync', 'tensor', 'vector', 'scalar', 'gpsimd'):
            ops = c.ops[name]
            def runner(e, ops=ops):
                for f in ops:
                    f(e)
            getattr(block, name)(runner)
    stack.close()
    return nc, c


def kernel(**inputs):
    nc, _ = build_nc(dbg=tuple(DBG))
    in_maps = [_per_core_inputs(core, inputs) for core in range(8)]
    res = run_bass_kernel_spmd(nc, in_maps, core_ids=list(range(8)))
    out = np.zeros((4, 4096, 1024), np.float32)
    for core in range(8):
        b, u = core // 2, core % 2
        out[b, u * TC:(u + 1) * TC] = res.results[core]['y']
    return out


# version 08
# speedup vs baseline: 1.0866x; 1.0240x over previous
import numpy as np
import ml_dtypes
import concourse.bass as bass
import concourse.mybir as mybir
from concourse.bass_utils import run_bass_kernel_spmd

F32 = mybir.dt.float32
BF16 = mybir.dt.bfloat16
AF = mybir.ActivationFunctionType
ALU = mybir.AluOpType
BF = ml_dtypes.bfloat16

D = 1024; TC = 2048; SLAB = 512; NSLAB = 4; TT = 128; NTT = 16
CH = 128; NCH = 16
H = 32; PH = 64; NST = 128; DIN = 2048; CONVD = 2304; NCT = 18
DZ = 4384; DZP = 4480; MZ = 35
NS = 64; DH = 512; DQ = 256
EPS = 1e-5

DBG = []
STAGE = 99
DEBUG_P1 = False
GROUPS = [[0, 1], [2, 3], [4, 5], [6, 7]]


def _core_constants(u):
    t0 = u * TC
    pos = np.arange(t0, t0 + TC, dtype=np.float64)
    inv_freq = 1.0 / 10000.0 ** (np.arange(0, 64, 2, dtype=np.float64) / 64)
    fr = pos[:, None] * inv_freq[None, :]
    cos = np.cos(fr); sin = np.sin(fr)
    rope_cos = np.concatenate([cos, cos], axis=1).astype(np.float32)
    rope_sin = np.concatenate([-sin, sin], axis=1).astype(np.float32)
    return rope_cos, rope_sin

def _shared_constants():
    c = {}
    sel0 = np.zeros((32, 2048), np.float32); sel1 = np.zeros((32, 2048), np.float32)
    for h in range(16):
        sel0[h, h * 128:(h + 1) * 128] = 1.0
        sel1[16 + h, h * 128:(h + 1) * 128] = 1.0
    c['sel_h0'] = sel0.astype(BF); c['sel_h1'] = sel1.astype(BF)
    s32 = np.zeros((32, 2048), np.float32)
    for h in range(32):
        s32[h, h * 64:(h + 1) * 64] = 1.0
    c['sel32'] = s32.astype(BF)
    tl = np.zeros((127, 128), np.float32); tr = np.zeros((127, 2048), np.float32)
    for j in range(127):
        tl[j, j + 1:] = 1.0
        tr[j, j::128] = -1e30
    c['trilL'] = tl.astype(BF); c['trilR'] = tr.astype(BF)
    c['sel64'] = np.vstack([s32, s32]).astype(BF)
    c['ident'] = np.eye(128, dtype=np.float32).astype(BF)
    c['identf'] = np.eye(128, dtype=np.float32)
    c['ones32b'] = np.ones((32, 128), np.float32).astype(BF)
    c['ones_colb'] = np.ones((128, 1), np.float32).astype(BF)
    c['ones_rowf'] = np.ones((1, 128), np.float32)
    return c

def _per_core_inputs(core, inp):
    b, u = core // 2, core % 2
    t0 = u * TC
    m = {}
    x = np.asarray(inp['L0_attn_x'])[b, t0:t0 + TC]
    m['x_tok'] = np.ascontiguousarray(x)
    m['xT'] = np.ascontiguousarray(x.T)
    m['wmod'] = np.ascontiguousarray(np.asarray(inp['L0_wmod']))
    m['c_col'] = np.ascontiguousarray(np.asarray(inp['L0_c'])[b].reshape(8, 128).T)
    m['wqT'] = np.ascontiguousarray(np.asarray(inp['L0_attn_W_in'])[:1024, :].T)
    wz = np.asarray(inp['L0_attn_W_zxbcdt'])
    wzT = np.zeros((1024, DZP), np.float32); wzT[:, :DZ] = wz.T
    m['wzT'] = wzT
    m['woutT'] = np.ascontiguousarray(np.asarray(inp['L0_attn_W_out']).T)
    cw = np.asarray(inp['L0_attn_conv_w'])
    m['conv_w'] = np.ascontiguousarray(cw.reshape(NCT, 128, 4).transpose(1, 0, 2).reshape(128, NCT * 4))
    m['conv_b'] = np.ascontiguousarray(np.asarray(inp['L0_attn_conv_b']).reshape(NCT, 128).T)
    m['dtb_col'] = np.asarray(inp['L0_attn_dt_bias']).reshape(32, 1).astype(np.float32)
    m['alog_col'] = np.asarray(inp['L0_attn_A_log']).reshape(32, 1).astype(np.float32)
    m['d_hp'] = np.ascontiguousarray(np.repeat(np.asarray(inp['L0_attn_D_param']), 64).reshape(16, 128).T)
    m['npw_col'] = np.ascontiguousarray(np.asarray(inp['L0_attn_norm_pre_w']).reshape(8, 128).T)
    m['ngw_col'] = np.ascontiguousarray(np.asarray(inp['L0_attn_norm_gated_w']).reshape(16, 128).T)
    m['w1'] = np.ascontiguousarray(np.asarray(inp['L0_ffn_w1']))
    m['w2'] = np.ascontiguousarray(np.asarray(inp['L0_ffn_w2']))
    m['wihT'] = np.ascontiguousarray(np.asarray(inp['L0_ffn_Wih']).T)
    m['whhT'] = np.ascontiguousarray(np.asarray(inp['L0_ffn_Whh']).T)
    m['slot_init'] = np.ascontiguousarray(np.asarray(inp['L0_ffn_slot_init']))
    m['slot_initT'] = np.ascontiguousarray(np.asarray(inp['L0_ffn_slot_init']).T)
    m['wqf'] = np.ascontiguousarray(np.asarray(inp['L0_ffn_wq']))
    m['wkf'] = np.ascontiguousarray(np.asarray(inp['L0_ffn_wk']))
    m['wvf'] = np.ascontiguousarray(np.asarray(inp['L0_ffn_wv']))
    m['wvp'] = np.ascontiguousarray(np.asarray(inp['L0_ffn_wvp']))
    m['wof'] = np.ascontiguousarray(np.asarray(inp['L0_ffn_wo']))
    m['b1_col'] = np.ascontiguousarray(np.asarray(inp['L0_ffn_b1']).reshape(4, 128).T)
    m['b2_row'] = np.asarray(inp['L0_ffn_b2']).reshape(1, 64).astype(np.float32)
    m['tau'] = np.asarray(inp['L0_ffn_tau']).reshape(1, 1).astype(np.float32)
    m['bih_row'] = np.asarray(inp['L0_ffn_bih']).reshape(1, 3072).astype(np.float32)
    m['bhh_row'] = np.asarray(inp['L0_ffn_bhh']).reshape(1, 3072).astype(np.float32)
    m['bq_col'] = np.ascontiguousarray(np.asarray(inp['L0_ffn_bq']).reshape(2, 128).T)
    m['bk_col'] = np.ascontiguousarray(np.asarray(inp['L0_ffn_bk']).reshape(2, 128).T)
    m['bv_row'] = np.asarray(inp['L0_ffn_bv']).reshape(1, 1024).astype(np.float32)
    m['lng_col'] = np.ascontiguousarray(np.asarray(inp['L0_ffn_ln_g']).reshape(8, 128).T)
    m['lnb_col'] = np.ascontiguousarray(np.asarray(inp['L0_ffn_ln_b']).reshape(8, 128).T)
    m['bvp_col'] = np.ascontiguousarray(np.asarray(inp['L0_ffn_bvp']).reshape(8, 128).T)
    m['bo_row'] = np.asarray(inp['L0_ffn_bo']).reshape(1, 1024).astype(np.float32)
    rc, rs = _core_constants(u)
    m['rope_cos'] = np.ascontiguousarray(rc.reshape(16, 128, 64).transpose(1, 0, 2).reshape(128, 1024))
    m['rope_sin'] = np.ascontiguousarray(rs.reshape(16, 128, 64).transpose(1, 0, 2).reshape(128, 1024))
    m.update(_shared_constants())
    m['g_state'] = np.full((128, 1), 1.0 if u == 0 else 0.0, np.float32)
    m['g_use'] = np.full((128, 1), 0.0 if u == 0 else 1.0, np.float32)
    return m


class Ctx:
    def __init__(self, nc, stack):
        self.nc = nc
        self.ops = {k: [] for k in ('sync', 'tensor', 'vector', 'scalar', 'gpsimd')}
        self.cnt = {}
        self.sems = {}
        self.sb = {}
        self.ps = {}
        self.dr = {}
        self.dma_sems = set()
        self.vlast_w = {}
        self.vlast_r = {}
        self.vwaited = 0
        self._stack = stack

    def sem(self, name):
        if name not in self.sems:
            self.sems[name] = self._stack.enter_context(self.nc.semaphore(name))
            self.cnt[name] = 0
        return self.sems[name]

    def op(self, eng, builder, sem=None, amt=1):
        v = None
        if sem is not None:
            self.sem(sem)
            self.cnt[sem] += amt
            v = self.cnt[sem]
        def f(e, b=builder, s=sem, a=amt):
            ins = b(e)
            if s is not None:
                ins.then_inc(self.sems[s], a)
        self.ops[eng].append(f)
        return v

    def wait(self, eng, sem, val):
        if val is None or val <= 0:
            return
        self.sem(sem)
        assert val <= self.cnt[sem], (eng, sem, val, self.cnt[sem])
        def f(e, s=sem, v=val):
            e.wait_ge(self.sems[s], v)
        self.ops[eng].append(f)

    def phase(self, tag):
        import contextlib
        ctx = self
        class _Ph:
            def __init__(self):
                self.stack = contextlib.ExitStack()
                self.names = []
            def sb(self, name, shape, dtype=F32):
                h = self.stack.enter_context(ctx.nc.sbuf_tensor(name, list(shape), dtype))
                ctx.sb[name] = h
                self.names.append(name)
                return h
            def close(self):
                ctx.barrier(tag)
                for n in self.names:
                    ctx.sb.pop(n, None)
                self.stack.close()
        return _Ph()

    def barrier(self, tag):
        engs = ('sync', 'tensor', 'vector', 'scalar', 'gpsimd')
        for sname in sorted(self.dma_sems):
            self.wait('sync', sname, self.cnt.get(sname, 0))
        for e in engs:
            self.op(e, lambda x: x.nop(), sem='bar_' + e)
        tgt = {e: self.cnt['bar_' + e] for e in engs}
        for e in engs:
            for o in engs:
                if o != e:
                    self.wait(e, 'bar_' + o, tgt[o])

    def vop(self, builder, reads, writes, sem=None, amt=1):
        def names(aps):
            out = []
            for a in aps:
                if a is None:
                    continue
                t = getattr(a, 'tensor', None)
                if t is None and hasattr(a, 'ap'):
                    t = getattr(a.ap(), 'tensor', None)
                if t is not None:
                    out.append(t.name)
            return out
        rn = names(reads); wn = names(writes)
        need = 0
        for t in rn:
            need = max(need, self.vlast_w.get(t, 0))
        for t in wn:
            need = max(need, self.vlast_w.get(t, 0), self.vlast_r.get(t, 0))
        if need > self.vwaited:
            self.wait('vector', 'vv', need)
            self.vwaited = need
        v = self.op('vector', builder, sem='vv', amt=amt)
        for t in wn:
            self.vlast_w[t] = v
        for t in rn:
            self.vlast_r[t] = max(self.vlast_r.get(t, 0), v)
        return v

    def dma(self, q, out, in_, sem):
        self.dma_sems.add(sem)
        return self.op(q, lambda e, o=out, i=in_: e.dma_start(out=o, in_=i), sem=sem, amt=16)

    def dma_slow(self, q, out, in_, sem):
        self.dma_sems.add(sem)
        return self.op(q, lambda e, o=out, i=in_:
                       e.dma_start(out=o, in_=i, allow_slow_non_contiguous=True), sem=sem, amt=16)

    def dmat(self, q, out, in_, sem):
        self.dma_sems.add(sem)
        return self.op(q, lambda e, o=out, i=in_: e.dma_start_transpose(o, i), sem=sem, amt=16)

def mm(ctx, out, lhsT, rhs, start, stop, sem=None):
    return ctx.op('tensor', lambda e, o=out, l=lhsT, r=rhs, st=start, sp=stop:
                  e.matmul(o, l, r, start=st, stop=sp), sem=sem)

def act(ctx, out, in_, func, bias=None, scale=None, accum_out=None, sem=None):
    kw = {}
    if bias is not None: kw['bias'] = bias
    if scale is not None: kw['scale'] = scale
    if accum_out is not None: kw['accum_out'] = accum_out
    return ctx.op('scalar', lambda e, o=out, i=in_, f=func, k=dict(kw):
                  e.activation(out=o, in_=i, func=f, **k), sem=sem)

def tt(ctx, out, in0, in1, alu, sem=None):
    return ctx.vop(lambda e, o=out, a=in0, b=in1, p=alu:
                   e.tensor_tensor(out=o, in0=a, in1=b, op=p), [in0, in1], [out])

def ts(ctx, out, in0, s1, op0, s2=None, op1=None, sem=None):
    def b(e, o=out, i=in0, x1=s1, x2=s2, p0=op0, p1=op1):
        if p1 is None:
            return e.tensor_scalar(out=o, in0=i, scalar1=x1, scalar2=None, op0=p0)
        return e.tensor_scalar(out=o, in0=i, scalar1=x1, scalar2=x2, op0=p0, op1=p1)
    rd = [in0] + [x for x in (s1, s2) if not isinstance(x, (int, float, type(None)))]
    return ctx.vop(b, rd, [out])

def stt(ctx, out, in0, scalar, in1, op0, op1, sem=None):
    rd = [in0, in1] + ([scalar] if not isinstance(scalar, (int, float)) else [])
    return ctx.vop(lambda e, o=out, a=in0, s=scalar, b=in1, p0=op0, p1=op1:
                   e.scalar_tensor_tensor(out=o, in0=a, scalar=s, in1=b, op0=p0, op1=p1), rd, [out])

def vcopy(ctx, out, in_, sem=None):
    return ctx.vop(lambda e, o=out, i=in_: e.tensor_copy(o, i), [in_], [out])

def scopy(ctx, out, in_, sem=None):
    return ctx.op('scalar', lambda e, o=out, i=in_: e.copy(o, i), sem=sem)

def vscan(ctx, out, d0, d1, initial, op0, op1, sem=None):
    rd = [d0, d1] + ([initial] if not isinstance(initial, (int, float)) else [])
    return ctx.vop(lambda e, o=out, a=d0, b=d1, i=initial, p0=op0, p1=op1:
                   e.tensor_tensor_scan(o, a, b, i, p0, p1), rd, [out])


def sinv(ctx, out, in_, power=-1.0, sem=None):
    act(ctx, out, in_, AF.Ln)
    return act(ctx, out, out, AF.Exp, scale=power, sem=sem)

def vmemset(ctx, ap, val, sem=None):
    return ctx.vop(lambda e, a=ap, v=val: e.memset(a, v), [], [ap])

def vrecip(ctx, out, in_, sem=None):
    return ctx.vop(lambda e, o=out, i=in_: e.reciprocal(o, i), [in_], [out])

def vmax8(ctx, out, in_, sem=None):
    return ctx.vop(lambda e, o=out, i=in_: e.max(o, i), [in_], [out])

def vreduce(ctx, out, in_, alu, sem=None):
    return ctx.vop(lambda e, o=out, i=in_, p=alu:
                   e.tensor_reduce(out=o, in_=i, axis=mybir.AxisListType.X, op=p), [in_], [out])


def build_nc(dbg=()):
    import contextlib
    nc = bass.Bass("TRN2", target_bir_lowering=False, debug=False, detect_race_conditions=False)
    stack = contextlib.ExitStack()
    c = Ctx(nc, stack)

    def din(name, shape, dtype=F32):
        c.dr[name] = nc.dram_tensor(name, list(shape), dtype, kind="ExternalInput").ap()

    def dtmp(name, shape, dtype=F32):
        kind = "ExternalOutput" if name in dbg else "Internal"
        c.dr[name] = nc.dram_tensor(name, list(shape), dtype, kind=kind).ap()

    def sb(name, shape, dtype=F32):
        c.sb[name] = stack.enter_context(nc.sbuf_tensor(name, list(shape), dtype))

    for n, s in [('x_tok', (TC, D)), ('xT', (D, TC)), ('wmod', (D, 6144)), ('c_col', (128, 8)),
                 ('wqT', (D, D)), ('wzT', (D, DZP)), ('woutT', (DIN, D)),
                 ('conv_w', (128, NCT * 4)), ('conv_b', (128, NCT)),
                 ('dtb_col', (32, 1)), ('alog_col', (32, 1)), ('d_hp', (128, 16)),
                 ('npw_col', (128, 8)), ('ngw_col', (128, 16)),
                 ('w1', (DIN, DH)), ('w2', (DH, NS)), ('wihT', (D, 3072)), ('whhT', (D, 3072)),
                 ('slot_init', (NS, D)), ('slot_initT', (D, NS)),
                 ('wqf', (D, DQ)), ('wkf', (D, DQ)), ('wvf', (D, D)), ('wvp', (D, D)), ('wof', (D, D)),
                 ('b1_col', (128, 4)), ('b2_row', (1, NS)), ('tau', (1, 1)),
                 ('bih_row', (1, 3072)), ('bhh_row', (1, 3072)),
                 ('bq_col', (128, 2)), ('bk_col', (128, 2)), ('bv_row', (1, D)),
                 ('lng_col', (128, 8)), ('lnb_col', (128, 8)), ('bvp_col', (128, 8)), ('bo_row', (1, D)),
                 ('rope_cos', (128, NTT * 64)), ('rope_sin', (128, NTT * 64)), ('ones_rowf', (1, 128)), ('identf', (128, 128)),
                 ('g_state', (128, 1)), ('g_use', (128, 1))]:
        din(n, s, F32)
    for n, s in [('sel_h0', (32, 2048)), ('sel_h1', (32, 2048)), ('sel32', (32, 2048)),
                 ('trilL', (127, 128)), ('trilR', (127, 2048)), ('ones_colb', (128, 1)),
                 ('sel64', (64, 2048)), ('ones32b', (32, 128)), ('ident', (128, 128))]:
        din(n, s, BF16)

    c.dr['y'] = nc.dram_tensor('y', [TC, D], F32, kind="ExternalOutput").ap()

    dtmp('qhat_d', (TC, D), BF16)
    dtmp('resid_d', (TC, D), F32)
    dtmp('xconv_d', (CONVD, TC), BF16)
    dtmp('zsilu_d', (DIN, TC), BF16)
    dtmp('ypre_d', (DIN, TC), BF16)
    dtmp('xmid_d', (TC, D), F32)
    dtmp('h2_d', (TC, D), BF16)
    dtmp('acshi_d', (32, TC), BF16)
    dtmp('acslo_d', (32, TC), BF16)
    dtmp('nahi_d', (32, TC), BF16)
    dtmp('nalo_d', (32, TC), BF16)
    dtmp('acsghi_d', (32, TC), BF16)
    dtmp('acsglo_d', (32, TC), BF16)
    dtmp('acs_d', (32, TC), F32)
    dtmp('dt_d', (32, TC), F32)
    dtmp('st_in', (NST, DIN), F32)
    dtmp('st_out', (NST, DIN), F32)
    dtmp('sl_in', (NS, 1025), F32)
    dtmp('sl_out', (NS, 1025), F32)
    dtmp('brow_d', (1, 8192), F32)

    sb('onesf', (1, 128), F32)
    sb('onesb', (128, 1), BF16)
    sb('c_sb', (128, 8), F32)
    sb('siluc', (128, 8), BF16)
    sb('silucf', (128, 8), F32)
    sb('modrow', (1, 6144), F32)
    sb('garow', (1, 2048), F32)
    sb('mcol', (128, 48), F32)
    sb('gab', (128, D), F32); sb('cb1b', (128, D), F32); sb('sbb', (128, D), F32)
    sb('stat_sum', (128, NTT + 16), F32); sb('stat_sq', (128, NTT + 16), F32)
    sb('r1', (128, NTT + 16), F32); sb('nmr1', (128, NTT + 16), F32)
    sb('r2', (128, NTT + 16), F32); sb('sq2', (128, NTT + 16), F32)
    sb('r3', (128, NTT + 16), F32); sb('sq3', (128, NTT + 16), F32)
    sb('tiny', (128, 64), F32)
    sb('tcol', (32, 16), F32)
    sb('alpha', (128, NTT * 64), BF16)
    sb('alphaT', (64, TC), BF16)

    for i in range(8):
        c.ps['pb%d' % i] = stack.enter_context(nc.psum_tensor('pb%d' % i, [128, 512], F32))
    P = [c.ps['pb%d' % i] for i in range(8)]

    A = lambda n: c.sb[n]
    Dr = lambda n: c.dr[n]

    c.dma('sync', A('onesf')[:, :], Dr('ones_rowf')[:, :], 'c0')
    c.dma('sync', A('onesb')[:, :], Dr('ones_colb')[:, :], 'c0')

    def row_bcast(dst_ap, row_ap, n, banks=(4, 5)):
        last = 0
        evs = {b: 0 for b in banks}
        for idx, j in enumerate(range(0, n, 512)):
            w = min(512, n - j)
            bk = banks[idx % len(banks)]
            c.wait('tensor', 'vv', evs[bk])
            vm = mm(c, P[bk][:, 0:w], A('onesf')[0:1, :], row_ap[0:1, j:j + w], start=True, stop=True, sem='mm')
            c.wait('vector', 'mm', vm)
            last = vcopy(c, dst_ap[:, j:j + w], P[bk][:, 0:w], sem='vv')
            evs[bk] = last
        return last

    ph = c.phase('p0')
    ph.sb('wmodb', (128, 8 * 4096), BF16)
    ph.sb('wmodf', (128, 8 * 2048), F32)
    ph.sb('identP', (128, 128), F32)
    c.dma('sync', A('c_sb')[:, :], Dr('c_col')[:, :], 'c0')
    c.dma('sync', A('identP')[:, :], Dr('identf')[:, :], 'c0')
    v_c0 = c.cnt['c0']
    vbf = []
    vfp = []
    for k in range(8):
        c.dma('gpsimd', A('wmodb')[:, k * 4096:k * 4096 + 2048],
              Dr('wmod')[k * 128:(k + 1) * 128, 0:2048], 'wma')
        vbf.append(c.dma('gpsimd', A('wmodb')[:, k * 4096 + 2048:k * 4096 + 4096],
                         Dr('wmod')[k * 128:(k + 1) * 128, 3072:5120], 'wma'))
        c.dma('sync', A('wmodf')[:, k * 2048:k * 2048 + 1024],
              Dr('wmod')[k * 128:(k + 1) * 128, 2048:3072], 'wmb')
        vfp.append(c.dma('sync', A('wmodf')[:, k * 2048 + 1024:k * 2048 + 2048],
                         Dr('wmod')[k * 128:(k + 1) * 128, 5120:6144], 'wmb'))
    c.wait('scalar', 'c0', v_c0)
    act(c, A('siluc')[:, :], A('c_sb')[:, :], AF.Silu)
    v_siluc = act(c, A('silucf')[:, :], A('c_sb')[:, :], AF.Silu, sem='ss')

    bank_ev = [0, 0]
    c.wait('tensor', 'ss', v_siluc)
    for n in range(12):
        fp32n = n in (4, 5, 10, 11)
        lhs = A('silucf') if fp32n else A('siluc')
        bank = P[n % 2]
        c.wait('tensor', 'vv', bank_ev[n % 2])
        for k in range(8):
            if fp32n:
                c.wait('tensor', 'wmb', vfp[k])
                j0 = k * 2048 + (1024 if n >= 10 else 0) + (n % 2) * 512
                rhs = A('wmodf')[:, j0:j0 + 512]
            else:
                c.wait('tensor', 'wma', vbf[k])
                j0 = k * 4096 + (2048 if n >= 6 else 0) + ((n - 6 if n >= 6 else n) % 4) * 512
                rhs = A('wmodb')[:, j0:j0 + 512]
            vmm = mm(c, bank[0:1, :], lhs[:, k:k + 1], rhs,
                     start=(k == 0), stop=(k == 7), sem='mm')
        c.wait('vector', 'mm', vmm)
        bank_ev[n % 2] = vcopy(c, A('modrow')[0:1, n * 512:(n + 1) * 512], bank[0:1, :], sem='vv')
    v_modrow = max(bank_ev)

    c.wait('tensor', 'vv', v_modrow)
    c.wait('tensor', 'c0', v_c0)
    for a in range(48):
        bk, col = (2, a) if a < 24 else (3, a - 24)
        vm = mm(c, P[bk][:, col:col + 1], A('modrow').ap()[0:1, a * 128:(a + 1) * 128],
                A('onesf')[0:1, 0:1], start=True, stop=True, sem='mm')
    c.wait('vector', 'mm', vm)
    vcopy(c, A('mcol')[:, 0:24], P[2][:, 0:24], sem='vv')
    v_mcol = vcopy(c, A('mcol')[:, 24:48], P[3][:, 0:24], sem='vv')

    c.wait('scalar', 'vv', v_mcol)
    v_sg = act(c, A('tiny')[:, 0:8], A('mcol')[:, 16:24], AF.Sign, sem='ss')
    v_sg2 = act(c, A('tiny')[:, 8:16], A('mcol')[:, 40:48], AF.Sign, sem='ss')
    v_ab = act(c, A('tiny')[:, 16:24], A('mcol')[:, 16:24], AF.Abs, sem='ss')
    v_ab2 = act(c, A('tiny')[:, 24:32], A('mcol')[:, 40:48], AF.Abs, sem='ss')
    c.wait('vector', 'ss', v_ab2)
    ts(c, A('tiny')[:, 16:24], A('tiny')[:, 16:24], 0.25, ALU.max)
    ts(c, A('tiny')[:, 24:32], A('tiny')[:, 24:32], 0.25, ALU.max)
    tt(c, A('mcol')[:, 16:24], A('tiny')[:, 0:8], A('tiny')[:, 16:24], ALU.mult)
    v_gclamp = tt(c, A('mcol')[:, 40:48], A('tiny')[:, 8:16], A('tiny')[:, 24:32], ALU.mult, sem='vv')

    c.wait('tensor', 'vv', v_gclamp)
    ev67 = 0
    for g in range(2):
        c.wait('tensor', 'vv', ev67)
        for j in range(8):
            bk, col = (6, j) if j < 4 else (7, j - 4)
            vm = mm(c, P[bk][0:1, col * 128:(col + 1) * 128], A('mcol')[:, (16 if g == 0 else 40) + j:(17 if g == 0 else 41) + j],
                    A('identP')[:, :], start=True, stop=True, sem='mm')
        c.wait('vector', 'mm', vm)
        vcopy(c, A('garow')[0:1, g * 1024:g * 1024 + 512], P[6][0:1, 0:512], sem='vv')
        ev67 = vcopy(c, A('garow')[0:1, g * 1024 + 512:g * 1024 + 1024], P[7][0:1, 0:512], sem='vv')
    c.wait('tensor', 'vv', ev67)

    row_bcast(A('gab').ap(), A('garow').ap()[0:1, 0:1024], D)
    row_bcast(A('cb1b').ap(), A('modrow').ap()[0:1, 4 * 1024:5 * 1024], D)
    ts(c, A('cb1b')[:, :], A('cb1b')[:, :], 1.0, ALU.add)
    row_bcast(A('sbb').ap(), A('modrow').ap()[0:1, 3 * 1024:4 * 1024], D)
    ph.close()

    if STAGE >= 1:
        ph = c.phase('p1')
        _p1(c, P, A, Dr, row_bcast, ph)
        ph.close()
    if STAGE >= 2:
        ph = c.phase('p2')
        _p2(c, P, A, Dr, ph)
        ph.close()
    if STAGE >= 3:
        ph = c.phase('p3')
        _p3(c, P, A, Dr, ph)
        ph.close()
    if STAGE >= 4:
        ph = c.phase('p4')
        _p4(c, P, A, Dr, ph)
        ph.close()
    if STAGE >= 5:
        ph = c.phase('p5')
        _p5(c, P, A, Dr, ph)
        ph.close()
    if STAGE >= 6:
        ph = c.phase('p6')
        _p6(c, P, A, Dr, ph)
        ph.close()
    return _finish(nc, c, stack)


def _p1(c, P, A, Dr, row_bcast, ph):
    ph.sb('v1row', (1, D), F32); ph.sb('v2row', (1, D), F32)
    ph.sb('v1b', (128, D), F32); ph.sb('v2b', (128, D), F32)
    ph.sb('x_sb', (128, 4 * D), F32)
    ph.sb('xT_sb', (128, 2 * 8 * SLAB), BF16)
    ph.sb('wq_sb', (128, 8 * D), BF16)
    ph.sb('sacol', (128, 8), BF16)
    ph.sb('rope_c', (128, NTT * 64), F32)
    ph.sb('rope_s', (128, NTT * 64), F32)
    ph.sb('muv', (128, 4 * D), F32)
    ph.sb('qpre', (128, 4 * D), F32)
    ph.sb('ropet', (128, 4 * D), F32)
    ph.sb('qhat_t', (128, 4 * D), BF16)
    ph.sb('resid_t', (128, 4 * D), F32)
    for k in range(8):
        c.dma('gpsimd', A('wq_sb')[:, k * D:(k + 1) * D], Dr('wqT')[k * 128:(k + 1) * 128, :], 'wql')
    v_wq = c.cnt['wql']
    v_sab = vcopy(c, A('sacol')[:, :], A('mcol')[:, 0:8], sem='vv')
    c.wait('tensor', 'vv', v_sab)
    c.wait('tensor', 'wql', v_wq)
    v_v2r = 0
    for j in range(2):
        c.wait('tensor', 'vv', v_v2r)
        for k in range(8):
            vmm = mm(c, P[6][0:1, :], A('sacol')[:, k:k + 1], A('wq_sb')[:, k * D + j * 512:k * D + j * 512 + 512],
                     start=(k == 0), stop=(k == 7), sem='mm')
        c.wait('vector', 'mm', vmm)
        v_v2r = vcopy(c, A('v2row')[0:1, j * 512:(j + 1) * 512], P[6][0:1, :], sem='vv')
    ts(c, A('tiny')[:, 32:40], A('mcol')[:, 8:16], 1.0, ALU.add)
    v_fold = 0
    for k in range(8):
        v_fold = ts(c, A('wq_sb')[:, k * D:(k + 1) * D], A('wq_sb')[:, k * D:(k + 1) * D],
                    A('tiny').ap()[:, 32 + k:33 + k], ALU.mult, sem='vv')
    c.wait('tensor', 'vv', v_fold)
    v_v1r = 0
    for j in range(2):
        c.wait('tensor', 'vv', v_v1r)
        for k in range(8):
            vmm = mm(c, P[7][0:1, :], A('onesb')[:, 0:1], A('wq_sb')[:, k * D + j * 512:k * D + j * 512 + 512],
                     start=(k == 0), stop=(k == 7), sem='mm')
        c.wait('vector', 'mm', vmm)
        v_v1r = vcopy(c, A('v1row')[0:1, j * 512:(j + 1) * 512], P[7][0:1, :], sem='vv')
    row_bcast(A('v1b').ap(), A('v1row').ap(), D, banks=(4, 5))
    v_v2b = row_bcast(A('v2b').ap(), A('v2row').ap(), D, banks=(4, 5))

    c.dma('sync', A('rope_c')[:, :], Dr('rope_cos')[:, :], 'rt')
    v_ropetab = c.dma('sync', A('rope_s')[:, :], Dr('rope_sin')[:, :], 'rt')

    xfree = 0
    spill_q = 0
    spill_r = 0
    qbank_free = 0
    sq2_done = 0
    slab_mm = [0, 0]
    for s in range(NSLAB):
        t0 = s * 4
        c.wait('gpsimd', 'mm', slab_mm[s % 2])
        for k in range(8):
            c.dma('gpsimd', A('xT_sb')[:, (s % 2) * 8 * SLAB + k * SLAB:(s % 2) * 8 * SLAB + (k + 1) * SLAB],
                  Dr('xT')[k * 128:(k + 1) * 128, s * SLAB:(s + 1) * SLAB], 'xt%d' % (s % 2))
        v_xT = c.cnt['xt%d' % (s % 2)]
        c.wait('sync', 'ss', xfree)
        for tl in range(4):
            vx = c.dma('sync', A('x_sb')[:, tl * D:(tl + 1) * D],
                       Dr('x_tok')[(t0 + tl) * 128:(t0 + tl + 1) * 128, :], 'xk')
        c.wait('scalar', 'xk', vx)
        for tl in range(4):
            xs = A('x_sb')[:, tl * D:(tl + 1) * D]
            act(c, xs, xs, AF.Copy, accum_out=A('stat_sum').ap()[:, t0 + tl:t0 + tl + 1])
            stats_done = act(c, xs, xs, AF.Square, accum_out=A('stat_sq').ap()[:, t0 + tl:t0 + tl + 1], sem='ss')
        xfree = stats_done
        c.wait('vector', 'ss', stats_done)
        ts(c, A('tiny')[:, 40:44], A('stat_sum').ap()[:, t0:t0 + 4], 1.0 / D, ALU.mult)
        ts(c, A('tiny')[:, 44:48], A('stat_sq').ap()[:, t0:t0 + 4], 1.0 / D, ALU.mult)
        tt(c, A('tiny')[:, 48:52], A('tiny')[:, 40:44], A('tiny')[:, 40:44], ALU.mult)
        tt(c, A('tiny')[:, 52:56], A('tiny')[:, 44:48], A('tiny')[:, 48:52], ALU.subtract)
        v_var = ts(c, A('tiny')[:, 52:56], A('tiny')[:, 52:56], EPS, ALU.add, sem='vv')
        c.wait('scalar', 'vv', v_var)
        v_r1 = sinv(c, A('r1').ap()[:, t0:t0 + 4], A('tiny').ap()[:, 52:56], -0.5, sem='ss')
        c.wait('vector', 'ss', v_r1)
        stt(c, A('nmr1').ap()[:, t0:t0 + 4], A('tiny').ap()[:, 40:44], -1.0,
            A('r1').ap()[:, t0:t0 + 4], ALU.mult, ALU.mult)
        c.wait('tensor', 'xt%d' % (s % 2), v_xT)
        if s == 0:
            c.wait('tensor', 'vv', v_v2b)
        c.wait('tensor', 'vv', qbank_free)
        for tl in range(4):
            for j in range(2):
                for k in range(8):
                    v_qmm = mm(c, P[2 * tl + j][:, :],
                               A('xT_sb').ap()[:, (s % 2) * 8 * SLAB + k * SLAB + tl * 128:
                                               (s % 2) * 8 * SLAB + k * SLAB + tl * 128 + 128],
                               A('wq_sb')[:, k * D + j * 512:k * D + j * 512 + 512],
                               start=(k == 0), stop=(k == 7), sem='mm')
        slab_mm[s % 2] = v_qmm
        c.wait('vector', 'mm', v_qmm)
        c.wait('vector', 'ss', sq2_done)
        c.wait('vector', 'sq0', spill_q)
        c.wait('vector', 'sr0', spill_r)
        for tl in range(4):
            stt(c, A('muv')[:, tl * D:(tl + 1) * D], A('v1b')[:, :], A('nmr1').ap()[:, t0 + tl:t0 + tl + 1],
                A('v2b')[:, :], ALU.mult, ALU.add)
        for tl in range(4):
            qp = A('qpre').ap()[:, tl * D:(tl + 1) * D]
            for j in range(2):
                qbank_free = stt(c, qp[:, j * 512:(j + 1) * 512], P[2 * tl + j][:, :],
                                 A('r1').ap()[:, t0 + tl:t0 + tl + 1],
                                 A('muv').ap()[:, tl * D + j * 512:tl * D + j * 512 + 512],
                                 ALU.mult, ALU.add, sem='vv')
        if s == 0:
            c.wait('vector', 'rt', v_ropetab)
        for tl in range(4):
            t = t0 + tl
            qp = A('qpre').ap()[:, tl * D:(tl + 1) * D]
            cosap = A('rope_c').ap()[:, t * 64:(t + 1) * 64].unsqueeze(1).broadcast_to([128, 16, 64])
            qpv = qp.rearrange("p (h f) -> p h f", h=16)
            rpv = A('ropet').ap()[:, tl * D:(tl + 1) * D].rearrange("p (h f) -> p h f", h=16)
            tt(c, rpv, qpv, cosap, ALU.mult)
        for tl in range(4):
            t = t0 + tl
            qp4 = A('qpre').ap()[:, tl * D:(tl + 1) * D].rearrange("p (h a f) -> p h a f", h=16, a=2)
            crv = A('muv').ap()[:, tl * D:(tl + 1) * D].rearrange("p (h a f) -> p h a f", h=16, a=2)
            sin0 = A('rope_s').ap()[:, t * 64:t * 64 + 32].unsqueeze(1).broadcast_to([128, 16, 32]).unsqueeze(2)
            sin1 = A('rope_s').ap()[:, t * 64 + 32:t * 64 + 64].unsqueeze(1).broadcast_to([128, 16, 32]).unsqueeze(2)
            tt(c, crv[:, :, 0:1, :], qp4[:, :, 1:2, :], sin0, ALU.mult)
            tt(c, crv[:, :, 1:2, :], qp4[:, :, 0:1, :], sin1, ALU.mult)
        for tl in range(4):
            rp = A('ropet').ap()[:, tl * D:(tl + 1) * D]
            v_rope = tt(c, rp, rp, A('muv').ap()[:, tl * D:(tl + 1) * D], ALU.add, sem='vv')
        c.wait('scalar', 'vv', v_rope)
        for tl in range(4):
            sq2_done = act(c, A('muv')[:, tl * D:(tl + 1) * D], A('ropet').ap()[:, tl * D:(tl + 1) * D],
                           AF.Square, accum_out=A('sq2').ap()[:, t0 + tl:t0 + tl + 1], sem='ss')
        c.wait('vector', 'ss', sq2_done)
        v_t = ts(c, A('tiny')[:, 56:60], A('sq2').ap()[:, t0:t0 + 4], 1.0 / D, ALU.mult, EPS, ALU.add, sem='vv')
        c.wait('scalar', 'vv', v_t)
        v_r2 = sinv(c, A('r2').ap()[:, t0:t0 + 4], A('tiny').ap()[:, 56:60], -0.5, sem='ss')
        c.wait('vector', 'ss', v_r2)
        v_qh = 0
        for tl in range(4):
            v_qh = ts(c, A('qhat_t').ap()[:, tl * D:(tl + 1) * D], A('ropet').ap()[:, tl * D:(tl + 1) * D],
                      A('r2').ap()[:, t0 + tl:t0 + tl + 1], ALU.mult, sem='vv')
        v_res = 0
        for tl in range(4):
            v_res = tt(c, A('resid_t').ap()[:, tl * D:(tl + 1) * D], A('ropet').ap()[:, tl * D:(tl + 1) * D],
                       A('gab').ap(), ALU.mult, sem='vv')
        c.wait('gpsimd', 'vv', v_qh)
        for tl in range(4):
            spill_q = c.dma('gpsimd', Dr('qhat_d')[(t0 + tl) * 128:(t0 + tl + 1) * 128, :],
                            A('qhat_t').ap()[:, tl * D:(tl + 1) * D], 'sq0')
        c.wait('sync', 'vv', v_res)
        for tl in range(4):
            spill_r = c.dma('sync', Dr('resid_d')[(t0 + tl) * 128:(t0 + tl + 1) * 128, :],
                            A('resid_t').ap()[:, tl * D:(tl + 1) * D], 'sr0')
    if DEBUG_P1:
        c.wait('sync', 'vv', c.cnt['vv'])
        c.wait('sync', 'ss', c.cnt['ss'])
        c.wait('sync', 'mm', c.cnt['mm'])
        c.dma('sync', Dr('xmid_d')[0:128, :], A('v1b')[:, :], 'dbg1')
        c.dma('sync', Dr('xmid_d')[128:256, :], A('v2b')[:, :], 'dbg1')
        c.dma('sync', Dr('xmid_d')[256:384, :], A('gab')[:, :], 'dbg1')
        c.dma('sync', Dr('xmid_d')[384:385, :], A('v1row')[0:1, :], 'dbg1')
        c.dma('sync', Dr('xmid_d')[385:386, :], A('v2row')[0:1, :], 'dbg1')
        c.dma('sync', Dr('xmid_d')[386:387, :], A('garow').ap()[0:1, 0:1024], 'dbg1')
        c.dma('sync', Dr('xmid_d')[387:388, :], A('garow').ap()[0:1, 1024:2048], 'dbg1')
        c.dma('sync', Dr('xmid_d')[388:389, :], A('modrow').ap()[0:1, 1024:2048], 'dbg1')
        c.dma('sync', Dr('xmid_d')[400:528, :], A('rope_c')[:, :], 'dbg1')
        c.dma('sync', Dr('xmid_d')[528:656, :], A('rope_s')[:, :], 'dbg1')
        c.dma('sync', Dr('xmid_d')[656:784, :], A('muv')[:, 0:1024], 'dbg1')
        c.dma('sync', Dr('xmid_d')[784:912, :], A('qpre')[:, 0:1024], 'dbg1')
        c.dma('sync', Dr('xmid_d')[912:1040, :], A('ropet')[:, 0:1024], 'dbg1')
        c.dma('sync', Dr('xmid_d')[1040:1168, 0:16], A('stat_sum')[:, 0:16], 'dbg1')
        c.dma('sync', Dr('xmid_d')[1168:1296, 0:16], A('stat_sq')[:, 0:16], 'dbg1')
        c.dma('sync', Dr('xmid_d')[1296:1424, 0:16], A('r1')[:, 0:16], 'dbg1')
        c.dma('sync', Dr('xmid_d')[1424:1552, 0:16], A('nmr1')[:, 0:16], 'dbg1')
        c.dma('sync', Dr('xmid_d')[1552:1680, 0:64], A('tiny')[:, :], 'dbg1')
        c.dma('sync', Dr('xmid_d')[394:395, 0:48], A('mcol')[0:1, :], 'dbg1')




def _p2(c, P, A, Dr, ph):
    ph.sb('wzres', (128, 8 * DZP), BF16)
    ph.sb('qhT', (128, 2 * 8 * SLAB), BF16)
    ph.sb('xbcT', (128, NCT * 516), BF16)
    ph.sb('xcv', (128, 2 * 512), BF16)
    ph.sb('zsl', (128, 2 * 512), BF16)
    ph.sb('cacc', (128, 2 * 512), BF16)
    ph.sb('dtr', (32, 512), F32)
    ph.sb('dtv', (32, 512), F32)
    ph.sb('lndt', (32, 512), F32)
    ph.sb('adt', (32, 512), F32)
    ph.sb('acsf', (32, 512), F32)
    ph.sb('acsg', (32, 512), F32)
    ph.sb('bhalf', (32, 2 * 512), BF16)
    ph.sb('blow', (32, 2 * 512), BF16)
    ph.sb('acol', (32, 1), F32)
    ph.sb('gcar', (32, 1), F32)
    ph.sb('convw', (128, NCT * 4), F32)
    ph.sb('convb', (128, NCT), F32)
    ph.sb('npw', (128, 8), F32)
    ph.sb('dtbc', (32, 1), F32)
    ph.sb('t32', (32, 512), F32)

    for k in range(8):
        c.dma('gpsimd', A('wzres')[:, k * DZP:(k + 1) * DZP], Dr('wzT')[k * 128:(k + 1) * 128, :], 'wzl')
    v_wz = c.cnt['wzl']
    c.dma('sync', A('convw')[:, :], Dr('conv_w')[:, :], 'c2')
    c.dma('sync', A('convb')[:, :], Dr('conv_b')[:, :], 'c2')
    c.dma('sync', A('npw')[:, :], Dr('npw_col')[:, :], 'c2')
    c.dma('sync', A('dtbc')[:, :], Dr('dtb_col')[:, :], 'c2')
    c.dma('sync', A('acol')[:, :], Dr('alog_col')[:, :], 'c2')
    v_c2 = c.cnt['c2']
    c.wait('scalar', 'c2', v_c2)
    v_a2 = act(c, A('acol')[:, :], A('acol')[:, :], AF.Exp, sem='ss')
    c.wait('vector', 'ss', v_a2)
    v_a3 = ts(c, A('acol')[:, :], A('acol')[:, :], -1.0, ALU.mult)

    vmemset(c, A('xbcT')[:, :], 0.0)
    v_ms = vmemset(c, A('gcar')[:, :], 0.0)

    zslv = [0, 0]
    v_dtr_used = 0
    caccv = [0, 0]
    bank_ev = [0, 0, 0, 0]
    v_dtev = 0
    v_lastmm = 0
    slab_mm = [0, 0]
    v_halo = v_ms
    v_sp = 0
    for s in range(NSLAB):
        sh = s % 2
        qbase = sh * 8 * SLAB
        c.wait('sync', 'mm', slab_mm[sh])
        vq = 0
        for k in range(8):
            vq = c.dmat('sync', A('qhT')[:, qbase + k * SLAB:qbase + (k + 1) * SLAB],
                        Dr('qhat_d')[s * SLAB:(s + 1) * SLAB, k * 128:(k + 1) * 128], 'qh%d' % sh)
        c.wait('vector', 'qh%d' % sh, vq)
        v_fold = 0
        for k in range(8):
            v_fold = ts(c, A('qhT')[:, qbase + k * SLAB:qbase + (k + 1) * SLAB],
                        A('qhT')[:, qbase + k * SLAB:qbase + (k + 1) * SLAB],
                        A('npw').ap()[:, k:k + 1], ALU.mult)
        if s > 0:
            c.wait('vector', 'ss', v_dtev)
            for ct in range(NCT):
                base = ct * 516
                v_halo = vcopy(c, A('xbcT')[:, base:base + 3], A('xbcT')[:, base + 512:base + 515])
        if s == 0:
            c.wait('tensor', 'wzl', v_wz)
        for m in range(MZ):
            bk = m % 4
            if m == 0:
                c.wait('tensor', 'vv', v_fold)
            c.wait('tensor', 'ss', bank_ev[bk])
            for k in range(8):
                vmm = mm(c, P[bk][:, :], A('wzres')[:, k * DZP + m * 128:k * DZP + (m + 1) * 128],
                         A('qhT')[:, qbase + k * SLAB:qbase + (k + 1) * SLAB],
                         start=(k == 0), stop=(k == 7), sem='mm')
            v_lastmm = vmm
            c.wait('scalar', 'mm', vmm)
            if m < 16:
                zh = m % 2
                c.wait('scalar', 'zs%d' % zh, zslv[zh])
                v_ev = act(c, A('zsl')[:, zh * 512:(zh + 1) * 512], P[bk][:, :], AF.Silu, sem='ss')
                c.wait('gpsimd', 'ss', v_ev)
                zslv[zh] = c.dma('gpsimd', Dr('zsilu_d')[m * 128:(m + 1) * 128, s * 512:(s + 1) * 512],
                                 A('zsl')[:, zh * 512:(zh + 1) * 512], 'zs%d' % zh)
                bank_ev[bk] = v_ev
            elif m < 34:
                ct = m - 16
                if s > 0 and m == 16:
                    c.wait('scalar', 'vv', v_halo)
                v_ev = scopy(c, A('xbcT')[:, ct * 516 + 3:ct * 516 + 515], P[bk][:, :], sem='ss')
                bank_ev[bk] = v_ev
                base = ct * 516
                hb = ct % 2
                ac = A('cacc').ap()[:, hb * 512:(hb + 1) * 512]
                c.wait('vector', 'ss', v_ev)
                ts(c, ac, A('xbcT')[:, base + 0:base + 512], A('convw').ap()[:, ct * 4:ct * 4 + 1], ALU.mult)
                for kk in range(1, 4):
                    v_cv = stt(c, ac, A('xbcT')[:, base + kk:base + kk + 512],
                               A('convw').ap()[:, ct * 4 + kk:ct * 4 + kk + 1], ac, ALU.mult, ALU.add)
                c.wait('scalar', 'vv', v_cv)
                c.wait('scalar', 'xc%d' % hb, caccv[hb])
                v_sl = act(c, A('xcv')[:, hb * 512:(hb + 1) * 512], ac, AF.Silu,
                           bias=A('convb').ap()[:, ct:ct + 1], sem='ss')
                c.wait('gpsimd', 'ss', v_sl)
                caccv[hb] = c.dma('gpsimd', Dr('xconv_d')[ct * 128:(ct + 1) * 128, s * 512:(s + 1) * 512],
                                  A('xcv')[:, hb * 512:(hb + 1) * 512], 'xc%d' % hb)
            else:
                c.wait('scalar', 'vv', v_dtr_used)
                v_ev = scopy(c, A('dtr')[:, :], P[bk][0:32, :], sem='ss')
                bank_ev[bk] = v_ev
                v_dtev = v_ev
        slab_mm[sh] = v_lastmm
        c.wait('vector', 'ss', v_dtev)
        v_v = ts(c, A('t32')[:, :], A('dtr')[:, :], A('dtbc').ap()[:, 0:1], ALU.add)
        v_dtr_used = v_v
        c.wait('scalar', 'vv', v_v)
        v_av = act(c, A('dtv')[:, :], A('t32')[:, :], AF.Abs, sem='ss')
        c.wait('scalar', 'ss', v_av)
        v_e = act(c, A('dtv')[:, :], A('dtv')[:, :], AF.Exp, scale=-1.0, sem='ss')
        c.wait('scalar', 'ss', v_e)
        v_l = act(c, A('dtv')[:, :], A('dtv')[:, :], AF.Ln, bias=1.0, sem='ss')
        c.wait('vector', 'ss', v_l)
        ts(c, A('t32')[:, :], A('t32')[:, :], 0.0, ALU.max)
        v_dt = tt(c, A('dtv')[:, :], A('dtv')[:, :], A('t32')[:, :], ALU.add)
        c.wait('scalar', 'vv', v_dt)
        v_ln = act(c, A('lndt')[:, :], A('dtv')[:, :], AF.Ln, sem='ss')
        c.wait('vector', 'ss', v_ln)
        v_adt = ts(c, A('adt')[:, :], A('dtv')[:, :], A('acol').ap()[:, 0:1], ALU.mult)
        for cl in range(4):
            sl = slice(cl * 128, (cl + 1) * 128)
            vs = vscan(c, A('acsf').ap()[:, sl], A('adt').ap()[:, sl], A('adt').ap()[:, sl], 0.0,
                       ALU.add, ALU.bypass)
        vg = vscan(c, A('acsg')[:, :], A('adt')[:, :], A('adt')[:, :], A('gcar').ap()[:, 0:1],
                   ALU.add, ALU.bypass)
        v_car = vcopy(c, A('gcar')[:, :], A('acsg').ap()[:, 511:512])
        bh = A('bhalf').ap()[:, sh * 512:(sh + 1) * 512]
        bl = A('blow').ap()[:, sh * 512:(sh + 1) * 512]
        c.wait('vector', 'sp%d' % sh, c.cnt.get('sp%d' % sh, 0))
        v1 = vcopy(c, bh, A('acsf')[:, :])
        v2 = tt(c, bl, A('acsf')[:, :], bh, ALU.subtract)
        c.wait('gpsimd', 'vv', v2)
        c.dma('gpsimd', Dr('acshi_d')[:, s * 512:(s + 1) * 512], bh, 'sp%d' % sh)
        vsp1 = c.dma('gpsimd', Dr('acslo_d')[:, s * 512:(s + 1) * 512], bl, 'sp%d' % sh)
        v3 = tt(c, A('t32')[:, :], A('lndt')[:, :], A('acsf')[:, :], ALU.subtract)
        c.wait('vector', 'sp%d' % sh, vsp1)
        v4 = vcopy(c, bh, A('t32')[:, :])
        v5 = tt(c, bl, A('t32')[:, :], bh, ALU.subtract)
        c.wait('gpsimd', 'vv', v5)
        c.dma('gpsimd', Dr('nahi_d')[:, s * 512:(s + 1) * 512], bh, 'sp%d' % sh)
        vsp2 = c.dma('gpsimd', Dr('nalo_d')[:, s * 512:(s + 1) * 512], bl, 'sp%d' % sh)
        c.wait('vector', 'sp%d' % sh, vsp2)
        v6 = vcopy(c, bh, A('acsg')[:, :])
        v7 = tt(c, bl, A('acsg')[:, :], bh, ALU.subtract)
        c.wait('gpsimd', 'vv', v7)
        c.dma('gpsimd', Dr('acsghi_d')[:, s * 512:(s + 1) * 512], bh, 'sp%d' % sh)
        c.dma('gpsimd', Dr('acsglo_d')[:, s * 512:(s + 1) * 512], bl, 'sp%d' % sh)
        c.dma('gpsimd', Dr('acs_d')[:, s * 512:(s + 1) * 512], A('acsf')[:, :], 'sp%d' % sh)
        v_sp = c.dma('gpsimd', Dr('dt_d')[:, s * 512:(s + 1) * 512], A('dtv')[:, :], 'sp%d' % sh)
        c.wait('vector', 'sp%d' % sh, v_sp)


def _p3(c, P, A, Dr, ph):
    ph.sb('bhi', (33, 2048), BF16)
    ph.sb('blo', (33, 2048), BF16)
    ph.sb('r00', (33, 2048), BF16)
    ph.sb('r01', (33, 2048), BF16)
    ph.sb('r10', (33, 2048), BF16)
    ph.sb('r11', (33, 2048), BF16)
    ph.sb('eacs', (64, 2048), BF16)
    ph.sb('acsf2', (32, 2048), F32)
    ph.sb('dtf2', (32, 2048), F32)
    ph.sb('xcf', (128, 2 * NCT * 128), BF16)
    ph.sb('xtok', (128, 2 * 2176), BF16)
    ph.sb('scob', (128, 128), BF16)
    ph.sb('gbuf', (128, 2048), BF16)
    ph.sb('eeb', (128, 2 * 2048), BF16)
    ph.sb('v2b2', (128, 2048), BF16)
    ph.sb('yoe', (128, 2048), BF16)
    ph.sb('dxt', (128, 2048), BF16)
    ph.sb('dxm', (128, 2048), BF16)
    ph.sb('ya', (128, 2048), F32)
    ph.sb('ypb', (128, 2048), BF16)
    ph.sb('srun', (128, 2048), F32)
    ph.sb('srunb', (128, 2048), BF16)
    ph.sb('stsb', (128, 2048), F32)
    ph.sb('dsel', (32, 2048), BF16)
    ph.sb('dexpt', (128, 2048), BF16)
    ph.sb('f2b', (32, 128), BF16)
    ph.sb('e32', (32, 128), F32)
    ph.sb('dcol', (32, 1), F32)
    ph.sb('trilRs', (127, 2048), BF16)
    ph.sb('trilLs', (127, 128), BF16)
    ph.sb('sel32s', (32, 2048), BF16)
    ph.sb('sel64s', (64, 2048), BF16)
    ph.sb('ones32', (32, 128), BF16)
    ph.sb('dhp', (128, 16), F32)
    ph.sb('gcol', (128, 1), F32)

    c.dma('sync', A('trilRs')[:, :], Dr('trilR')[:, :], 'c3')
    c.dma('sync', A('trilLs')[:, :], Dr('trilL')[:, :], 'c3')
    c.dma('sync', A('sel32s')[:, :], Dr('sel32')[:, :], 'c3')
    c.dma('sync', A('sel64s')[:, :], Dr('sel64')[:, :], 'c3')
    c.dma('sync', A('ones32')[:, :], Dr('ones32b')[:, :], 'c3')
    c.dma('sync', A('dhp')[:, :], Dr('d_hp')[:, :], 'c3')
    c.dma('sync', A('gcol')[:, :], Dr('g_state')[:, :], 'c3')
    for rn, sn in (('r00', 'sel_h0'), ('r01', 'sel_h0'), ('r10', 'sel_h1'), ('r11', 'sel_h1')):
        c.dma('sync', A(rn)[0:32, :], Dr(sn)[:, :], 'c3')
    c.dma('sync', A('bhi')[0:32, :], Dr('nahi_d')[:, :], 'c3')
    c.dma('sync', A('blo')[0:32, :], Dr('nalo_d')[:, :], 'c3')
    c.dma('sync', A('eacs')[0:32, :], Dr('acshi_d')[:, :], 'c3')
    c.dma('sync', A('eacs')[32:64, :], Dr('acslo_d')[:, :], 'c3')
    c.dma('sync', A('acsf2')[:, :], Dr('acs_d')[:, :], 'c3')
    v_c3 = c.dma('sync', A('dtf2')[:, :], Dr('dt_d')[:, :], 'c3')
    vmemset(c, A('bhi')[32:33, :], 1.0)
    vmemset(c, A('blo')[32:33, :], 1.0)
    vmemset(c, A('srun')[:, :], 0.0)
    vmemset(c, A('srunb')[:, :], 0.0)
    c.wait('vector', 'c3', v_c3)
    v_dx = vcopy(c, A('dxt').ap().rearrange("p (a l) -> p a l", a=16),
                 A('dhp').ap().unsqueeze(2).broadcast_to([128, 16, 128]), sem='vv')
    c.wait('tensor', 'c3', v_c3)
    c.wait('scalar', 'c3', v_c3)

    rhs_free = {'r00': 0, 'r01': 0, 'r10': 0, 'r11': 0}
    RD3 = {}
    v_yoffp = [0, 0]; v_dxmp = [0, 0]; v_stsp = [0, 0]; v_v2cp = [0, 0]
    a1p = [0, 0]; v_a1 = 0; v_a2 = 0; v_v2c = 0; v_scan = 0
    v_gev = [0, 0]; v_gtt1 = 0; v_diag = [0, 0]; v_eeev = [0, 0]
    v_ypsp = 0; v_f2p = 0; v_dsp = 0; stev = 0; dexpev = [0]

    def p3reads(ch):
        pp = ch % 2
        sl = slice(ch * 128, (ch + 1) * 128)
        c.wait('sync', 'mm', v_yoffp[pp])
        c.wait('sync', 'vv', max(v_dxmp[pp], v_dx))
        xf = c.dma('sync', A('xcf').ap()[:, pp * NCT * 128:(pp + 1) * NCT * 128].rearrange("p (a l) -> p a l", a=NCT),
                   Dr('xconv_d').rearrange("(a p) t -> a p t", p=128)[:, :, sl].transpose([1, 0, 2]), 'xf')
        c.wait('sync', 'mm', v_stsp[pp])
        c.wait('sync', 'vv', v_v2cp[pp])
        xk = c.dmat('sync', A('xtok')[:, pp * 2176:(pp + 1) * 2176], Dr('xconv_d')[0:2176, sl], 'xk2')
        RD3[ch] = [xf, xk, 0]

    def p3reads_fl(ch):
        sl = slice(ch * 128, (ch + 1) * 128)
        for half, (rh, rl) in enumerate((('r00', 'r01'), ('r10', 'r11'))):
            h0 = half * 16
            c.wait('sync', 'mm', rhs_free[rh])
            c.dma('sync', A(rh).ap()[32:33, :].rearrange("p (a b) -> p a b", a=16),
                  Dr('acshi_d')[h0:h0 + 16, sl].unsqueeze(0), 'fl')
            c.dma('sync', A(rl).ap()[32:33, :].rearrange("p (a b) -> p a b", a=16),
                  Dr('acslo_d')[h0:h0 + 16, sl].unsqueeze(0), 'fl')
        RD3[ch][2] = c.cnt['fl']

    p3reads(0)
    p3reads_fl(0)
    for ch in range(NCH):
        pp = ch % 2
        sl = slice(ch * 128, (ch + 1) * 128)
        if ch + 1 < NCH:
            p3reads(ch + 1)
        xcf_ = lambda a, b=None: A('xcf').ap()[:, pp * NCT * 128 + a * 128:pp * NCT * 128 + (a if b is None else b) * 128 + (128 if b is None else 0)]
        xtok_ = A('xtok').ap()[:, pp * 2176:(pp + 1) * 2176]
        eeb_ = A('eeb').ap()[:, pp * 2048:(pp + 1) * 2048]
        c.wait('tensor', 'xf', RD3[ch][0])
        c.wait('tensor', 'vv', v_a1)
        vsc = mm(c, P[4][:, 0:128], xcf_(16), xcf_(17), start=True, stop=True, sem='mm')
        c.wait('vector', 'mm', vsc)
        v_scev = vcopy(c, A('scob')[:, :], P[4][:, 0:128], sem='vv')
        for half in range(2):
            rh, rl = (('r00', 'r01'), ('r10', 'r11'))[half]
            c.wait('tensor', 'fl', RD3[ch][2])
            if half == 0:
                c.wait('tensor', 'vv', v_scev)
            for j in range(4):
                mm(c, P[4 + j][:, :], A('bhi').ap()[:, sl], A(rh)[:, j * 512:(j + 1) * 512],
                   start=True, stop=False, sem=None)
                mm(c, P[4 + j][:, :], A('blo').ap()[:, sl], A(rl)[:, j * 512:(j + 1) * 512],
                   start=False, stop=False, sem=None)
                vgb = mm(c, P[4 + j][:, :], A('trilLs')[:, :], A('trilRs')[:, j * 512:(j + 1) * 512],
                         start=False, stop=True, sem='mm')
            rhs_free[rh] = vgb
            rhs_free[rl] = vgb
            c.wait('scalar', 'mm', vgb)
            c.wait('scalar', 'mm', v_diag[1 - half] if half == 1 else v_diag[1])
            c.wait('scalar', 'vv', v_gtt1 if half == 0 else 0)
            for j in range(4):
                v_gev[half] = act(c, A('gbuf')[:, j * 512:(j + 1) * 512], P[4 + j][:, :], AF.Exp, sem='ss')
            c.wait('vector', 'ss', v_gev[half])
            v_gtt = tt(c, A('gbuf').ap().rearrange("p (h l) -> p h l", h=16),
                       A('gbuf').ap().rearrange("p (h l) -> p h l", h=16),
                       A('scob').ap().unsqueeze(1).broadcast_to([128, 16, 128]), ALU.mult, sem='vv')
            if half == 1:
                v_gtt1 = v_gtt
            c.wait('tensor', 'vv', v_gtt)
            c.wait('tensor', 'xk2', RD3[ch][1])
            if half == 0:
                c.wait('tensor', 'vv', dexpev[0])
            for hl in range(16):
                h = half * 16 + hl
                ct = h // 2
                po = (ct % 4) * 128
                rows = slice(0, 64) if h % 2 == 0 else slice(64, 128)
                v_dg = mm(c, P[ct // 4][rows, po:po + 128], xtok_[:, h * 64:(h + 1) * 64],
                          A('gbuf')[:, hl * 128:(hl + 1) * 128],
                          start=True, stop=True, sem='mm')
            v_diag[half] = v_dg
        if ch + 1 < NCH:
            p3reads_fl(ch + 1)
        c.wait('tensor', 'ss', v_gev[1])
        for ct in range(16):
            v_ee = mm(c, P[4 + ct // 4][:, (ct % 4) * 128:(ct % 4) * 128 + 128],
                      A('sel64s')[:, ct * 128:(ct + 1) * 128], A('eacs').ap()[:, sl],
                      start=True, stop=True, sem='mm')
        c.wait('scalar', 'mm', v_ee)
        c.wait('scalar', 'vv', a1p[pp])
        for j in range(4):
            v_eev = act(c, A('eeb')[:, pp * 2048 + j * 512:pp * 2048 + (j + 1) * 512], P[4 + j][:, :], AF.Exp, sem='ss')
        v_eeev[pp] = v_eev
        v_dxm = tt(c, A('dxm')[:, :], A('xcf').ap()[:, pp * NCT * 128:pp * NCT * 128 + 2048],
                   A('dxt').ap()[:, :], ALU.mult, sem='vv')
        v_dxmp[pp] = v_dxm
        c.wait('vector', 'mm', v_diag[1])
        for j in range(4):
            v_a2 = stt(c, A('ya')[:, j * 512:(j + 1) * 512], A('dxm').ap()[:, j * 512:(j + 1) * 512],
                       1.0, P[j][:, :], ALU.mult, ALU.add, sem='vv')
        aend = A('acsf2').ap()[:, ch * 128 + 127:ch * 128 + 128]
        c.wait('scalar', 'vv', v_f2p)
        v_f2e = act(c, A('e32')[:, :], A('acsf2').ap()[:, sl], AF.Exp, scale=-1.0, bias=aend, sem='ss')
        c.wait('vector', 'ss', v_f2e)
        v_f2 = tt(c, A('f2b')[:, :], A('e32')[:, :], A('dtf2').ap()[:, sl], ALU.mult, sem='vv')
        v_f2p = v_f2
        c.wait('tensor', 'vv', max(v_f2, v_a2))
        for j in range(4):
            v_f2mm = mm(c, P[j][:, :], A('f2b')[:, :], A('sel32s')[:, j * 512:(j + 1) * 512],
                        start=True, stop=True, sem='mm')
        c.wait('vector', 'mm', v_f2mm)
        for j in range(4):
            v_v2c = stt(c, A('v2b2')[:, j * 512:(j + 1) * 512], xtok_[:, j * 512:(j + 1) * 512],
                        1.0, P[j][:, :], ALU.mult, ALU.mult, sem='vv')
        v_v2cp[pp] = v_v2c
        c.wait('tensor', 'ss', v_eev)
        c.wait('tensor', 'vv', v_v2c)
        for j in range(4):
            v_sts = mm(c, P[4 + j][:, :], xtok_[:, 2048:2176], A('v2b2')[:, j * 512:(j + 1) * 512],
                       start=True, stop=True, sem='mm')
        v_stsp[pp] = v_sts
        c.wait('vector', 'mm', v_sts)
        for j in range(4):
            stev = vcopy(c, A('stsb')[:, j * 512:(j + 1) * 512], P[4 + j][:, :], sem='vv')
        c.wait('scalar', 'vv', v_dsp)
        v_de = act(c, A('dcol')[:, :], aend, AF.Exp, sem='ss')
        c.wait('vector', 'ss', v_de)
        v_ds = ts(c, A('dsel')[:, :], A('sel32s')[:, :], A('dcol').ap()[:, 0:1], ALU.mult, sem='vv')
        v_dsp = v_ds
        c.wait('tensor', 'vv', max(v_ds, v_v2c))
        for j in range(4):
            v_dexpmm = mm(c, P[j][:, :], A('ones32')[:, :], A('dsel')[:, j * 512:(j + 1) * 512],
                          start=True, stop=True, sem='mm')
        c.wait('vector', 'mm', v_dexpmm)
        for j in range(4):
            v_dxe = vcopy(c, A('dexpt')[:, j * 512:(j + 1) * 512], P[j][:, :], sem='vv')
        dexpev[0] = v_dxe
        c.wait('tensor', 'vv', max(stev, v_scan))
        for ct in range(16):
            v_yoff = mm(c, P[4 + ct // 4][:, (ct % 4) * 128:(ct % 4) * 128 + 128],
                        A('srunb').ap()[:, ct * 128:(ct + 1) * 128], xcf_(17),
                        start=True, stop=True, sem='mm')
        v_yoffp[pp] = v_yoff
        c.wait('vector', 'mm', v_yoff)
        for j in range(4):
            v_a1 = stt(c, A('yoe')[:, j * 512:(j + 1) * 512], eeb_[:, j * 512:(j + 1) * 512],
                       1.0, P[4 + j][:, :], ALU.mult, ALU.mult, sem='vv')
        a1p[pp] = v_a1
        c.wait('vector', 'yp', v_ypsp)
        v_a3 = tt(c, A('ypb')[:, :], A('ya')[:, :], A('yoe')[:, :], ALU.add, sem='vv')
        c.wait('gpsimd', 'vv', v_a3)
        v_ypsp = c.dma('gpsimd', Dr('ypre_d').rearrange("(a p) t -> a p t", p=128)[:, :, sl].transpose([1, 0, 2]),
                       A('ypb').ap().rearrange("p (a l) -> p a l", a=16), 'yp')
        tt(c, A('ya')[:, :], A('srun')[:, :], A('dexpt')[:, :], ALU.mult)
        tt(c, A('srun')[:, :], A('ya')[:, :], A('stsb')[:, :], ALU.add)
        v_scan = vcopy(c, A('srunb')[:, :], A('srun')[:, :], sem='vv')
        del RD3[ch]
    v_g = ts(c, A('ya')[:, :], A('srun')[:, :], A('gcol').ap()[:, 0:1], ALU.mult, sem='vv')
    c.wait('gpsimd', 'vv', v_g)
    c.dma('gpsimd', Dr('st_in')[:, :], A('ya')[:, :], 'stw')


def _p4(c, P, A, Dr, ph):
    ph.sb('sinit', (128, 2048), F32)
    ph.sb('sinitb', (128, 2048), BF16)
    ph.sb('ws', (128, 16 * 1024), BF16)
    ph.sb('eag', (64, 2048), BF16)
    ph.sb('sel64s2', (64, 2048), BF16)
    ph.sb('ypt', (128, 2 * 2048), BF16)
    ph.sb('zst', (128, 2 * 2048), BF16)
    ph.sb('ctb', (128, 2 * 128), BF16)
    ph.sb('egb', (128, 2 * 2048), BF16)
    ph.sb('ycb', (128, 2048), BF16)
    ph.sb('ygz', (128, 2 * 2048), BF16)
    ph.sb('sqb', (128, 2 * 2048), BF16)
    ph.sb('xmt', (128, 2 * 1024), F32)
    ph.sb('rst', (128, 2 * 1024), F32)
    ph.sb('xdl', (128, 2 * 1024), F32)
    ph.sb('xev', (128, 2 * 1024), F32)
    ph.sb('guse', (128, 1), F32)
    ph.sb('ngw', (128, 16), F32)
    ph.sb('mrow', (1, 2 * 128), F32)

    v_cc = c.op('gpsimd', lambda e: e.collective_compute(
        "AllReduce", ALU.add, replica_groups=GROUPS,
        ins=[c.dr['st_in'].opt()], outs=[c.dr['st_out'].opt()]), sem='cc')
    c.wait('gpsimd', 'cc', v_cc)
    v_si = c.dma('gpsimd', A('sinit')[:, :], Dr('st_out')[:, :], 'si')
    c.dma('sync', A('guse')[:, :], Dr('g_use')[:, :], 'c4')
    c.dma('sync', A('ngw')[:, :], Dr('ngw_col')[:, :], 'c4')
    c.dma('sync', A('sel64s2')[:, :], Dr('sel64')[:, :], 'c4')
    c.dma('sync', A('eag')[0:32, :], Dr('acsghi_d')[:, :], 'c4')
    v_c4 = c.dma('sync', A('eag')[32:64, :], Dr('acsglo_d')[:, :], 'c4')
    for k in range(16):
        c.dma('gpsimd', A('ws')[:, k * 1024:(k + 1) * 1024], Dr('woutT')[k * 128:(k + 1) * 128, :], 'wo')
    v_wo = c.cnt['wo']
    c.wait('vector', 'si', v_si)
    c.wait('vector', 'c4', v_c4)
    ts(c, A('sinit')[:, :], A('sinit')[:, :], A('guse').ap()[:, 0:1], ALU.mult)
    v_sib = vcopy(c, A('sinitb')[:, :], A('sinit')[:, :], sem='vv')
    c.wait('vector', 'wo', v_wo)
    for k in range(16):
        ts(c, A('ws')[:, k * 1024:(k + 1) * 1024], A('ws')[:, k * 1024:(k + 1) * 1024],
           A('ngw').ap()[:, k:k + 1], ALU.mult)
        v_wf = tt(c, A('ws')[:, k * 1024:(k + 1) * 1024], A('ws')[:, k * 1024:(k + 1) * 1024],
                  A('gab').ap(), ALU.mult, sem='vv')

    b3p = [0, 0]; ycp = [0, 0]; o2p = [0, 0]; sqmmp = [0, 0]; b1p = [0, 0]; spillp = [0, 0]
    RD = {}; EG = {}; YC = {}

    def reads(t):
        pp = t % 2
        tsl = slice(t * 128, (t + 1) * 128)
        c.wait('sync', 'vv', b3p[pp])
        yp = c.dma('sync', A('ypt').ap()[:, pp * 2048:(pp + 1) * 2048].rearrange("p (a l) -> p a l", a=16),
                   Dr('ypre_d').rearrange("(a p) t -> a p t", p=128)[:, :, tsl].transpose([1, 0, 2]), 'yp2')
        zs = c.dma('sync', A('zst').ap()[:, pp * 2048:(pp + 1) * 2048].rearrange("p (a l) -> p a l", a=16),
                   Dr('zsilu_d').rearrange("(a p) t -> a p t", p=128)[:, :, tsl].transpose([1, 0, 2]), 'zs2')
        c.wait('sync', 'mm', ycp[pp])
        ct = c.dma('sync', A('ctb')[:, pp * 128:(pp + 1) * 128], Dr('xconv_d')[2176:2304, tsl], 'ct2')
        c.wait('sync', 'vv', o2p[pp])
        xm = c.dma('sync', A('xmt')[:, pp * 1024:(pp + 1) * 1024], Dr('x_tok')[tsl, :], 'xk4')
        rr = c.dma('sync', A('rst')[:, pp * 1024:(pp + 1) * 1024], Dr('resid_d')[tsl, :], 'rr')
        RD[t] = (yp, zs, ct, xm, rr)

    def front(t, gate_vv):
        pp = t % 2
        tsl = slice(t * 128, (t + 1) * 128)
        c.wait('tensor', 'c4', v_c4)
        c.wait('tensor', 'vv', gate_vv)
        for ck in range(16):
            eg = mm(c, P[ck // 4][:, (ck % 4) * 128:(ck % 4) * 128 + 128],
                    A('sel64s2')[:, ck * 128:(ck + 1) * 128], A('eag').ap()[:, tsl],
                    start=True, stop=True, sem='mm')
        c.wait('scalar', 'mm', eg)
        c.wait('scalar', 'vv', b1p[pp])
        for j in range(4):
            egev = act(c, A('egb')[:, pp * 2048 + j * 512:pp * 2048 + (j + 1) * 512], P[j][:, :], AF.Exp, sem='ss')
        return eg, egev

    def ycorr(t, gate_vv):
        pp = t % 2
        c.wait('tensor', 'ct2', RD[t][2])
        c.wait('tensor', 'vv', max(v_sib, v_wf))
        c.wait('tensor', 'vv', gate_vv)
        for ck in range(16):
            yc = mm(c, P[4 + ck // 4][:, (ck % 4) * 128:(ck % 4) * 128 + 128],
                    A('sinitb')[:, ck * 128:(ck + 1) * 128], A('ctb')[:, pp * 128:(pp + 1) * 128],
                    start=True, stop=True, sem='mm')
        ycp[pp] = yc
        return yc

    reads(0)
    EG[0] = front(0, 0)
    YC[0] = ycorr(0, 0)
    for t in range(NTT):
        pp = t % 2
        tsl = slice(t * 128, (t + 1) * 128)
        if t + 1 < NTT:
            reads(t + 1)
        eg_t, egev_t = EG.pop(t)
        yc_t = YC.pop(t)
        c.wait('vector', 'mm', yc_t)
        c.wait('vector', 'ss', egev_t)
        c.wait('vector', 'yp2', RD[t][0])
        c.wait('vector', 'zs2', RD[t][1])
        for j in range(4):
            b1 = stt(c, A('ycb')[:, j * 512:(j + 1) * 512],
                     A('egb').ap()[:, pp * 2048 + j * 512:pp * 2048 + (j + 1) * 512],
                     1.0, P[4 + j][:, :], ALU.mult, ALU.mult, sem='vv')
        b1p[pp] = b1
        ypt_ = A('ypt').ap()[:, pp * 2048:(pp + 1) * 2048]
        zst_ = A('zst').ap()[:, pp * 2048:(pp + 1) * 2048]
        ygz_ = A('ygz').ap()[:, pp * 2048:(pp + 1) * 2048]
        tt(c, ypt_, ypt_, A('ycb')[:, :], ALU.add)
        b3 = tt(c, ygz_, ypt_, zst_, ALU.mult, sem='vv')
        b3p[pp] = b3
        c.wait('scalar', 'vv', b3)
        c.wait('scalar', 'mm', sqmmp[pp])
        sq = act(c, A('sqb')[:, pp * 2048:(pp + 1) * 2048], ygz_, AF.Square, sem='ss')
        c.wait('tensor', 'vv', b3)
        c.wait('tensor', 'ss', egev_t)
        for j in range(2):
            for ck in range(16):
                wmm = mm(c, P[j][:, :], ygz_[:, ck * 128:(ck + 1) * 128],
                         A('ws')[:, ck * 1024 + j * 512:ck * 1024 + j * 512 + 512],
                         start=(ck == 0), stop=(ck == 15), sem='mm')
        c.wait('tensor', 'ss', sq)
        for ck in range(16):
            sqmm = mm(c, P[6][0:1, 0:128], A('onesb')[:, 0:1],
                      A('sqb')[:, pp * 2048 + ck * 128:pp * 2048 + (ck + 1) * 128],
                      start=(ck == 0), stop=(ck == 15), sem='mm')
        sqmmp[pp] = sqmm
        c.wait('vector', 'mm', sqmm)
        mr = vcopy(c, A('mrow')[0:1, pp * 128:(pp + 1) * 128], P[6][0:1, 0:128], sem='vv')
        m2 = ts(c, A('mrow')[0:1, pp * 128:(pp + 1) * 128], A('mrow')[0:1, pp * 128:(pp + 1) * 128],
                1.0 / DIN, ALU.mult, EPS, ALU.add, sem='vv')
        c.wait('vector', 'mm', wmm)
        for j in range(2):
            wev = vcopy(c, A('xev')[:, pp * 1024 + j * 512:pp * 1024 + j * 512 + 512], P[j][:, :], sem='vv')
        c.wait('scalar', 'vv', m2)
        rc = sinv(c, A('mrow').ap()[0:1, pp * 128:(pp + 1) * 128],
                  A('mrow').ap()[0:1, pp * 128:(pp + 1) * 128], -0.5, sem='ss')
        if t + 1 < NTT:
            EG[t + 1] = front(t + 1, wev)
        c.wait('tensor', 'ss', rc)
        r3t = mm(c, P[6][:, 256:257], A('mrow')[0:1, pp * 128:(pp + 1) * 128], A('onesf')[0:1, 0:1],
                 start=True, stop=True, sem='mm')
        c.wait('vector', 'mm', r3t)
        r3v = vcopy(c, A('r3').ap()[:, t:t + 1], P[6][:, 256:257], sem='vv')
        if t + 1 < NTT:
            YC[t + 1] = ycorr(t + 1, r3v)
        c.wait('vector', 'xk4', RD[t][3])
        c.wait('vector', 'rr', RD[t][4])
        c.wait('vector', 'xm', spillp[pp])
        for j in range(2):
            stt(c, A('xdl')[:, pp * 1024 + j * 512:pp * 1024 + j * 512 + 512],
                A('xev').ap()[:, pp * 1024 + j * 512:pp * 1024 + j * 512 + 512],
                A('r3').ap()[:, t:t + 1],
                A('rst').ap()[:, pp * 1024 + j * 512:pp * 1024 + j * 512 + 512], ALU.mult, ALU.add)
        o2 = tt(c, A('xdl').ap()[:, pp * 1024:(pp + 1) * 1024], A('xmt').ap()[:, pp * 1024:(pp + 1) * 1024],
                A('xdl').ap()[:, pp * 1024:(pp + 1) * 1024], ALU.add, sem='vv')
        o2p[pp] = o2
        c.wait('gpsimd', 'vv', o2)
        spillp[pp] = c.dma('gpsimd', Dr('xmid_d')[tsl, :], A('xdl').ap()[:, pp * 1024:(pp + 1) * 1024], 'xm')
        del RD[t]


def _p5(c, P, A, Dr, ph):
    ph.sb('xmt2', (128, 1024), F32)
    ph.sb('h2f', (128, 1024), F32)
    ph.sb('h2t', (128, 1024), BF16)
    ph.sb('w1s', (128, 8 * 512), BF16)
    ph.sb('w1Ls', (128, 8 * 512), BF16)
    ph.sb('w2s', (128, 4 * 64), BF16)
    ph.sb('sib', (64, 1024), BF16)
    ph.sb('o64', (64, 1), BF16)
    ph.sb('meanT', (128, 8), BF16)
    ph.sb('b1e', (128, 4), F32)
    ph.sb('b2b', (128, 64), F32)
    ph.sb('itc', (128, 1), F32)
    ph.sb('taus', (1, 1), F32)
    ph.sb('b2r', (1, 64), F32)
    ph.sb('hro', (128, 4 * 512), BF16)
    ph.sb('h2T', (128, 8 * 512), BF16)
    ph.sb('rlog', (128, 64), F32)
    ph.sb('m8', (128, 8), F32)
    ph.sb('msk', (128, 64), F32)
    ph.sb('erow', (128, 64), F32)
    ph.sb('alf', (128, 64), F32)
    ph.sb('identS', (128, 128), F32)
    ph.sb('h2r', (128, 1024), BF16)
    ph.sb('slin', (64, 1040), F32)

    for k in range(8):
        c.dma('gpsimd', A('w1s')[:, k * 512:(k + 1) * 512], Dr('w1')[k * 128:(k + 1) * 128, :], 'w1l')
        c.dma('gpsimd', A('w1Ls')[:, k * 512:(k + 1) * 512], Dr('w1')[1024 + k * 128:1024 + (k + 1) * 128, :], 'w1l')
    for k in range(4):
        c.dma('gpsimd', A('w2s')[:, k * 64:(k + 1) * 64], Dr('w2')[k * 128:(k + 1) * 128, :], 'w1l')
    c.dma('gpsimd', A('sib')[:, :], Dr('slot_init')[:, :], 'w1l')
    v_w1 = c.cnt['w1l']
    c.dma('sync', A('taus')[:, :], Dr('tau')[:, :], 'c5')
    c.dma('sync', A('b2r')[:, :], Dr('b2_row')[:, :], 'c5')
    c.dma('sync', A('identS')[:, :], Dr('identf')[:, :], 'c5')
    v_c5 = c.dma('sync', A('b1e')[:, 0:4], Dr('b1_col')[:, :], 'c5')
    v_o64 = vmemset(c, A('o64')[:, :], 1.0 / 64, sem='vv')
    c.wait('scalar', 'c5', v_c5)
    v_at = act(c, A('taus')[:, :], A('taus')[:, :], AF.Abs, sem='ss')
    c.wait('vector', 'ss', v_at)
    v_ta = ts(c, A('taus')[:, :], A('taus')[:, :], 0.1, ALU.add, sem='vv')
    c.wait('scalar', 'vv', v_ta)
    v_it = sinv(c, A('taus').ap()[:, :], A('taus').ap()[:, :], -1.0, sem='ss')
    c.wait('tensor', 'ss', v_it)
    v_itmm = mm(c, P[4][:, 300:301], A('onesf')[0:1, :], A('taus')[0:1, 0:1], start=True, stop=True, sem='mm')
    c.wait('vector', 'mm', v_itmm)
    v_itc = vcopy(c, A('itc')[:, :], P[4][:, 300:301], sem='vv')
    c.wait('tensor', 'vv', v_itc)
    v_b2mm = mm(c, P[4][:, 320:384], A('onesf')[0:1, :], A('b2r')[0:1, :], start=True, stop=True, sem='mm')
    c.wait('vector', 'mm', v_b2mm)
    vcopy(c, A('b2b')[:, :], P[4][:, 320:384])
    v_b2b = ts(c, A('b2b')[:, :], A('b2b')[:, :], A('itc').ap()[:, 0:1], ALU.mult, sem='vv')
    c.wait('tensor', 'w1l', v_w1)
    c.wait('tensor', 'vv', max(v_b2b, v_o64))
    for k in range(8):
        v_mmn = mm(c, P[4][:, 400 + k:401 + k], A('sib')[:, k * 128:(k + 1) * 128], A('o64')[:, 0:1],
                   start=True, stop=True, sem='mm')
    c.wait('vector', 'mm', v_mmn)
    v_mt = vcopy(c, A('meanT')[:, :], P[4][:, 400:408], sem='vv')
    c.wait('tensor', 'vv', v_mt)
    for m in range(4):
        for k in range(8):
            v_b1mm = mm(c, P[5][:, 400 + m:401 + m], A('w1Ls')[:, k * 512 + m * 128:k * 512 + (m + 1) * 128],
                        A('meanT')[:, k:k + 1], start=(k == 0), stop=(k == 7), sem='mm')
    c.wait('vector', 'mm', v_b1mm)
    c.wait('vector', 'c5', v_c5)
    v_b1e = stt(c, A('b1e')[:, 0:4], P[5][:, 400:404], 1.0, A('b1e').ap()[:, 0:4], ALU.mult, ALU.add, sem='vv')

    v_h2sp = [0, 0]
    v_gel = 0; v_slmm = 0; v_h2rd = 0; v_alt = 0; v_lnt = 0
    gel_ev = {}; gel_used = {}
    for s in range(NSLAB):
        for tl in range(4):
            t = s * 4 + tl
            tsl = slice(t * 128, (t + 1) * 128)
            hb = t % 2
            c.wait('sync', 'vv', v_lnt)
            v_xm = c.dma('sync', A('xmt2')[:, :], Dr('xmid_d')[tsl, :], 'xm2')
            c.wait('scalar', 'xm2', v_xm)
            act(c, A('xmt2')[:, :], A('xmt2')[:, :], AF.Copy, accum_out=A('stat_sum').ap()[:, t:t + 1])
            v_st = act(c, A('h2f')[:, :], A('xmt2')[:, :], AF.Square, accum_out=A('stat_sq').ap()[:, t:t + 1], sem='ss')
            c.wait('vector', 'ss', v_st)
            ts(c, A('tiny')[:, 40:41], A('stat_sum').ap()[:, t:t + 1], 1.0 / D, ALU.mult)
            ts(c, A('tiny')[:, 41:42], A('stat_sq').ap()[:, t:t + 1], 1.0 / D, ALU.mult)
            tt(c, A('tiny')[:, 42:43], A('tiny')[:, 40:41], A('tiny')[:, 40:41], ALU.mult)
            tt(c, A('tiny')[:, 43:44], A('tiny')[:, 41:42], A('tiny')[:, 42:43], ALU.subtract)
            v_var = ts(c, A('tiny')[:, 43:44], A('tiny')[:, 43:44], EPS, ALU.add, sem='vv')
            c.wait('scalar', 'vv', v_var)
            v_sq = sinv(c, A('tiny').ap()[:, 45:46], A('tiny').ap()[:, 43:44], -0.5, sem='ss')
            c.wait('vector', 'ss', v_sq)
            stt(c, A('tiny')[:, 46:47], A('tiny').ap()[:, 40:41], -1.0, A('tiny').ap()[:, 45:46],
                ALU.mult, ALU.mult)
            ts(c, A('h2f')[:, :], A('xmt2')[:, :], A('tiny').ap()[:, 45:46], ALU.mult,
               A('tiny').ap()[:, 46:47], ALU.add)
            tt(c, A('h2f')[:, :], A('h2f')[:, :], A('cb1b').ap(), ALU.mult)
            v_lnt = tt(c, A('h2t')[:, :], A('h2f')[:, :], A('sbb').ap(), ALU.add, sem='vv')
            c.wait('gpsimd', 'vv', v_lnt)
            v_h2sp[hb] = c.dma('gpsimd', Dr('h2_d')[tsl, :], A('h2t')[:, :], 'h2s%d' % hb)
        c.wait('sync', 'h2s0', v_h2sp[0])
        c.wait('sync', 'h2s1', v_h2sp[1])
        c.wait('sync', 'mm', v_gel)
        vq = 0
        for k in range(8):
            vq = c.dmat('sync', A('h2T')[:, k * SLAB:(k + 1) * SLAB],
                        Dr('h2_d')[s * SLAB:(s + 1) * SLAB, k * 128:(k + 1) * 128], 'h2t2')
        c.wait('tensor', 'h2t2', vq)
        if s == 0:
            c.wait('tensor', 'vv', v_b1e)
            c.wait('tensor', 'c5', v_c5)
        for m in range(4):
            bk = m % 4
            c.wait('tensor', 'ss', gel_ev.get(bk, 0))
            for k in range(8):
                v_w1mm = mm(c, P[bk][:, :], A('w1s')[:, k * 512 + m * 128:k * 512 + (m + 1) * 128],
                            A('h2T')[:, k * SLAB:(k + 1) * SLAB], start=(k == 0), stop=(k == 7), sem='mm')
            c.wait('scalar', 'mm', v_w1mm)
            v_gl = act(c, A('hro')[:, m * 512:(m + 1) * 512], P[bk][:, :], AF.Gelu,
                       bias=A('b1e').ap()[:, m:m + 1], sem='ss')
            gel_ev[bk] = v_gl
        v_gel = v_w1mm
        for tl in range(4):
            t = s * 4 + tl
            tsl = slice(t * 128, (t + 1) * 128)
            c.wait('tensor', 'ss', v_gl)
            c.wait('tensor', 'vv', v_alt)
            for m in range(4):
                v_w2mm = mm(c, P[4][:, 0:64], A('hro')[:, m * 512 + tl * 128:m * 512 + (tl + 1) * 128],
                            A('w2s')[:, m * 64:(m + 1) * 64], start=(m == 0), stop=(m == 3), sem='mm')

            c.wait('vector', 'mm', v_w2mm)
            c.wait('vector', 'vv', v_b2b)
            v_rl = stt(c, A('rlog')[:, :], P[4][:, 0:64], A('itc').ap()[:, 0:1], A('b2b').ap()[:, :],
                       ALU.mult, ALU.add, sem='vv')
            vmax8(c, A('m8')[:, :], A('rlog')[:, :])
            ts(c, A('tiny')[:, 48:49], A('m8').ap()[:, 0:1], -1.0, ALU.mult)
            v_msk = ts(c, A('msk')[:, :], A('rlog')[:, :], A('m8').ap()[:, 7:8], ALU.is_ge, sem='vv')
            c.wait('scalar', 'vv', v_msk)
            v_er = act(c, A('erow')[:, :], A('rlog')[:, :], AF.Exp, bias=A('tiny').ap()[:, 48:49], sem='ss')
            c.wait('vector', 'ss', v_er)
            tt(c, A('erow')[:, :], A('erow')[:, :], A('msk')[:, :], ALU.mult)
            v_sumr = vreduce(c, A('tiny').ap()[:, 49:50], A('erow').ap()[:, :], ALU.add, sem='vv')
            c.wait('scalar', 'vv', v_sumr)
            v_rs = sinv(c, A('tiny').ap()[:, 50:51], A('tiny').ap()[:, 49:50], -1.0, sem='ss')
            c.wait('vector', 'ss', v_rs)
            ts(c, A('alf')[:, :], A('erow')[:, :], A('tiny').ap()[:, 50:51], ALU.mult)
            v_al = vcopy(c, A('alpha').ap()[:, t * 64:(t + 1) * 64], A('alf')[:, :], sem='vv')
            c.wait('tensor', 'vv', v_al)
            v_atr = c.op('tensor', lambda e, o=P[4][0:64, 64:192], i=A('alf').ap()[:, :],
                         idt=A('identS')[:, :]: e.transpose(o, i, idt), sem='mm')
            c.wait('vector', 'mm', v_atr)
            v_alt = vcopy(c, A('alphaT').ap()[:, tsl], P[4][0:64, 64:192], sem='vv')
            c.wait('sync', 'mm', v_slmm)
            v_h2rd = c.dma('sync', A('h2r')[:, :], Dr('h2_d')[tsl, :], 'h2r')
            c.wait('tensor', 'h2r', v_h2rd)
            mm(c, P[5][0:64, :], A('alpha').ap()[:, t * 64:(t + 1) * 64], A('h2r')[:, 0:512],
               start=(t == 0), stop=(t == NTT - 1), sem=None)
            mm(c, P[6][0:64, :], A('alpha').ap()[:, t * 64:(t + 1) * 64], A('h2r')[:, 512:1024],
               start=(t == 0), stop=(t == NTT - 1), sem=None)
            v_slmm = mm(c, P[7][0:64, 0:1], A('alpha').ap()[:, t * 64:(t + 1) * 64], A('onesb')[:, 0:1],
                        start=(t == 0), stop=(t == NTT - 1), sem='mm')
    c.wait('vector', 'mm', v_slmm)
    vcopy(c, A('slin')[:, 0:512], P[5][0:64, :])
    vcopy(c, A('slin')[:, 512:1024], P[6][0:64, :])
    v_sl = vcopy(c, A('slin')[:, 1024:1025], P[7][0:64, 0:1], sem='vv')
    c.wait('gpsimd', 'vv', v_sl)
    v_slw = c.dma('gpsimd', Dr('sl_in')[:, :], A('slin')[:, 0:1025], 'slw')
    c.wait('gpsimd', 'slw', v_slw)
    v_cc2 = c.op('gpsimd', lambda e: e.collective_compute(
        "AllReduce", ALU.add, replica_groups=GROUPS,
        ins=[c.dr['sl_in'].opt()], outs=[c.dr['sl_out'].opt()]), sem='cc')
    c.wait('gpsimd', 'cc', v_cc2)


def _p6(c, P, A, Dr, ph):
    ph.sb('slot', (64, 1040), F32)
    ph.sb('sif', (64, 1024), F32)
    ph.sb('sibb', (64, 1024), BF16)
    ph.sb('sitT', (128, 8 * 64), BF16)
    ph.sb('siT', (128, 8 * 64), BF16)
    ph.sb('wres', (128, 8 * 3072), BF16)
    ph.sb('wqs', (128, 8 * 256), BF16)
    ph.sb('wks', (128, 8 * 256), BF16)
    ph.sb('gi', (64, 3072), F32)
    ph.sb('gh', (64, 3072), F32)
    ph.sb('gt', (64, 1024), F32)
    ph.sb('snew', (64, 1024), F32)
    ph.sb('snb', (64, 1024), BF16)
    ph.sb('snT', (128, 8 * 64), BF16)
    ph.sb('qf', (128, 2 * 64), BF16)
    ph.sb('kf', (128, 2 * 64), BF16)
    ph.sb('vtok', (64, 1024), BF16)
    ph.sb('att', (64, 64), F32)
    ph.sb('attT', (64, 64), BF16)
    ph.sb('sc2', (64, 1024), F32)
    ph.sb('yv', (64, 1024), F32)
    ph.sb('ylnT', (128, 8 * 64), BF16)
    ph.sb('svf', (128, 8 * 64), BF16)
    ph.sb('svw', (64, 1024), BF16)
    ph.sb('gb64', (64, 1024), F32)
    ph.sb('bo64', (64, 1024), F32)
    ph.sb('bihb', (1, 3072), BF16)
    ph.sb('bhhb', (1, 3072), BF16)
    ph.sb('bvb', (1, 1024), BF16)
    ph.sb('o64r', (1, 64), BF16)
    ph.sb('lnbb', (128, 8), BF16)
    ph.sb('bcol2', (128, 8), F32)
    ph.sb('bvpc', (128, 8), F32)
    ph.sb('lngc', (128, 8), F32)
    ph.sb('bqc', (128, 2), F32)
    ph.sb('bkc', (128, 2), F32)
    ph.sb('bias2', (128, 8), F32)
    ph.sb('xmt3', (128, 1024), F32)
    ph.sb('yout', (128, 1024), F32)
    ph.sb('t64', (64, 16), F32)
    ph.sb('borow', (1, 1024), F32)
    ph.sb('identS2', (128, 128), F32)

    c.dma('sync', A('slot')[:, 0:1025], Dr('sl_out')[:, :], 'c6')
    c.dma('sync', A('sif')[:, :], Dr('slot_init')[:, :], 'c6')
    c.dma('sync', A('bcol2')[:, 0:8], Dr('lnb_col')[:, :], 'c6')
    c.dma('sync', A('bvpc')[:, 0:8], Dr('bvp_col')[:, :], 'c6')
    c.dma('sync', A('lngc')[:, 0:8], Dr('lng_col')[:, :], 'c6')
    c.dma('sync', A('bqc')[:, 0:2], Dr('bq_col')[:, :], 'c6')
    c.dma('sync', A('bkc')[:, 0:2], Dr('bk_col')[:, :], 'c6')
    c.dma('sync', A('identS2')[:, :], Dr('identf')[:, :], 'c6')
    c.dma('sync', A('borow')[:, :], Dr('bo_row')[:, :], 'c6')
    for k in range(8):
        c.dma('gpsimd', A('sitT')[:, k * 64:(k + 1) * 64], Dr('slot_initT')[k * 128:(k + 1) * 128, :], 'c6g')
    for k in range(8):
        c.dma('gpsimd', A('wqs')[:, k * 256:(k + 1) * 256], Dr('wqf')[k * 128:(k + 1) * 128, :], 'c6g')
        c.dma('gpsimd', A('wks')[:, k * 256:(k + 1) * 256], Dr('wkf')[k * 128:(k + 1) * 128, :], 'c6g')
    for k in range(8):
        c.dma('gpsimd', A('wres')[:, k * 3072:(k + 1) * 3072], Dr('wihT')[k * 128:(k + 1) * 128, :], 'wih')
    v_wih = c.cnt['wih']
    v_o64r = vmemset(c, A('o64r')[:, :], 1.0, sem='vv')

    c.dma('gpsimd', A('bihb')[:, :], Dr('bih_row')[:, :], 'c6g')
    c.dma('gpsimd', A('bhhb')[:, :], Dr('bhh_row')[:, :], 'c6g')
    v_c6g = c.dma('gpsimd', A('bvb')[:, :], Dr('bv_row')[:, :], 'c6g')
    v_bihb = v_c6g; v_bhhb = v_c6g; v_bvb = 0
    c.wait('vector', 'c6', c.cnt['c6'])
    v_lnbb = vcopy(c, A('lnbb')[:, :], A('bcol2')[:, 0:8], sem='vv')

    v_dn0 = ts(c, A('slot')[:, 1024:1025], A('slot')[:, 1024:1025], 1e-8, ALU.add, sem='vv')
    c.wait('scalar', 'vv', v_dn0)
    v_dn1 = sinv(c, A('slot').ap()[:, 1025:1026], A('slot').ap()[:, 1024:1025], -1.0, sem='ss')
    c.wait('vector', 'ss', v_dn1)
    ts(c, A('slot')[:, 0:1024], A('slot')[:, 0:1024], A('slot').ap()[:, 1025:1026], ALU.mult)
    v_sib = vcopy(c, A('sibb')[:, :], A('slot')[:, 0:1024], sem='vv')
    c.wait('tensor', 'vv', v_sib)
    v_tr = 0
    for k in range(8):
        v_trm = c.op('tensor', lambda e, o=P[4][:, 0:64], i=A('slot').ap()[:, k * 128:(k + 1) * 128],
                     idt=A('identS2')[0:64, 0:64]: e.transpose(o, i, idt), sem='mm')
        c.wait('vector', 'mm', v_trm)
        v_tr = vcopy(c, A('siT')[:, k * 64:(k + 1) * 64], P[4][:, 0:64], sem='vv')
        c.wait('tensor', 'vv', v_tr)
    c.wait('tensor', 'wih', v_wih)
    c.wait('tensor', 'c6g', v_bihb)
    c.wait('tensor', 'vv', v_o64r)
    v_giev = 0
    for n in range(6):
        bk = n % 2
        for k in range(8):
            mm(c, P[bk][0:64, :], A('siT')[:, k * 64:(k + 1) * 64],
               A('wres')[:, k * 3072 + n * 512:k * 3072 + (n + 1) * 512],
               start=(k == 0), stop=False, sem=None)
        v_gimm = mm(c, P[bk][0:64, :], A('o64r')[0:1, 0:64],
                    A('bihb')[0:1, n * 512:(n + 1) * 512], start=False, stop=True, sem='mm')
        c.wait('vector', 'mm', v_gimm)
        v_giev = vcopy(c, A('gi')[:, n * 512:(n + 1) * 512], P[bk][0:64, :], sem='vv')
        c.wait('tensor', 'vv', v_giev)
    c.wait('gpsimd', 'mm', v_gimm)
    for k in range(8):
        c.dma('gpsimd', A('wres')[:, k * 3072:(k + 1) * 3072], Dr('whhT')[k * 128:(k + 1) * 128, :], 'wih')
    v_whh = c.cnt['wih']
    c.wait('tensor', 'wih', v_whh)
    c.wait('tensor', 'c6g', v_c6g)
    for n in range(6):
        bk = n % 2
        for k in range(8):
            mm(c, P[bk][0:64, :], A('sitT')[:, k * 64:(k + 1) * 64],
               A('wres')[:, k * 3072 + n * 512:k * 3072 + (n + 1) * 512],
               start=(k == 0), stop=False, sem=None)
        v_ghmm = mm(c, P[bk][0:64, :], A('o64r')[0:1, 0:64],
                    A('bhhb')[0:1, n * 512:(n + 1) * 512], start=False, stop=True, sem='mm')
        c.wait('vector', 'mm', v_ghmm)
        v_ghev = vcopy(c, A('gh')[:, n * 512:(n + 1) * 512], P[bk][0:64, :], sem='vv')
        c.wait('tensor', 'vv', v_ghev)
    c.wait('gpsimd', 'mm', v_ghmm)
    for k in range(8):
        c.dma('gpsimd', A('wres')[:, k * 1024:(k + 1) * 1024], Dr('wvp')[k * 128:(k + 1) * 128, :], 'wih')
        c.dma('gpsimd', A('wres')[:, 8192 + k * 1024:8192 + (k + 1) * 1024], Dr('wvf')[k * 128:(k + 1) * 128, :], 'wih')
        c.dma('gpsimd', A('wres')[:, 16384 + k * 1024:16384 + (k + 1) * 1024], Dr('wof')[k * 128:(k + 1) * 128, :], 'wih')
    v_wv3 = c.cnt['wih']
    tt(c, A('gt')[:, :], A('gi')[:, 0:1024], A('gh')[:, 0:1024], ALU.add)
    v_g1 = c.op('vector', lambda e: e.nop(), sem='vv')
    c.wait('scalar', 'vv', v_g1)
    v_r = act(c, A('gh')[:, 0:1024], A('gt')[:, :], AF.Sigmoid, sem='ss')
    tt(c, A('gt')[:, :], A('gi')[:, 1024:2048], A('gh')[:, 1024:2048], ALU.add)
    v_g2 = c.op('vector', lambda e: e.nop(), sem='vv')
    c.wait('scalar', 'vv', v_g2)
    v_z = act(c, A('gh')[:, 1024:2048], A('gt')[:, :], AF.Sigmoid, sem='ss')
    c.wait('vector', 'ss', v_z)
    tt(c, A('gt')[:, :], A('gh')[:, 0:1024], A('gh')[:, 2048:3072], ALU.mult)
    v_g3 = tt(c, A('gt')[:, :], A('gt')[:, :], A('gi')[:, 2048:3072], ALU.add, sem='vv')
    c.wait('scalar', 'vv', v_g3)
    v_n = act(c, A('gt')[:, :], A('gt')[:, :], AF.Tanh, sem='ss')
    c.wait('vector', 'ss', max(v_n, v_z))
    tt(c, A('snew')[:, :], A('sif')[:, :], A('gt')[:, :], ALU.subtract)
    tt(c, A('snew')[:, :], A('gh')[:, 1024:2048], A('snew')[:, :], ALU.mult)
    tt(c, A('snew')[:, :], A('gt')[:, :], A('snew')[:, :], ALU.add)
    v_snb = vcopy(c, A('snb')[:, :], A('snew')[:, :], sem='vv')
    c.wait('tensor', 'vv', v_snb)
    for k in range(8):
        v_trm = c.op('tensor', lambda e, o=P[4][:, 0:64], i=A('snew').ap()[:, k * 128:(k + 1) * 128],
                     idt=A('identS2')[0:64, 0:64]: e.transpose(o, i, idt), sem='mm')
        c.wait('vector', 'mm', v_trm)
        v_tr = vcopy(c, A('snT')[:, k * 64:(k + 1) * 64], P[4][:, 0:64], sem='vv')
        c.wait('tensor', 'vv', v_tr)
    c.wait('scalar', 'c6', c.cnt['c6'])
    v_qf = 0; v_kf = 0
    for m in range(2):
        c.wait('tensor', 'vv', max(v_qf, v_kf))
        for k in range(8):
            v_qmm = mm(c, P[0][:, 0:64], A('wqs')[:, k * 256 + m * 128:k * 256 + (m + 1) * 128],
                       A('snT')[:, k * 64:(k + 1) * 64], start=(k == 0), stop=(k == 7), sem='mm')
        c.wait('vector', 'mm', v_qmm)
        v_qf = ts(c, A('qf')[:, m * 64:(m + 1) * 64], P[0][:, 0:64],
                  A('bqc').ap()[:, m:m + 1], ALU.add, sem='vv')
        for k in range(8):
            v_kmm = mm(c, P[1][:, 0:64], A('wks')[:, k * 256 + m * 128:k * 256 + (m + 1) * 128],
                       A('snT')[:, k * 64:(k + 1) * 64], start=(k == 0), stop=(k == 7), sem='mm')
        c.wait('vector', 'mm', v_kmm)
        v_kf = ts(c, A('kf')[:, m * 64:(m + 1) * 64], P[1][:, 0:64],
                  A('bkc').ap()[:, m:m + 1], ALU.add, sem='vv')
    c.wait('tensor', 'wih', v_wv3)
    for n in range(2):
        for k in range(8):
            mm(c, P[2][0:64, :], A('snT')[:, k * 64:(k + 1) * 64],
               A('wres')[:, 8192 + k * 1024 + n * 512:8192 + k * 1024 + (n + 1) * 512],
               start=(k == 0), stop=False, sem=None)
        v_vmm = mm(c, P[2][0:64, :], A('o64r')[0:1, 0:64], A('bvb')[0:1, n * 512:(n + 1) * 512],
                   start=False, stop=True, sem='mm')
        c.wait('vector', 'mm', v_vmm)
        v_vt = vcopy(c, A('vtok')[:, n * 512:(n + 1) * 512], P[2][0:64, :], sem='vv')
        c.wait('tensor', 'vv', v_vt)
    c.wait('tensor', 'vv', max(v_qf, v_kf))
    for k in range(2):
        v_amm = mm(c, P[3][0:64, 0:64], A('qf')[:, k * 64:(k + 1) * 64], A('kf')[:, k * 64:(k + 1) * 64],
                   start=(k == 0), stop=(k == 1), sem='mm')
    c.wait('vector', 'mm', v_amm)
    ts(c, A('att')[:, :], P[3][0:64, 0:64], DQ ** (-0.5), ALU.mult)
    vreduce(c, A('t64').ap()[:, 0:1], A('att').ap()[:, :], ALU.max)
    v_ng = ts(c, A('t64')[:, 0:1], A('t64')[:, 0:1], -1.0, ALU.mult, sem='vv')
    c.wait('scalar', 'vv', v_ng)
    v_ex = act(c, A('att')[:, :], A('att')[:, :], AF.Exp, bias=A('t64').ap()[:, 0:1], sem='ss')
    c.wait('vector', 'ss', v_ex)
    v_sm = vreduce(c, A('t64').ap()[:, 1:2], A('att').ap()[:, :], ALU.add, sem='vv')
    c.wait('scalar', 'vv', v_sm)
    v_rq = sinv(c, A('t64').ap()[:, 2:3], A('t64').ap()[:, 1:2], -1.0, sem='ss')
    c.wait('vector', 'ss', v_rq)
    v_ab = ts(c, A('att')[:, :], A('att')[:, :], A('t64').ap()[:, 2:3], ALU.mult, sem='vv')
    c.wait('tensor', 'vv', v_ab)
    v_att = c.op('tensor', lambda e, o=P[3][0:64, 64:128], i=A('att')[:, :],
                 idt=A('identS2')[0:64, 0:64]: e.transpose(o, i, idt), sem='mm')
    c.wait('vector', 'mm', v_att)
    v_atT = vcopy(c, A('attT')[:, :], P[3][0:64, 64:128], sem='vv')
    c.wait('tensor', 'vv', v_atT)
    v_scv = 0
    for n in range(2):
        c.wait('tensor', 'vv', v_scv)
        v_scm = mm(c, P[5][0:64, :], A('attT')[:, :], A('vtok')[:, n * 512:(n + 1) * 512],
                   start=True, stop=True, sem='mm')
        c.wait('vector', 'mm', v_scm)
        v_scv = vcopy(c, A('sc2')[:, n * 512:(n + 1) * 512], P[5][0:64, :], sem='vv')
    v_y = tt(c, A('yv')[:, :], A('snew')[:, :], A('sc2')[:, :], ALU.add, sem='vv')
    c.wait('scalar', 'vv', v_y)
    act(c, A('yv')[:, :], A('yv')[:, :], AF.Copy, accum_out=A('t64').ap()[:, 3:4])
    v_sacc = act(c, A('gt')[:, :], A('yv')[:, :], AF.Square, accum_out=A('t64').ap()[:, 4:5], sem='ss')
    c.wait('vector', 'ss', v_sacc)
    ts(c, A('t64')[:, 5:6], A('t64').ap()[:, 3:4], 1.0 / D, ALU.mult)
    ts(c, A('t64')[:, 6:7], A('t64').ap()[:, 4:5], 1.0 / D, ALU.mult)
    tt(c, A('t64')[:, 7:8], A('t64')[:, 5:6], A('t64')[:, 5:6], ALU.mult)
    tt(c, A('t64')[:, 7:8], A('t64')[:, 6:7], A('t64')[:, 7:8], ALU.subtract)
    v_v6 = ts(c, A('t64')[:, 7:8], A('t64')[:, 7:8], EPS, ALU.add, sem='vv')
    c.wait('scalar', 'vv', v_v6)
    v_s6 = sinv(c, A('t64').ap()[:, 10:11], A('t64').ap()[:, 7:8], -0.5, sem='ss')
    c.wait('vector', 'ss', v_s6)
    stt(c, A('t64')[:, 11:12], A('t64').ap()[:, 5:6], -1.0, A('t64').ap()[:, 10:11], ALU.mult, ALU.mult)
    v_yln = ts(c, A('sc2')[:, :], A('yv')[:, :], A('t64').ap()[:, 11:12], ALU.add,
               A('t64').ap()[:, 10:11], ALU.mult, sem='vv')
    c.wait('vector', 'wih', v_wv3)
    c.wait('vector', 'c6', c.cnt['c6'])
    for k in range(8):
        v_wvf = ts(c, A('wres')[:, k * 1024:(k + 1) * 1024], A('wres')[:, k * 1024:(k + 1) * 1024],
                   A('lngc').ap()[:, k:k + 1], ALU.mult, sem='vv')
    c.wait('tensor', 'vv', max(v_wvf, v_lnbb, v_yln))
    for m in range(8):
        for k in range(8):
            v_b2mm = mm(c, P[6][:, m:m + 1], A('wres')[:, k * 1024 + m * 128:k * 1024 + (m + 1) * 128],
                        A('lnbb')[:, k:k + 1], start=(k == 0), stop=(k == 7), sem='mm')
    c.wait('vector', 'mm', v_b2mm)
    vcopy(c, A('bias2')[:, 0:8], P[6][:, 0:8])
    v_bi2 = tt(c, A('bias2')[:, 0:8], A('bias2')[:, 0:8], A('bvpc')[:, 0:8], ALU.add, sem='vv')
    c.wait('tensor', 'vv', v_yln)
    for k in range(8):
        v_trm = c.op('tensor', lambda e, o=P[4][:, 0:64], i=A('sc2').ap()[:, k * 128:(k + 1) * 128],
                     idt=A('identS2')[0:64, 0:64]: e.transpose(o, i, idt), sem='mm')
        c.wait('vector', 'mm', v_trm)
        v_tr = vcopy(c, A('ylnT')[:, k * 64:(k + 1) * 64], P[4][:, 0:64], sem='vv')
        c.wait('tensor', 'vv', v_tr)
    c.wait('scalar', 'vv', v_bi2)
    v_sv = 0
    for m in range(8):
        c.wait('tensor', 'vv', v_sv)
        for k in range(8):
            v_svm = mm(c, P[7][:, 0:64], A('wres')[:, k * 1024 + m * 128:k * 1024 + (m + 1) * 128],
                       A('ylnT')[:, k * 64:(k + 1) * 64], start=(k == 0), stop=(k == 7), sem='mm')
        c.wait('vector', 'mm', v_svm)
        v_sv = ts(c, A('svf')[:, m * 64:(m + 1) * 64], P[7][:, 0:64],
                  A('bias2').ap()[:, m:m + 1], ALU.add, sem='vv')
    c.wait('tensor', 'gr0', c.cnt.get('gr0', 0))
    c.wait('tensor', 'vv', v_bi2)
    v_bom = mm(c, P[6][0:64, 0:512], A('onesf')[0:1, 0:64], A('borow')[0:1, 0:512], start=True, stop=True, sem='mm')
    v_bom2 = mm(c, P[6][0:64, 0:512] if False else P[5][0:64, 0:512], A('onesf')[0:1, 0:64], A('borow')[0:1, 512:1024], start=True, stop=True, sem='mm')
    c.wait('vector', 'mm', v_bom2)
    vcopy(c, A('bo64')[:, 0:512], P[6][0:64, 0:512])
    v_bo = vcopy(c, A('bo64')[:, 512:1024], P[5][0:64, 0:512], sem='vv')
    c.wait('tensor', 'vv', v_bo)
    v_gbm = mm(c, P[6][0:64, 0:512], A('onesf')[0:1, 0:64], A('garow')[0:1, 1024:1536], start=True, stop=True, sem='mm')
    v_gbm2 = mm(c, P[5][0:64, 0:512], A('onesf')[0:1, 0:64], A('garow')[0:1, 1536:2048], start=True, stop=True, sem='mm')
    c.wait('vector', 'mm', v_gbm2)
    vcopy(c, A('gb64')[:, 0:512], P[6][0:64, 0:512])
    v_gb = vcopy(c, A('gb64')[:, 512:1024], P[5][0:64, 0:512], sem='vv')
    c.wait('tensor', 'vv', max(v_sv, v_gb))
    for n in range(2):
        for k in range(8):
            v_svwm = mm(c, P[3][0:64, :], A('svf')[:, k * 64:(k + 1) * 64],
                        A('wres')[:, 16384 + k * 1024 + n * 512:16384 + k * 1024 + (n + 1) * 512],
                        start=(k == 0), stop=(k == 7), sem='mm')
        c.wait('vector', 'mm', v_svwm)
        stt(c, A('sc2')[:, n * 512:(n + 1) * 512], P[3][0:64, :], 1.0,
            A('bo64').ap()[:, n * 512:(n + 1) * 512], ALU.mult, ALU.add)
        v_svw = tt(c, A('svw')[:, n * 512:(n + 1) * 512], A('sc2')[:, n * 512:(n + 1) * 512],
                   A('gb64')[:, n * 512:(n + 1) * 512], ALU.mult, sem='vv')
        c.wait('tensor', 'vv', v_svw)
    v_om = 0; v_ost = 0
    for t in range(NTT):
        tsl = slice(t * 128, (t + 1) * 128)
        c.wait('sync', 'vv', v_ost)
        v_xm3 = c.dma('sync', A('xmt3')[:, :], Dr('xmid_d')[tsl, :], 'xm3')
        c.wait('tensor', 'vv', v_ost)
        for j in range(2):
            v_om = mm(c, P[j][:, :], A('alphaT').ap()[:, tsl], A('svw')[:, j * 512:(j + 1) * 512],
                      start=True, stop=True, sem='mm')
        c.wait('vector', 'mm', v_om)
        c.wait('vector', 'xm3', v_xm3)
        for j in range(2):
            v_ost = stt(c, A('yout')[:, j * 512:(j + 1) * 512], P[j][:, :], 1.0,
                        A('xmt3').ap()[:, j * 512:(j + 1) * 512], ALU.mult, ALU.add, sem='vv')
        c.wait('sync', 'vv', v_ost)
        c.dma('sync', Dr('y')[tsl, :], A('yout')[:, :], 'yo')

def _finish(nc, c, stack):
    for sname in list(c.cnt.keys()):
        c.wait('sync', sname, c.cnt.get(sname, 0))
    with nc.Block() as block:
        for name in ('sync', 'tensor', 'vector', 'scalar', 'gpsimd'):
            ops = c.ops[name]
            def runner(e, ops=ops):
                for f in ops:
                    f(e)
            getattr(block, name)(runner)
    stack.close()
    return nc, c


def kernel(**inputs):
    nc, _ = build_nc(dbg=tuple(DBG))
    in_maps = [_per_core_inputs(core, inputs) for core in range(8)]
    res = run_bass_kernel_spmd(nc, in_maps, core_ids=list(range(8)))
    out = np.zeros((4, 4096, 1024), np.float32)
    for core in range(8):
        b, u = core // 2, core % 2
        out[b, u * TC:(u + 1) * TC] = res.results[core]['y']
    return out


# version 09
# speedup vs baseline: 1.1198x; 1.0306x over previous
import numpy as np
import ml_dtypes
import concourse.bass as bass
import concourse.mybir as mybir
from concourse.bass_utils import run_bass_kernel_spmd

F32 = mybir.dt.float32
BF16 = mybir.dt.bfloat16
AF = mybir.ActivationFunctionType
ALU = mybir.AluOpType
BF = ml_dtypes.bfloat16

D = 1024; TC = 2048; SLAB = 512; NSLAB = 4; TT = 128; NTT = 16
CH = 128; NCH = 16
H = 32; PH = 64; NST = 128; DIN = 2048; CONVD = 2304; NCT = 18
DZ = 4384; DZP = 4480; MZ = 35
NS = 64; DH = 512; DQ = 256
EPS = 1e-5

DBG = []
STAGE = 99
DEBUG_P1 = False
GROUPS = [[0, 1], [2, 3], [4, 5], [6, 7]]


def _core_constants(u):
    t0 = u * TC
    pos = np.arange(t0, t0 + TC, dtype=np.float64)
    inv_freq = 1.0 / 10000.0 ** (np.arange(0, 64, 2, dtype=np.float64) / 64)
    fr = pos[:, None] * inv_freq[None, :]
    cos = np.cos(fr); sin = np.sin(fr)
    rope_cos = np.concatenate([cos, cos], axis=1).astype(np.float32)
    rope_sin = np.concatenate([-sin, sin], axis=1).astype(np.float32)
    return rope_cos, rope_sin

def _shared_constants():
    c = {}
    sel0 = np.zeros((32, 2048), np.float32); sel1 = np.zeros((32, 2048), np.float32)
    for h in range(16):
        sel0[h, h * 128:(h + 1) * 128] = 1.0
        sel1[16 + h, h * 128:(h + 1) * 128] = 1.0
    c['sel_h0'] = sel0.astype(BF); c['sel_h1'] = sel1.astype(BF)
    s32 = np.zeros((32, 2048), np.float32)
    for h in range(32):
        s32[h, h * 64:(h + 1) * 64] = 1.0
    c['sel32'] = s32.astype(BF)
    tl = np.zeros((127, 128), np.float32); tr = np.zeros((127, 2048), np.float32)
    for j in range(127):
        tl[j, j + 1:] = 1.0
        tr[j, j::128] = -1e30
    c['trilL'] = tl.astype(BF); c['trilR'] = tr.astype(BF)
    c['sel64'] = np.vstack([s32, s32]).astype(BF)
    c['ident'] = np.eye(128, dtype=np.float32).astype(BF)
    c['identf'] = np.eye(128, dtype=np.float32)
    c['ones32b'] = np.ones((32, 128), np.float32).astype(BF)
    c['ones_colb'] = np.ones((128, 1), np.float32).astype(BF)
    c['ones_rowf'] = np.ones((1, 128), np.float32)
    return c

def _per_core_inputs(core, inp):
    b, u = core // 2, core % 2
    t0 = u * TC
    m = {}
    x = np.asarray(inp['L0_attn_x'])[b, t0:t0 + TC]
    m['x_tok'] = np.ascontiguousarray(x)
    m['xT'] = np.ascontiguousarray(x.T)
    m['wmod'] = np.ascontiguousarray(np.asarray(inp['L0_wmod']))
    m['c_col'] = np.ascontiguousarray(np.asarray(inp['L0_c'])[b].reshape(8, 128).T)
    m['wqT'] = np.ascontiguousarray(np.asarray(inp['L0_attn_W_in'])[:1024, :].T)
    wz = np.asarray(inp['L0_attn_W_zxbcdt'])
    wzT = np.zeros((1024, DZP), np.float32); wzT[:, :DZ] = wz.T
    m['wzT'] = wzT
    m['woutT'] = np.ascontiguousarray(np.asarray(inp['L0_attn_W_out']).T)
    cw = np.asarray(inp['L0_attn_conv_w'])
    m['conv_w'] = np.ascontiguousarray(cw.reshape(NCT, 128, 4).transpose(1, 0, 2).reshape(128, NCT * 4))
    m['conv_b'] = np.ascontiguousarray(np.asarray(inp['L0_attn_conv_b']).reshape(NCT, 128).T)
    m['dtb_col'] = np.asarray(inp['L0_attn_dt_bias']).reshape(32, 1).astype(np.float32)
    m['alog_col'] = np.asarray(inp['L0_attn_A_log']).reshape(32, 1).astype(np.float32)
    m['d_hp'] = np.ascontiguousarray(np.repeat(np.asarray(inp['L0_attn_D_param']), 64).reshape(16, 128).T)
    m['npw_col'] = np.ascontiguousarray(np.asarray(inp['L0_attn_norm_pre_w']).reshape(8, 128).T)
    m['ngw_col'] = np.ascontiguousarray(np.asarray(inp['L0_attn_norm_gated_w']).reshape(16, 128).T)
    m['w1'] = np.ascontiguousarray(np.asarray(inp['L0_ffn_w1']))
    m['w2'] = np.ascontiguousarray(np.asarray(inp['L0_ffn_w2']))
    m['wihT'] = np.ascontiguousarray(np.asarray(inp['L0_ffn_Wih']).T)
    m['whhT'] = np.ascontiguousarray(np.asarray(inp['L0_ffn_Whh']).T)
    m['slot_init'] = np.ascontiguousarray(np.asarray(inp['L0_ffn_slot_init']))
    m['slot_initT'] = np.ascontiguousarray(np.asarray(inp['L0_ffn_slot_init']).T)
    m['wqf'] = np.ascontiguousarray(np.asarray(inp['L0_ffn_wq']))
    m['wkf'] = np.ascontiguousarray(np.asarray(inp['L0_ffn_wk']))
    m['wvf'] = np.ascontiguousarray(np.asarray(inp['L0_ffn_wv']))
    m['wvp'] = np.ascontiguousarray(np.asarray(inp['L0_ffn_wvp']))
    m['wof'] = np.ascontiguousarray(np.asarray(inp['L0_ffn_wo']))
    m['b1_col'] = np.ascontiguousarray(np.asarray(inp['L0_ffn_b1']).reshape(4, 128).T)
    m['b2_row'] = np.asarray(inp['L0_ffn_b2']).reshape(1, 64).astype(np.float32)
    m['tau'] = np.asarray(inp['L0_ffn_tau']).reshape(1, 1).astype(np.float32)
    m['bih_row'] = np.asarray(inp['L0_ffn_bih']).reshape(1, 3072).astype(np.float32)
    m['bhh_row'] = np.asarray(inp['L0_ffn_bhh']).reshape(1, 3072).astype(np.float32)
    m['bq_col'] = np.ascontiguousarray(np.asarray(inp['L0_ffn_bq']).reshape(2, 128).T)
    m['bk_col'] = np.ascontiguousarray(np.asarray(inp['L0_ffn_bk']).reshape(2, 128).T)
    m['bv_row'] = np.asarray(inp['L0_ffn_bv']).reshape(1, 1024).astype(np.float32)
    m['lng_col'] = np.ascontiguousarray(np.asarray(inp['L0_ffn_ln_g']).reshape(8, 128).T)
    m['lnb_col'] = np.ascontiguousarray(np.asarray(inp['L0_ffn_ln_b']).reshape(8, 128).T)
    m['bvp_col'] = np.ascontiguousarray(np.asarray(inp['L0_ffn_bvp']).reshape(8, 128).T)
    m['bo_row'] = np.asarray(inp['L0_ffn_bo']).reshape(1, 1024).astype(np.float32)
    rc, rs = _core_constants(u)
    m['rope_cos'] = np.ascontiguousarray(rc.reshape(16, 128, 64).transpose(1, 0, 2).reshape(128, 1024))
    m['rope_sin'] = np.ascontiguousarray(rs.reshape(16, 128, 64).transpose(1, 0, 2).reshape(128, 1024))
    m.update(_shared_constants())
    m['g_state'] = np.full((128, 1), 1.0 if u == 0 else 0.0, np.float32)
    m['g_use'] = np.full((128, 1), 0.0 if u == 0 else 1.0, np.float32)
    return m


class Ctx:
    def __init__(self, nc, stack):
        self.nc = nc
        self.ops = {k: [] for k in ('sync', 'tensor', 'vector', 'scalar', 'gpsimd')}
        self.cnt = {}
        self.sems = {}
        self.sb = {}
        self.ps = {}
        self.dr = {}
        self.dma_sems = set()
        self.vlast_w = {}
        self.vlast_r = {}
        self.vwaited = 0
        self._stack = stack

    def sem(self, name):
        if name not in self.sems:
            self.sems[name] = self._stack.enter_context(self.nc.semaphore(name))
            self.cnt[name] = 0
        return self.sems[name]

    def op(self, eng, builder, sem=None, amt=1):
        v = None
        if sem is not None:
            self.sem(sem)
            self.cnt[sem] += amt
            v = self.cnt[sem]
        def f(e, b=builder, s=sem, a=amt):
            ins = b(e)
            if s is not None:
                ins.then_inc(self.sems[s], a)
        self.ops[eng].append(f)
        return v

    def wait(self, eng, sem, val):
        if val is None or val <= 0:
            return
        self.sem(sem)
        assert val <= self.cnt[sem], (eng, sem, val, self.cnt[sem])
        def f(e, s=sem, v=val):
            e.wait_ge(self.sems[s], v)
        self.ops[eng].append(f)

    def phase(self, tag):
        import contextlib
        ctx = self
        class _Ph:
            def __init__(self):
                self.stack = contextlib.ExitStack()
                self.names = []
            def sb(self, name, shape, dtype=F32):
                h = self.stack.enter_context(ctx.nc.sbuf_tensor(name, list(shape), dtype))
                ctx.sb[name] = h
                self.names.append(name)
                return h
            def close(self):
                ctx.barrier(tag)
                for n in self.names:
                    ctx.sb.pop(n, None)
                self.stack.close()
        return _Ph()

    def barrier(self, tag):
        engs = ('sync', 'tensor', 'vector', 'scalar', 'gpsimd')
        for sname in sorted(self.dma_sems):
            self.wait('sync', sname, self.cnt.get(sname, 0))
        for e in engs:
            self.op(e, lambda x: x.nop(), sem='bar_' + e)
        tgt = {e: self.cnt['bar_' + e] for e in engs}
        for e in engs:
            for o in engs:
                if o != e:
                    self.wait(e, 'bar_' + o, tgt[o])

    def vop(self, builder, reads, writes, sem=None, amt=1):
        def names(aps):
            out = []
            for a in aps:
                if a is None:
                    continue
                t = getattr(a, 'tensor', None)
                if t is None and hasattr(a, 'ap'):
                    t = getattr(a.ap(), 'tensor', None)
                if t is not None:
                    out.append(t.name)
            return out
        rn = names(reads); wn = names(writes)
        need = 0
        for t in rn:
            need = max(need, self.vlast_w.get(t, 0))
        for t in wn:
            need = max(need, self.vlast_w.get(t, 0), self.vlast_r.get(t, 0))
        if need > self.vwaited:
            self.wait('vector', 'vv', need)
            self.vwaited = need
        v = self.op('vector', builder, sem='vv', amt=amt)
        for t in wn:
            self.vlast_w[t] = v
        for t in rn:
            self.vlast_r[t] = max(self.vlast_r.get(t, 0), v)
        return v

    def dma(self, q, out, in_, sem):
        self.dma_sems.add(sem)
        return self.op(q, lambda e, o=out, i=in_: e.dma_start(out=o, in_=i), sem=sem, amt=16)

    def dma_slow(self, q, out, in_, sem):
        self.dma_sems.add(sem)
        return self.op(q, lambda e, o=out, i=in_:
                       e.dma_start(out=o, in_=i, allow_slow_non_contiguous=True), sem=sem, amt=16)

    def dmat(self, q, out, in_, sem):
        self.dma_sems.add(sem)
        return self.op(q, lambda e, o=out, i=in_: e.dma_start_transpose(o, i), sem=sem, amt=16)

def mm(ctx, out, lhsT, rhs, start, stop, sem=None):
    return ctx.op('tensor', lambda e, o=out, l=lhsT, r=rhs, st=start, sp=stop:
                  e.matmul(o, l, r, start=st, stop=sp), sem=sem)

def act(ctx, out, in_, func, bias=None, scale=None, accum_out=None, sem=None):
    kw = {}
    if bias is not None: kw['bias'] = bias
    if scale is not None: kw['scale'] = scale
    if accum_out is not None: kw['accum_out'] = accum_out
    return ctx.op('scalar', lambda e, o=out, i=in_, f=func, k=dict(kw):
                  e.activation(out=o, in_=i, func=f, **k), sem=sem)

def tt(ctx, out, in0, in1, alu, sem=None):
    return ctx.vop(lambda e, o=out, a=in0, b=in1, p=alu:
                   e.tensor_tensor(out=o, in0=a, in1=b, op=p), [in0, in1], [out])

def ts(ctx, out, in0, s1, op0, s2=None, op1=None, sem=None):
    def b(e, o=out, i=in0, x1=s1, x2=s2, p0=op0, p1=op1):
        if p1 is None:
            return e.tensor_scalar(out=o, in0=i, scalar1=x1, scalar2=None, op0=p0)
        return e.tensor_scalar(out=o, in0=i, scalar1=x1, scalar2=x2, op0=p0, op1=p1)
    rd = [in0] + [x for x in (s1, s2) if not isinstance(x, (int, float, type(None)))]
    return ctx.vop(b, rd, [out])

def stt(ctx, out, in0, scalar, in1, op0, op1, sem=None):
    rd = [in0, in1] + ([scalar] if not isinstance(scalar, (int, float)) else [])
    return ctx.vop(lambda e, o=out, a=in0, s=scalar, b=in1, p0=op0, p1=op1:
                   e.scalar_tensor_tensor(out=o, in0=a, scalar=s, in1=b, op0=p0, op1=p1), rd, [out])

def vcopy(ctx, out, in_, sem=None):
    return ctx.vop(lambda e, o=out, i=in_: e.tensor_copy(o, i), [in_], [out])

def scopy(ctx, out, in_, sem=None):
    return ctx.op('scalar', lambda e, o=out, i=in_: e.copy(o, i), sem=sem)

def vscan(ctx, out, d0, d1, initial, op0, op1, sem=None):
    rd = [d0, d1] + ([initial] if not isinstance(initial, (int, float)) else [])
    return ctx.vop(lambda e, o=out, a=d0, b=d1, i=initial, p0=op0, p1=op1:
                   e.tensor_tensor_scan(o, a, b, i, p0, p1), rd, [out])


def sinv(ctx, out, in_, power=-1.0, sem=None):
    act(ctx, out, in_, AF.Ln)
    return act(ctx, out, out, AF.Exp, scale=power, sem=sem)

def vmemset(ctx, ap, val, sem=None):
    return ctx.vop(lambda e, a=ap, v=val: e.memset(a, v), [], [ap])

def vrecip(ctx, out, in_, sem=None):
    return ctx.vop(lambda e, o=out, i=in_: e.reciprocal(o, i), [in_], [out])

def vmax8(ctx, out, in_, sem=None):
    return ctx.vop(lambda e, o=out, i=in_: e.max(o, i), [in_], [out])

def vreduce(ctx, out, in_, alu, sem=None):
    return ctx.vop(lambda e, o=out, i=in_, p=alu:
                   e.tensor_reduce(out=o, in_=i, axis=mybir.AxisListType.X, op=p), [in_], [out])


def build_nc(dbg=()):
    import contextlib
    nc = bass.Bass("TRN2", target_bir_lowering=False, debug=False, detect_race_conditions=False)
    stack = contextlib.ExitStack()
    c = Ctx(nc, stack)

    def din(name, shape, dtype=F32):
        c.dr[name] = nc.dram_tensor(name, list(shape), dtype, kind="ExternalInput").ap()

    def dtmp(name, shape, dtype=F32):
        kind = "ExternalOutput" if name in dbg else "Internal"
        c.dr[name] = nc.dram_tensor(name, list(shape), dtype, kind=kind).ap()

    def sb(name, shape, dtype=F32):
        c.sb[name] = stack.enter_context(nc.sbuf_tensor(name, list(shape), dtype))

    for n, s in [('x_tok', (TC, D)), ('xT', (D, TC)), ('wmod', (D, 6144)), ('c_col', (128, 8)),
                 ('wqT', (D, D)), ('wzT', (D, DZP)), ('woutT', (DIN, D)),
                 ('conv_w', (128, NCT * 4)), ('conv_b', (128, NCT)),
                 ('dtb_col', (32, 1)), ('alog_col', (32, 1)), ('d_hp', (128, 16)),
                 ('npw_col', (128, 8)), ('ngw_col', (128, 16)),
                 ('w1', (DIN, DH)), ('w2', (DH, NS)), ('wihT', (D, 3072)), ('whhT', (D, 3072)),
                 ('slot_init', (NS, D)), ('slot_initT', (D, NS)),
                 ('wqf', (D, DQ)), ('wkf', (D, DQ)), ('wvf', (D, D)), ('wvp', (D, D)), ('wof', (D, D)),
                 ('b1_col', (128, 4)), ('b2_row', (1, NS)), ('tau', (1, 1)),
                 ('bih_row', (1, 3072)), ('bhh_row', (1, 3072)),
                 ('bq_col', (128, 2)), ('bk_col', (128, 2)), ('bv_row', (1, D)),
                 ('lng_col', (128, 8)), ('lnb_col', (128, 8)), ('bvp_col', (128, 8)), ('bo_row', (1, D)),
                 ('rope_cos', (128, NTT * 64)), ('rope_sin', (128, NTT * 64)), ('ones_rowf', (1, 128)), ('identf', (128, 128)),
                 ('g_state', (128, 1)), ('g_use', (128, 1))]:
        din(n, s, F32)
    for n, s in [('sel_h0', (32, 2048)), ('sel_h1', (32, 2048)), ('sel32', (32, 2048)),
                 ('trilL', (127, 128)), ('trilR', (127, 2048)), ('ones_colb', (128, 1)),
                 ('sel64', (64, 2048)), ('ones32b', (32, 128)), ('ident', (128, 128))]:
        din(n, s, BF16)

    c.dr['y'] = nc.dram_tensor('y', [TC, D], F32, kind="ExternalOutput").ap()

    dtmp('qhat_d', (TC, D), BF16)
    dtmp('resid_d', (TC, D), F32)
    dtmp('xconv_d', (CONVD, TC), BF16)
    dtmp('zsilu_d', (DIN, TC), BF16)
    dtmp('ypre_d', (DIN, TC), BF16)
    dtmp('xmid_d', (TC, D), F32)
    dtmp('h2_d', (TC, D), BF16)
    dtmp('acshi_d', (32, TC), BF16)
    dtmp('acslo_d', (32, TC), BF16)
    dtmp('nahi_d', (32, TC), BF16)
    dtmp('nalo_d', (32, TC), BF16)
    dtmp('acsghi_d', (32, TC), BF16)
    dtmp('acsglo_d', (32, TC), BF16)
    dtmp('acs_d', (32, TC), F32)
    dtmp('dt_d', (32, TC), F32)
    dtmp('st_in', (NST, DIN), F32)
    dtmp('st_out', (NST, DIN), F32)
    dtmp('sl_in', (NS, 1025), F32)
    dtmp('sl_out', (NS, 1025), F32)
    dtmp('brow_d', (1, 8192), F32)
    dtmp('alp_d', (TC, 128), BF16)

    sb('onesf', (1, 128), F32)
    sb('onesb', (128, 1), BF16)
    sb('c_sb', (128, 8), F32)
    sb('siluc', (128, 8), BF16)
    sb('silucf', (128, 8), F32)
    sb('modrow', (1, 6144), F32)
    sb('garow', (1, 2048), F32)
    sb('mcol', (128, 48), F32)
    sb('gab', (128, D), F32); sb('cb1b', (128, D), F32); sb('sbb', (128, D), F32)
    sb('stat_sum', (128, NTT + 16), F32); sb('stat_sq', (128, NTT + 16), F32)
    sb('r1', (128, NTT + 16), F32); sb('nmr1', (128, NTT + 16), F32)
    sb('r2', (128, NTT + 16), F32); sb('sq2', (128, NTT + 16), F32)
    sb('r3', (128, NTT + 16), F32); sb('sq3', (128, NTT + 16), F32)
    sb('tiny', (128, 64), F32)
    sb('tcol', (32, 16), F32)
    sb('alpha', (128, NTT * 64), BF16)
    sb('alphaT', (128, TC), BF16)

    for i in range(8):
        c.ps['pb%d' % i] = stack.enter_context(nc.psum_tensor('pb%d' % i, [128, 512], F32))
    P = [c.ps['pb%d' % i] for i in range(8)]

    A = lambda n: c.sb[n]
    Dr = lambda n: c.dr[n]

    c.dma('sync', A('onesf')[:, :], Dr('ones_rowf')[:, :], 'c0')
    c.dma('sync', A('onesb')[:, :], Dr('ones_colb')[:, :], 'c0')

    def row_bcast(dst_ap, row_ap, n, banks=(4, 5)):
        last = 0
        evs = {b: 0 for b in banks}
        for idx, j in enumerate(range(0, n, 512)):
            w = min(512, n - j)
            bk = banks[idx % len(banks)]
            c.wait('tensor', 'vv', evs[bk])
            vm = mm(c, P[bk][:, 0:w], A('onesf')[0:1, :], row_ap[0:1, j:j + w], start=True, stop=True, sem='mm')
            c.wait('vector', 'mm', vm)
            last = vcopy(c, dst_ap[:, j:j + w], P[bk][:, 0:w], sem='vv')
            evs[bk] = last
        return last

    ph = c.phase('p0')
    ph.sb('wmodb', (128, 8 * 4096), BF16)
    ph.sb('wmodf', (128, 8 * 2048), F32)
    ph.sb('identP', (128, 128), F32)
    c.dma('sync', A('c_sb')[:, :], Dr('c_col')[:, :], 'c0')
    c.dma('sync', A('identP')[:, :], Dr('identf')[:, :], 'c0')
    v_c0 = c.cnt['c0']
    vbf = []
    vfp = []
    for k in range(8):
        c.dma('gpsimd', A('wmodb')[:, k * 4096:k * 4096 + 2048],
              Dr('wmod')[k * 128:(k + 1) * 128, 0:2048], 'wma')
        vbf.append(c.dma('gpsimd', A('wmodb')[:, k * 4096 + 2048:k * 4096 + 4096],
                         Dr('wmod')[k * 128:(k + 1) * 128, 3072:5120], 'wma'))
        c.dma('sync', A('wmodf')[:, k * 2048:k * 2048 + 1024],
              Dr('wmod')[k * 128:(k + 1) * 128, 2048:3072], 'wmb')
        vfp.append(c.dma('sync', A('wmodf')[:, k * 2048 + 1024:k * 2048 + 2048],
                         Dr('wmod')[k * 128:(k + 1) * 128, 5120:6144], 'wmb'))
    c.wait('scalar', 'c0', v_c0)
    act(c, A('siluc')[:, :], A('c_sb')[:, :], AF.Silu)
    v_siluc = act(c, A('silucf')[:, :], A('c_sb')[:, :], AF.Silu, sem='ss')

    bank_ev = [0, 0]
    c.wait('tensor', 'ss', v_siluc)
    for n in range(12):
        fp32n = n in (4, 5, 10, 11)
        lhs = A('silucf') if fp32n else A('siluc')
        bank = P[n % 2]
        c.wait('tensor', 'vv', bank_ev[n % 2])
        for k in range(8):
            if fp32n:
                c.wait('tensor', 'wmb', vfp[k])
                j0 = k * 2048 + (1024 if n >= 10 else 0) + (n % 2) * 512
                rhs = A('wmodf')[:, j0:j0 + 512]
            else:
                c.wait('tensor', 'wma', vbf[k])
                j0 = k * 4096 + (2048 if n >= 6 else 0) + ((n - 6 if n >= 6 else n) % 4) * 512
                rhs = A('wmodb')[:, j0:j0 + 512]
            vmm = mm(c, bank[0:1, :], lhs[:, k:k + 1], rhs,
                     start=(k == 0), stop=(k == 7), sem='mm')
        c.wait('vector', 'mm', vmm)
        bank_ev[n % 2] = vcopy(c, A('modrow')[0:1, n * 512:(n + 1) * 512], bank[0:1, :], sem='vv')
    v_modrow = max(bank_ev)

    c.wait('tensor', 'vv', v_modrow)
    c.wait('tensor', 'c0', v_c0)
    for a in range(48):
        bk, col = (2, a) if a < 24 else (3, a - 24)
        vm = mm(c, P[bk][:, col:col + 1], A('modrow').ap()[0:1, a * 128:(a + 1) * 128],
                A('onesf')[0:1, 0:1], start=True, stop=True, sem='mm')
    c.wait('vector', 'mm', vm)
    vcopy(c, A('mcol')[:, 0:24], P[2][:, 0:24], sem='vv')
    v_mcol = vcopy(c, A('mcol')[:, 24:48], P[3][:, 0:24], sem='vv')

    c.wait('scalar', 'vv', v_mcol)
    v_sg = act(c, A('tiny')[:, 0:8], A('mcol')[:, 16:24], AF.Sign, sem='ss')
    v_sg2 = act(c, A('tiny')[:, 8:16], A('mcol')[:, 40:48], AF.Sign, sem='ss')
    v_ab = act(c, A('tiny')[:, 16:24], A('mcol')[:, 16:24], AF.Abs, sem='ss')
    v_ab2 = act(c, A('tiny')[:, 24:32], A('mcol')[:, 40:48], AF.Abs, sem='ss')
    c.wait('vector', 'ss', v_ab2)
    ts(c, A('tiny')[:, 16:24], A('tiny')[:, 16:24], 0.25, ALU.max)
    ts(c, A('tiny')[:, 24:32], A('tiny')[:, 24:32], 0.25, ALU.max)
    tt(c, A('mcol')[:, 16:24], A('tiny')[:, 0:8], A('tiny')[:, 16:24], ALU.mult)
    v_gclamp = tt(c, A('mcol')[:, 40:48], A('tiny')[:, 8:16], A('tiny')[:, 24:32], ALU.mult, sem='vv')

    c.wait('tensor', 'vv', v_gclamp)
    ev67 = 0
    for g in range(2):
        c.wait('tensor', 'vv', ev67)
        for j in range(8):
            bk, col = (6, j) if j < 4 else (7, j - 4)
            vm = mm(c, P[bk][0:1, col * 128:(col + 1) * 128], A('mcol')[:, (16 if g == 0 else 40) + j:(17 if g == 0 else 41) + j],
                    A('identP')[:, :], start=True, stop=True, sem='mm')
        c.wait('vector', 'mm', vm)
        vcopy(c, A('garow')[0:1, g * 1024:g * 1024 + 512], P[6][0:1, 0:512], sem='vv')
        ev67 = vcopy(c, A('garow')[0:1, g * 1024 + 512:g * 1024 + 1024], P[7][0:1, 0:512], sem='vv')
    c.wait('tensor', 'vv', ev67)

    row_bcast(A('gab').ap(), A('garow').ap()[0:1, 0:1024], D)
    row_bcast(A('cb1b').ap(), A('modrow').ap()[0:1, 4 * 1024:5 * 1024], D)
    ts(c, A('cb1b')[:, :], A('cb1b')[:, :], 1.0, ALU.add)
    row_bcast(A('sbb').ap(), A('modrow').ap()[0:1, 3 * 1024:4 * 1024], D)
    ph.close()

    if STAGE >= 1:
        ph = c.phase('p1')
        _p1(c, P, A, Dr, row_bcast, ph)
        ph.close()
    if STAGE >= 2:
        ph = c.phase('p2')
        _p2(c, P, A, Dr, ph)
        ph.close()
    if STAGE >= 3:
        ph = c.phase('p3')
        _p3(c, P, A, Dr, ph)
        ph.close()
    if STAGE >= 4:
        ph = c.phase('p4')
        _p4(c, P, A, Dr, ph)
        ph.close()
    if STAGE >= 5:
        ph = c.phase('p5')
        _p5(c, P, A, Dr, ph)
        ph.close()
    if STAGE >= 6:
        ph = c.phase('p6')
        _p6(c, P, A, Dr, ph)
        ph.close()
    return _finish(nc, c, stack)


def _p1(c, P, A, Dr, row_bcast, ph):
    ph.sb('v1row', (1, D), F32); ph.sb('v2row', (1, D), F32)
    ph.sb('v1b', (128, D), F32); ph.sb('v2b', (128, D), F32)
    ph.sb('x_sb', (128, 4 * D), F32)
    ph.sb('xT_sb', (128, 2 * 8 * SLAB), BF16)
    ph.sb('wq_sb', (128, 8 * D), BF16)
    ph.sb('sacol', (128, 8), BF16)
    ph.sb('rope_c', (128, NTT * 64), F32)
    ph.sb('rope_s', (128, NTT * 64), F32)
    ph.sb('muv', (128, 4 * D), F32)
    ph.sb('qpre', (128, 4 * D), F32)
    ph.sb('ropet', (128, 4 * D), F32)
    ph.sb('qhat_t', (128, 4 * D), BF16)
    ph.sb('resid_t', (128, 4 * D), F32)
    for k in range(8):
        c.dma('gpsimd', A('wq_sb')[:, k * D:(k + 1) * D], Dr('wqT')[k * 128:(k + 1) * 128, :], 'wql')
    v_wq = c.cnt['wql']
    v_sab = vcopy(c, A('sacol')[:, :], A('mcol')[:, 0:8], sem='vv')
    c.wait('tensor', 'vv', v_sab)
    c.wait('tensor', 'wql', v_wq)
    v_v2r = 0
    for j in range(2):
        c.wait('tensor', 'vv', v_v2r)
        for k in range(8):
            vmm = mm(c, P[6][0:1, :], A('sacol')[:, k:k + 1], A('wq_sb')[:, k * D + j * 512:k * D + j * 512 + 512],
                     start=(k == 0), stop=(k == 7), sem='mm')
        c.wait('vector', 'mm', vmm)
        v_v2r = vcopy(c, A('v2row')[0:1, j * 512:(j + 1) * 512], P[6][0:1, :], sem='vv')
    ts(c, A('tiny')[:, 32:40], A('mcol')[:, 8:16], 1.0, ALU.add)
    v_fold = 0
    for k in range(8):
        v_fold = ts(c, A('wq_sb')[:, k * D:(k + 1) * D], A('wq_sb')[:, k * D:(k + 1) * D],
                    A('tiny').ap()[:, 32 + k:33 + k], ALU.mult, sem='vv')
    c.wait('tensor', 'vv', v_fold)
    v_v1r = 0
    for j in range(2):
        c.wait('tensor', 'vv', v_v1r)
        for k in range(8):
            vmm = mm(c, P[7][0:1, :], A('onesb')[:, 0:1], A('wq_sb')[:, k * D + j * 512:k * D + j * 512 + 512],
                     start=(k == 0), stop=(k == 7), sem='mm')
        c.wait('vector', 'mm', vmm)
        v_v1r = vcopy(c, A('v1row')[0:1, j * 512:(j + 1) * 512], P[7][0:1, :], sem='vv')
    row_bcast(A('v1b').ap(), A('v1row').ap(), D, banks=(4, 5))
    v_v2b = row_bcast(A('v2b').ap(), A('v2row').ap(), D, banks=(4, 5))

    c.dma('sync', A('rope_c')[:, :], Dr('rope_cos')[:, :], 'rt')
    v_ropetab = c.dma('sync', A('rope_s')[:, :], Dr('rope_sin')[:, :], 'rt')

    xfree = 0
    spill_q = 0
    spill_r = 0
    qbank_free = 0
    sq2_done = 0
    slab_mm = [0, 0]
    for s in range(NSLAB):
        t0 = s * 4
        c.wait('gpsimd', 'mm', slab_mm[s % 2])
        for k in range(8):
            c.dma('gpsimd', A('xT_sb')[:, (s % 2) * 8 * SLAB + k * SLAB:(s % 2) * 8 * SLAB + (k + 1) * SLAB],
                  Dr('xT')[k * 128:(k + 1) * 128, s * SLAB:(s + 1) * SLAB], 'xt%d' % (s % 2))
        v_xT = c.cnt['xt%d' % (s % 2)]
        c.wait('sync', 'ss', xfree)
        for tl in range(4):
            vx = c.dma('sync', A('x_sb')[:, tl * D:(tl + 1) * D],
                       Dr('x_tok')[(t0 + tl) * 128:(t0 + tl + 1) * 128, :], 'xk')
        c.wait('scalar', 'xk', vx)
        for tl in range(4):
            xs = A('x_sb')[:, tl * D:(tl + 1) * D]
            act(c, xs, xs, AF.Copy, accum_out=A('stat_sum').ap()[:, t0 + tl:t0 + tl + 1])
            stats_done = act(c, xs, xs, AF.Square, accum_out=A('stat_sq').ap()[:, t0 + tl:t0 + tl + 1], sem='ss')
        xfree = stats_done
        c.wait('vector', 'ss', stats_done)
        ts(c, A('tiny')[:, 40:44], A('stat_sum').ap()[:, t0:t0 + 4], 1.0 / D, ALU.mult)
        ts(c, A('tiny')[:, 44:48], A('stat_sq').ap()[:, t0:t0 + 4], 1.0 / D, ALU.mult)
        tt(c, A('tiny')[:, 48:52], A('tiny')[:, 40:44], A('tiny')[:, 40:44], ALU.mult)
        tt(c, A('tiny')[:, 52:56], A('tiny')[:, 44:48], A('tiny')[:, 48:52], ALU.subtract)
        v_var = ts(c, A('tiny')[:, 52:56], A('tiny')[:, 52:56], EPS, ALU.add, sem='vv')
        c.wait('scalar', 'vv', v_var)
        v_r1 = sinv(c, A('r1').ap()[:, t0:t0 + 4], A('tiny').ap()[:, 52:56], -0.5, sem='ss')
        c.wait('vector', 'ss', v_r1)
        stt(c, A('nmr1').ap()[:, t0:t0 + 4], A('tiny').ap()[:, 40:44], -1.0,
            A('r1').ap()[:, t0:t0 + 4], ALU.mult, ALU.mult)
        c.wait('tensor', 'xt%d' % (s % 2), v_xT)
        if s == 0:
            c.wait('tensor', 'vv', v_v2b)
        c.wait('tensor', 'vv', qbank_free)
        for tl in range(4):
            for j in range(2):
                for k in range(8):
                    v_qmm = mm(c, P[2 * tl + j][:, :],
                               A('xT_sb').ap()[:, (s % 2) * 8 * SLAB + k * SLAB + tl * 128:
                                               (s % 2) * 8 * SLAB + k * SLAB + tl * 128 + 128],
                               A('wq_sb')[:, k * D + j * 512:k * D + j * 512 + 512],
                               start=(k == 0), stop=(k == 7), sem='mm')
        slab_mm[s % 2] = v_qmm
        c.wait('vector', 'mm', v_qmm)
        c.wait('vector', 'ss', sq2_done)
        c.wait('vector', 'sq0', spill_q)
        c.wait('vector', 'sr0', spill_r)
        for tl in range(4):
            stt(c, A('muv')[:, tl * D:(tl + 1) * D], A('v1b')[:, :], A('nmr1').ap()[:, t0 + tl:t0 + tl + 1],
                A('v2b')[:, :], ALU.mult, ALU.add)
        for tl in range(4):
            qp = A('qpre').ap()[:, tl * D:(tl + 1) * D]
            for j in range(2):
                qbank_free = stt(c, qp[:, j * 512:(j + 1) * 512], P[2 * tl + j][:, :],
                                 A('r1').ap()[:, t0 + tl:t0 + tl + 1],
                                 A('muv').ap()[:, tl * D + j * 512:tl * D + j * 512 + 512],
                                 ALU.mult, ALU.add, sem='vv')
        if s == 0:
            c.wait('vector', 'rt', v_ropetab)
        for tl in range(4):
            t = t0 + tl
            qp = A('qpre').ap()[:, tl * D:(tl + 1) * D]
            cosap = A('rope_c').ap()[:, t * 64:(t + 1) * 64].unsqueeze(1).broadcast_to([128, 16, 64])
            qpv = qp.rearrange("p (h f) -> p h f", h=16)
            rpv = A('ropet').ap()[:, tl * D:(tl + 1) * D].rearrange("p (h f) -> p h f", h=16)
            tt(c, rpv, qpv, cosap, ALU.mult)
        for tl in range(4):
            t = t0 + tl
            qp4 = A('qpre').ap()[:, tl * D:(tl + 1) * D].rearrange("p (h a f) -> p h a f", h=16, a=2)
            crv = A('muv').ap()[:, tl * D:(tl + 1) * D].rearrange("p (h a f) -> p h a f", h=16, a=2)
            sin0 = A('rope_s').ap()[:, t * 64:t * 64 + 32].unsqueeze(1).broadcast_to([128, 16, 32]).unsqueeze(2)
            sin1 = A('rope_s').ap()[:, t * 64 + 32:t * 64 + 64].unsqueeze(1).broadcast_to([128, 16, 32]).unsqueeze(2)
            tt(c, crv[:, :, 0:1, :], qp4[:, :, 1:2, :], sin0, ALU.mult)
            tt(c, crv[:, :, 1:2, :], qp4[:, :, 0:1, :], sin1, ALU.mult)
        for tl in range(4):
            rp = A('ropet').ap()[:, tl * D:(tl + 1) * D]
            v_rope = tt(c, rp, rp, A('muv').ap()[:, tl * D:(tl + 1) * D], ALU.add, sem='vv')
        c.wait('scalar', 'vv', v_rope)
        for tl in range(4):
            sq2_done = act(c, A('muv')[:, tl * D:(tl + 1) * D], A('ropet').ap()[:, tl * D:(tl + 1) * D],
                           AF.Square, accum_out=A('sq2').ap()[:, t0 + tl:t0 + tl + 1], sem='ss')
        c.wait('vector', 'ss', sq2_done)
        v_t = ts(c, A('tiny')[:, 56:60], A('sq2').ap()[:, t0:t0 + 4], 1.0 / D, ALU.mult, EPS, ALU.add, sem='vv')
        c.wait('scalar', 'vv', v_t)
        v_r2 = sinv(c, A('r2').ap()[:, t0:t0 + 4], A('tiny').ap()[:, 56:60], -0.5, sem='ss')
        c.wait('vector', 'ss', v_r2)
        v_qh = 0
        for tl in range(4):
            v_qh = ts(c, A('qhat_t').ap()[:, tl * D:(tl + 1) * D], A('ropet').ap()[:, tl * D:(tl + 1) * D],
                      A('r2').ap()[:, t0 + tl:t0 + tl + 1], ALU.mult, sem='vv')
        v_res = 0
        for tl in range(4):
            v_res = tt(c, A('resid_t').ap()[:, tl * D:(tl + 1) * D], A('ropet').ap()[:, tl * D:(tl + 1) * D],
                       A('gab').ap(), ALU.mult, sem='vv')
        c.wait('gpsimd', 'vv', v_qh)
        for tl in range(4):
            spill_q = c.dma('gpsimd', Dr('qhat_d')[(t0 + tl) * 128:(t0 + tl + 1) * 128, :],
                            A('qhat_t').ap()[:, tl * D:(tl + 1) * D], 'sq0')
        c.wait('sync', 'vv', v_res)
        for tl in range(4):
            spill_r = c.dma('sync', Dr('resid_d')[(t0 + tl) * 128:(t0 + tl + 1) * 128, :],
                            A('resid_t').ap()[:, tl * D:(tl + 1) * D], 'sr0')
    if DEBUG_P1:
        c.wait('sync', 'vv', c.cnt['vv'])
        c.wait('sync', 'ss', c.cnt['ss'])
        c.wait('sync', 'mm', c.cnt['mm'])
        c.dma('sync', Dr('xmid_d')[0:128, :], A('v1b')[:, :], 'dbg1')
        c.dma('sync', Dr('xmid_d')[128:256, :], A('v2b')[:, :], 'dbg1')
        c.dma('sync', Dr('xmid_d')[256:384, :], A('gab')[:, :], 'dbg1')
        c.dma('sync', Dr('xmid_d')[384:385, :], A('v1row')[0:1, :], 'dbg1')
        c.dma('sync', Dr('xmid_d')[385:386, :], A('v2row')[0:1, :], 'dbg1')
        c.dma('sync', Dr('xmid_d')[386:387, :], A('garow').ap()[0:1, 0:1024], 'dbg1')
        c.dma('sync', Dr('xmid_d')[387:388, :], A('garow').ap()[0:1, 1024:2048], 'dbg1')
        c.dma('sync', Dr('xmid_d')[388:389, :], A('modrow').ap()[0:1, 1024:2048], 'dbg1')
        c.dma('sync', Dr('xmid_d')[400:528, :], A('rope_c')[:, :], 'dbg1')
        c.dma('sync', Dr('xmid_d')[528:656, :], A('rope_s')[:, :], 'dbg1')
        c.dma('sync', Dr('xmid_d')[656:784, :], A('muv')[:, 0:1024], 'dbg1')
        c.dma('sync', Dr('xmid_d')[784:912, :], A('qpre')[:, 0:1024], 'dbg1')
        c.dma('sync', Dr('xmid_d')[912:1040, :], A('ropet')[:, 0:1024], 'dbg1')
        c.dma('sync', Dr('xmid_d')[1040:1168, 0:16], A('stat_sum')[:, 0:16], 'dbg1')
        c.dma('sync', Dr('xmid_d')[1168:1296, 0:16], A('stat_sq')[:, 0:16], 'dbg1')
        c.dma('sync', Dr('xmid_d')[1296:1424, 0:16], A('r1')[:, 0:16], 'dbg1')
        c.dma('sync', Dr('xmid_d')[1424:1552, 0:16], A('nmr1')[:, 0:16], 'dbg1')
        c.dma('sync', Dr('xmid_d')[1552:1680, 0:64], A('tiny')[:, :], 'dbg1')
        c.dma('sync', Dr('xmid_d')[394:395, 0:48], A('mcol')[0:1, :], 'dbg1')




def _p2(c, P, A, Dr, ph):
    ph.sb('wzres', (128, 8 * DZP), BF16)
    ph.sb('qhT', (128, 2 * 8 * SLAB), BF16)
    ph.sb('xbcT', (128, NCT * 516), BF16)
    ph.sb('xcv', (128, 2 * 512), BF16)
    ph.sb('zsl', (128, 2 * 512), BF16)
    ph.sb('cacc', (128, 2 * 512), BF16)
    ph.sb('dtr', (32, 512), F32)
    ph.sb('dtv', (32, 512), F32)
    ph.sb('lndt', (32, 512), F32)
    ph.sb('adt', (32, 512), F32)
    ph.sb('acsf', (32, 512), F32)
    ph.sb('acsg', (32, 512), F32)
    ph.sb('bhalf', (32, 2 * 512), BF16)
    ph.sb('blow', (32, 2 * 512), BF16)
    ph.sb('acol', (32, 1), F32)
    ph.sb('gcar', (32, 1), F32)
    ph.sb('convw', (128, NCT * 4), F32)
    ph.sb('convb', (128, NCT), F32)
    ph.sb('npw', (128, 8), F32)
    ph.sb('dtbc', (32, 1), F32)
    ph.sb('t32', (32, 512), F32)

    for k in range(8):
        c.dma('gpsimd', A('wzres')[:, k * DZP:(k + 1) * DZP], Dr('wzT')[k * 128:(k + 1) * 128, :], 'wzl')
    v_wz = c.cnt['wzl']
    c.dma('sync', A('convw')[:, :], Dr('conv_w')[:, :], 'c2')
    c.dma('sync', A('convb')[:, :], Dr('conv_b')[:, :], 'c2')
    c.dma('sync', A('npw')[:, :], Dr('npw_col')[:, :], 'c2')
    c.dma('sync', A('dtbc')[:, :], Dr('dtb_col')[:, :], 'c2')
    c.dma('sync', A('acol')[:, :], Dr('alog_col')[:, :], 'c2')
    v_c2 = c.cnt['c2']
    c.wait('scalar', 'c2', v_c2)
    v_a2 = act(c, A('acol')[:, :], A('acol')[:, :], AF.Exp, sem='ss')
    c.wait('vector', 'ss', v_a2)
    v_a3 = ts(c, A('acol')[:, :], A('acol')[:, :], -1.0, ALU.mult)

    vmemset(c, A('xbcT')[:, :], 0.0)
    v_ms = vmemset(c, A('gcar')[:, :], 0.0)

    zslv = [0, 0]
    v_dtr_used = 0
    caccv = [0, 0]
    bank_ev = [0, 0, 0, 0]
    v_dtev = 0
    v_lastmm = 0
    slab_mm = [0, 0]
    v_halo = v_ms
    v_sp = 0
    for s in range(NSLAB):
        sh = s % 2
        qbase = sh * 8 * SLAB
        c.wait('sync', 'mm', slab_mm[sh])
        vq = 0
        for k in range(8):
            vq = c.dmat('sync', A('qhT')[:, qbase + k * SLAB:qbase + (k + 1) * SLAB],
                        Dr('qhat_d')[s * SLAB:(s + 1) * SLAB, k * 128:(k + 1) * 128], 'qh%d' % sh)
        c.wait('vector', 'qh%d' % sh, vq)
        v_fold = 0
        for k in range(8):
            v_fold = ts(c, A('qhT')[:, qbase + k * SLAB:qbase + (k + 1) * SLAB],
                        A('qhT')[:, qbase + k * SLAB:qbase + (k + 1) * SLAB],
                        A('npw').ap()[:, k:k + 1], ALU.mult)
        if s > 0:
            c.wait('vector', 'ss', v_dtev)
            for ct in range(NCT):
                base = ct * 516
                v_halo = vcopy(c, A('xbcT')[:, base:base + 3], A('xbcT')[:, base + 512:base + 515])
        if s == 0:
            c.wait('tensor', 'wzl', v_wz)
        for m in range(MZ):
            bk = m % 4
            if m == 0:
                c.wait('tensor', 'vv', v_fold)
            c.wait('tensor', 'ss', bank_ev[bk])
            for k in range(8):
                vmm = mm(c, P[bk][:, :], A('wzres')[:, k * DZP + m * 128:k * DZP + (m + 1) * 128],
                         A('qhT')[:, qbase + k * SLAB:qbase + (k + 1) * SLAB],
                         start=(k == 0), stop=(k == 7), sem='mm')
            v_lastmm = vmm
            c.wait('scalar', 'mm', vmm)
            if m < 16:
                zh = m % 2
                c.wait('scalar', 'zs%d' % zh, zslv[zh])
                v_ev = act(c, A('zsl')[:, zh * 512:(zh + 1) * 512], P[bk][:, :], AF.Silu, sem='ss')
                c.wait('gpsimd', 'ss', v_ev)
                zslv[zh] = c.dma('gpsimd', Dr('zsilu_d')[m * 128:(m + 1) * 128, s * 512:(s + 1) * 512],
                                 A('zsl')[:, zh * 512:(zh + 1) * 512], 'zs%d' % zh)
                bank_ev[bk] = v_ev
            elif m < 34:
                ct = m - 16
                if s > 0 and m == 16:
                    c.wait('scalar', 'vv', v_halo)
                v_ev = scopy(c, A('xbcT')[:, ct * 516 + 3:ct * 516 + 515], P[bk][:, :], sem='ss')
                bank_ev[bk] = v_ev
                base = ct * 516
                hb = ct % 2
                ac = A('cacc').ap()[:, hb * 512:(hb + 1) * 512]
                c.wait('vector', 'ss', v_ev)
                ts(c, ac, A('xbcT')[:, base + 0:base + 512], A('convw').ap()[:, ct * 4:ct * 4 + 1], ALU.mult)
                for kk in range(1, 4):
                    v_cv = stt(c, ac, A('xbcT')[:, base + kk:base + kk + 512],
                               A('convw').ap()[:, ct * 4 + kk:ct * 4 + kk + 1], ac, ALU.mult, ALU.add)
                c.wait('scalar', 'vv', v_cv)
                c.wait('scalar', 'xc%d' % hb, caccv[hb])
                v_sl = act(c, A('xcv')[:, hb * 512:(hb + 1) * 512], ac, AF.Silu,
                           bias=A('convb').ap()[:, ct:ct + 1], sem='ss')
                c.wait('gpsimd', 'ss', v_sl)
                caccv[hb] = c.dma('gpsimd', Dr('xconv_d')[ct * 128:(ct + 1) * 128, s * 512:(s + 1) * 512],
                                  A('xcv')[:, hb * 512:(hb + 1) * 512], 'xc%d' % hb)
            else:
                c.wait('scalar', 'vv', v_dtr_used)
                v_ev = scopy(c, A('dtr')[:, :], P[bk][0:32, :], sem='ss')
                bank_ev[bk] = v_ev
                v_dtev = v_ev
        slab_mm[sh] = v_lastmm
        c.wait('vector', 'ss', v_dtev)
        v_v = ts(c, A('t32')[:, :], A('dtr')[:, :], A('dtbc').ap()[:, 0:1], ALU.add)
        v_dtr_used = v_v
        c.wait('scalar', 'vv', v_v)
        v_av = act(c, A('dtv')[:, :], A('t32')[:, :], AF.Abs, sem='ss')
        c.wait('scalar', 'ss', v_av)
        v_e = act(c, A('dtv')[:, :], A('dtv')[:, :], AF.Exp, scale=-1.0, sem='ss')
        c.wait('scalar', 'ss', v_e)
        v_l = act(c, A('dtv')[:, :], A('dtv')[:, :], AF.Ln, bias=1.0, sem='ss')
        c.wait('vector', 'ss', v_l)
        ts(c, A('t32')[:, :], A('t32')[:, :], 0.0, ALU.max)
        v_dt = tt(c, A('dtv')[:, :], A('dtv')[:, :], A('t32')[:, :], ALU.add)
        c.wait('scalar', 'vv', v_dt)
        v_ln = act(c, A('lndt')[:, :], A('dtv')[:, :], AF.Ln, sem='ss')
        c.wait('vector', 'ss', v_ln)
        v_adt = ts(c, A('adt')[:, :], A('dtv')[:, :], A('acol').ap()[:, 0:1], ALU.mult)
        for cl in range(4):
            sl = slice(cl * 128, (cl + 1) * 128)
            vs = vscan(c, A('acsf').ap()[:, sl], A('adt').ap()[:, sl], A('adt').ap()[:, sl], 0.0,
                       ALU.add, ALU.bypass)
        vg = vscan(c, A('acsg')[:, :], A('adt')[:, :], A('adt')[:, :], A('gcar').ap()[:, 0:1],
                   ALU.add, ALU.bypass)
        v_car = vcopy(c, A('gcar')[:, :], A('acsg').ap()[:, 511:512])
        bh = A('bhalf').ap()[:, sh * 512:(sh + 1) * 512]
        bl = A('blow').ap()[:, sh * 512:(sh + 1) * 512]
        c.wait('vector', 'sp%d' % sh, c.cnt.get('sp%d' % sh, 0))
        v1 = vcopy(c, bh, A('acsf')[:, :])
        v2 = tt(c, bl, A('acsf')[:, :], bh, ALU.subtract)
        c.wait('gpsimd', 'vv', v2)
        c.dma('gpsimd', Dr('acshi_d')[:, s * 512:(s + 1) * 512], bh, 'sp%d' % sh)
        vsp1 = c.dma('gpsimd', Dr('acslo_d')[:, s * 512:(s + 1) * 512], bl, 'sp%d' % sh)
        v3 = tt(c, A('t32')[:, :], A('lndt')[:, :], A('acsf')[:, :], ALU.subtract)
        c.wait('vector', 'sp%d' % sh, vsp1)
        v4 = vcopy(c, bh, A('t32')[:, :])
        v5 = tt(c, bl, A('t32')[:, :], bh, ALU.subtract)
        c.wait('gpsimd', 'vv', v5)
        c.dma('gpsimd', Dr('nahi_d')[:, s * 512:(s + 1) * 512], bh, 'sp%d' % sh)
        vsp2 = c.dma('gpsimd', Dr('nalo_d')[:, s * 512:(s + 1) * 512], bl, 'sp%d' % sh)
        c.wait('vector', 'sp%d' % sh, vsp2)
        v6 = vcopy(c, bh, A('acsg')[:, :])
        v7 = tt(c, bl, A('acsg')[:, :], bh, ALU.subtract)
        c.wait('gpsimd', 'vv', v7)
        c.dma('gpsimd', Dr('acsghi_d')[:, s * 512:(s + 1) * 512], bh, 'sp%d' % sh)
        c.dma('gpsimd', Dr('acsglo_d')[:, s * 512:(s + 1) * 512], bl, 'sp%d' % sh)
        c.dma('gpsimd', Dr('acs_d')[:, s * 512:(s + 1) * 512], A('acsf')[:, :], 'sp%d' % sh)
        v_sp = c.dma('gpsimd', Dr('dt_d')[:, s * 512:(s + 1) * 512], A('dtv')[:, :], 'sp%d' % sh)
        c.wait('vector', 'sp%d' % sh, v_sp)


def _p3(c, P, A, Dr, ph):
    ph.sb('bhi', (33, 2048), BF16)
    ph.sb('blo', (33, 2048), BF16)
    ph.sb('r00', (33, 2048), BF16)
    ph.sb('r01', (33, 2048), BF16)
    ph.sb('r10', (33, 2048), BF16)
    ph.sb('r11', (33, 2048), BF16)
    ph.sb('eacs', (64, 2048), BF16)
    ph.sb('acsf2', (32, 2048), F32)
    ph.sb('dtf2', (32, 2048), F32)
    ph.sb('xcf', (128, 2 * NCT * 128), BF16)
    ph.sb('xtok', (128, 2 * 2176), BF16)
    ph.sb('scob', (128, 128), BF16)
    ph.sb('gbuf', (128, 2048), BF16)
    ph.sb('eeb', (128, 2 * 2048), BF16)
    ph.sb('v2b2', (128, 2048), BF16)
    ph.sb('yoe', (128, 2048), BF16)
    ph.sb('dxt', (128, 2048), BF16)
    ph.sb('dxm', (128, 2048), BF16)
    ph.sb('ya', (128, 2048), F32)
    ph.sb('ypb', (128, 2048), BF16)
    ph.sb('srun', (128, 2048), F32)
    ph.sb('srunb', (128, 2048), BF16)
    ph.sb('stsb', (128, 2048), F32)
    ph.sb('dsel', (32, 2048), BF16)
    ph.sb('dexpt', (128, 2048), BF16)
    ph.sb('f2b', (32, 128), BF16)
    ph.sb('e32', (32, 128), F32)
    ph.sb('dcol', (32, 1), F32)
    ph.sb('trilRs', (127, 2048), BF16)
    ph.sb('trilLs', (127, 128), BF16)
    ph.sb('sel32s', (32, 2048), BF16)
    ph.sb('sel64s', (64, 2048), BF16)
    ph.sb('ones32', (32, 128), BF16)
    ph.sb('dhp', (128, 16), F32)
    ph.sb('gcol', (128, 1), F32)

    c.dma('sync', A('trilRs')[:, :], Dr('trilR')[:, :], 'c3')
    c.dma('sync', A('trilLs')[:, :], Dr('trilL')[:, :], 'c3')
    c.dma('sync', A('sel32s')[:, :], Dr('sel32')[:, :], 'c3')
    c.dma('sync', A('sel64s')[:, :], Dr('sel64')[:, :], 'c3')
    c.dma('sync', A('ones32')[:, :], Dr('ones32b')[:, :], 'c3')
    c.dma('sync', A('dhp')[:, :], Dr('d_hp')[:, :], 'c3')
    c.dma('sync', A('gcol')[:, :], Dr('g_state')[:, :], 'c3')
    for rn, sn in (('r00', 'sel_h0'), ('r01', 'sel_h0'), ('r10', 'sel_h1'), ('r11', 'sel_h1')):
        c.dma('sync', A(rn)[0:32, :], Dr(sn)[:, :], 'c3')
    c.dma('sync', A('bhi')[0:32, :], Dr('nahi_d')[:, :], 'c3')
    c.dma('sync', A('blo')[0:32, :], Dr('nalo_d')[:, :], 'c3')
    c.dma('sync', A('eacs')[0:32, :], Dr('acshi_d')[:, :], 'c3')
    c.dma('sync', A('eacs')[32:64, :], Dr('acslo_d')[:, :], 'c3')
    c.dma('sync', A('acsf2')[:, :], Dr('acs_d')[:, :], 'c3')
    v_c3 = c.dma('sync', A('dtf2')[:, :], Dr('dt_d')[:, :], 'c3')
    vmemset(c, A('bhi')[32:33, :], 1.0)
    vmemset(c, A('blo')[32:33, :], 1.0)
    vmemset(c, A('srun')[:, :], 0.0)
    vmemset(c, A('srunb')[:, :], 0.0)
    c.wait('vector', 'c3', v_c3)
    v_dx = vcopy(c, A('dxt').ap().rearrange("p (a l) -> p a l", a=16),
                 A('dhp').ap().unsqueeze(2).broadcast_to([128, 16, 128]), sem='vv')
    c.wait('tensor', 'c3', v_c3)
    c.wait('scalar', 'c3', v_c3)

    rhs_free = {'r00': 0, 'r01': 0, 'r10': 0, 'r11': 0}
    RD3 = {}
    v_yoffp = [0, 0]; v_dxmp = [0, 0]; v_stsp = [0, 0]; v_v2cp = [0, 0]
    a1p = [0, 0]; v_a1 = 0; v_a2 = 0; v_v2c = 0; v_scan = 0
    v_gev = [0, 0]; v_gtt1 = 0; v_diag = [0, 0]; v_eeev = [0, 0]
    v_ypsp = 0; v_f2p = 0; v_dsp = 0; stev = 0; dexpev = [0]

    def p3reads(ch):
        pp = ch % 2
        sl = slice(ch * 128, (ch + 1) * 128)
        c.wait('sync', 'mm', v_yoffp[pp])
        c.wait('sync', 'vv', max(v_dxmp[pp], v_dx))
        xf = c.dma('sync', A('xcf').ap()[:, pp * NCT * 128:(pp + 1) * NCT * 128].rearrange("p (a l) -> p a l", a=NCT),
                   Dr('xconv_d').rearrange("(a p) t -> a p t", p=128)[:, :, sl].transpose([1, 0, 2]), 'xf')
        c.wait('sync', 'mm', v_stsp[pp])
        c.wait('sync', 'vv', v_v2cp[pp])
        xk = c.dmat('sync', A('xtok')[:, pp * 2176:(pp + 1) * 2176], Dr('xconv_d')[0:2176, sl], 'xk2')
        RD3[ch] = [xf, xk, 0]

    def p3reads_fl(ch):
        sl = slice(ch * 128, (ch + 1) * 128)
        for half, (rh, rl) in enumerate((('r00', 'r01'), ('r10', 'r11'))):
            h0 = half * 16
            c.wait('sync', 'mm', rhs_free[rh])
            c.dma('sync', A(rh).ap()[32:33, :].rearrange("p (a b) -> p a b", a=16),
                  Dr('acshi_d')[h0:h0 + 16, sl].unsqueeze(0), 'fl')
            c.dma('sync', A(rl).ap()[32:33, :].rearrange("p (a b) -> p a b", a=16),
                  Dr('acslo_d')[h0:h0 + 16, sl].unsqueeze(0), 'fl')
        RD3[ch][2] = c.cnt['fl']

    p3reads(0)
    p3reads_fl(0)
    for ch in range(NCH):
        pp = ch % 2
        sl = slice(ch * 128, (ch + 1) * 128)
        if ch + 1 < NCH:
            p3reads(ch + 1)
        xcf_ = lambda a, b=None: A('xcf').ap()[:, pp * NCT * 128 + a * 128:pp * NCT * 128 + (a if b is None else b) * 128 + (128 if b is None else 0)]
        xtok_ = A('xtok').ap()[:, pp * 2176:(pp + 1) * 2176]
        eeb_ = A('eeb').ap()[:, pp * 2048:(pp + 1) * 2048]
        c.wait('tensor', 'xf', RD3[ch][0])
        c.wait('tensor', 'vv', v_a1)
        vsc = mm(c, P[4][:, 0:128], xcf_(16), xcf_(17), start=True, stop=True, sem='mm')
        c.wait('vector', 'mm', vsc)
        v_scev = vcopy(c, A('scob')[:, :], P[4][:, 0:128], sem='vv')
        for half in range(2):
            rh, rl = (('r00', 'r01'), ('r10', 'r11'))[half]
            c.wait('tensor', 'fl', RD3[ch][2])
            if half == 0:
                c.wait('tensor', 'vv', v_scev)
            for j in range(4):
                mm(c, P[4 + j][:, :], A('bhi').ap()[:, sl], A(rh)[:, j * 512:(j + 1) * 512],
                   start=True, stop=False, sem=None)
                mm(c, P[4 + j][:, :], A('blo').ap()[:, sl], A(rl)[:, j * 512:(j + 1) * 512],
                   start=False, stop=False, sem=None)
                vgb = mm(c, P[4 + j][:, :], A('trilLs')[:, :], A('trilRs')[:, j * 512:(j + 1) * 512],
                         start=False, stop=True, sem='mm')
            rhs_free[rh] = vgb
            rhs_free[rl] = vgb
            c.wait('scalar', 'mm', vgb)
            c.wait('scalar', 'mm', v_diag[1 - half] if half == 1 else v_diag[1])
            c.wait('scalar', 'vv', v_gtt1 if half == 0 else 0)
            for j in range(4):
                v_gev[half] = act(c, A('gbuf')[:, j * 512:(j + 1) * 512], P[4 + j][:, :], AF.Exp, sem='ss')
            c.wait('vector', 'ss', v_gev[half])
            v_gtt = tt(c, A('gbuf').ap().rearrange("p (h l) -> p h l", h=16),
                       A('gbuf').ap().rearrange("p (h l) -> p h l", h=16),
                       A('scob').ap().unsqueeze(1).broadcast_to([128, 16, 128]), ALU.mult, sem='vv')
            if half == 1:
                v_gtt1 = v_gtt
            c.wait('tensor', 'vv', v_gtt)
            c.wait('tensor', 'xk2', RD3[ch][1])
            if half == 0:
                c.wait('tensor', 'vv', dexpev[0])
            for hl in range(16):
                h = half * 16 + hl
                ct = h // 2
                po = (ct % 4) * 128
                rows = slice(0, 64) if h % 2 == 0 else slice(64, 128)
                v_dg = mm(c, P[ct // 4][rows, po:po + 128], xtok_[:, h * 64:(h + 1) * 64],
                          A('gbuf')[:, hl * 128:(hl + 1) * 128],
                          start=True, stop=True, sem='mm')
            v_diag[half] = v_dg
        if ch + 1 < NCH:
            p3reads_fl(ch + 1)
        c.wait('tensor', 'ss', v_gev[1])
        for ct in range(16):
            v_ee = mm(c, P[4 + ct // 4][:, (ct % 4) * 128:(ct % 4) * 128 + 128],
                      A('sel64s')[:, ct * 128:(ct + 1) * 128], A('eacs').ap()[:, sl],
                      start=True, stop=True, sem='mm')
        c.wait('scalar', 'mm', v_ee)
        c.wait('scalar', 'vv', a1p[pp])
        for j in range(4):
            v_eev = act(c, A('eeb')[:, pp * 2048 + j * 512:pp * 2048 + (j + 1) * 512], P[4 + j][:, :], AF.Exp, sem='ss')
        v_eeev[pp] = v_eev
        v_dxm = tt(c, A('dxm')[:, :], A('xcf').ap()[:, pp * NCT * 128:pp * NCT * 128 + 2048],
                   A('dxt').ap()[:, :], ALU.mult, sem='vv')
        v_dxmp[pp] = v_dxm
        c.wait('vector', 'mm', v_diag[1])
        for j in range(4):
            v_a2 = stt(c, A('ya')[:, j * 512:(j + 1) * 512], A('dxm').ap()[:, j * 512:(j + 1) * 512],
                       1.0, P[j][:, :], ALU.mult, ALU.add, sem='vv')
        aend = A('acsf2').ap()[:, ch * 128 + 127:ch * 128 + 128]
        c.wait('scalar', 'vv', v_f2p)
        v_f2e = act(c, A('e32')[:, :], A('acsf2').ap()[:, sl], AF.Exp, scale=-1.0, bias=aend, sem='ss')
        c.wait('vector', 'ss', v_f2e)
        v_f2 = tt(c, A('f2b')[:, :], A('e32')[:, :], A('dtf2').ap()[:, sl], ALU.mult, sem='vv')
        v_f2p = v_f2
        c.wait('tensor', 'vv', max(v_f2, v_a2))
        for j in range(4):
            v_f2mm = mm(c, P[j][:, :], A('f2b')[:, :], A('sel32s')[:, j * 512:(j + 1) * 512],
                        start=True, stop=True, sem='mm')
        c.wait('vector', 'mm', v_f2mm)
        for j in range(4):
            v_v2c = stt(c, A('v2b2')[:, j * 512:(j + 1) * 512], xtok_[:, j * 512:(j + 1) * 512],
                        1.0, P[j][:, :], ALU.mult, ALU.mult, sem='vv')
        v_v2cp[pp] = v_v2c
        c.wait('tensor', 'ss', v_eev)
        c.wait('tensor', 'vv', v_v2c)
        for j in range(4):
            v_sts = mm(c, P[4 + j][:, :], xtok_[:, 2048:2176], A('v2b2')[:, j * 512:(j + 1) * 512],
                       start=True, stop=True, sem='mm')
        v_stsp[pp] = v_sts
        c.wait('vector', 'mm', v_sts)
        for j in range(4):
            stev = vcopy(c, A('stsb')[:, j * 512:(j + 1) * 512], P[4 + j][:, :], sem='vv')
        c.wait('scalar', 'vv', v_dsp)
        v_de = act(c, A('dcol')[:, :], aend, AF.Exp, sem='ss')
        c.wait('vector', 'ss', v_de)
        v_ds = ts(c, A('dsel')[:, :], A('sel32s')[:, :], A('dcol').ap()[:, 0:1], ALU.mult, sem='vv')
        v_dsp = v_ds
        c.wait('tensor', 'vv', max(v_ds, v_v2c))
        for j in range(4):
            v_dexpmm = mm(c, P[j][:, :], A('ones32')[:, :], A('dsel')[:, j * 512:(j + 1) * 512],
                          start=True, stop=True, sem='mm')
        c.wait('vector', 'mm', v_dexpmm)
        for j in range(4):
            v_dxe = vcopy(c, A('dexpt')[:, j * 512:(j + 1) * 512], P[j][:, :], sem='vv')
        dexpev[0] = v_dxe
        c.wait('tensor', 'vv', max(stev, v_scan))
        for ct in range(16):
            v_yoff = mm(c, P[4 + ct // 4][:, (ct % 4) * 128:(ct % 4) * 128 + 128],
                        A('srunb').ap()[:, ct * 128:(ct + 1) * 128], xcf_(17),
                        start=True, stop=True, sem='mm')
        v_yoffp[pp] = v_yoff
        c.wait('vector', 'mm', v_yoff)
        for j in range(4):
            v_a1 = stt(c, A('yoe')[:, j * 512:(j + 1) * 512], eeb_[:, j * 512:(j + 1) * 512],
                       1.0, P[4 + j][:, :], ALU.mult, ALU.mult, sem='vv')
        a1p[pp] = v_a1
        c.wait('vector', 'yp', v_ypsp)
        v_a3 = tt(c, A('ypb')[:, :], A('ya')[:, :], A('yoe')[:, :], ALU.add, sem='vv')
        c.wait('gpsimd', 'vv', v_a3)
        v_ypsp = c.dma('gpsimd', Dr('ypre_d').rearrange("(a p) t -> a p t", p=128)[:, :, sl].transpose([1, 0, 2]),
                       A('ypb').ap().rearrange("p (a l) -> p a l", a=16), 'yp')
        tt(c, A('ya')[:, :], A('srun')[:, :], A('dexpt')[:, :], ALU.mult)
        tt(c, A('srun')[:, :], A('ya')[:, :], A('stsb')[:, :], ALU.add)
        v_scan = vcopy(c, A('srunb')[:, :], A('srun')[:, :], sem='vv')
        del RD3[ch]
    v_g = ts(c, A('ya')[:, :], A('srun')[:, :], A('gcol').ap()[:, 0:1], ALU.mult, sem='vv')
    c.wait('gpsimd', 'vv', v_g)
    c.dma('gpsimd', Dr('st_in')[:, :], A('ya')[:, :], 'stw')


def _p4(c, P, A, Dr, ph):
    ph.sb('sinit', (128, 2048), F32)
    ph.sb('sinitb', (128, 2048), BF16)
    ph.sb('ws', (128, 16 * 1024), BF16)
    ph.sb('eag', (64, 2048), BF16)
    ph.sb('sel64s2', (64, 2048), BF16)
    ph.sb('ypt', (128, 2 * 2048), BF16)
    ph.sb('zst', (128, 2 * 2048), BF16)
    ph.sb('ctb', (128, 2 * 128), BF16)
    ph.sb('egb', (128, 2 * 2048), BF16)
    ph.sb('ycb', (128, 2048), BF16)
    ph.sb('ygz', (128, 2 * 2048), BF16)
    ph.sb('sqb', (128, 2 * 2048), BF16)
    ph.sb('xmt', (128, 2 * 1024), F32)
    ph.sb('rst', (128, 2 * 1024), F32)
    ph.sb('xdl', (128, 2 * 1024), F32)
    ph.sb('xev', (128, 2 * 1024), F32)
    ph.sb('guse', (128, 1), F32)
    ph.sb('ngw', (128, 16), F32)
    ph.sb('mrow', (1, 2 * 128), F32)

    v_cc = c.op('gpsimd', lambda e: e.collective_compute(
        "AllReduce", ALU.add, replica_groups=GROUPS,
        ins=[c.dr['st_in'].opt()], outs=[c.dr['st_out'].opt()]), sem='cc')
    c.wait('gpsimd', 'cc', v_cc)
    v_si = c.dma('gpsimd', A('sinit')[:, :], Dr('st_out')[:, :], 'si')
    c.dma('sync', A('guse')[:, :], Dr('g_use')[:, :], 'c4')
    c.dma('sync', A('ngw')[:, :], Dr('ngw_col')[:, :], 'c4')
    c.dma('sync', A('sel64s2')[:, :], Dr('sel64')[:, :], 'c4')
    c.dma('sync', A('eag')[0:32, :], Dr('acsghi_d')[:, :], 'c4')
    v_c4 = c.dma('sync', A('eag')[32:64, :], Dr('acsglo_d')[:, :], 'c4')
    for k in range(16):
        c.dma('gpsimd', A('ws')[:, k * 1024:(k + 1) * 1024], Dr('woutT')[k * 128:(k + 1) * 128, :], 'wo')
    v_wo = c.cnt['wo']
    c.wait('vector', 'si', v_si)
    c.wait('vector', 'c4', v_c4)
    ts(c, A('sinit')[:, :], A('sinit')[:, :], A('guse').ap()[:, 0:1], ALU.mult)
    v_sib = vcopy(c, A('sinitb')[:, :], A('sinit')[:, :], sem='vv')
    c.wait('vector', 'wo', v_wo)
    for k in range(16):
        ts(c, A('ws')[:, k * 1024:(k + 1) * 1024], A('ws')[:, k * 1024:(k + 1) * 1024],
           A('ngw').ap()[:, k:k + 1], ALU.mult)
        v_wf = tt(c, A('ws')[:, k * 1024:(k + 1) * 1024], A('ws')[:, k * 1024:(k + 1) * 1024],
                  A('gab').ap(), ALU.mult, sem='vv')

    b3p = [0, 0]; ycp = [0, 0]; o2p = [0, 0]; sqmmp = [0, 0]; b1p = [0, 0]; spillp = [0, 0]
    RD = {}; EG = {}; YC = {}

    def reads(t):
        pp = t % 2
        tsl = slice(t * 128, (t + 1) * 128)
        c.wait('sync', 'vv', b3p[pp])
        yp = c.dma('sync', A('ypt').ap()[:, pp * 2048:(pp + 1) * 2048].rearrange("p (a l) -> p a l", a=16),
                   Dr('ypre_d').rearrange("(a p) t -> a p t", p=128)[:, :, tsl].transpose([1, 0, 2]), 'yp2')
        zs = c.dma('sync', A('zst').ap()[:, pp * 2048:(pp + 1) * 2048].rearrange("p (a l) -> p a l", a=16),
                   Dr('zsilu_d').rearrange("(a p) t -> a p t", p=128)[:, :, tsl].transpose([1, 0, 2]), 'zs2')
        c.wait('sync', 'mm', ycp[pp])
        ct = c.dma('sync', A('ctb')[:, pp * 128:(pp + 1) * 128], Dr('xconv_d')[2176:2304, tsl], 'ct2')
        c.wait('sync', 'vv', o2p[pp])
        xm = c.dma('sync', A('xmt')[:, pp * 1024:(pp + 1) * 1024], Dr('x_tok')[tsl, :], 'xk4')
        rr = c.dma('sync', A('rst')[:, pp * 1024:(pp + 1) * 1024], Dr('resid_d')[tsl, :], 'rr')
        RD[t] = (yp, zs, ct, xm, rr)

    def front(t, gate_vv):
        pp = t % 2
        tsl = slice(t * 128, (t + 1) * 128)
        c.wait('tensor', 'c4', v_c4)
        c.wait('tensor', 'vv', gate_vv)
        for ck in range(16):
            eg = mm(c, P[ck // 4][:, (ck % 4) * 128:(ck % 4) * 128 + 128],
                    A('sel64s2')[:, ck * 128:(ck + 1) * 128], A('eag').ap()[:, tsl],
                    start=True, stop=True, sem='mm')
        c.wait('scalar', 'mm', eg)
        c.wait('scalar', 'vv', b1p[pp])
        for j in range(4):
            egev = act(c, A('egb')[:, pp * 2048 + j * 512:pp * 2048 + (j + 1) * 512], P[j][:, :], AF.Exp, sem='ss')
        return eg, egev

    def ycorr(t, gate_vv):
        pp = t % 2
        c.wait('tensor', 'ct2', RD[t][2])
        c.wait('tensor', 'vv', max(v_sib, v_wf))
        c.wait('tensor', 'vv', gate_vv)
        for ck in range(16):
            yc = mm(c, P[4 + ck // 4][:, (ck % 4) * 128:(ck % 4) * 128 + 128],
                    A('sinitb')[:, ck * 128:(ck + 1) * 128], A('ctb')[:, pp * 128:(pp + 1) * 128],
                    start=True, stop=True, sem='mm')
        ycp[pp] = yc
        return yc

    reads(0)
    EG[0] = front(0, 0)
    YC[0] = ycorr(0, 0)
    for t in range(NTT):
        pp = t % 2
        tsl = slice(t * 128, (t + 1) * 128)
        if t + 1 < NTT:
            reads(t + 1)
        eg_t, egev_t = EG.pop(t)
        yc_t = YC.pop(t)
        c.wait('vector', 'mm', yc_t)
        c.wait('vector', 'ss', egev_t)
        c.wait('vector', 'yp2', RD[t][0])
        c.wait('vector', 'zs2', RD[t][1])
        for j in range(4):
            b1 = stt(c, A('ycb')[:, j * 512:(j + 1) * 512],
                     A('egb').ap()[:, pp * 2048 + j * 512:pp * 2048 + (j + 1) * 512],
                     1.0, P[4 + j][:, :], ALU.mult, ALU.mult, sem='vv')
        b1p[pp] = b1
        ypt_ = A('ypt').ap()[:, pp * 2048:(pp + 1) * 2048]
        zst_ = A('zst').ap()[:, pp * 2048:(pp + 1) * 2048]
        ygz_ = A('ygz').ap()[:, pp * 2048:(pp + 1) * 2048]
        tt(c, ypt_, ypt_, A('ycb')[:, :], ALU.add)
        b3 = tt(c, ygz_, ypt_, zst_, ALU.mult, sem='vv')
        b3p[pp] = b3
        c.wait('scalar', 'vv', b3)
        c.wait('scalar', 'mm', sqmmp[pp])
        sq = act(c, A('sqb')[:, pp * 2048:(pp + 1) * 2048], ygz_, AF.Square, sem='ss')
        c.wait('tensor', 'vv', b3)
        c.wait('tensor', 'ss', egev_t)
        for j in range(2):
            for ck in range(16):
                wmm = mm(c, P[j][:, :], ygz_[:, ck * 128:(ck + 1) * 128],
                         A('ws')[:, ck * 1024 + j * 512:ck * 1024 + j * 512 + 512],
                         start=(ck == 0), stop=(ck == 15), sem='mm')
        c.wait('tensor', 'ss', sq)
        for ck in range(16):
            sqmm = mm(c, P[6][0:1, 0:128], A('onesb')[:, 0:1],
                      A('sqb')[:, pp * 2048 + ck * 128:pp * 2048 + (ck + 1) * 128],
                      start=(ck == 0), stop=(ck == 15), sem='mm')
        sqmmp[pp] = sqmm
        c.wait('vector', 'mm', sqmm)
        mr = vcopy(c, A('mrow')[0:1, pp * 128:(pp + 1) * 128], P[6][0:1, 0:128], sem='vv')
        m2 = ts(c, A('mrow')[0:1, pp * 128:(pp + 1) * 128], A('mrow')[0:1, pp * 128:(pp + 1) * 128],
                1.0 / DIN, ALU.mult, EPS, ALU.add, sem='vv')
        c.wait('vector', 'mm', wmm)
        for j in range(2):
            wev = vcopy(c, A('xev')[:, pp * 1024 + j * 512:pp * 1024 + j * 512 + 512], P[j][:, :], sem='vv')
        c.wait('scalar', 'vv', m2)
        rc = sinv(c, A('mrow').ap()[0:1, pp * 128:(pp + 1) * 128],
                  A('mrow').ap()[0:1, pp * 128:(pp + 1) * 128], -0.5, sem='ss')
        if t + 1 < NTT:
            EG[t + 1] = front(t + 1, wev)
        c.wait('tensor', 'ss', rc)
        r3t = mm(c, P[6][:, 256:257], A('mrow')[0:1, pp * 128:(pp + 1) * 128], A('onesf')[0:1, 0:1],
                 start=True, stop=True, sem='mm')
        c.wait('vector', 'mm', r3t)
        r3v = vcopy(c, A('r3').ap()[:, t:t + 1], P[6][:, 256:257], sem='vv')
        if t + 1 < NTT:
            YC[t + 1] = ycorr(t + 1, r3v)
        c.wait('vector', 'xk4', RD[t][3])
        c.wait('vector', 'rr', RD[t][4])
        c.wait('vector', 'xm', spillp[pp])
        for j in range(2):
            stt(c, A('xdl')[:, pp * 1024 + j * 512:pp * 1024 + j * 512 + 512],
                A('xev').ap()[:, pp * 1024 + j * 512:pp * 1024 + j * 512 + 512],
                A('r3').ap()[:, t:t + 1],
                A('rst').ap()[:, pp * 1024 + j * 512:pp * 1024 + j * 512 + 512], ALU.mult, ALU.add)
        o2 = tt(c, A('xdl').ap()[:, pp * 1024:(pp + 1) * 1024], A('xmt').ap()[:, pp * 1024:(pp + 1) * 1024],
                A('xdl').ap()[:, pp * 1024:(pp + 1) * 1024], ALU.add, sem='vv')
        o2p[pp] = o2
        c.wait('gpsimd', 'vv', o2)
        spillp[pp] = c.dma('gpsimd', Dr('xmid_d')[tsl, :], A('xdl').ap()[:, pp * 1024:(pp + 1) * 1024], 'xm')
        del RD[t]


def _p5(c, P, A, Dr, ph):
    ph.sb('xmt2', (128, 4 * 1024), F32)
    ph.sb('h2f', (128, 4 * 1024), F32)
    ph.sb('h2t', (128, 4 * 1024), BF16)
    ph.sb('w1s', (128, 8 * 512), BF16)
    ph.sb('w1Ls', (128, 8 * 512), BF16)
    ph.sb('w2s', (128, 4 * 64), BF16)
    ph.sb('sib', (64, 1024), BF16)
    ph.sb('o64', (64, 1), BF16)
    ph.sb('meanT', (128, 8), BF16)
    ph.sb('b1e', (128, 4), F32)
    ph.sb('b2b', (128, 64), F32)
    ph.sb('itc', (128, 1), F32)
    ph.sb('taus', (1, 1), F32)
    ph.sb('b2r', (1, 64), F32)
    ph.sb('hro', (128, 4 * 512), BF16)
    ph.sb('h2T', (128, 8 * 512), BF16)
    ph.sb('rlog', (128, 4 * 64), F32)
    ph.sb('m8', (128, 4 * 8), F32)
    ph.sb('msk', (128, 4 * 64), F32)
    ph.sb('erow', (128, 4 * 64), F32)
    ph.sb('alf', (128, 4 * 64), F32)
    ph.sb('identS', (128, 128), F32)
    ph.sb('h2r', (128, 4 * 1024), BF16)
    ph.sb('slin', (64, 1040), F32)

    for k in range(8):
        c.dma('gpsimd', A('w1s')[:, k * 512:(k + 1) * 512], Dr('w1')[k * 128:(k + 1) * 128, :], 'w1l')
        c.dma('gpsimd', A('w1Ls')[:, k * 512:(k + 1) * 512], Dr('w1')[1024 + k * 128:1024 + (k + 1) * 128, :], 'w1l')
    for k in range(4):
        c.dma('gpsimd', A('w2s')[:, k * 64:(k + 1) * 64], Dr('w2')[k * 128:(k + 1) * 128, :], 'w1l')
    c.dma('gpsimd', A('sib')[:, :], Dr('slot_init')[:, :], 'w1l')
    v_w1 = c.cnt['w1l']
    c.dma('sync', A('taus')[:, :], Dr('tau')[:, :], 'c5')
    c.dma('sync', A('b2r')[:, :], Dr('b2_row')[:, :], 'c5')
    c.dma('sync', A('identS')[:, :], Dr('identf')[:, :], 'c5')
    v_c5 = c.dma('sync', A('b1e')[:, 0:4], Dr('b1_col')[:, :], 'c5')
    v_o64 = vmemset(c, A('o64')[:, :], 1.0 / 64, sem='vv')
    c.wait('scalar', 'c5', v_c5)
    v_at = act(c, A('taus')[:, :], A('taus')[:, :], AF.Abs, sem='ss')
    c.wait('vector', 'ss', v_at)
    v_ta = ts(c, A('taus')[:, :], A('taus')[:, :], 0.1, ALU.add, sem='vv')
    c.wait('scalar', 'vv', v_ta)
    v_it = sinv(c, A('taus').ap()[:, :], A('taus').ap()[:, :], -1.0, sem='ss')
    c.wait('tensor', 'ss', v_it)
    v_itmm = mm(c, P[4][:, 300:301], A('onesf')[0:1, :], A('taus')[0:1, 0:1], start=True, stop=True, sem='mm')
    c.wait('vector', 'mm', v_itmm)
    v_itc = vcopy(c, A('itc')[:, :], P[4][:, 300:301], sem='vv')
    c.wait('tensor', 'vv', v_itc)
    v_b2mm = mm(c, P[4][:, 320:384], A('onesf')[0:1, :], A('b2r')[0:1, :], start=True, stop=True, sem='mm')
    c.wait('vector', 'mm', v_b2mm)
    vcopy(c, A('b2b')[:, :], P[4][:, 320:384])
    v_b2b = ts(c, A('b2b')[:, :], A('b2b')[:, :], A('itc').ap()[:, 0:1], ALU.mult, sem='vv')
    c.wait('tensor', 'w1l', v_w1)
    c.wait('tensor', 'vv', max(v_b2b, v_o64))
    for k in range(8):
        v_mmn = mm(c, P[4][:, 400 + k:401 + k], A('sib')[:, k * 128:(k + 1) * 128], A('o64')[:, 0:1],
                   start=True, stop=True, sem='mm')
    c.wait('vector', 'mm', v_mmn)
    v_mt = vcopy(c, A('meanT')[:, :], P[4][:, 400:408], sem='vv')
    c.wait('tensor', 'vv', v_mt)
    for m in range(4):
        for k in range(8):
            v_b1mm = mm(c, P[5][:, 400 + m:401 + m], A('w1Ls')[:, k * 512 + m * 128:k * 512 + (m + 1) * 128],
                        A('meanT')[:, k:k + 1], start=(k == 0), stop=(k == 7), sem='mm')
    c.wait('vector', 'mm', v_b1mm)
    c.wait('vector', 'c5', v_c5)
    v_b1e = stt(c, A('b1e')[:, 0:4], P[5][:, 400:404], 1.0, A('b1e').ap()[:, 0:4], ALU.mult, ALU.add, sem='vv')

    v_h2sp = 0
    v_gel = 0; v_slmm = 0; v_alsp = 0
    v_xread = 0; v_rl = 0
    gel_ev = {}
    for s in range(NSLAB):
        t0 = s * 4
        c.wait('sync', 'vv', v_xread)
        for tl in range(4):
            t = t0 + tl
            v_xm = c.dma('sync', A('xmt2')[:, tl * D:(tl + 1) * D],
                         Dr('xmid_d')[t * 128:(t + 1) * 128, :], 'xm2')
        c.wait('scalar', 'xm2', v_xm)
        for tl in range(4):
            t = t0 + tl
            xs = A('xmt2')[:, tl * D:(tl + 1) * D]
            act(c, xs, xs, AF.Copy, accum_out=A('stat_sum').ap()[:, t:t + 1])
            v_st = act(c, A('h2f')[:, tl * D:(tl + 1) * D], xs, AF.Square,
                       accum_out=A('stat_sq').ap()[:, t:t + 1], sem='ss')
        c.wait('vector', 'ss', v_st)
        ts(c, A('tiny')[:, 32:36], A('stat_sum').ap()[:, t0:t0 + 4], 1.0 / D, ALU.mult)
        ts(c, A('tiny')[:, 36:40], A('stat_sq').ap()[:, t0:t0 + 4], 1.0 / D, ALU.mult)
        tt(c, A('tiny')[:, 40:44], A('tiny')[:, 32:36], A('tiny')[:, 32:36], ALU.mult)
        tt(c, A('tiny')[:, 44:48], A('tiny')[:, 36:40], A('tiny')[:, 40:44], ALU.subtract)
        v_var = ts(c, A('tiny')[:, 44:48], A('tiny')[:, 44:48], EPS, ALU.add, sem='vv')
        c.wait('scalar', 'vv', v_var)
        v_sq = sinv(c, A('r2').ap()[:, t0:t0 + 4], A('tiny').ap()[:, 44:48], -0.5, sem='ss')
        c.wait('vector', 'ss', v_sq)
        stt(c, A('nmr1').ap()[:, t0:t0 + 4], A('tiny').ap()[:, 32:36], -1.0,
            A('r2').ap()[:, t0:t0 + 4], ALU.mult, ALU.mult)
        c.wait('vector', 'h2s', v_h2sp)
        for tl in range(4):
            v_xread = ts(c, A('h2f')[:, tl * D:(tl + 1) * D], A('xmt2')[:, tl * D:(tl + 1) * D],
                         A('r2').ap()[:, t0 + tl:t0 + tl + 1], ALU.mult,
                         A('nmr1').ap()[:, t0 + tl:t0 + tl + 1], ALU.add, sem='vv')
        for tl in range(4):
            tt(c, A('h2f')[:, tl * D:(tl + 1) * D], A('h2f')[:, tl * D:(tl + 1) * D], A('cb1b').ap(), ALU.mult)
        for tl in range(4):
            v_lnt = tt(c, A('h2t')[:, tl * D:(tl + 1) * D], A('h2f')[:, tl * D:(tl + 1) * D],
                       A('sbb').ap(), ALU.add, sem='vv')
        c.wait('gpsimd', 'vv', v_lnt)
        for tl in range(4):
            t = t0 + tl
            v_h2sp = c.dma('gpsimd', Dr('h2_d')[t * 128:(t + 1) * 128, :],
                           A('h2t')[:, tl * D:(tl + 1) * D], 'h2s')
        c.wait('sync', 'h2s', v_h2sp)
        c.wait('sync', 'mm', v_gel)
        vq = 0
        for k in range(8):
            vq = c.dmat('sync', A('h2T')[:, k * SLAB:(k + 1) * SLAB],
                        Dr('h2_d')[s * SLAB:(s + 1) * SLAB, k * 128:(k + 1) * 128], 'h2t2')
        c.wait('tensor', 'h2t2', vq)
        if s == 0:
            c.wait('tensor', 'vv', v_b1e)
            c.wait('tensor', 'c5', v_c5)
        for m in range(4):
            bk = m % 4
            c.wait('tensor', 'ss', gel_ev.get(bk, 0))
            for k in range(8):
                v_w1mm = mm(c, P[bk][:, :], A('w1s')[:, k * 512 + m * 128:k * 512 + (m + 1) * 128],
                            A('h2T')[:, k * SLAB:(k + 1) * SLAB], start=(k == 0), stop=(k == 7), sem='mm')
            c.wait('scalar', 'mm', v_w1mm)
            v_gl = act(c, A('hro')[:, m * 512:(m + 1) * 512], P[bk][:, :], AF.Gelu,
                       bias=A('b1e').ap()[:, m:m + 1], sem='ss')
            gel_ev[bk] = v_gl
        v_gel = v_w1mm
        for tl in range(4):
            c.wait('tensor', 'ss', v_gl)
            c.wait('tensor', 'vv', v_rl)
            if s == 0 and tl == 0:
                c.wait('tensor', 'vv', max(v_b2b, v_mt))
            for m in range(4):
                v_w2mm = mm(c, P[4][:, 0:64], A('hro')[:, m * 512 + tl * 128:m * 512 + (tl + 1) * 128],
                            A('w2s')[:, m * 64:(m + 1) * 64], start=(m == 0), stop=(m == 3), sem='mm')
            c.wait('vector', 'mm', v_w2mm)
            if s == 0 and tl == 0:
                c.wait('vector', 'vv', v_b2b)
            v_rl = stt(c, A('rlog')[:, tl * 64:(tl + 1) * 64], P[4][:, 0:64], A('itc').ap()[:, 0:1],
                       A('b2b').ap()[:, :], ALU.mult, ALU.add, sem='vv')
        for tl in range(4):
            vmax8(c, A('m8')[:, tl * 8:(tl + 1) * 8], A('rlog')[:, tl * 64:(tl + 1) * 64])
        for tl in range(4):
            ts(c, A('sq2').ap()[:, t0 + tl:t0 + tl + 1], A('m8').ap()[:, tl * 8:tl * 8 + 1], -1.0, ALU.mult)
        for tl in range(4):
            v_msk = ts(c, A('msk')[:, tl * 64:(tl + 1) * 64], A('rlog')[:, tl * 64:(tl + 1) * 64],
                       A('m8').ap()[:, tl * 8 + 7:tl * 8 + 8], ALU.is_ge, sem='vv')
        c.wait('scalar', 'vv', v_msk)
        for tl in range(4):
            v_er = act(c, A('erow')[:, tl * 64:(tl + 1) * 64], A('rlog')[:, tl * 64:(tl + 1) * 64],
                       AF.Exp, bias=A('sq2').ap()[:, t0 + tl:t0 + tl + 1], sem='ss')
        c.wait('vector', 'ss', v_er)
        for tl in range(4):
            tt(c, A('erow')[:, tl * 64:(tl + 1) * 64], A('erow')[:, tl * 64:(tl + 1) * 64],
               A('msk')[:, tl * 64:(tl + 1) * 64], ALU.mult)
        for tl in range(4):
            v_sumr = vreduce(c, A('sq3').ap()[:, t0 + tl:t0 + tl + 1],
                             A('erow').ap()[:, tl * 64:(tl + 1) * 64], ALU.add, sem='vv')
        c.wait('scalar', 'vv', v_sumr)
        v_rs = sinv(c, A('r3').ap()[:, t0:t0 + 4], A('sq3').ap()[:, t0:t0 + 4], -1.0, sem='ss')
        c.wait('vector', 'ss', v_rs)
        for tl in range(4):
            ts(c, A('alf')[:, tl * 64:(tl + 1) * 64], A('erow')[:, tl * 64:(tl + 1) * 64],
               A('r3').ap()[:, t0 + tl:t0 + tl + 1], ALU.mult)
        v_al = vcopy(c, A('alpha').ap()[:, t0 * 64:(t0 + 4) * 64], A('alf')[:, 0:256], sem='vv')
        c.wait('gpsimd', 'vv', v_al)
        for tl in range(4):
            t = t0 + tl
            v_alsp = c.dma('gpsimd', Dr('alp_d')[t * 128:(t + 1) * 128, 0:64],
                           A('alpha').ap()[:, t * 64:(t + 1) * 64], 'als')
        c.wait('sync', 'mm', v_slmm)
        for tl in range(4):
            t = t0 + tl
            v_h2rd = c.dma('sync', A('h2r')[:, tl * 1024:(tl + 1) * 1024],
                           Dr('h2_d')[t * 128:(t + 1) * 128, :], 'h2r')
        c.wait('tensor', 'h2r', v_h2rd)
        c.wait('tensor', 'vv', v_al)
        for tl in range(4):
            t = t0 + tl
            mm(c, P[5][0:64, :], A('alpha').ap()[:, t * 64:(t + 1) * 64], A('h2r')[:, tl * 1024:tl * 1024 + 512],
               start=(t == 0), stop=(t == NTT - 1), sem=None)
            mm(c, P[6][0:64, :], A('alpha').ap()[:, t * 64:(t + 1) * 64], A('h2r')[:, tl * 1024 + 512:tl * 1024 + 1024],
               start=(t == 0), stop=(t == NTT - 1), sem=None)
            v_slmm = mm(c, P[7][0:64, 0:1], A('alpha').ap()[:, t * 64:(t + 1) * 64], A('onesb')[:, 0:1],
                        start=(t == 0), stop=(t == NTT - 1), sem='mm')
    c.wait('sync', 'als', v_alsp)
    c.dmat('sync', A('alphaT')[:, :], Dr('alp_d')[:, :], 'alt2')
    c.wait('vector', 'mm', v_slmm)
    vcopy(c, A('slin')[:, 0:512], P[5][0:64, :])
    vcopy(c, A('slin')[:, 512:1024], P[6][0:64, :])
    v_sl = vcopy(c, A('slin')[:, 1024:1025], P[7][0:64, 0:1], sem='vv')
    c.wait('gpsimd', 'vv', v_sl)
    v_slw = c.dma('gpsimd', Dr('sl_in')[:, :], A('slin')[:, 0:1025], 'slw')
    c.wait('gpsimd', 'slw', v_slw)
    v_cc2 = c.op('gpsimd', lambda e: e.collective_compute(
        "AllReduce", ALU.add, replica_groups=GROUPS,
        ins=[c.dr['sl_in'].opt()], outs=[c.dr['sl_out'].opt()]), sem='cc')
    c.wait('gpsimd', 'cc', v_cc2)


def _p6(c, P, A, Dr, ph):
    ph.sb('slot', (64, 1040), F32)
    ph.sb('sif', (64, 1024), F32)
    ph.sb('sibb', (64, 1024), BF16)
    ph.sb('sitT', (128, 8 * 64), BF16)
    ph.sb('siT', (128, 8 * 64), BF16)
    ph.sb('wres', (128, 8 * 3072), BF16)
    ph.sb('wqs', (128, 8 * 256), BF16)
    ph.sb('wks', (128, 8 * 256), BF16)
    ph.sb('gi', (64, 3072), F32)
    ph.sb('gh', (64, 3072), F32)
    ph.sb('gt', (64, 1024), F32)
    ph.sb('snew', (64, 1024), F32)
    ph.sb('snb', (64, 1024), BF16)
    ph.sb('snT', (128, 8 * 64), BF16)
    ph.sb('qf', (128, 2 * 64), BF16)
    ph.sb('kf', (128, 2 * 64), BF16)
    ph.sb('vtok', (64, 1024), BF16)
    ph.sb('att', (64, 64), F32)
    ph.sb('attT', (64, 64), BF16)
    ph.sb('sc2', (64, 1024), F32)
    ph.sb('yv', (64, 1024), F32)
    ph.sb('ylnT', (128, 8 * 64), BF16)
    ph.sb('svf', (128, 8 * 64), BF16)
    ph.sb('svw', (64, 1024), BF16)
    ph.sb('gb64', (64, 1024), F32)
    ph.sb('bo64', (64, 1024), F32)
    ph.sb('bihb', (1, 3072), BF16)
    ph.sb('bhhb', (1, 3072), BF16)
    ph.sb('bvb', (1, 1024), BF16)
    ph.sb('o64r', (1, 64), BF16)
    ph.sb('lnbb', (128, 8), BF16)
    ph.sb('bcol2', (128, 8), F32)
    ph.sb('bvpc', (128, 8), F32)
    ph.sb('lngc', (128, 8), F32)
    ph.sb('bqc', (128, 2), F32)
    ph.sb('bkc', (128, 2), F32)
    ph.sb('bias2', (128, 8), F32)
    ph.sb('xmt3', (128, 1024), F32)
    ph.sb('yout', (128, 1024), F32)
    ph.sb('t64', (64, 16), F32)
    ph.sb('borow', (1, 1024), F32)
    ph.sb('identS2', (128, 128), F32)

    c.dma('sync', A('slot')[:, 0:1025], Dr('sl_out')[:, :], 'c6')
    c.dma('sync', A('sif')[:, :], Dr('slot_init')[:, :], 'c6')
    c.dma('sync', A('bcol2')[:, 0:8], Dr('lnb_col')[:, :], 'c6')
    c.dma('sync', A('bvpc')[:, 0:8], Dr('bvp_col')[:, :], 'c6')
    c.dma('sync', A('lngc')[:, 0:8], Dr('lng_col')[:, :], 'c6')
    c.dma('sync', A('bqc')[:, 0:2], Dr('bq_col')[:, :], 'c6')
    c.dma('sync', A('bkc')[:, 0:2], Dr('bk_col')[:, :], 'c6')
    c.dma('sync', A('identS2')[:, :], Dr('identf')[:, :], 'c6')
    c.dma('sync', A('borow')[:, :], Dr('bo_row')[:, :], 'c6')
    for k in range(8):
        c.dma('gpsimd', A('sitT')[:, k * 64:(k + 1) * 64], Dr('slot_initT')[k * 128:(k + 1) * 128, :], 'c6g')
    for k in range(8):
        c.dma('gpsimd', A('wqs')[:, k * 256:(k + 1) * 256], Dr('wqf')[k * 128:(k + 1) * 128, :], 'c6g')
        c.dma('gpsimd', A('wks')[:, k * 256:(k + 1) * 256], Dr('wkf')[k * 128:(k + 1) * 128, :], 'c6g')
    for k in range(8):
        c.dma('gpsimd', A('wres')[:, k * 3072:(k + 1) * 3072], Dr('wihT')[k * 128:(k + 1) * 128, :], 'wih')
    v_wih = c.cnt['wih']
    v_o64r = vmemset(c, A('o64r')[:, :], 1.0, sem='vv')

    c.dma('gpsimd', A('bihb')[:, :], Dr('bih_row')[:, :], 'c6g')
    c.dma('gpsimd', A('bhhb')[:, :], Dr('bhh_row')[:, :], 'c6g')
    v_c6g = c.dma('gpsimd', A('bvb')[:, :], Dr('bv_row')[:, :], 'c6g')
    v_bihb = v_c6g; v_bhhb = v_c6g; v_bvb = 0
    c.wait('vector', 'c6', c.cnt['c6'])
    v_lnbb = vcopy(c, A('lnbb')[:, :], A('bcol2')[:, 0:8], sem='vv')

    v_dn0 = ts(c, A('slot')[:, 1024:1025], A('slot')[:, 1024:1025], 1e-8, ALU.add, sem='vv')
    c.wait('scalar', 'vv', v_dn0)
    v_dn1 = sinv(c, A('slot').ap()[:, 1025:1026], A('slot').ap()[:, 1024:1025], -1.0, sem='ss')
    c.wait('vector', 'ss', v_dn1)
    ts(c, A('slot')[:, 0:1024], A('slot')[:, 0:1024], A('slot').ap()[:, 1025:1026], ALU.mult)
    v_sib = vcopy(c, A('sibb')[:, :], A('slot')[:, 0:1024], sem='vv')
    c.wait('tensor', 'vv', v_sib)
    v_tr = 0
    for k in range(8):
        v_trm = c.op('tensor', lambda e, o=P[4][:, 0:64], i=A('slot').ap()[:, k * 128:(k + 1) * 128],
                     idt=A('identS2')[0:64, 0:64]: e.transpose(o, i, idt), sem='mm')
        c.wait('vector', 'mm', v_trm)
        v_tr = vcopy(c, A('siT')[:, k * 64:(k + 1) * 64], P[4][:, 0:64], sem='vv')
        c.wait('tensor', 'vv', v_tr)
    c.wait('tensor', 'wih', v_wih)
    c.wait('tensor', 'c6g', v_bihb)
    c.wait('tensor', 'vv', v_o64r)
    v_giev = 0
    for n in range(6):
        bk = n % 2
        for k in range(8):
            mm(c, P[bk][0:64, :], A('siT')[:, k * 64:(k + 1) * 64],
               A('wres')[:, k * 3072 + n * 512:k * 3072 + (n + 1) * 512],
               start=(k == 0), stop=False, sem=None)
        v_gimm = mm(c, P[bk][0:64, :], A('o64r')[0:1, 0:64],
                    A('bihb')[0:1, n * 512:(n + 1) * 512], start=False, stop=True, sem='mm')
        c.wait('vector', 'mm', v_gimm)
        v_giev = vcopy(c, A('gi')[:, n * 512:(n + 1) * 512], P[bk][0:64, :], sem='vv')
        c.wait('tensor', 'vv', v_giev)
    c.wait('gpsimd', 'mm', v_gimm)
    for k in range(8):
        c.dma('gpsimd', A('wres')[:, k * 3072:(k + 1) * 3072], Dr('whhT')[k * 128:(k + 1) * 128, :], 'wih')
    v_whh = c.cnt['wih']
    c.wait('tensor', 'wih', v_whh)
    c.wait('tensor', 'c6g', v_c6g)
    for n in range(6):
        bk = n % 2
        for k in range(8):
            mm(c, P[bk][0:64, :], A('sitT')[:, k * 64:(k + 1) * 64],
               A('wres')[:, k * 3072 + n * 512:k * 3072 + (n + 1) * 512],
               start=(k == 0), stop=False, sem=None)
        v_ghmm = mm(c, P[bk][0:64, :], A('o64r')[0:1, 0:64],
                    A('bhhb')[0:1, n * 512:(n + 1) * 512], start=False, stop=True, sem='mm')
        c.wait('vector', 'mm', v_ghmm)
        v_ghev = vcopy(c, A('gh')[:, n * 512:(n + 1) * 512], P[bk][0:64, :], sem='vv')
        c.wait('tensor', 'vv', v_ghev)
    c.wait('gpsimd', 'mm', v_ghmm)
    for k in range(8):
        c.dma('gpsimd', A('wres')[:, k * 1024:(k + 1) * 1024], Dr('wvp')[k * 128:(k + 1) * 128, :], 'wih')
        c.dma('gpsimd', A('wres')[:, 8192 + k * 1024:8192 + (k + 1) * 1024], Dr('wvf')[k * 128:(k + 1) * 128, :], 'wih')
        c.dma('gpsimd', A('wres')[:, 16384 + k * 1024:16384 + (k + 1) * 1024], Dr('wof')[k * 128:(k + 1) * 128, :], 'wih')
    v_wv3 = c.cnt['wih']
    tt(c, A('gt')[:, :], A('gi')[:, 0:1024], A('gh')[:, 0:1024], ALU.add)
    v_g1 = c.op('vector', lambda e: e.nop(), sem='vv')
    c.wait('scalar', 'vv', v_g1)
    v_r = act(c, A('gh')[:, 0:1024], A('gt')[:, :], AF.Sigmoid, sem='ss')
    tt(c, A('gt')[:, :], A('gi')[:, 1024:2048], A('gh')[:, 1024:2048], ALU.add)
    v_g2 = c.op('vector', lambda e: e.nop(), sem='vv')
    c.wait('scalar', 'vv', v_g2)
    v_z = act(c, A('gh')[:, 1024:2048], A('gt')[:, :], AF.Sigmoid, sem='ss')
    c.wait('vector', 'ss', v_z)
    tt(c, A('gt')[:, :], A('gh')[:, 0:1024], A('gh')[:, 2048:3072], ALU.mult)
    v_g3 = tt(c, A('gt')[:, :], A('gt')[:, :], A('gi')[:, 2048:3072], ALU.add, sem='vv')
    c.wait('scalar', 'vv', v_g3)
    v_n = act(c, A('gt')[:, :], A('gt')[:, :], AF.Tanh, sem='ss')
    c.wait('vector', 'ss', max(v_n, v_z))
    tt(c, A('snew')[:, :], A('sif')[:, :], A('gt')[:, :], ALU.subtract)
    tt(c, A('snew')[:, :], A('gh')[:, 1024:2048], A('snew')[:, :], ALU.mult)
    tt(c, A('snew')[:, :], A('gt')[:, :], A('snew')[:, :], ALU.add)
    v_snb = vcopy(c, A('snb')[:, :], A('snew')[:, :], sem='vv')
    c.wait('tensor', 'vv', v_snb)
    for k in range(8):
        v_trm = c.op('tensor', lambda e, o=P[4][:, 0:64], i=A('snew').ap()[:, k * 128:(k + 1) * 128],
                     idt=A('identS2')[0:64, 0:64]: e.transpose(o, i, idt), sem='mm')
        c.wait('vector', 'mm', v_trm)
        v_tr = vcopy(c, A('snT')[:, k * 64:(k + 1) * 64], P[4][:, 0:64], sem='vv')
        c.wait('tensor', 'vv', v_tr)
    c.wait('scalar', 'c6', c.cnt['c6'])
    v_qf = 0; v_kf = 0
    for m in range(2):
        c.wait('tensor', 'vv', max(v_qf, v_kf))
        for k in range(8):
            v_qmm = mm(c, P[0][:, 0:64], A('wqs')[:, k * 256 + m * 128:k * 256 + (m + 1) * 128],
                       A('snT')[:, k * 64:(k + 1) * 64], start=(k == 0), stop=(k == 7), sem='mm')
        c.wait('vector', 'mm', v_qmm)
        v_qf = ts(c, A('qf')[:, m * 64:(m + 1) * 64], P[0][:, 0:64],
                  A('bqc').ap()[:, m:m + 1], ALU.add, sem='vv')
        for k in range(8):
            v_kmm = mm(c, P[1][:, 0:64], A('wks')[:, k * 256 + m * 128:k * 256 + (m + 1) * 128],
                       A('snT')[:, k * 64:(k + 1) * 64], start=(k == 0), stop=(k == 7), sem='mm')
        c.wait('vector', 'mm', v_kmm)
        v_kf = ts(c, A('kf')[:, m * 64:(m + 1) * 64], P[1][:, 0:64],
                  A('bkc').ap()[:, m:m + 1], ALU.add, sem='vv')
    c.wait('tensor', 'wih', v_wv3)
    for n in range(2):
        for k in range(8):
            mm(c, P[2][0:64, :], A('snT')[:, k * 64:(k + 1) * 64],
               A('wres')[:, 8192 + k * 1024 + n * 512:8192 + k * 1024 + (n + 1) * 512],
               start=(k == 0), stop=False, sem=None)
        v_vmm = mm(c, P[2][0:64, :], A('o64r')[0:1, 0:64], A('bvb')[0:1, n * 512:(n + 1) * 512],
                   start=False, stop=True, sem='mm')
        c.wait('vector', 'mm', v_vmm)
        v_vt = vcopy(c, A('vtok')[:, n * 512:(n + 1) * 512], P[2][0:64, :], sem='vv')
        c.wait('tensor', 'vv', v_vt)
    c.wait('tensor', 'vv', max(v_qf, v_kf))
    for k in range(2):
        v_amm = mm(c, P[3][0:64, 0:64], A('qf')[:, k * 64:(k + 1) * 64], A('kf')[:, k * 64:(k + 1) * 64],
                   start=(k == 0), stop=(k == 1), sem='mm')
    c.wait('vector', 'mm', v_amm)
    ts(c, A('att')[:, :], P[3][0:64, 0:64], DQ ** (-0.5), ALU.mult)
    vreduce(c, A('t64').ap()[:, 0:1], A('att').ap()[:, :], ALU.max)
    v_ng = ts(c, A('t64')[:, 0:1], A('t64')[:, 0:1], -1.0, ALU.mult, sem='vv')
    c.wait('scalar', 'vv', v_ng)
    v_ex = act(c, A('att')[:, :], A('att')[:, :], AF.Exp, bias=A('t64').ap()[:, 0:1], sem='ss')
    c.wait('vector', 'ss', v_ex)
    v_sm = vreduce(c, A('t64').ap()[:, 1:2], A('att').ap()[:, :], ALU.add, sem='vv')
    c.wait('scalar', 'vv', v_sm)
    v_rq = sinv(c, A('t64').ap()[:, 2:3], A('t64').ap()[:, 1:2], -1.0, sem='ss')
    c.wait('vector', 'ss', v_rq)
    v_ab = ts(c, A('att')[:, :], A('att')[:, :], A('t64').ap()[:, 2:3], ALU.mult, sem='vv')
    c.wait('tensor', 'vv', v_ab)
    v_att = c.op('tensor', lambda e, o=P[3][0:64, 64:128], i=A('att')[:, :],
                 idt=A('identS2')[0:64, 0:64]: e.transpose(o, i, idt), sem='mm')
    c.wait('vector', 'mm', v_att)
    v_atT = vcopy(c, A('attT')[:, :], P[3][0:64, 64:128], sem='vv')
    c.wait('tensor', 'vv', v_atT)
    v_scv = 0
    for n in range(2):
        c.wait('tensor', 'vv', v_scv)
        v_scm = mm(c, P[5][0:64, :], A('attT')[:, :], A('vtok')[:, n * 512:(n + 1) * 512],
                   start=True, stop=True, sem='mm')
        c.wait('vector', 'mm', v_scm)
        v_scv = vcopy(c, A('sc2')[:, n * 512:(n + 1) * 512], P[5][0:64, :], sem='vv')
    v_y = tt(c, A('yv')[:, :], A('snew')[:, :], A('sc2')[:, :], ALU.add, sem='vv')
    c.wait('scalar', 'vv', v_y)
    act(c, A('yv')[:, :], A('yv')[:, :], AF.Copy, accum_out=A('t64').ap()[:, 3:4])
    v_sacc = act(c, A('gt')[:, :], A('yv')[:, :], AF.Square, accum_out=A('t64').ap()[:, 4:5], sem='ss')
    c.wait('vector', 'ss', v_sacc)
    ts(c, A('t64')[:, 5:6], A('t64').ap()[:, 3:4], 1.0 / D, ALU.mult)
    ts(c, A('t64')[:, 6:7], A('t64').ap()[:, 4:5], 1.0 / D, ALU.mult)
    tt(c, A('t64')[:, 7:8], A('t64')[:, 5:6], A('t64')[:, 5:6], ALU.mult)
    tt(c, A('t64')[:, 7:8], A('t64')[:, 6:7], A('t64')[:, 7:8], ALU.subtract)
    v_v6 = ts(c, A('t64')[:, 7:8], A('t64')[:, 7:8], EPS, ALU.add, sem='vv')
    c.wait('scalar', 'vv', v_v6)
    v_s6 = sinv(c, A('t64').ap()[:, 10:11], A('t64').ap()[:, 7:8], -0.5, sem='ss')
    c.wait('vector', 'ss', v_s6)
    stt(c, A('t64')[:, 11:12], A('t64').ap()[:, 5:6], -1.0, A('t64').ap()[:, 10:11], ALU.mult, ALU.mult)
    v_yln = ts(c, A('sc2')[:, :], A('yv')[:, :], A('t64').ap()[:, 11:12], ALU.add,
               A('t64').ap()[:, 10:11], ALU.mult, sem='vv')
    c.wait('vector', 'wih', v_wv3)
    c.wait('vector', 'c6', c.cnt['c6'])
    for k in range(8):
        v_wvf = ts(c, A('wres')[:, k * 1024:(k + 1) * 1024], A('wres')[:, k * 1024:(k + 1) * 1024],
                   A('lngc').ap()[:, k:k + 1], ALU.mult, sem='vv')
    c.wait('tensor', 'vv', max(v_wvf, v_lnbb, v_yln))
    for m in range(8):
        for k in range(8):
            v_b2mm = mm(c, P[6][:, m:m + 1], A('wres')[:, k * 1024 + m * 128:k * 1024 + (m + 1) * 128],
                        A('lnbb')[:, k:k + 1], start=(k == 0), stop=(k == 7), sem='mm')
    c.wait('vector', 'mm', v_b2mm)
    vcopy(c, A('bias2')[:, 0:8], P[6][:, 0:8])
    v_bi2 = tt(c, A('bias2')[:, 0:8], A('bias2')[:, 0:8], A('bvpc')[:, 0:8], ALU.add, sem='vv')
    c.wait('tensor', 'vv', v_yln)
    for k in range(8):
        v_trm = c.op('tensor', lambda e, o=P[4][:, 0:64], i=A('sc2').ap()[:, k * 128:(k + 1) * 128],
                     idt=A('identS2')[0:64, 0:64]: e.transpose(o, i, idt), sem='mm')
        c.wait('vector', 'mm', v_trm)
        v_tr = vcopy(c, A('ylnT')[:, k * 64:(k + 1) * 64], P[4][:, 0:64], sem='vv')
        c.wait('tensor', 'vv', v_tr)
    c.wait('scalar', 'vv', v_bi2)
    v_sv = 0
    for m in range(8):
        c.wait('tensor', 'vv', v_sv)
        for k in range(8):
            v_svm = mm(c, P[7][:, 0:64], A('wres')[:, k * 1024 + m * 128:k * 1024 + (m + 1) * 128],
                       A('ylnT')[:, k * 64:(k + 1) * 64], start=(k == 0), stop=(k == 7), sem='mm')
        c.wait('vector', 'mm', v_svm)
        v_sv = ts(c, A('svf')[:, m * 64:(m + 1) * 64], P[7][:, 0:64],
                  A('bias2').ap()[:, m:m + 1], ALU.add, sem='vv')
    c.wait('tensor', 'gr0', c.cnt.get('gr0', 0))
    c.wait('tensor', 'vv', v_bi2)
    v_bom = mm(c, P[6][0:64, 0:512], A('onesf')[0:1, 0:64], A('borow')[0:1, 0:512], start=True, stop=True, sem='mm')
    v_bom2 = mm(c, P[6][0:64, 0:512] if False else P[5][0:64, 0:512], A('onesf')[0:1, 0:64], A('borow')[0:1, 512:1024], start=True, stop=True, sem='mm')
    c.wait('vector', 'mm', v_bom2)
    vcopy(c, A('bo64')[:, 0:512], P[6][0:64, 0:512])
    v_bo = vcopy(c, A('bo64')[:, 512:1024], P[5][0:64, 0:512], sem='vv')
    c.wait('tensor', 'vv', v_bo)
    v_gbm = mm(c, P[6][0:64, 0:512], A('onesf')[0:1, 0:64], A('garow')[0:1, 1024:1536], start=True, stop=True, sem='mm')
    v_gbm2 = mm(c, P[5][0:64, 0:512], A('onesf')[0:1, 0:64], A('garow')[0:1, 1536:2048], start=True, stop=True, sem='mm')
    c.wait('vector', 'mm', v_gbm2)
    vcopy(c, A('gb64')[:, 0:512], P[6][0:64, 0:512])
    v_gb = vcopy(c, A('gb64')[:, 512:1024], P[5][0:64, 0:512], sem='vv')
    c.wait('tensor', 'vv', max(v_sv, v_gb))
    for n in range(2):
        for k in range(8):
            v_svwm = mm(c, P[3][0:64, :], A('svf')[:, k * 64:(k + 1) * 64],
                        A('wres')[:, 16384 + k * 1024 + n * 512:16384 + k * 1024 + (n + 1) * 512],
                        start=(k == 0), stop=(k == 7), sem='mm')
        c.wait('vector', 'mm', v_svwm)
        stt(c, A('sc2')[:, n * 512:(n + 1) * 512], P[3][0:64, :], 1.0,
            A('bo64').ap()[:, n * 512:(n + 1) * 512], ALU.mult, ALU.add)
        v_svw = tt(c, A('svw')[:, n * 512:(n + 1) * 512], A('sc2')[:, n * 512:(n + 1) * 512],
                   A('gb64')[:, n * 512:(n + 1) * 512], ALU.mult, sem='vv')
        c.wait('tensor', 'vv', v_svw)
    v_om = 0; v_ost = 0
    for t in range(NTT):
        tsl = slice(t * 128, (t + 1) * 128)
        c.wait('sync', 'vv', v_ost)
        v_xm3 = c.dma('sync', A('xmt3')[:, :], Dr('xmid_d')[tsl, :], 'xm3')
        c.wait('tensor', 'vv', v_ost)
        for j in range(2):
            v_om = mm(c, P[j][:, :], A('alphaT').ap()[0:64, tsl], A('svw')[:, j * 512:(j + 1) * 512],
                      start=True, stop=True, sem='mm')
        c.wait('vector', 'mm', v_om)
        c.wait('vector', 'xm3', v_xm3)
        for j in range(2):
            v_ost = stt(c, A('yout')[:, j * 512:(j + 1) * 512], P[j][:, :], 1.0,
                        A('xmt3').ap()[:, j * 512:(j + 1) * 512], ALU.mult, ALU.add, sem='vv')
        c.wait('sync', 'vv', v_ost)
        c.dma('sync', Dr('y')[tsl, :], A('yout')[:, :], 'yo')

def _finish(nc, c, stack):
    for sname in list(c.cnt.keys()):
        c.wait('sync', sname, c.cnt.get(sname, 0))
    with nc.Block() as block:
        for name in ('sync', 'tensor', 'vector', 'scalar', 'gpsimd'):
            ops = c.ops[name]
            def runner(e, ops=ops):
                for f in ops:
                    f(e)
            getattr(block, name)(runner)
    stack.close()
    return nc, c


def kernel(**inputs):
    nc, _ = build_nc(dbg=tuple(DBG))
    in_maps = [_per_core_inputs(core, inputs) for core in range(8)]
    res = run_bass_kernel_spmd(nc, in_maps, core_ids=list(range(8)))
    out = np.zeros((4, 4096, 1024), np.float32)
    for core in range(8):
        b, u = core // 2, core % 2
        out[b, u * TC:(u + 1) * TC] = res.results[core]['y']
    return out
